# version 04
# baseline (speedup 1.0000x reference)
import numpy as np
import concourse.bass as bass
import concourse.mybir as mybir
from concourse.bass_utils import run_bass_kernel_spmd

F32 = mybir.dt.float32
BF16 = mybir.dt.bfloat16
AF = mybir.ActivationFunctionType
ALU = mybir.AluOpType
AX = mybir.AxisListType

T = 4096
D = 1024
TC = 512
NT = T // TC
NF = D // 128
NCORE = 8

SBUF_BASE = 16384
SBUF_BUDGET = 229376 - 384

ENGINES = ("sync", "gpsimd", "vector", "scalar", "tensor")

CP_ZERO, CP_EPS, CP_INVD, CP_ONE, CP_NEG, CP_EPS6, CP_EPS12, CP_EPS8, CP_M1, CP_MH, CP_EPS24, CP_SPARE = range(12)
CP_VALS = np.array([0.0, 1e-5, 1.0 / D, 1.0, -1e30, 1e-6, 1e-12, 1e-8, -1.0, -0.5, 1e-24, 0.0], np.float32)


class Prog:

    def __init__(self):
        self.nc = bass.Bass("TRN2", target_bir_lowering=False)
        self.q = {e: [] for e in ENGINES}
        self._sems = {}
        self._off = SBUF_BASE
        self._phase_base = SBUF_BASE
        self._entry_marks = []
        self._entered = set(ENGINES)

    def sbuf(self, name, shape, dtype):
        nbytes = int(np.prod(shape[1:])) * mybir.dt.size(dtype)
        t = self.nc.alloc_sbuf_tensor_at(name, shape, dtype, offset=self._off)
        self._off = (self._off + nbytes + 31) & ~31
        assert self._off <= SBUF_BUDGET, f"SBUF overflow at {name}: {self._off}"
        return t

    def mark_global(self):
        self._phase_base = self._off

    def phase_reset(self):
        self._off = self._phase_base
        self._entry_marks = [(s, rec[1]) for s, rec in self._sems.items()
                             if s in ("s_t", "s_v", "s_a", "s_o")]
        self._entered = set()

    def sem(self, name):
        if name not in self._sems:
            self._sems[name] = [self.nc.alloc_semaphore(name), 0]
        return name

    def n(self, s):
        return self._sems[s][1]

    def op(self, eng, fn, waits=(), inc=None, k=1):
        waits = list(waits)
        if eng not in self._entered:
            self._entered.add(eng)
            waits = self._entry_marks + waits
        w = [(self._sems[s][0], int(v)) for (s, v) in waits if int(v) > 0]
        entry_inc = None
        r = None
        if inc is not None:
            rec = self._sems[inc]
            rec[1] += k
            entry_inc = (rec[0], k)
            r = rec[1]
        self.q[eng].append((w, fn, entry_inc))
        return r

    def barrier(self):
        return {s: rec[1] for s, rec in self._sems.items()}

    def build(self):
        with self.nc.Block() as block:
            for e in ENGINES:
                entries = self.q[e]
                if not entries:
                    continue

                def mk(entries=entries):
                    def body(eng):
                        hi = {}
                        for w, fn, inc in entries:
                            for (h, v) in w:
                                if hi.get(h.num, -1) < v:
                                    eng.wait_ge(h, v)
                                    hi[h.num] = v
                            inst = fn(eng) if fn is not None else None
                            if inc is not None:
                                if inst is None:
                                    inst = eng.nop()
                                inst.then_inc(inc[0], inc[1])
                    return body

                getattr(block, e)(mk())
        return self.nc


def cs(c, w=TC):
    return slice(c * w, (c + 1) * w)


class Common:

    def __init__(self, p, dram):
        nc = p.nc
        self.pj = [nc.alloc_psum_tensor(f"pj{i}", [128, TC], F32) for i in range(8)]
        self.cp = p.sbuf("g_cp", [128, 12], F32)
        self.onb = p.sbuf("g_onb", [128, 1], BF16)
        self.onf = p.sbuf("g_onf", [1, 128], F32)
        self.idf = p.sbuf("g_idf", [128, 128], F32)
        self.rows = p.sbuf("g_rows", [1, 3 * TC], F32)
        self.IN = p.sem("s_in")
        self.GIN = p.sem("s_gin")
        self.GW = [p.sem("s_gw0"), p.sem("s_gw1")]
        self.TT = p.sem("s_t")
        self.VV = p.sem("s_v")
        self.AA = p.sem("s_a")
        self.OUT = p.sem("s_o")
        for dst, key in [(self.cp, "g_cp"), (self.onb, "g_onb"), (self.onf, "g_onf"),
                         (self.idf, "g_idf")]:
            p.op("sync", lambda e, d=dst, s=dram[key]: e.dma_start(out=d[:, :], in_=s),
                 inc=self.IN, k=16)
        self.init_in = p.n(self.IN)


def common_host_inputs():
    import ml_dtypes
    return {
        "g_cp": np.broadcast_to(CP_VALS, (128, 12)).copy(),
        "g_onb": np.ones((128, 1), dtype=ml_dtypes.bfloat16),
        "g_onf": np.ones((1, 128), dtype=np.float32),
        "g_idf": np.eye(128, dtype=np.float32),
    }


def emit_ln(p, cm, i, src, dst, tmp, xb, sq, gcol=None, bcol=None, guards=(), tc=TC):
    TT_, VV, AA = cm.TT, cm.VV, cm.AA
    pa, pb, pbm, pbr = cm.pj[0], cm.pj[1], cm.pj[2], cm.pj[3]
    sf = src if callable(src) else (lambda c: src[:, cs(c, tc)])
    df = dst if callable(dst) else (lambda c: dst[:, cs(c, tc)])
    v_src = p.n(VV)
    for c in range(NF):
        p.op("scalar", lambda e, c=c: e.copy(out=xb[:, cs(c, tc)], in_=sf(c)),
             waits=[(VV, v_src)] + list(guards), inc=None)
    for c in range(NF):
        p.op("scalar", lambda e, c=c: e.activation(out=sq[:, cs(c, tc)], in_=sf(c), func=AF.Square),
             inc=(AA if c == NF - 1 else None))
    a_cast = p.n(AA)
    for k in range(8):
        p.op("tensor", lambda e, k=k: e.matmul(pa[0:1, 0:tc], cm.onb[:, :], xb[:, cs(k, tc)],
                                               start=(k == 0), stop=(k == 7)),
             waits=[(AA, a_cast)] + list(guards) if k == 0 else (), inc=None)
    for k in range(8):
        p.op("tensor", lambda e, k=k: e.matmul(pb[0:1, 0:tc], cm.onb[:, :], sq[:, cs(k, tc)],
                                               start=(k == 0), stop=(k == 7)),
             inc=(TT_ if k == 7 else None))
    t_stat = p.n(TT_)
    R_mean, R_B, R_C = (cm.rows[0:1, j * TC: j * TC + tc] for j in range(3))
    R_ey2, R_msq, R_var, R_std, R_rstd = R_B, R_C, R_B, R_C, R_B
    v1 = p.op("vector", lambda e: e.tensor_scalar(out=R_mean, in0=pa[0:1, 0:tc], scalar1=cm.cp[0:1, CP_INVD:CP_INVD + 1],
                                                  scalar2=None, op0=ALU.mult),
              waits=[(TT_, t_stat)], inc=VV)
    p.op("vector", lambda e: e.tensor_scalar(out=R_ey2, in0=pb[0:1, 0:tc], scalar1=cm.cp[0:1, CP_INVD:CP_INVD + 1],
                                             scalar2=None, op0=ALU.mult), inc=VV)
    v2 = p.n(VV)
    a1 = p.op("scalar", lambda e: e.activation(out=R_msq, in_=R_mean, func=AF.Square),
              waits=[(VV, v2)], inc=AA)
    v3 = p.op("vector", lambda e: e.tensor_tensor(out=R_var, in0=R_ey2, in1=R_msq, op=ALU.subtract),
              waits=[(AA, a1)], inc=VV)
    a2 = p.op("scalar", lambda e: e.activation(out=R_std, in_=R_var, func=AF.Ln,
                                               bias=cm.cp[0:1, CP_EPS:CP_EPS + 1]),
              waits=[(VV, v3)], inc=AA)
    a3 = p.op("scalar", lambda e: e.activation(out=R_rstd, in_=R_std, func=AF.Exp,
                                               scale=cm.cp[0:1, CP_MH:CP_MH + 1]),
              waits=[(AA, a2)], inc=AA)
    p.op("tensor", lambda e: e.matmul(pbm[:, 0:tc], cm.onf[:, :], R_mean, start=True, stop=True),
         waits=[(VV, v1)], inc=None)
    p.op("tensor", lambda e: e.matmul(pbr[:, 0:tc], cm.onf[:, :], R_rstd, start=True, stop=True),
         waits=[(AA, a3)], inc=TT_)
    t_bc = p.n(TT_)
    for c in range(NF):
        p.op("vector", lambda e, c=c: e.tensor_tensor(out=tmp[:, cs(c, tc)], in0=sf(c),
                                                      in1=pbm[:, 0:tc], op=ALU.subtract),
             waits=[(TT_, t_bc)], inc=None)
        if gcol is None:
            p.op("vector", lambda e, c=c: e.tensor_tensor(out=df(c), in0=tmp[:, cs(c, tc)],
                                                          in1=pbr[:, 0:tc], op=ALU.mult),
                 inc=(VV if c == NF - 1 else None))
        else:
            p.op("vector", lambda e, c=c: e.tensor_tensor(out=tmp[:, cs(c, tc)], in0=tmp[:, cs(c, tc)],
                                                          in1=pbr[:, 0:tc], op=ALU.mult), inc=None)
            p.op("vector", lambda e, c=c: e.tensor_scalar(out=df(c), in0=tmp[:, cs(c, tc)],
                                                          scalar1=gcol[c], scalar2=bcol[c],
                                                          op0=ALU.mult, op1=ALU.add),
                 inc=(VV if c == NF - 1 else None))
    return p.n(VV)



def emit_p0(p, cm, A):
    p.phase_reset()
    s_wq = p.sbuf("p0_wq", [128, 8 * D], BF16)
    s_wv = p.sbuf("p0_wv", [128, 8 * D], BF16)
    s_wg = p.sbuf("p0_wg", [128, 8 * D], BF16)
    s_wo = p.sbuf("p0_wo", [128, 8 * D], BF16)
    xf = p.sbuf("p0_xf", [128, T], F32)
    xb = p.sbuf("p0_xb", [128, T], BF16)
    sq = p.sbuf("p0_sq", [128, T], BF16)
    qf = p.sbuf("p0_qf", [128, T], F32)
    vf = p.sbuf("p0_vf", [128, T], F32)
    stt = p.sbuf("p0_st", [128, T], F32)
    gf = p.sbuf("p0_gf", [128, T], F32)
    ub = p.sbuf("p0_ub", [128, T], BF16)
    lamt = p.sbuf("p0_lamt", [128, T], F32)
    s_lam = p.sbuf("p0_lam8", [128, 8], F32)
    s_sig = p.sbuf("p0_sig8", [128, 8], F32)
    s_g8 = p.sbuf("p0_g8", [128, 8], F32)
    s_b8 = p.sbuf("p0_b8", [128, 8], F32)
    carry = p.sbuf("p0_carry", [128, 8], F32)

    pj = cm.pj
    IN, GIN, TT_, VV, AA, OUT = cm.IN, cm.GIN, cm.TT, cm.VV, cm.AA, cm.OUT
    x0, x1 = A["x0"], A["x1"]

    for dst, src in [(s_lam, A["lamsrc"]), (s_g8, A["g8a"]), (s_b8, A["b8a"])]:
        p.op("sync", lambda e, d=dst, s=src: e.dma_start(out=d[:, :], in_=s), inc=IN, k=16)
    in_init = p.n(IN)
    for w_sb, w_hbm in [(s_wq, A["wq"]), (s_wv, A["wv"]), (s_wg, A["wg"]), (s_wo, A["wo"])]:
        for k in range(8):
            p.op("gpsimd", lambda e, d=w_sb, s=w_hbm, k=k:
                 e.dma_start(out=d[:, cs(k, D)], in_=s[k * 128:(k + 1) * 128, :]), inc=GIN, k=16)

    a_sig = p.op("scalar", lambda e: e.activation(out=s_sig[:, :], in_=s_lam[:, :], func=AF.Sigmoid),
                 waits=[(IN, in_init)], inc=AA)
    v0 = p.op("vector", lambda e: e.memset(gf[:, 0:TC], 1.0), inc=VV)
    for c in range(NF):
        p.op("vector", lambda e, c=c: e.tensor_scalar(
            out=lamt[:, cs(c)], in0=gf[:, 0:TC], scalar1=s_sig[:, c:c + 1], scalar2=None, op0=ALU.mult),
            waits=[(VV, v0), (AA, a_sig)], inc=(VV if c == NF - 1 else None))
    p.op("vector", lambda e: e.memset(carry[:, :], 0.0), inc=VV)
    v_init = p.n(VV)

    t_end = a_end = out_end = 0
    v_end = v_init
    gcols = [s_g8[:, c:c + 1] for c in range(NF)]
    bcols = [s_b8[:, c:c + 1] for c in range(NF)]

    for i in range(NT):
        tsl = slice(i * TC, (i + 1) * TC)
        bar = [(TT_, t_end), (VV, v_end), (AA, a_end), (OUT, out_end)]
        for c in range(NF):
            p.op("sync", lambda e, c=c, tsl=tsl: e.dma_start(out=xf[:, cs(c)], in_=x0[c * 128:(c + 1) * 128, tsl]),
                 waits=bar, inc=IN, k=16)
            p.op("gpsimd", lambda e, c=c, tsl=tsl: e.dma_start(out=xb[:, cs(c)], in_=x0[c * 128:(c + 1) * 128, tsl]),
                 waits=bar, inc=GIN, k=16)
        in_x = p.n(IN)
        gin_x = p.n(GIN)

        a_evac0 = p.n(AA)
        t_grp = []
        for g24 in range(24):
            m = g24 % 8
            wsb = (s_wq, s_wv, s_wg)[g24 // 8]
            pjt = pj[4 + g24 % 4]
            w = [(GIN, gin_x)]
            if g24 >= 4:
                w.append((AA, a_evac0 + g24 - 3))
            for k in range(8):
                p.op("tensor", lambda e, pjt=pjt, wsb=wsb, k=k, m=m: e.matmul(
                    pjt[:, :], wsb[:, k * D + m * 128: k * D + (m + 1) * 128], xb[:, cs(k)],
                    start=(k == 0), stop=(k == 7)),
                    waits=w if k == 0 else (), inc=(TT_ if k == 7 else None))
            t_grp.append(p.n(TT_))
        for g24 in range(24):
            m = g24 % 8
            pjt = pj[4 + g24 % 4]
            w = [(TT_, t_grp[g24])]
            if g24 < 8:
                fn = lambda e, pjt=pjt, m=m: e.copy(out=qf[:, cs(m)], in_=pjt[:, :])
            elif g24 < 16:
                fn = lambda e, pjt=pjt, m=m: e.copy(out=vf[:, cs(m)], in_=pjt[:, :])
            else:
                fn = lambda e, pjt=pjt, m=m: e.activation(out=gf[:, cs(m)], in_=pjt[:, :], func=AF.Silu)
                w.append((OUT, out_end))
            p.op("scalar", fn, waits=w, inc=AA)
        a_evacs = p.n(AA)

        v_pre = p.n(VV)
        for c in range(NF):
            p.op("vector", lambda e, c=c: e.tensor_tensor_scan(
                out=stt[:, cs(c)], data0=lamt[:, cs(c)], data1=vf[:, cs(c)],
                initial=carry[:, c:c + 1], op0=ALU.mult, op1=ALU.add),
                waits=[(AA, a_evac0 + 16), (VV, v_init if i == 0 else v_pre)],
                inc=(VV if c == NF - 1 else None))
        v_scan = p.n(VV)
        for c in range(NF):
            p.op("vector", lambda e, c=c: e.tensor_copy(out=carry[:, c:c + 1],
                                                        in_=stt[:, c * TC + TC - 1: c * TC + TC]),
                 waits=[(VV, v_scan)], inc=(VV if c == NF - 1 else None))
        v_carry = p.n(VV)
        for c in range(NF):
            p.op("vector", lambda e, c=c: e.tensor_tensor(
                out=vf[:, cs(c)], in0=qf[:, cs(c)], in1=stt[:, cs(c)], op=ALU.mult),
                waits=[(VV, v_carry), (AA, a_evac0 + 8)], inc=(VV if c == NF - 1 else None))

        v_ln1 = emit_ln(p, cm, i, vf, stt, stt, xb, sq, gcol=gcols, bcol=bcols,
                        guards=[(AA, a_evacs)])

        for c in range(NF):
            p.op("vector", lambda e, c=c: e.tensor_tensor(out=ub[:, cs(c)], in0=stt[:, cs(c)],
                                                          in1=gf[:, cs(c)], op=ALU.mult),
                 waits=[(VV, v_ln1), (AA, a_evacs)], inc=(VV if c == NF - 1 else None))
        v_u = p.n(VV)

        t_wo = []
        for m in range(8):
            w = [(VV, v_u)]
            if m >= 4:
                w.append((VV, v_u + (m - 4) + 1))
            for k in range(8):
                p.op("tensor", lambda e, m=m, k=k: e.matmul(
                    pj[4 + m % 4][:, :], s_wo[:, k * D + m * 128: k * D + (m + 1) * 128], ub[:, cs(k)],
                    start=(k == 0), stop=(k == 7)),
                    waits=w if k == 0 else (), inc=(TT_ if k == 7 else None))
            t_wo.append(p.n(TT_))
        for m in range(8):
            p.op("vector", lambda e, m=m: e.scalar_tensor_tensor(
                out=qf[:, cs(m)], in0=pj[4 + m % 4][:, :], scalar=cm.cp[:, CP_ZERO:CP_ZERO + 1],
                in1=xf[:, cs(m)], op0=ALU.bypass, op1=ALU.add),
                waits=[(TT_, t_wo[m]), (IN, in_x)], inc=VV)

        v_x1 = emit_ln(p, cm, i, qf, gf, stt, xb, sq)

        for c in range(NF):
            p.op("sync", lambda e, c=c, tsl=tsl: e.dma_start(out=x1[c * 128:(c + 1) * 128, tsl], in_=gf[:, cs(c)]),
                 waits=[(VV, v_x1)], inc=OUT, k=16)

        t_end, v_end, a_end, out_end = p.n(TT_), p.n(VV), p.n(AA), p.n(OUT)



def emit_p1(p, cm, A):
    p.phase_reset()
    NE = 32
    s_w1 = p.sbuf("p1_w1", [128, 8 * 4096], BF16)
    s_rwt = p.sbuf("p1_rwt", [128, 8 * 16], F32)
    s_rb = p.sbuf("p1_rb", [16, 1], F32)
    s_b1 = p.sbuf("p1_b1", [128, NE], F32)
    s_b2 = p.sbuf("p1_b2", [128, 8], F32)
    s_sel = p.sbuf("p1_sel", [16, 16 * 128], BF16)
    xf = p.sbuf("p1_xf", [128, T], F32)
    xb = p.sbuf("p1_xb", [128, T], BF16)
    sq = p.sbuf("p1_sq", [128, T], BF16)
    hb = p.sbuf("p1_hb", [128, NE * TC], BF16)
    w2b = p.sbuf("p1_w2b", [128, 2 * D], BF16)
    zf = p.sbuf("p1_zf", [128, T], F32)
    tmp = p.sbuf("p1_tmp", [128, T], F32)
    rl = p.sbuf("p1_rl", [16, TC], F32)
    rlT = p.sbuf("p1_rlT", [128, 64], F32)
    work = p.sbuf("p1_work", [128, 64], F32)
    mtm = p.sbuf("p1_mtm", [128, 64], F32)
    mcol = p.sbuf("p1_mcol", [128, 1], F32)
    eqt = p.sbuf("p1_eqt", [128, 16], F32)
    mbf = p.sbuf("p1_mbf", [16, TC], BF16)

    pj = cm.pj
    IN, GIN, TT_, VV, AA, OUT = cm.IN, cm.GIN, cm.TT, cm.VV, cm.AA, cm.OUT
    GW = cm.GW
    x1, x2, w2t = A["x1"], A["x2"], A["w2t"]

    for dst, src in [(s_rwt, A["rwt"]), (s_rb, A["rb16"]), (s_b1, A["b1c"]), (s_b2, A["b2c"])]:
        p.op("sync", lambda e, d=dst, s=src: e.dma_start(out=d[:, :], in_=s), inc=IN, k=16)
    p.op("gpsimd", lambda e: e.dma_start(out=s_sel[:, :], in_=A["sel16s"]), inc=GIN, k=16)
    in_init = p.n(IN)
    for k in range(8):
        p.op("gpsimd", lambda e, k=k: e.dma_start(out=s_w1[:, k * 4096:(k + 1) * 4096],
                                                  in_=A["w1t"][k * 128:(k + 1) * 128, :]), inc=GIN, k=16)

    t_end, v_end, a_end, out_end = p.n(TT_), p.n(VV), p.n(AA), p.n(OUT)

    for i in range(NT):
        tsl = slice(i * TC, (i + 1) * TC)
        bar = [(TT_, t_end), (VV, v_end), (AA, a_end), (OUT, out_end)]
        for c in range(NF):
            p.op("sync", lambda e, c=c, tsl=tsl: e.dma_start(out=xf[:, cs(c)], in_=x1[c * 128:(c + 1) * 128, tsl]),
                 waits=bar, inc=IN, k=16)
            p.op("gpsimd", lambda e, c=c, tsl=tsl: e.dma_start(out=xb[:, cs(c)], in_=x1[c * 128:(c + 1) * 128, tsl]),
                 waits=bar, inc=GIN, k=16)
        in_x = p.n(IN)
        gin_x = p.n(GIN)

        for k in range(8):
            p.op("tensor", lambda e, k=k: e.matmul(pj[4][0:16, :], s_rwt[:, k * 16:(k + 1) * 16],
                                                   xf[:, cs(k)], start=(k == 0), stop=(k == 7)),
                 waits=[(IN, in_x)] + bar if k == 0 else (), inc=(TT_ if k == 7 else None))
        t_rl = p.n(TT_)
        v_rl = p.op("vector", lambda e: e.tensor_scalar(out=rl[:, :], in0=pj[4][0:16, :],
                                                        scalar1=s_rb[:, :], scalar2=None, op0=ALU.add),
                    waits=[(TT_, t_rl)] + bar, inc=VV)
        for j in range(4):
            p.op("tensor", lambda e, j=j: e.transpose(pj[5][:, j * 16:(j + 1) * 16],
                                                      rl[0:16, j * 128:(j + 1) * 128], cm.idf[0:16, 0:16]),
                 waits=[(VV, v_rl)] if j == 0 else (), inc=(TT_ if j == 3 else None))
        t_tp = p.n(TT_)
        v_t0 = p.op("vector", lambda e: e.tensor_copy(out=rlT[:, :], in_=pj[5][:, 0:64]),
                    waits=[(TT_, t_tp)], inc=VV)
        for j in range(4):
            blk = rlT[:, j * 16:(j + 1) * 16]
            wblk = work[:, j * 16:(j + 1) * 16]
            last = p.n(VV)
            for r in range(3):
                srcb = blk if r == 0 else wblk
                v_m = p.op("vector", lambda e, srcb=srcb: e.reduce_max(out=mcol[:, :], in_=srcb, axis=AX.X),
                           waits=[(VV, last)], inc=VV)
                v_e = p.op("vector", lambda e, srcb=srcb: e.tensor_scalar(
                    out=eqt[:, :], in0=srcb, scalar1=mcol[:, :], scalar2=None, op0=ALU.is_ge),
                    waits=[(VV, v_m)], inc=VV)
                last = p.op("vector", lambda e, srcb=srcb, wblk=wblk: e.scalar_tensor_tensor(
                    out=wblk, in0=eqt[:, :], scalar=cm.cp[:, CP_NEG:CP_NEG + 1], in1=srcb,
                    op0=ALU.mult, op1=ALU.add),
                    waits=[(VV, v_e)], inc=VV)
            v_m4 = p.op("vector", lambda e, wblk=wblk: e.reduce_max(out=mcol[:, :], in_=wblk, axis=AX.X),
                        waits=[(VV, last)], inc=VV)
            p.op("vector", lambda e, blk=blk, j=j: e.tensor_scalar(
                out=mtm[:, j * 16:(j + 1) * 16], in0=blk, scalar1=mcol[:, :], scalar2=None, op0=ALU.is_ge),
                waits=[(VV, v_m4)], inc=VV)
        v_mask = p.n(VV)
        for j in range(4):
            p.op("tensor", lambda e, j=j: e.transpose(pj[4][0:16, j * 128:(j + 1) * 128],
                                                      mtm[:, j * 16:(j + 1) * 16], cm.idf[:, :]),
                 waits=[(VV, v_mask)] if j == 0 else (), inc=(TT_ if j == 3 else None))
        t_mt = p.n(TT_)
        v_mbf = p.op("vector", lambda e: e.tensor_copy(out=mbf[:, :], in_=pj[4][0:16, 0:TC]),
                     waits=[(TT_, t_mt)], inc=VV)
        if i == 0 and "dbg_rl" in A:
            p.op("sync", lambda e: e.dma_start(out=A["dbg_rl"], in_=rl[:, :]),
                 waits=[(VV, v_mbf)], inc=OUT, k=16)
            p.op("sync", lambda e: e.dma_start(out=A["dbg_rlT"], in_=rlT[:, :]),
                 waits=[(VV, v_mbf)], inc=OUT, k=16)
            p.op("sync", lambda e: e.dma_start(out=A["dbg_mtm"], in_=mtm[:, :]),
                 waits=[(VV, v_mbf)], inc=OUT, k=16)
            p.op("sync", lambda e: e.dma_start(out=A["dbg_mbf"], in_=mbf[:, :]),
                 waits=[(VV, v_mbf)], inc=OUT, k=16)
            out_end = p.n(OUT)

        a_h0 = p.n(AA)
        v_h0 = p.n(VV)
        t_hgrp = []
        t_bc = []
        for c in range(NE):
            pjt = pj[c % 4]
            w = [(GIN, gin_x)]
            if c >= 4:
                w.append((AA, a_h0 + (c - 4) + 1))
            for k in range(8):
                p.op("tensor", lambda e, pjt=pjt, c=c, k=k: e.matmul(
                    pjt[:, :], s_w1[:, k * 4096 + c * 128: k * 4096 + (c + 1) * 128], xb[:, cs(k)],
                    start=(k == 0), stop=(k == 7)),
                    waits=w if k == 0 else (), inc=None)
            wb = [(VV, v_mbf)]
            if c >= 2:
                wb.append((VV, v_mbf + (c - 2) + 1))
            p.op("tensor", lambda e, c=c: e.matmul(
                pj[4 + c % 2][:, :], s_sel[:, (c // 2) * 128:(c // 2 + 1) * 128], mbf[:, :],
                start=True, stop=True),
                waits=wb, inc=TT_)
            t_bc.append(p.n(TT_))
        for c in range(NE):
            p.op("scalar", lambda e, c=c: e.activation(out=hb[:, cs(c)], in_=pj[c % 4][:, :],
                                                       func=AF.Relu, bias=s_b1[:, c:c + 1]),
                 waits=[(TT_, t_bc[c])], inc=AA)
            p.op("vector", lambda e, c=c: e.tensor_tensor(out=hb[:, cs(c)], in0=hb[:, cs(c)],
                                                          in1=pj[4 + c % 2][:, :], op=ALU.mult),
                 waits=[(AA, a_h0 + c + 1)], inc=VV)
        v_hdone = p.n(VV)
        a_hdone = p.n(AA)
        if i == 0 and "dbg_hb" in A:
            p.op("sync", lambda e: e.dma_start(out=A["dbg_hb"], in_=hb[:, 0:4 * TC]),
                 waits=[(VV, v_hdone)], inc=OUT, k=16)
            out_end = p.n(OUT)

        t_y0 = p.n(TT_)
        for c in range(NE):
            par = c % 2
            gw_r = p.op("gpsimd", lambda e, c=c, par=par: e.dma_start(
                out=w2b[:, par * D:(par + 1) * D], in_=w2t[c * 128:(c + 1) * 128, :]),
                waits=[(TT_, t_y0 if c < 2 else t_hgrp[c - 2])] if (i > 0 or c >= 2) else [],
                inc=GW[par], k=16)
            w = [(GW[par], gw_r), (VV, v_hdone)]
            for m in range(8):
                p.op("tensor", lambda e, c=c, par=par, m=m: e.matmul(
                    pj[m][:, :], w2b[:, par * D + m * 128: par * D + (m + 1) * 128], hb[:, cs(c)],
                    start=(c == 0), stop=(c == NE - 1)),
                    waits=w if m == 0 else (), inc=(TT_ if m == 7 else None))
            t_hgrp.append(p.n(TT_))
        t_y = p.n(TT_)
        for m in range(8):
            p.op("vector", lambda e, m=m: e.scalar_tensor_tensor(
                out=zf[:, cs(m)], in0=pj[m][:, :], scalar=s_b2[:, m:m + 1], in1=xf[:, cs(m)],
                op0=ALU.add, op1=ALU.add),
                waits=[(TT_, t_y)], inc=(VV if m == 7 else None))

        v_x2 = emit_ln(p, cm, i, zf, tmp, tmp, xb, sq)
        for c in range(NF):
            p.op("sync", lambda e, c=c, tsl=tsl: e.dma_start(out=x2[c * 128:(c + 1) * 128, tsl], in_=tmp[:, cs(c)]),
                 waits=[(VV, v_x2)], inc=OUT, k=16)

        t_end, v_end, a_end, out_end = p.n(TT_), p.n(VV), p.n(AA), p.n(OUT)



def emit_p2(p, cm, A):
    p.phase_reset()
    tc = 256
    nt = T // tc
    s_wqkv = p.sbuf("p2_wqkv", [128, 8 * 3072], BF16)
    s_wout = p.sbuf("p2_wout", [128, 16 * D], BF16)
    s_e2h = p.sbuf("p2_e2h", [128, 8 * 16], BF16)
    s_h2e = p.sbuf("p2_h2e", [16, 8 * 128], BF16)
    s_ab = p.sbuf("p2_ab", [128, 16], F32)
    s_sig = p.sbuf("p2_sig", [128, 16], F32)
    lam_t = p.sbuf("p2_lamt", [128, 16 * tc], F32)
    xf = p.sbuf("p2_xf", [128, 8 * tc], F32)
    xb = p.sbuf("p2_xb", [128, 8 * tc], BF16)
    sq = p.sbuf("p2_sq", [128, 8 * tc], BF16)
    phiq = p.sbuf("p2_phiq", [128, 8 * tc], F32)
    phik = p.sbuf("p2_phik", [128, 8 * tc], F32)
    vf = p.sbuf("p2_vf", [128, 8 * tc], F32)
    kvf = p.sbuf("p2_kvf", [128, 8 * tc], F32)
    ta = p.sbuf("p2_ta", [128, 8 * tc], F32)
    tb = p.sbuf("p2_tb", [128, 8 * tc], F32)
    occ = p.sbuf("p2_occ", [128, 16 * tc], BF16)
    den = p.sbuf("p2_den", [16, 2 * tc], F32)
    denb = p.sbuf("p2_denb", [16, 2 * tc], BF16)
    carry = p.sbuf("p2_carry", [128, 32], F32)

    pj = cm.pj
    IN, GIN, TT_, VV, AA, OUT = cm.IN, cm.GIN, cm.TT, cm.VV, cm.AA, cm.OUT
    x2, x3 = A["x2"], A["x3"]

    for dst, src in [(s_ab, A["ab_rep"])]:
        p.op("sync", lambda e, d=dst, s=src: e.dma_start(out=d[:, :], in_=s), inc=IN, k=16)
    in_init = p.n(IN)
    p.op("gpsimd", lambda e: e.dma_start(out=s_e2h[:, :], in_=A["e2h"]), inc=GIN, k=16)
    p.op("gpsimd", lambda e: e.dma_start(out=s_h2e[:, :], in_=A["h2e"]), inc=GIN, k=16)
    for k in range(8):
        p.op("gpsimd", lambda e, k=k: e.dma_start(out=s_wqkv[:, k * 3072:(k + 1) * 3072],
                                                  in_=A["wqkv"][:, k * 3072:(k + 1) * 3072]), inc=GIN, k=16)
    for k in range(4):
        p.op("gpsimd", lambda e, k=k: e.dma_start(out=s_wout[:, k * 4 * D:(k + 1) * 4 * D],
                                                  in_=A["wout"][:, k * 4 * D:(k + 1) * 4 * D]), inc=GIN, k=16)

    a_sig = p.op("scalar", lambda e: e.activation(out=s_sig[:, :], in_=s_ab[:, :], func=AF.Sigmoid),
                 waits=[(IN, in_init)], inc=AA)
    v_cl = p.op("vector", lambda e: e.tensor_scalar(out=s_sig[:, :], in0=s_sig[:, :],
                                                    scalar1=cm.cp[:, CP_EPS8:CP_EPS8 + 1], scalar2=None,
                                                    op0=ALU.max),
                waits=[(AA, a_sig)], inc=VV)
    v0 = p.op("vector", lambda e: e.memset(xf[:, 0:tc], 1.0), waits=[(VV, v_cl)], inc=VV)
    for c in range(16):
        p.op("vector", lambda e, c=c: e.tensor_scalar(
            out=lam_t[:, cs(c, tc)], in0=xf[:, 0:tc], scalar1=s_sig[:, c:c + 1], scalar2=None, op0=ALU.mult),
            waits=[(VV, v0)], inc=(VV if c == 15 else None))
    p.op("vector", lambda e: e.memset(carry[:, :], 0.0), inc=VV)
    v_init = p.n(VV)

    t_end, a_end, out_end = p.n(TT_), p.n(AA), p.n(OUT)
    v_end = max(p.n(VV), v_init)

    for i in range(nt):
        tsl = slice(i * tc, (i + 1) * tc)
        bar = [(TT_, t_end), (VV, v_end), (AA, a_end), (OUT, out_end)]
        for c in range(NF):
            p.op("sync", lambda e, c=c, tsl=tsl: e.dma_start(out=xf[:, cs(c, tc)],
                                                             in_=x2[c * 128:(c + 1) * 128, tsl]),
                 waits=bar, inc=IN, k=16)
            p.op("gpsimd", lambda e, c=c, tsl=tsl: e.dma_start(out=xb[:, cs(c, tc)],
                                                               in_=x2[c * 128:(c + 1) * 128, tsl]),
                 waits=bar, inc=GIN, k=16)
        in_x = p.n(IN)
        gin_x = p.n(GIN)

        v_ev0 = p.n(VV)
        free_mark = []
        t_grp = []
        for g in range(24):
            pjt = pj[4 + g % 4]
            w = [(GIN, gin_x)] + (bar if g == 0 else [])
            if g >= 4:
                w.append(free_mark[g - 4])
            for k in range(8):
                p.op("tensor", lambda e, pjt=pjt, g=g, k=k: e.matmul(
                    pjt[:, 0:tc], s_wqkv[:, k * 3072 + g * 128: k * 3072 + (g + 1) * 128],
                    xb[:, cs(k, tc)], start=(k == 0), stop=(k == 7)),
                    waits=w if k == 0 else (), inc=(TT_ if k == 7 else None))
            t_grp.append(p.n(TT_))
            m = g % 8
            if g < 16:
                dst = phiq if g < 8 else phik
                v1 = p.op("vector", lambda e, pjt=pjt, dst=dst, m=m: e.tensor_scalar(
                    out=dst[:, cs(m, tc)], in0=pjt[:, 0:tc], scalar1=cm.cp[:, CP_ZERO:CP_ZERO + 1],
                    scalar2=None, op0=ALU.min),
                    waits=[(TT_, t_grp[-1])], inc=VV)
                a1 = p.op("scalar", lambda e, dst=dst, m=m: e.activation(
                    out=dst[:, cs(m, tc)], in_=dst[:, cs(m, tc)], func=AF.Exp),
                    waits=[(VV, v1)], inc=AA)
                v2 = p.op("vector", lambda e, pjt=pjt, dst=dst, m=m: e.scalar_tensor_tensor(
                    out=dst[:, cs(m, tc)], in0=pjt[:, 0:tc], scalar=cm.cp[:, CP_ZERO:CP_ZERO + 1],
                    in1=dst[:, cs(m, tc)], op0=ALU.max, op1=ALU.add),
                    waits=[(AA, a1)], inc=VV)
                free_mark.append((VV, v2))
            else:
                a1 = p.op("scalar", lambda e, pjt=pjt, m=m: e.copy(out=vf[:, cs(m, tc)], in_=pjt[:, 0:tc]),
                          waits=[(TT_, t_grp[-1]), (VV, v_end)], inc=AA)
                free_mark.append((AA, a1))
        a_v = p.n(AA)

        v_kv0 = p.n(VV)
        for c in range(NF):
            p.op("vector", lambda e, c=c: e.tensor_tensor(out=kvf[:, cs(c, tc)], in0=phik[:, cs(c, tc)],
                                                          in1=vf[:, cs(c, tc)], op=ALU.mult),
                 waits=[(AA, a_v), (VV, v_kv0)], inc=(VV if c == NF - 1 else None))
        v_kv = p.n(VV)

        scans = [
            (0, kvf, vf),
            (1, kvf, ta),
            (2, phik, tb),
            (3, phik, phik),
        ]
        v_before = p.n(VV)
        for c in range(NF):
            p.op("vector", lambda e, c=c: e.tensor_tensor_scan(
                out=vf[:, cs(c, tc)], data0=lam_t[:, cs(c, tc)], data1=kvf[:, cs(c, tc)],
                initial=carry[:, c:c + 1], op0=ALU.mult, op1=ALU.add),
                waits=[(VV, v_init if i == 0 else v_before), (VV, v_kv)],
                inc=(VV if c == NF - 1 else None))
        v_s1 = p.n(VV)
        for c in range(NF):
            p.op("vector", lambda e, c=c: e.tensor_copy(out=carry[:, c:c + 1],
                                                        in_=vf[:, c * tc + tc - 1: c * tc + tc]),
                 waits=[(VV, v_s1)], inc=(VV if c == NF - 1 else None))
        for c in range(NF):
            p.op("vector", lambda e, c=c: e.tensor_tensor_scan(
                out=ta[:, cs(c, tc)], data0=lam_t[:, cs(8 + c, tc)], data1=kvf[:, cs(c, tc)],
                initial=carry[:, 8 + c:9 + c], op0=ALU.mult, op1=ALU.add),
                inc=(VV if c == NF - 1 else None))
        v_s2 = p.n(VV)
        for c in range(NF):
            p.op("vector", lambda e, c=c: e.tensor_copy(out=carry[:, 8 + c:9 + c],
                                                        in_=ta[:, c * tc + tc - 1: c * tc + tc]),
                 waits=[(VV, v_s2)], inc=(VV if c == NF - 1 else None))
        for c in range(NF):
            p.op("vector", lambda e, c=c: e.tensor_tensor_scan(
                out=tb[:, cs(c, tc)], data0=lam_t[:, cs(c, tc)], data1=phik[:, cs(c, tc)],
                initial=carry[:, 16 + c:17 + c], op0=ALU.mult, op1=ALU.add),
                inc=(VV if c == NF - 1 else None))
        v_s3 = p.n(VV)
        for c in range(NF):
            p.op("vector", lambda e, c=c: e.tensor_copy(out=carry[:, 16 + c:17 + c],
                                                        in_=tb[:, c * tc + tc - 1: c * tc + tc]),
                 waits=[(VV, v_s3)], inc=(VV if c == NF - 1 else None))
        for c in range(NF):
            p.op("vector", lambda e, c=c: e.tensor_tensor_scan(
                out=kvf[:, cs(c, tc)], data0=lam_t[:, cs(8 + c, tc)], data1=phik[:, cs(c, tc)],
                initial=carry[:, 24 + c:25 + c], op0=ALU.mult, op1=ALU.add),
                inc=(VV if c == NF - 1 else None))
        v_s4 = p.n(VV)
        for c in range(NF):
            p.op("vector", lambda e, c=c: e.tensor_copy(out=carry[:, 24 + c:25 + c],
                                                        in_=kvf[:, c * tc + tc - 1: c * tc + tc]),
                 waits=[(VV, v_s4)], inc=(VV if c == NF - 1 else None))
        v_scans = p.n(VV)

        for ch, (rks, pd) in enumerate([(tb, pj[4]), (kvf, pj[5])]):
            for c in range(NF):
                p.op("vector", lambda e, c=c, rks=rks: e.tensor_tensor(
                    out=xb[:, cs(c, tc)], in0=phiq[:, cs(c, tc)], in1=rks[:, cs(c, tc)], op=ALU.mult),
                    waits=[(VV, v_scans), (TT_, t_grp[-1])], inc=(VV if c == NF - 1 else None))
            v_pr = p.n(VV)
            for c in range(NF):
                p.op("tensor", lambda e, c=c, pd=pd: e.matmul(
                    pd[0:16, 0:tc], s_e2h[:, c * 16:(c + 1) * 16], xb[:, cs(c, tc)],
                    start=(c == 0), stop=(c == 7)),
                    waits=[(VV, v_pr)] if c == 0 else (), inc=(TT_ if c == 7 else None))
        t_den = p.n(TT_)
        v_d1 = p.op("vector", lambda e: e.tensor_scalar(out=den[:, 0:tc], in0=pj[4][0:16, 0:tc],
                                                        scalar1=cm.cp[0:16, CP_EPS6:CP_EPS6 + 1],
                                                        scalar2=None, op0=ALU.max),
                    waits=[(TT_, t_den)], inc=VV)
        p.op("vector", lambda e: e.tensor_scalar(out=den[:, tc:2 * tc], in0=pj[5][0:16, 0:tc],
                                                 scalar1=cm.cp[0:16, CP_EPS6:CP_EPS6 + 1],
                                                 scalar2=None, op0=ALU.max), inc=VV)
        v_d2 = p.n(VV)
        a_l1 = p.op("scalar", lambda e: e.activation(out=den[:, :], in_=den[:, :], func=AF.Ln),
                    waits=[(VV, v_d2)], inc=AA)
        a_l2 = p.op("scalar", lambda e: e.activation(out=den[:, :], in_=den[:, :], func=AF.Exp,
                                                     scale=cm.cp[0:16, CP_M1:CP_M1 + 1]),
                    waits=[(AA, a_l1)], inc=AA)
        v_d4 = p.op("vector", lambda e: e.tensor_copy(out=denb[:, :], in_=den[:, :]),
                    waits=[(AA, a_l2)], inc=VV)

        v_occ0 = p.n(VV)
        for ch in range(2):
            rkv = vf if ch == 0 else ta
            for c in range(NF):
                pb_ = pj[6 + (c % 2)]
                wb_ = [(VV, v_d4)]
                if ch * 8 + c >= 2:
                    wb_.append((VV, v_occ0 + 2 * (ch * 8 + c) - 2))
                p.op("tensor", lambda e, c=c, ch=ch, pb_=pb_: e.matmul(
                    pb_[:, 0:tc], s_h2e[:, c * 128:(c + 1) * 128],
                    denb[:, ch * tc:(ch + 1) * tc], start=True, stop=True),
                    waits=wb_, inc=TT_)
                t_b = p.n(TT_)
                p.op("vector", lambda e, c=c, ch=ch, rkv=rkv: e.tensor_tensor(
                    out=occ[:, cs(ch * 8 + c, tc)], in0=phiq[:, cs(c, tc)], in1=rkv[:, cs(c, tc)],
                    op=ALU.mult),
                    waits=[(VV, v_d4)], inc=VV)
                p.op("vector", lambda e, c=c, ch=ch, pb_=pb_: e.tensor_tensor(
                    out=occ[:, cs(ch * 8 + c, tc)], in0=occ[:, cs(ch * 8 + c, tc)], in1=pb_[:, 0:tc],
                    op=ALU.mult),
                    waits=[(TT_, t_b)], inc=VV)
        v_occ = p.n(VV)

        t_wo = []
        for m in range(8):
            w = [(VV, v_occ)]
            if m >= 4:
                w.append((VV, v_occ + (m - 4) + 1))
            for kk in range(16):
                p.op("tensor", lambda e, m=m, kk=kk: e.matmul(
                    pj[m % 4][:, 0:tc], s_wout[:, kk * D + m * 128: kk * D + (m + 1) * 128],
                    occ[:, cs(kk, tc)], start=(kk == 0), stop=(kk == 15)),
                    waits=w if kk == 0 else (), inc=(TT_ if kk == 15 else None))
            t_wo.append(p.n(TT_))
        for m in range(8):
            p.op("vector", lambda e, m=m: e.scalar_tensor_tensor(
                out=vf[:, cs(m, tc)], in0=pj[m % 4][:, 0:tc], scalar=cm.cp[:, CP_ZERO:CP_ZERO + 1],
                in1=xf[:, cs(m, tc)], op0=ALU.bypass, op1=ALU.add),
                waits=[(TT_, t_wo[m]), (IN, in_x)], inc=VV)

        v_x3 = emit_ln(p, cm, i, vf, kvf, kvf, xb, sq, tc=tc)
        for c in range(NF):
            p.op("sync", lambda e, c=c, tsl=tsl: e.dma_start(out=x3[c * 128:(c + 1) * 128, tsl],
                                                             in_=kvf[:, cs(c, tc)]),
                 waits=[(VV, v_x3)], inc=OUT, k=16)

        t_end, v_end, a_end, out_end = p.n(TT_), p.n(VV), p.n(AA), p.n(OUT)



W1_WC = [128] * 21 + [42]
W1_OFF = [0]
for _wc in W1_WC:
    W1_OFF.append(W1_OFF[-1] + 16 * _wc)
W1_COLS = W1_OFF[-1]
GB_OFF = [0]
for _wc in W1_WC:
    GB_OFF.append(GB_OFF[-1] + TC)
GB_COLS = GB_OFF[-1]


def emit_p3f(p, cm, A, li, xin, xout, final=False, x3s=None):
    p.phase_reset()
    NEC = 22
    s_w2r = p.sbuf(f"f{li}_w2r", [128, NEC * D], BF16)
    w1b = p.sbuf(f"f{li}_w1b", [128, 2 * 2048], BF16)
    xsf = p.sbuf(f"f{li}_xsf", [128, 8 * 1025], F32)
    xnb_off = p._off
    xnb = p.sbuf(f"f{li}_xnb", [128, 8 * 1024], BF16)
    xb = p.sbuf(f"f{li}_xb", [128, T], BF16)
    sq = p.sbuf(f"f{li}_sq", [128, T], BF16)
    gb = p.sbuf(f"f{li}_gb", [128, 2 * GB_COLS], BF16)
    asb = p.sbuf(f"f{li}_asb", [128, TC], F32)
    gsb = p.sbuf(f"f{li}_gsb", [128, TC], F32)
    tmpf = p.sbuf(f"f{li}_tmp", [128, T], F32)
    off_xnb = None
    xtr = p.nc.alloc_sbuf_tensor_at(f"f{li}_xtr", [128, T], F32, offset=xnb_off)
    lngc = p.sbuf(f"f{li}_lng", [128, 8], F32)
    lnbc = p.sbuf(f"f{li}_lnb", [128, 8], F32)
    b1a = p.sbuf(f"f{li}_b1a", [128, NEC], F32)
    b1g = p.sbuf(f"f{li}_b1g", [128, NEC], F32)
    b2c = p.sbuf(f"f{li}_b2", [128, 8], F32)

    pj = cm.pj
    IN, GIN, TT_, VV, AA, OUT = cm.IN, cm.GIN, cm.TT, cm.VV, cm.AA, cm.OUT
    GW = cm.GW

    for dst, key in [(lngc, "lng"), (lnbc, "lnb"), (b1a, "b1a"), (b1g, "b1g"), (b2c, "b2c3")]:
        p.op("sync", lambda e, d=dst, s=A[key], li=li: e.dma_start(out=d[:, :], in_=s[li]), inc=IN, k=16)
    in_init = p.n(IN)
    for k in range(11):
        p.op("gpsimd", lambda e, k=k, li=li: e.dma_start(
            out=s_w2r[:, k * 2048:(k + 1) * 2048], in_=A["w2s"][li, :, k * 2048:(k + 1) * 2048]),
            inc=GIN, k=16)

    t_end, v_end, a_end, out_end = p.n(TT_), p.n(VV), p.n(AA), p.n(OUT)
    gcols = [lngc[:, c:c + 1] for c in range(NF)]
    bcols = [lnbc[:, c:c + 1] for c in range(NF)]

    GGs = p.sem("s_g")
    g_end = p.n(GGs)
    for i2 in range(4):
        t0 = i2 * 1024
        bar = [(TT_, t_end), (VV, v_end), (AA, a_end), (OUT, out_end), (GGs, g_end)]
        if i2 == 0:
            vz = p.op("vector", lambda e: e.memset(xsf[:, 0:8 * 1025:1025], 0.0), inc=VV)
            for c in range(NF):
                p.op("sync", lambda e, c=c: e.dma_start(
                    out=xsf[:, c * 1025 + 1: (c + 1) * 1025],
                    in_=xin[c * 128:(c + 1) * 128, 0:1024]),
                    waits=bar + [(VV, vz)], inc=IN, k=16)
        else:
            for c in range(NF):
                p.op("sync", lambda e, c=c, t0=t0: e.dma_start(
                    out=xsf[:, c * 1025: (c + 1) * 1025],
                    in_=xin[c * 128:(c + 1) * 128, t0 - 1: t0 + 1024]),
                    waits=bar, inc=IN, k=16)
        in_x = p.n(IN)

        def sh_ap(c, s):
            off = c * 1025 + (1 if c < 4 else 0) + s * TC
            return xsf[:, off: off + TC]

        def un_ap(c, s):
            off = c * 1025 + 1 + s * TC
            return xsf[:, off: off + TC]

        v_lns = []
        for s in range(2):
            p.op("vector", None, waits=[(IN, in_x)], inc=VV)
            v_ln = emit_ln(p, cm, i2, (lambda c, s=s: sh_ap(c, s)),
                           (lambda c, s=s: xnb[:, c * 1024 + s * TC: c * 1024 + (s + 1) * TC]),
                           tmpf, xb, sq, gcol=gcols, bcol=bcols,
                           guards=[(IN, in_x), (IN, in_init)], tc=TC)
            v_lns.append(v_ln)
        v_ln_all = p.n(VV)

        t_wgrp = [0] * NEC
        g_gb0 = p.n(p.sem("s_g"))
        a_w0 = p.n(AA)
        for c in range(NEC):
            wc = W1_WC[c]
            par = c % 2
            gw_r = p.op("gpsimd", lambda e, c=c, par=par, li=li: e.dma_start(
                out=w1b[:, par * 2048: par * 2048 + 16 * W1_WC[c]],
                in_=A["w1s"][li, :, W1_OFF[c]:W1_OFF[c + 1]]),
                waits=[(TT_, t_wgrp[c - 2] if c >= 2 else t_end)], inc=GW[par], k=16)
            for s in range(2):
                idx = 2 * c + s
                pa = pj[(idx % 2) * 2]
                pg = pj[(idx % 2) * 2 + 1]
                w = [(GW[par], gw_r), (VV, v_ln_all)]
                if idx >= 2:
                    w.append(("s_g", g_gb0 + (idx - 2) + 1))
                for k in range(8):
                    p.op("tensor", lambda e, c=c, s=s, k=k, par=par, pa=pa, wc=wc: e.matmul(
                        pa[0:wc, 0:TC], w1b[:, par * 2048 + k * 2 * wc: par * 2048 + k * 2 * wc + wc],
                        xnb[:, k * 1024 + s * TC: k * 1024 + (s + 1) * TC],
                        start=(k == 0), stop=(k == 7)),
                        waits=w if k == 0 else (), inc=None)
                for k in range(8):
                    p.op("tensor", lambda e, c=c, s=s, k=k, par=par, pg=pg, wc=wc: e.matmul(
                        pg[0:wc, 0:TC], w1b[:, par * 2048 + k * 2 * wc + wc: par * 2048 + (k + 1) * 2 * wc],
                        xnb[:, k * 1024 + s * TC: k * 1024 + (s + 1) * TC],
                        start=(k == 0), stop=(k == 7)),
                        inc=(TT_ if k == 7 else None))
                t_mm = p.n(TT_)
                a1 = p.op("scalar", lambda e, pa=pa, wc=wc, c=c: e.activation(
                    out=asb[0:wc, :], in_=pa[0:wc, 0:TC], func=AF.Identity, bias=b1a[0:wc, c:c + 1]),
                    waits=[(TT_, t_mm), (IN, in_init)], inc=AA)
                a2 = p.op("scalar", lambda e, pg=pg, wc=wc, c=c: e.activation(
                    out=gsb[0:wc, :], in_=pg[0:wc, 0:TC], func=AF.Gelu, bias=b1g[0:wc, c:c + 1]),
                    inc=AA)
                p.op("gpsimd", lambda e, c=c, s=s, wc=wc: e.tensor_tensor(
                    out=gb[0:wc, s * GB_COLS + GB_OFF[c]: s * GB_COLS + GB_OFF[c] + TC],
                    in0=asb[0:wc, :], in1=gsb[0:wc, :], op=ALU.mult),
                    waits=[(AA, a2)], inc=p.sem("s_g"))
            t_wgrp[c] = p.n(TT_)
        v_gb = ("s_g", p.n(p.sem("s_g")))

        gin_w2 = 11 * 16
        for s in range(2):
            for c in range(NEC):
                wc = W1_WC[c]
                w = [v_gb, (GIN, p.n(GIN)), (VV, p.n(VV))]
                for m in range(8):
                    p.op("tensor", lambda e, c=c, s=s, m=m, wc=wc: e.matmul(
                        pj[m][:, 0:TC], s_w2r[0:wc, c * D + m * 128: c * D + (m + 1) * 128],
                        gb[0:wc, s * GB_COLS + GB_OFF[c]: s * GB_COLS + GB_OFF[c] + TC],
                        start=(c == 0), stop=(c == NEC - 1)),
                        waits=w if (m == 0 and c == 0) else (), inc=(TT_ if (m == 7 and c == NEC - 1) else None))
            t_y = p.n(TT_)
            for m in range(8):
                p.op("vector", lambda e, m=m, s=s: e.scalar_tensor_tensor(
                    out=tmpf[:, cs(m)], in0=pj[m][:, 0:TC], scalar=b2c[:, m:m + 1],
                    in1=un_ap(m, s), op0=ALU.add, op1=ALU.add),
                    waits=[(TT_, t_y)], inc=(VV if m == 7 else None))
            v_z = p.n(VV)
            if not final:
                for c in range(NF):
                    p.op("sync", lambda e, c=c, s=s, t0=t0: e.dma_start(
                        out=xout[c * 128:(c + 1) * 128, t0 + s * TC: t0 + (s + 1) * TC],
                        in_=tmpf[:, cs(c)]),
                        waits=[(VV, v_z)], inc=OUT, k=16)
            else:
                for c in range(NF):
                    p.op("sync", lambda e, c=c, s=s, t0=t0: e.dma_start(
                        out=xtr[:, cs(c)], in_=x3s[c * 128:(c + 1) * 128, t0 + s * TC: t0 + (s + 1) * TC]),
                        waits=[(VV, v_end), v_gb], inc=IN, k=16)
                in_x3 = p.n(IN)
                for c in range(NF):
                    p.op("vector", lambda e, c=c: e.tensor_tensor(
                        out=xtr[:, cs(c)], in0=tmpf[:, cs(c)], in1=xtr[:, cs(c)], op=ALU.add),
                        waits=[(VV, v_z), (IN, in_x3)], inc=(VV if c == NF - 1 else None))
                v_x4 = emit_ln(p, cm, i2, xtr, tmpf, tmpf, xb, sq, tc=TC)
                for c in range(NF):
                    p.op("sync", lambda e, c=c, s=s, t0=t0: e.dma_start(
                        out=xout[c * 128:(c + 1) * 128, t0 + s * TC: t0 + (s + 1) * TC],
                        in_=tmpf[:, cs(c)]),
                        waits=[(VV, v_x4)], inc=OUT, k=16)

        t_end, v_end, a_end, out_end = p.n(TT_), p.n(VV), p.n(AA), p.n(OUT)
        g_end = p.n(GGs)



def emit_p3a(p, cm, A, li, xin, xout):
    p.phase_reset()
    tc = TC
    s_wqkv = p.sbuf(f"a{li}_wqkv", [128, 8 * 3072], BF16)
    s_wo = p.sbuf(f"a{li}_wo", [128, 8 * D], BF16)
    s_h2e = p.sbuf(f"a{li}_h2e", [16, 8 * 128], BF16)
    s_e2h = p.sbuf(f"a{li}_e2h", [128, 8 * 16], BF16)
    s_prot = p.sbuf(f"a{li}_prot", [128, 128], BF16)
    s_trim = p.sbuf(f"a{li}_trim", [128, 256], BF16)
    qcol = p.sbuf(f"a{li}_qcol", [128, 4], F32)
    xsb = p.sbuf(f"a{li}_xsb", [128, 8 * 513], BF16)
    xf = p.sbuf(f"a{li}_xf", [128, T], F32)
    tabs = p.sbuf(f"a{li}_tabs", [128, 4 * tc], F32)
    qf_off = p._off
    qf = p.sbuf(f"a{li}_qf", [128, T], F32)
    kb_off = p._off
    kb = p.sbuf(f"a{li}_kb", [128, T], BF16)
    fb_off = p._off
    fb = p.sbuf(f"a{li}_fb", [128, T], BF16)
    rsc = p.sbuf(f"a{li}_rsc", [128, 2 * tc], F32)
    rsc2 = p.sbuf(f"a{li}_rsc2", [128, 2 * tc], F32)
    sqq = p.sbuf(f"a{li}_sqq", [128, 2 * tc], BF16)
    rnb = p.sbuf(f"a{li}_rnb", [16, 2 * tc], BF16)
    rnr = p.sbuf(f"a{li}_rnr", [16, 2 * tc], F32)
    q_aug = p.sbuf(f"a{li}_qaug", [65, 16 * tc], BF16)
    k_aug = p.sbuf(f"a{li}_kaug", [65, 16 * 768], BF16)
    vtm = p.sbuf(f"a{li}_vtm", [128, 6 * 1040], BF16)
    at = p.nc.alloc_sbuf_tensor_at
    ab = at(f"a{li}_ab", [128, T], BF16, offset=kb_off)
    zt = at(f"a{li}_zt", [128, T], F32, offset=fb_off)
    expb = at(f"a{li}_expb", [128, 4 * 256], BF16, offset=qf_off)
    qms = at(f"a{li}_qms", [128, 2 * 512], F32, offset=qf_off + 2048)
    mstack = at(f"a{li}_mst", [128, 32], F32, offset=qf_off + 6144)
    mT = at(f"a{li}_mT", [32, 128], BF16, offset=qf_off + 6400)
    recw = at(f"a{li}_recw", [1, 256], F32, offset=qf_off + 6656)
    recb = at(f"a{li}_recb", [64, 256], F32, offset=qf_off + 7680)

    pj = cm.pj
    IN, GIN, TT_, VV, AA, OUT = cm.IN, cm.GIN, cm.TT, cm.VV, cm.AA, cm.OUT
    GG = p.sem("s_g")

    p.op("sync", lambda e, li=li: e.dma_start(out=qcol[:, :], in_=A["qcols"][:, 4 * li:4 * li + 4]),
         inc=IN, k=16)
    in_init = p.n(IN)
    for dst, key in [(s_h2e, "h2e"), (s_e2h, "e2h"), (s_prot, "prot"), (s_trim, "trim")]:
        p.op("gpsimd", lambda e, d=dst, s=A[key]: e.dma_start(out=d[:, :], in_=s), inc=GIN, k=16)
    for k in range(8):
        p.op("gpsimd", lambda e, k=k, li=li: e.dma_start(
            out=s_wqkv[:, k * 3072:(k + 1) * 3072], in_=A["wqkv3"][li, :, k * 3072:(k + 1) * 3072]),
            inc=GIN, k=16)
    for k in range(4):
        p.op("gpsimd", lambda e, k=k, li=li: e.dma_start(
            out=s_wo[:, k * 2048:(k + 1) * 2048], in_=A["wo3"][li, :, k * 2048:(k + 1) * 2048]),
            inc=GIN, k=16)
    p.op("vector", lambda e: e.memset(k_aug[64:65, :], 1.0), inc=VV)
    p.op("vector", lambda e: e.memset(vtm[:, 64:6 * 1040:65], 1.0), inc=VV)
    v_init = p.n(VV)

    t_end, a_end, out_end = p.n(TT_), p.n(AA), p.n(OUT)
    v_end = max(p.n(VV), v_init)
    t_pv_last = 0

    for i in range(NT):
        t0 = i * tc
        bar = [(TT_, t_end), (VV, v_end), (AA, a_end), (OUT, out_end)]
        if i == 0:
            vz = p.op("vector", lambda e: e.memset(xsb[:, 0:8 * 513:513], 0.0), waits=bar, inc=VV)
            for c in range(NF):
                p.op("gpsimd", lambda e, c=c: e.dma_start(
                    out=xsb[:, c * 513 + 1:(c + 1) * 513], in_=xin[c * 128:(c + 1) * 128, 0:tc]),
                    waits=bar + [(VV, vz)], inc=GIN, k=16)
        else:
            for c in range(NF):
                p.op("gpsimd", lambda e, c=c, t0=t0: e.dma_start(
                    out=xsb[:, c * 513:(c + 1) * 513], in_=xin[c * 128:(c + 1) * 128, t0 - 1:t0 + tc]),
                    waits=bar, inc=GIN, k=16)
        for c in range(NF):
            p.op("sync", lambda e, c=c, t0=t0: e.dma_start(
                out=xf[:, cs(c)], in_=xin[c * 128:(c + 1) * 128, t0:t0 + tc]),
                waits=bar, inc=IN, k=16)
        for j in range(4):
            p.op("sync", lambda e, j=j, t0=t0: e.dma_start(
                out=tabs[:, j * tc:(j + 1) * tc], in_=A["tabs"][:, j * T + t0: j * T + t0 + tc]),
                waits=bar, inc=IN, k=16)
        in_x = p.n(IN)
        gin_x = p.n(GIN)

        for j in range(4):
            p.op("vector", lambda e, j=j: e.tensor_scalar(
                out=tabs[:, j * tc:(j + 1) * tc], in0=tabs[:, j * tc:(j + 1) * tc],
                scalar1=qcol[:, j:j + 1], scalar2=None, op0=ALU.mult),
                waits=[(IN, in_x), (IN, in_init)] + bar, inc=(VV if j == 3 else None))
        v_eff = p.n(VV)

        def sh(c):
            return c * 513 + (1 if c < 4 else 0)

        free_mark = []
        a_sq = []
        t_n2m = []
        for g in range(16):
            m = g % 8
            pjt = pj[4 + g % 4]
            w = [(GIN, gin_x)] + (bar if g == 0 else [])
            if g >= 4:
                w = w + [free_mark[g - 4]]
            for k in range(8):
                p.op("tensor", lambda e, pjt=pjt, g=g, k=k: e.matmul(
                    pjt[:, :], s_wqkv[:, k * 3072 + g * 128: k * 3072 + (g + 1) * 128],
                    xsb[:, sh(k): sh(k) + tc], start=(k == 0), stop=(k == 7)),
                    waits=w if k == 0 else (), inc=(TT_ if k == 7 else None))
            t_g = p.n(TT_)
            dstf = qf if g < 8 else kb
            p.op("scalar", lambda e, pjt=pjt, m=m, dstf=dstf: e.copy(out=dstf[:, cs(m)], in_=pjt[:, :]),
                 waits=[(TT_, t_g), (VV, v_end)], inc=None)
            if g < 8:
                p.op("scalar", lambda e, pjt=pjt, m=m: e.copy(out=fb[:, cs(m)], in_=pjt[:, :]), inc=None)
            wsq = [(TT_, t_n2m[g - 2])] if g >= 2 else []
            a2 = p.op("scalar", lambda e, pjt=pjt, g=g: e.activation(
                out=sqq[:, (g % 2) * tc:(g % 2 + 1) * tc], in_=pjt[:, :], func=AF.Square),
                waits=wsq, inc=AA)
            a_sq.append(a2)
            pn = pj[0] if g < 8 else pj[1]
            p.op("tensor", lambda e, g=g, m=m, pn=pn: e.matmul(
                pn[0:16, :], s_e2h[:, m * 16:(m + 1) * 16], sqq[:, (g % 2) * tc:(g % 2 + 1) * tc],
                start=(m == 0), stop=(m == 7)),
                waits=[(AA, a2)] + ([(VV, v_end)] if g == 0 else []), inc=TT_)
            t_n2m.append(p.n(TT_))
            free_mark.append((AA, a2))
        t_n2 = p.n(TT_)
        a_evacs = p.n(AA)

        v_vt0 = p.n(VV)
        for g in range(8):
            tt, n = g // 2, g % 2
            pjt = pj[4 + g % 4]
            w = [free_mark[12 + g % 4]] if g < 4 else [(VV, v_vt0 + (g - 4) + 1)]
            for k in range(8):
                p.op("tensor", lambda e, pjt=pjt, tt=tt, n=n, k=k: e.matmul(
                    pjt[:, :], xsb[:, sh(k) + tt * 128: sh(k) + (tt + 1) * 128],
                    s_wqkv[:, k * 3072 + 2048 + n * 512: k * 3072 + 2048 + (n + 1) * 512],
                    start=(k == 0), stop=(k == 7)),
                    waits=w if k == 0 else (), inc=(TT_ if k == 7 else None))
            t_v = p.n(TT_)
            slot = (4 * i + tt) % 6
            p.op("vector", lambda e, pjt=pjt, slot=slot, n=n: e.tensor_copy(
                out=vtm[:, slot * 1040 + 520 * n: slot * 1040 + 520 * (n + 1)].rearrange(
                    "p (h x) -> p h x", h=8)[:, :, 0:64],
                in_=pjt[:, :].rearrange("p (h x) -> p h x", h=8)),
                waits=[(TT_, t_v)], inc=VV)
        v_vt = p.n(VV)

        v_r1a = p.op("vector", lambda e: e.tensor_scalar(
            out=rnr[:, 0:tc], in0=pj[0][0:16, :], scalar1=cm.cp[0:16, CP_EPS24:CP_EPS24 + 1],
            scalar2=None, op0=ALU.max), waits=[(TT_, t_n2)], inc=VV)
        v_r1 = p.op("vector", lambda e: e.tensor_scalar(
            out=rnr[:, tc:2 * tc], in0=pj[1][0:16, :], scalar1=cm.cp[0:16, CP_EPS24:CP_EPS24 + 1],
            scalar2=None, op0=ALU.max), inc=VV)
        a_r1 = p.op("scalar", lambda e: e.activation(out=rnr[:, :], in_=rnr[:, :], func=AF.Ln),
                    waits=[(VV, v_r1)], inc=AA)
        a_r2 = p.op("scalar", lambda e: e.activation(out=rnr[:, :], in_=rnr[:, :], func=AF.Exp,
                                                     scale=cm.cp[0:16, CP_MH:CP_MH + 1]),
                    waits=[(AA, a_r1)], inc=AA)
        v_rn = p.op("vector", lambda e: e.tensor_copy(out=rnb[:, :], in_=rnr[:, :]),
                    waits=[(AA, a_r2)], inc=VV)

        def rotary(src_f32, tab_off, rn_off, dest_write, mm_extra, v_guard, perm_src):
            marks = []
            t_perm_list = []
            for c in range(NF):
                pp = pj[4 + c % 2]
                pb = pj[6 + c % 2]
                w1 = [(AA, a_evacs), (VV, v_vt), (VV, v_guard)] + mm_extra
                if c >= 2:
                    w1.append((VV, marks[c - 2]))
                p.op("tensor", lambda e, c=c, pp=pp, perm_src=perm_src: e.matmul(
                    pp[:, :], s_prot[:, :], perm_src[:, cs(c)], start=True, stop=True),
                    waits=w1, inc=None)
                p.op("tensor", lambda e, c=c, pb=pb, rn_off=rn_off: e.matmul(
                    pb[:, :], s_h2e[:, c * 128:(c + 1) * 128], rnb[0:16, rn_off:rn_off + tc],
                    start=True, stop=True),
                    waits=[(VV, v_rn)], inc=TT_)
                t_c = p.n(TT_)
                t_perm_list.append(t_c)
                g1 = p.op("gpsimd", lambda e, c=c, src_f32=src_f32, tab_off=tab_off: e.tensor_tensor(
                    out=rsc[:, (c % 2) * tc:(c % 2 + 1) * tc], in0=src_f32[:, cs(c)],
                    in1=tabs[:, tab_off:tab_off + tc], op=ALU.mult),
                    waits=[(VV, v_eff), (VV, v_guard), (AA, a_evacs)] +
                          ([(VV, marks[c - 2])] if c >= 2 else []),
                    inc=GG)
                v1 = p.op("vector", lambda e, c=c, pp=pp, tab_off=tab_off: e.tensor_tensor(
                    out=rsc2[:, (c % 2) * tc:(c % 2 + 1) * tc], in0=pp[:, :],
                    in1=tabs[:, tab_off + tc:tab_off + 2 * tc], op=ALU.mult),
                    waits=[(TT_, t_c), (VV, v_eff)] + ([(VV, marks[c - 2])] if c >= 2 else []),
                    inc=VV)
                g2 = p.op("gpsimd", lambda e, c=c: e.tensor_tensor(
                    out=rsc[:, (c % 2) * tc:(c % 2 + 1) * tc], in0=rsc[:, (c % 2) * tc:(c % 2 + 1) * tc],
                    in1=rsc2[:, (c % 2) * tc:(c % 2 + 1) * tc], op=ALU.add),
                    waits=[(GG, g1), (VV, v1)], inc=GG)
                last = None
                for half in range(2):
                    for (dst_ap, src_sl) in dest_write(c, half):
                        last = p.op("vector",
                                    lambda e, dst_ap=dst_ap, src_sl=src_sl, c=c, half=half, pb=pb:
                                    e.tensor_tensor(
                                        out=dst_ap,
                                        in0=rsc[half * 64:(half + 1) * 64,
                                                (c % 2) * tc + src_sl.start:(c % 2) * tc + src_sl.stop],
                                        in1=pb[half * 64:(half + 1) * 64, src_sl],
                                        op=ALU.mult),
                                    waits=[(GG, g2), (TT_, t_c)], inc=VV)
                marks.append(last)
            return marks, t_perm_list

        _, t_qperm = rotary(qf, 0, 0,
                            lambda c, half: [(q_aug[0:64, (2 * c + half) * tc: (2 * c + half + 1) * tc],
                                              slice(0, tc))],
                            [], v_eff, fb)
        v_qrot = p.n(VV)

        def k_dest(c, half):
            h = 2 * c + half
            out = []
            for wl in range(2):
                ss = (2 * i + wl) % 3
                out.append((k_aug[0:64, h * 768 + ss * 256: h * 768 + (ss + 1) * 256],
                            slice(wl * 256, (wl + 1) * 256)))
            return out

        rotary(kb, 2 * tc, tc, k_dest, [], v_qrot, kb)
        v_krot = p.n(VV)

        t_tp_prev = 0
        v_attn_prev = p.n(VV)
        for wl in range(2):
            gw = 2 * i + wl
            kts = [kt for kt in range(2 * gw - 2, 2 * gw + 2) if kt >= 0]
            v_qm0 = p.n(VV)
            nh = 0
            v_red = []
            for h in range(16):
                for qt in range(2):
                    valid = [kt for kt in kts if 2 * gw - 2 + qt <= kt <= 2 * gw + qt]
                    pq = pj[4 + qt]
                    w = [(VV, v_krot)]
                    if nh >= 2:
                        w.append((VV, v_red[nh - 2]))
                    if nh == 0:
                        w.append((TT_, t_tp_prev))
                        w.append((VV, v_qm0))
                    first = True
                    for kt in valid:
                        kl = kt - (2 * gw - 2)
                        ss = (kt // 2) % 3
                        hf = kt % 2
                        p.op("tensor",
                             lambda e, h=h, qt=qt, kl=kl, ss=ss, hf=hf, pq=pq, wl=wl, first=first:
                             e.matmul(pq[:, kl * 128:(kl + 1) * 128],
                                      q_aug[0:64, h * tc + wl * 256 + qt * 128: h * tc + wl * 256 + (qt + 1) * 128],
                                      k_aug[0:64, h * 768 + ss * 256 + hf * 128: h * 768 + ss * 256 + (hf + 1) * 128],
                                      start=first, stop=(kt == valid[-1])),
                             waits=w if first else (), inc=(TT_ if kt == valid[-1] else None))
                        first = False
                    t_qm = p.n(TT_)
                    lo = (valid[0] - (2 * gw - 2)) * 128
                    hi = (valid[-1] - (2 * gw - 2) + 1) * 128
                    r = p.op("vector", lambda e, h=h, qt=qt, lo=lo, hi=hi, pq=pq: e.reduce_max(
                        out=mstack[:, h + 16 * qt: h + 16 * qt + 1],
                        in_=pq[:, lo:hi], axis=AX.X, negate=True),
                        waits=[(TT_, t_qm)], inc=VV)
                    v_red.append(r)
                    nh += 1
            g_red = v_red[-1]
            t_tp = p.op("tensor", lambda e: e.transpose(pj[2][0:32, 0:128], mstack[:, :], cm.idf[:, :]),
                        waits=[(VV, g_red), (VV, v_attn_prev)], inc=TT_)
            t_tp_prev = t_tp
            v_mT = p.op("vector", lambda e: e.tensor_copy(out=mT[:, :], in_=pj[2][0:32, 0:128]),
                        waits=[(TT_, t_tp)], inc=VV)
            reh = q_aug[64:65, :].rearrange("a (h x) -> a h x", x=512)
            for qt in range(2):
                dst_ap = reh[:, :, wl * 256 + qt * 128: wl * 256 + (qt + 1) * 128]
                p.op("sync", lambda e, dst_ap=dst_ap, qt=qt: e.dma_start(
                    out=dst_ap, in_=mT[16 * qt:16 * (qt + 1), :]),
                    waits=[(VV, v_mT)], inc=OUT, k=16)
            sc = p.n(OUT)
            t_pv = {}
            v_out = {}
            v_rbm = {}
            a_exp = {}
            for h in range(16):
                pv = pj[h % 2]
                pcb = pj[2 + h % 2]
                wqk = [(OUT, sc)]
                if h > 0:
                    wqk.append((AA, a_exp[h - 1]))
                else:
                    wqk.append((VV, v_attn_prev))
                t_sc = []
                firstq = True
                for kt in kts:
                    kl = kt - (2 * gw - 2)
                    ss = (kt // 2) % 3
                    hf = kt % 2
                    role = kl
                    if role == 0:
                        qsl = slice(0, 128)
                    elif role == 3:
                        qsl = slice(128, 256)
                    else:
                        qsl = slice(0, 256)
                    pb6 = pj[6 + kl // 2]
                    coff = (kl % 2) * 256
                    p.op("tensor",
                         lambda e, h=h, ss=ss, hf=hf, qsl=qsl, pb6=pb6, coff=coff, wl=wl:
                         e.matmul(pb6[:, coff + qsl.start: coff + qsl.stop],
                                  k_aug[0:65, h * 768 + ss * 256 + hf * 128: h * 768 + ss * 256 + (hf + 1) * 128],
                                  q_aug[0:65, h * tc + wl * 256 + qsl.start: h * tc + wl * 256 + qsl.stop],
                                  start=True, stop=True),
                         waits=wqk if firstq else (), inc=TT_)
                    firstq = False
                    t_sc.append((p.n(TT_), kl, role, qsl, pb6, coff))
                t_all_qk = t_sc[-1][0]
                for (tmark, kl, role, qsl, pb6, coff) in t_sc:
                    wexp = [(TT_, t_all_qk)]
                    if h > 0:
                        wexp.append((TT_, t_pv[h - 1]))
                    elif i > 0 or wl > 0:
                        wexp.append((TT_, t_pv_last))
                    p.op("scalar", lambda e, kl=kl, qsl=qsl, pb6=pb6, coff=coff: e.activation(
                        out=expb[:, kl * 256 + qsl.start: kl * 256 + qsl.stop],
                        in_=pb6[:, coff + qsl.start: coff + qsl.stop], func=AF.Exp),
                        waits=wexp, inc=AA)
                a_exp[h] = p.n(AA)
                for (tmark, kl, role, qsl, pb6, coff) in t_sc:
                    if role == 0:
                        msl, esl = slice(0, 128), slice(kl * 256, kl * 256 + 128)
                    elif role == 1:
                        msl, esl = slice(0, 128), slice(kl * 256 + 128, kl * 256 + 256)
                    elif role == 2:
                        msl, esl = slice(128, 256), slice(kl * 256, kl * 256 + 128)
                    else:
                        msl, esl = slice(128, 256), slice(kl * 256 + 128, kl * 256 + 256)
                    p.op("gpsimd", lambda e, msl=msl, esl=esl: e.tensor_tensor(
                        out=expb[:, esl], in0=expb[:, esl], in1=s_trim[:, msl], op=ALU.mult),
                        waits=[(AA, a_exp[h])], inc=GG)
                v_msk = ("s_g", p.n(GG))
                firstp = True
                for j, (tmark, kl, role, qsl, pb6, coff) in enumerate(t_sc):
                    slot = (2 * gw - 2 + kl) % 6
                    wpv = []
                    if firstp:
                        wpv = [v_msk]
                        if h >= 2:
                            wpv.append((VV, v_out[h - 2]))
                    p.op("tensor",
                         lambda e, h=h, kl=kl, qsl=qsl, slot=slot, pv=pv, firstp=firstp,
                         lastp=(j == len(t_sc) - 1):
                         e.matmul(pv[0:65, qsl],
                                  vtm[:, slot * 1040 + h * 65: slot * 1040 + (h + 1) * 65],
                                  expb[:, kl * 256 + qsl.start: kl * 256 + qsl.stop],
                                  start=firstp, stop=lastp),
                         waits=wpv, inc=(TT_ if j == len(t_sc) - 1 else None))
                    firstp = False
                t_pv[h] = p.n(TT_)
                a_rc1 = p.op("scalar", lambda e, pv=pv: e.activation(out=recw[:, :], in_=pv[64:65, 0:256],
                                                                     func=AF.Ln),
                             waits=[(TT_, t_pv[h])] + ([(VV, v_out[h - 1])] if h >= 1 else []), inc=AA)
                v_rc = p.op("scalar", lambda e: e.activation(out=recw[:, :], in_=recw[:, :], func=AF.Exp,
                                                             scale=cm.cp[0:1, CP_M1:CP_M1 + 1]),
                            waits=[(AA, a_rc1)], inc=AA)
                wrb = [(AA, v_rc)]
                if h >= 2:
                    wrb.append((VV, v_rbm[h - 2]))
                t_rb = p.op("tensor", lambda e, pcb=pcb: e.matmul(
                    pcb[0:64, 0:256], cm.onf[0:1, 0:64], recw[:, :], start=True, stop=True),
                    waits=wrb, inc=TT_)
                v_rbm[h] = p.op("vector", lambda e, pcb=pcb: e.tensor_copy(out=recb[:, :], in_=pcb[0:64, 0:256]),
                                waits=[(TT_, t_rb)], inc=VV)
                co = h // 2
                ro = (h % 2) * 64
                v_out[h] = p.op("vector", lambda e, pv=pv, co=co, ro=ro, wl=wl: e.tensor_tensor(
                    out=ab[ro:ro + 64, co * tc + wl * 256: co * tc + wl * 256 + 256],
                    in0=pv[0:64, 0:256], in1=recb[:, :], op=ALU.mult),
                    waits=[(VV, v_rbm[h])], inc=VV)
            t_pv_last = t_pv[15]
            v_attn_prev = p.n(VV)
        v_attn = p.n(VV)
        a_attn = p.n(AA)

        t_wo = []
        v_wo0 = p.n(VV)
        for m in range(8):
            w = [(VV, v_attn), (AA, a_attn)]
            if m >= 4:
                w.append((VV, v_wo0 + (m - 4) + 1))
            for k in range(8):
                p.op("tensor", lambda e, m=m, k=k: e.matmul(
                    pj[4 + m % 4][:, :], s_wo[:, k * D + m * 128: k * D + (m + 1) * 128], ab[:, cs(k)],
                    start=(k == 0), stop=(k == 7)),
                    waits=w if k == 0 else (), inc=(TT_ if k == 7 else None))
            t_wo.append(p.n(TT_))
        for m in range(8):
            p.op("vector", lambda e, m=m: e.scalar_tensor_tensor(
                out=zt[:, cs(m)], in0=pj[4 + m % 4][:, :], scalar=cm.cp[:, CP_ZERO:CP_ZERO + 1],
                in1=xf[:, cs(m)], op0=ALU.bypass, op1=ALU.add),
                waits=[(TT_, t_wo[m]), (IN, in_x)], inc=VV)
        v_z = p.n(VV)
        for c in range(NF):
            p.op("sync", lambda e, c=c, t0=t0: e.dma_start(
                out=xout[c * 128:(c + 1) * 128, t0:t0 + tc], in_=zt[:, cs(c)]),
                waits=[(VV, v_z)], inc=OUT, k=16)

        t_end, v_end, a_end, out_end = p.n(TT_), p.n(VV), p.n(AA), p.n(OUT)



def _t(a):
    return np.ascontiguousarray(np.asarray(a).T)


def _col8(vec, n=8):
    return np.ascontiguousarray(np.asarray(vec).reshape(n, 128).T)


def _chunked(a):
    a = np.asarray(a)
    k = a.shape[0] // 128
    return np.ascontiguousarray(a.reshape(k, 128, a.shape[1]).transpose(1, 0, 2).reshape(128, -1))


def p0_inputs(inp, b):
    beta_rep = np.repeat(np.asarray(inp["L0_attn_beta"]), 64)
    return {
        "x0": _t(inp["L0_attn_x"][b]),
        "wq": _t(inp["L0_attn_Wq"]),
        "wv": _t(inp["L0_attn_Wv"]),
        "wg": _t(inp["L0_attn_Wg"]),
        "wo": _t(inp["L0_attn_Wo"]),
        "lamsrc": _col8(beta_rep),
        "g8a": _col8(inp["L0_attn_ln_gamma"]),
        "b8a": _col8(inp["L0_attn_ln_beta"]),
    }


def p1_inputs(inp, b=None):
    import ml_dtypes
    sel = np.zeros((16, 16 * 128), dtype=ml_dtypes.bfloat16)
    for e in range(16):
        sel[e, e * 128:(e + 1) * 128] = 1
    return {
        "rwt": _chunked(_t(inp["L0_ffn_router_w"])),
        "rb16": np.asarray(inp["L0_ffn_router_b"], np.float32).reshape(16, 1),
        "w1t": _t(inp["L0_ffn_w1"]),
        "b1c": _col8(inp["L0_ffn_b1"], 32),
        "w2t": _t(inp["L0_ffn_w2"]),
        "b2c": _col8(inp["L0_ffn_b2"]),
        "sel16s": sel,
    }


def p2_inputs(inp, b=None):
    import ml_dtypes
    alpha_rep = np.repeat(np.asarray(inp["L1_attn_alpha"]), 64)
    beta_rep = np.repeat(np.asarray(inp["L1_attn_beta"]), 64)
    ab = np.concatenate([_col8(alpha_rep), _col8(beta_rep)], axis=1)
    e2h = np.zeros((128, 8 * 16), dtype=ml_dtypes.bfloat16)
    h2e = np.zeros((16, 8 * 128), dtype=ml_dtypes.bfloat16)
    for c in range(8):
        for pp in range(128):
            h = 2 * c + pp // 64
            e2h[pp, c * 16 + h] = 1
            h2e[h, c * 128 + pp] = 1
    return {
        "ab_rep": np.ascontiguousarray(ab, np.float32),
        "e2h": e2h,
        "h2e": h2e,
        "wqkv": _chunked(np.asarray(inp["L1_attn_w_qkv"])),
        "wout": _chunked(np.asarray(inp["L1_attn_w_out"])),
    }


def p3_ffn_inputs(inp):
    W1 = np.asarray(inp["L1_ffn_W1"])
    W2 = np.asarray(inp["L1_ffn_W2"])
    b1 = np.asarray(inp["L1_ffn_b1"])
    w1s = np.zeros((6, 128, W1_COLS), np.float32)
    for li in range(6):
        for c in range(22):
            wc = W1_WC[c]
            blocks = []
            for k in range(8):
                blocks.append(W1[li, k * 128:(k + 1) * 128, c * 128:c * 128 + wc])
                blocks.append(W1[li, k * 128:(k + 1) * 128, 2730 + c * 128:2730 + c * 128 + wc])
            w1s[li, :, W1_OFF[c]:W1_OFF[c + 1]] = np.concatenate(blocks, axis=1)
    w2s = np.zeros((6, 128, 22 * 1024), np.float32)
    for li in range(6):
        for c in range(22):
            wc = W1_WC[c]
            w2s[li, 0:wc, c * 1024:(c + 1) * 1024] = W2[li, c * 128:c * 128 + wc, :]
    b1a = np.zeros((6, 128, 22), np.float32)
    b1g = np.zeros((6, 128, 22), np.float32)
    for li in range(6):
        for c in range(22):
            wc = W1_WC[c]
            b1a[li, 0:wc, c] = b1[li, c * 128:c * 128 + wc]
            b1g[li, 0:wc, c] = b1[li, 2730 + c * 128:2730 + c * 128 + wc]
    return {
        "w1s": w1s,
        "w2s": w2s,
        "b1a": b1a,
        "b1g": b1g,
        "b2c3": np.stack([_col8(np.asarray(inp["L1_ffn_b2"])[li]) for li in range(6)]),
        "lng": np.stack([_col8(np.asarray(inp["L1_ffn_ln_g"])[li]) for li in range(6)]),
        "lnb": np.stack([_col8(np.asarray(inp["L1_ffn_ln_b"])[li]) for li in range(6)]),
    }


def p3_attn_inputs(inp):
    import ml_dtypes
    Wqkv = np.asarray(inp["L1_ffn_Wqkv"])
    Wo = np.asarray(inp["L1_ffn_Wo"])
    qs = np.asarray(inp["L1_ffn_q_scale"])
    ks = np.asarray(inp["L1_ffn_k_scale"])
    wqkv3 = np.stack([_chunked(Wqkv[li]) for li in range(6)])
    wo3 = np.stack([_chunked(Wo[li]) for li in range(6)])
    half = np.arange(0, 64, 2, dtype=np.float64)
    inv_freq = 1.0 / 10000.0 ** (half / 64)
    j = np.arange(64)
    freq_j = inv_freq[j % 32]
    t = np.arange(T, dtype=np.float64)
    freqs = t[:, None] * freq_j[None, :]
    base_scale = (half + 0.4 * 64) / (1.4 * 64)
    bs_j = base_scale[j % 32]
    power = (t - T // 2) / 128.0
    scale = bs_j[None, :] ** power[:, None]
    sign = np.where(j < 32, -1.0, 1.0)
    cosf = np.cos(freqs)
    sinf = np.sin(freqs)
    t1q = 8.0 * cosf * scale
    t2q = 8.0 * sign[None, :] * sinf * scale
    t1k = cosf / scale
    t2k = sign[None, :] * sinf / scale
    tabs = np.zeros((128, 4 * T), np.float32)
    for jj, tab in enumerate([t1q, t2q, t1k, t2k]):
        block = tab.T.astype(np.float32)
        tabs[0:64, jj * T:(jj + 1) * T] = block
        tabs[64:128, jj * T:(jj + 1) * T] = block
    qcols = np.zeros((128, 4 * 6), np.float32)
    perm64 = (j + 32) % 64
    for li in range(6):
        for (jj, vec) in enumerate([qs[li], qs[li][perm64], ks[li], ks[li][perm64]]):
            qcols[0:64, 4 * li + jj] = vec
            qcols[64:128, 4 * li + jj] = vec
    prot = np.zeros((128, 128), dtype=ml_dtypes.bfloat16)
    for m in range(128):
        k = (m // 64) * 64 + ((m % 64) + 32) % 64
        prot[k, m] = 1
    trim = np.zeros((128, 256), dtype=ml_dtypes.bfloat16)
    for jj in range(128):
        trim[jj, 0:128][np.arange(128) <= jj] = 1
        trim[jj, 128:256][np.arange(128) >= jj] = 1
    e2h = np.zeros((128, 8 * 16), dtype=ml_dtypes.bfloat16)
    h2e = np.zeros((16, 8 * 128), dtype=ml_dtypes.bfloat16)
    for c in range(8):
        for pp in range(128):
            h = 2 * c + pp // 64
            e2h[pp, c * 16 + h] = 1
            h2e[h, c * 128 + pp] = 1
    return {"wqkv3": wqkv3, "wo3": wo3, "tabs": tabs, "qcols": qcols, "prot": prot,
            "trim": trim, "e2h_a": e2h, "h2e_a": h2e}


def dram_io(nc, names_in, names_out, shapes):
    d = {}
    for n in names_in:
        d[n] = nc.dram_tensor(n, shapes[n], BF16 if n in ("g_onb", "sel16s", "e2h", "h2e", "prot", "trim") else F32,
                              kind="ExternalInput").ap()
    for n in names_out:
        d[n] = nc.dram_tensor(n, shapes[n], F32, kind="ExternalOutput").ap()
    return d


COMMON_SHAPES = {"g_cp": [128, 12], "g_onb": [128, 1], "g_onf": [1, 128], "g_idf": [128, 128]}


def build_p0_solo():
    p = Prog()
    shapes = dict(COMMON_SHAPES)
    shapes.update({"x0": [D, T], "wq": [D, D], "wv": [D, D], "wg": [D, D], "wo": [D, D],
                   "lamsrc": [128, 8], "g8a": [128, 8], "b8a": [128, 8], "x1": [D, T]})
    A = dram_io(p.nc, ["g_cp", "g_onb", "g_onf", "g_idf", "x0", "wq", "wv", "wg", "wo",
                       "lamsrc", "g8a", "b8a"], ["x1"], shapes)
    cm = Common(p, A)
    p.mark_global()
    emit_p0(p, cm, A)
    p.op("sync", None, waits=[(cm.OUT, p.n(cm.OUT))])
    return p.build()


def build_p1_solo():
    p = Prog()
    shapes = dict(COMMON_SHAPES)
    shapes.update({"x1": [D, T], "rwt": [128, 128], "rb16": [16, 1], "w1t": [D, 4096],
                   "b1c": [128, 32], "w2t": [4096, D], "b2c": [128, 8],
                   "sel16s": [16, 16 * 128], "x2": [D, T]})
    A = dram_io(p.nc, ["g_cp", "g_onb", "g_onf", "g_idf", "x1", "rwt", "rb16", "w1t",
                       "b1c", "w2t", "b2c", "sel16s"], ["x2"], shapes)
    cm = Common(p, A)
    p.mark_global()
    emit_p1(p, cm, A)
    p.op("sync", None, waits=[(cm.OUT, p.n(cm.OUT))])
    return p.build()


def run_p0(inp):
    nc = build_p0_solo()
    cmh = common_host_inputs()
    in_maps = [dict(cmh, **p0_inputs(inp, b)) for b in range(NCORE)]
    res = run_bass_kernel_spmd(nc, in_maps, core_ids=list(range(NCORE)))
    return np.stack([np.ascontiguousarray(r["x1"].T) for r in res.results])


def build_p1_solo_dbg():
    p = Prog()
    shapes = dict(COMMON_SHAPES)
    shapes.update({"x1": [D, T], "rwt": [128, 128], "rb16": [16, 1], "w1t": [D, 4096],
                   "b1c": [128, 32], "w2t": [4096, D], "b2c": [128, 8],
                   "sel16s": [16, 16 * 128], "x2": [D, T],
                   "dbg_rl": [16, TC], "dbg_rlT": [128, 64], "dbg_mtm": [128, 64],
                   "dbg_mbf": [16, TC], "dbg_hb": [128, 4 * TC]})
    A = dram_io(p.nc, ["g_cp", "g_onb", "g_onf", "g_idf", "x1", "rwt", "rb16", "w1t",
                       "b1c", "w2t", "b2c", "sel16s"],
                ["x2", "dbg_rl", "dbg_rlT", "dbg_mtm", "dbg_mbf", "dbg_hb"], shapes)
    A["dbg_mbf"] = p.nc.dram_tensor("dbg_mbf_b", [16, TC], BF16, kind="ExternalOutput").ap()
    A["dbg_hb"] = p.nc.dram_tensor("dbg_hb_b", [128, 4 * TC], BF16, kind="ExternalOutput").ap()
    cm = Common(p, A)
    p.mark_global()
    emit_p1(p, cm, A)
    p.op("sync", None, waits=[(cm.OUT, p.n(cm.OUT))])
    return p.build()


def run_p1_dbg(inp, x1_full):
    nc = build_p1_solo_dbg()
    cmh = common_host_inputs()
    w = p1_inputs(inp)
    in_maps = [dict(cmh, **w, x1=_t(x1_full[b])) for b in range(1)] * 1
    res = run_bass_kernel_spmd(nc, in_maps, core_ids=[0])
    return res.results[0]


def build_p2_solo():
    p = Prog()
    shapes = dict(COMMON_SHAPES)
    shapes.update({"x2": [D, T], "ab_rep": [128, 16], "e2h": [128, 8 * 16], "h2e": [16, 8 * 128],
                   "wqkv": [128, 8 * 3072], "wout": [128, 16 * D], "x3": [D, T]})
    A = dram_io(p.nc, ["g_cp", "g_onb", "g_onf", "g_idf", "x2", "ab_rep", "e2h", "h2e",
                       "wqkv", "wout"], ["x3"], shapes)
    cm = Common(p, A)
    p.mark_global()
    emit_p2(p, cm, A)
    p.op("sync", None, waits=[(cm.OUT, p.n(cm.OUT))])
    return p.build()


def run_p2(inp, x2_full):
    nc = build_p2_solo()
    cmh = common_host_inputs()
    w = p2_inputs(inp)
    in_maps = [dict(cmh, **w, x2=_t(x2_full[b])) for b in range(NCORE)]
    res = run_bass_kernel_spmd(nc, in_maps, core_ids=list(range(NCORE)))
    return np.stack([np.ascontiguousarray(r["x3"].T) for r in res.results])


def build_p3f_solo(li, final=False):
    p = Prog()
    shapes = dict(COMMON_SHAPES)
    shapes.update({"xin": [D, T], "w1s": [1, 128, W1_COLS], "w2s": [1, 128, 22 * D],
                   "b1a": [1, 128, 22], "b1g": [1, 128, 22], "b2c3": [1, 128, 8],
                   "lng": [1, 128, 8], "lnb": [1, 128, 8], "x3s": [D, T], "xo": [D, T]})
    names = ["g_cp", "g_onb", "g_onf", "g_idf", "xin", "w1s", "w2s", "b1a", "b1g",
             "b2c3", "lng", "lnb"] + (["x3s"] if final else [])
    A = dram_io(p.nc, names, ["xo"], shapes)
    cm = Common(p, A)
    p.mark_global()
    emit_p3f(p, cm, A, 0, A["xin"], A["xo"], final=final, x3s=A.get("x3s"))
    p.op("sync", None, waits=[(cm.OUT, p.n(cm.OUT))])
    return p.build()


def run_p3f(inp, x_full, li, final=False, x3_full=None):
    nc = build_p3f_solo(li, final)
    cmh = common_host_inputs()
    w = slice_p3f_w(p3_ffn_inputs(inp), li)
    in_maps = []
    for b in range(NCORE):
        m = dict(cmh, **w, xin=_t(x_full[b]))
        if final:
            m["x3s"] = _t(x3_full[b])
        in_maps.append(m)
    res = run_bass_kernel_spmd(nc, in_maps, core_ids=list(range(NCORE)))
    return np.stack([np.ascontiguousarray(r["xo"].T) for r in res.results])


def build_p3a_solo(li):
    p = Prog()
    shapes = dict(COMMON_SHAPES)
    shapes.update({"xin": [D, T], "wqkv3": [1, 128, 8 * 3072], "wo3": [1, 128, 8 * D],
                   "tabs": [128, 4 * T], "qcols": [128, 4], "prot": [128, 128],
                   "trim": [128, 256], "e2h": [128, 8 * 16], "h2e": [16, 8 * 128],
                   "xo": [D, T]})
    A = dram_io(p.nc, ["g_cp", "g_onb", "g_onf", "g_idf", "xin", "wqkv3", "wo3", "tabs",
                       "qcols", "prot", "trim", "e2h", "h2e"], ["xo"], shapes)
    cm = Common(p, A)
    p.mark_global()
    emit_p3a(p, cm, A, 0, A["xin"], A["xo"])
    p.op("sync", None, waits=[(cm.OUT, p.n(cm.OUT))])
    return p.build()


def slice_p3a_w(w, li):
    w = dict(w)
    w["e2h"] = w.pop("e2h_a", w.get("e2h"))
    w["h2e"] = w.pop("h2e_a", w.get("h2e"))
    w["wqkv3"] = w["wqkv3"][li:li + 1]
    w["wo3"] = w["wo3"][li:li + 1]
    w["qcols"] = np.ascontiguousarray(w["qcols"][:, 4 * li:4 * li + 4])
    return w


def slice_p3f_w(w, li):
    w = dict(w)
    for k in ("w1s", "w2s", "b1a", "b1g", "b2c3", "lng", "lnb"):
        w[k] = w[k][li:li + 1]
    return w


def run_p3a(inp, x_full, li):
    nc = build_p3a_solo(li)
    cmh = common_host_inputs()
    w = slice_p3a_w(p3_attn_inputs(inp), li)
    in_maps = [dict(cmh, **w, xin=_t(x_full[b])) for b in range(NCORE)]
    res = run_bass_kernel_spmd(nc, in_maps, core_ids=list(range(NCORE)))
    return np.stack([np.ascontiguousarray(r["xo"].T) for r in res.results])


def run_p1(inp, x1_full):
    nc = build_p1_solo()
    cmh = common_host_inputs()
    w = p1_inputs(inp)
    in_maps = [dict(cmh, **w, x1=_t(x1_full[b])) for b in range(NCORE)]
    res = run_bass_kernel_spmd(nc, in_maps, core_ids=list(range(NCORE)))
    return np.stack([np.ascontiguousarray(r["x2"].T) for r in res.results])


def _run(nc, in_maps, outkey):
    res = run_bass_kernel_spmd(nc, in_maps, core_ids=list(range(NCORE)))
    return [r[outkey] for r in res.results]





def build_fused():
    p = Prog()
    nc = p.nc
    shapes = dict(COMMON_SHAPES)
    shapes.update({
        "x0": [D, T],
        "wq": [D, D], "wv": [D, D], "wg": [D, D], "wo": [D, D],
        "lamsrc": [128, 8], "g8a": [128, 8], "b8a": [128, 8],
        "rwt": [128, 128], "rb16": [16, 1], "w1t": [D, 4096], "b1c": [128, 32],
        "w2t": [4096, D], "b2c": [128, 8], "sel16s": [16, 16 * 128],
        "ab_rep": [128, 16], "e2h": [128, 8 * 16], "h2e": [16, 8 * 128],
        "wqkv": [128, 8 * 3072], "wout": [128, 16 * D],
        "wqkv3": [6, 128, 8 * 3072], "wo3": [6, 128, 8 * D],
        "tabs": [128, 4 * T], "qcols": [128, 24], "prot": [128, 128], "trim": [128, 256],
        "w1s": [6, 128, W1_COLS], "w2s": [6, 128, 22 * D],
        "b1a": [6, 128, 22], "b1g": [6, 128, 22], "b2c3": [6, 128, 8],
        "lng": [6, 128, 8], "lnb": [6, 128, 8],
    })
    names_in = list(shapes.keys())
    A = dram_io(nc, names_in, ["x4"], dict(shapes, x4=[D, T]))
    sA = nc.dram_tensor("sA", [D, T], F32).ap()
    sB = nc.dram_tensor("sB", [D, T], F32).ap()
    sC = nc.dram_tensor("sC", [D, T], F32).ap()
    cm = Common(p, A)
    p.mark_global()

    A0 = dict(A)
    A0["x0"], A0["x1"] = A["x0"], sA
    emit_p0(p, cm, A0)
    A1 = dict(A)
    A1["x1"], A1["x2"] = sA, sB
    emit_p1(p, cm, A1)
    A2 = dict(A)
    A2["x2"], A2["x3"] = sB, sC
    emit_p2(p, cm, A2)
    bufs = [sC, sA, sB]
    cur = 0
    for li in range(6):
        nxt = 1 if bufs[cur] is not sA else 2
        cand = [b for b in (sA, sB) if b is not bufs[cur]]
        xo = cand[0]
        emit_p3a(p, cm, A, li, bufs[cur], xo)
        cur = bufs.index(xo)
        cand = [b for b in (sA, sB) if b is not bufs[cur]]
        xo2 = A["x4"] if li == 5 else cand[0]
        emit_p3f(p, cm, A, li, bufs[cur], xo2, final=(li == 5), x3s=sC)
        if li < 5:
            cur = bufs.index(xo2)
    p.op("sync", None, waits=[(cm.OUT, p.n(cm.OUT))])
    return p.build()


def fused_inputs(inputs):
    cmh = common_host_inputs()
    w0 = p0_inputs(inputs, 0)
    del w0["x0"]
    w1 = p1_inputs(inputs)
    w2 = p2_inputs(inputs)
    wa = p3_attn_inputs(inputs)
    wa = dict(wa)
    wa.pop("e2h_a")
    wa.pop("h2e_a")
    wf = p3_ffn_inputs(inputs)
    base = {}
    for d in (cmh, w0, w1, w2, wa, wf):
        base.update(d)
    return base


_FUSED_NC = None


def kernel(**inputs):
    global _FUSED_NC
    if _FUSED_NC is None:
        _FUSED_NC = build_fused()
    base = fused_inputs(inputs)
    x = np.asarray(inputs["L0_attn_x"])
    in_maps = [dict(base, x0=_t(x[b])) for b in range(NCORE)]
    res = run_bass_kernel_spmd(_FUSED_NC, in_maps, core_ids=list(range(NCORE)))
    return np.stack([np.ascontiguousarray(r["x4"].T) for r in res.results]).astype(np.float32)


def kernel_unfused(**inputs):
    cmh = common_host_inputs()
    xT = [_t(np.asarray(inputs["L0_attn_x"])[b]) for b in range(NCORE)]

    w0 = p0_inputs(inputs, 0)
    del w0["x0"]
    x1 = _run(build_p0_solo(), [dict(cmh, **w0, x0=xT[b]) for b in range(NCORE)], "x1")

    w1 = p1_inputs(inputs)
    x2 = _run(build_p1_solo(), [dict(cmh, **w1, x1=x1[b]) for b in range(NCORE)], "x2")

    w2 = p2_inputs(inputs)
    x3 = _run(build_p2_solo(), [dict(cmh, **w2, x2=x2[b]) for b in range(NCORE)], "x3")

    wa = p3_attn_inputs(inputs)
    wf = p3_ffn_inputs(inputs)
    x = x3
    for li in range(6):
        wal = slice_p3a_w(wa, li)
        x = _run(build_p3a_solo(li), [dict(cmh, **wal, xin=x[b]) for b in range(NCORE)], "xo")
        wfl = slice_p3f_w(wf, li)
        maps = []
        for b in range(NCORE):
            m = dict(cmh, **wfl, xin=x[b])
            if li == 5:
                m["x3s"] = x3[b]
            maps.append(m)
        x = _run(build_p3f_solo(li, final=(li == 5)), maps, "xo")

    return np.stack([np.ascontiguousarray(x[b].T) for b in range(NCORE)]).astype(np.float32)


# version 05
# speedup vs baseline: 1.1766x; 1.1766x over previous
import numpy as np
import concourse.bass as bass
import concourse.mybir as mybir
from concourse.bass_utils import run_bass_kernel_spmd

F32 = mybir.dt.float32
BF16 = mybir.dt.bfloat16
AF = mybir.ActivationFunctionType
ALU = mybir.AluOpType
AX = mybir.AxisListType

T = 4096
D = 1024
TC = 512
NT = T // TC
NF = D // 128
NCORE = 8

SBUF_BASE = 16384
SBUF_BUDGET = 229376 - 384

ENGINES = ("sync", "gpsimd", "vector", "scalar", "tensor")

CP_ZERO, CP_EPS, CP_INVD, CP_ONE, CP_NEG, CP_EPS6, CP_EPS12, CP_EPS8, CP_M1, CP_MH, CP_EPS24, CP_SPARE = range(12)
CP_VALS = np.array([0.0, 1e-5, 1.0 / D, 1.0, -1e30, 1e-6, 1e-12, 1e-8, -1.0, -0.5, 1e-24, 0.0], np.float32)


class Prog:

    def __init__(self):
        self.nc = bass.Bass("TRN2", target_bir_lowering=False)
        self.q = {e: [] for e in ENGINES}
        self._sems = {}
        self._off = SBUF_BASE
        self._phase_base = SBUF_BASE
        self._entry_marks = []
        self._entered = set(ENGINES)

    def sbuf(self, name, shape, dtype):
        nbytes = int(np.prod(shape[1:])) * mybir.dt.size(dtype)
        t = self.nc.alloc_sbuf_tensor_at(name, shape, dtype, offset=self._off)
        self._off = (self._off + nbytes + 31) & ~31
        assert self._off <= SBUF_BUDGET, f"SBUF overflow at {name}: {self._off}"
        return t

    def mark_global(self):
        self._phase_base = self._off

    def phase_reset(self):
        self._off = self._phase_base
        self._entry_marks = [(s, rec[1]) for s, rec in self._sems.items()
                             if s in ("s_t", "s_v", "s_a", "s_o")]
        self._entered = set()

    def sem(self, name):
        if name not in self._sems:
            self._sems[name] = [self.nc.alloc_semaphore(name), 0]
        return name

    def n(self, s):
        return self._sems[s][1]

    def op(self, eng, fn, waits=(), inc=None, k=1):
        waits = list(waits)
        if eng not in self._entered:
            self._entered.add(eng)
            waits = self._entry_marks + waits
        w = [(self._sems[s][0], int(v)) for (s, v) in waits if int(v) > 0]
        entry_inc = None
        r = None
        if inc is not None:
            rec = self._sems[inc]
            rec[1] += k
            entry_inc = (rec[0], k)
            r = rec[1]
        self.q[eng].append((w, fn, entry_inc))
        return r

    def barrier(self):
        return {s: rec[1] for s, rec in self._sems.items()}

    def build(self):
        with self.nc.Block() as block:
            for e in ENGINES:
                entries = self.q[e]
                if not entries:
                    continue

                def mk(entries=entries):
                    def body(eng):
                        hi = {}
                        for w, fn, inc in entries:
                            for (h, v) in w:
                                if hi.get(h.num, -1) < v:
                                    eng.wait_ge(h, v)
                                    hi[h.num] = v
                            inst = fn(eng) if fn is not None else None
                            if inc is not None:
                                if inst is None:
                                    inst = eng.nop()
                                inst.then_inc(inc[0], inc[1])
                    return body

                getattr(block, e)(mk())
        return self.nc


def cs(c, w=TC):
    return slice(c * w, (c + 1) * w)


class Common:

    def __init__(self, p, dram):
        nc = p.nc
        self.pj = [nc.alloc_psum_tensor(f"pj{i}", [128, TC], F32) for i in range(8)]
        self.cp = p.sbuf("g_cp", [128, 12], F32)
        self.onb = p.sbuf("g_onb", [128, 1], BF16)
        self.onf = p.sbuf("g_onf", [1, 128], F32)
        self.idf = p.sbuf("g_idf", [128, 128], F32)
        self.rows = p.sbuf("g_rows", [1, 3 * TC], F32)
        self.IN = p.sem("s_in")
        self.GIN = p.sem("s_gin")
        self.GW = [p.sem("s_gw0"), p.sem("s_gw1")]
        self.TT = p.sem("s_t")
        self.VV = p.sem("s_v")
        self.AA = p.sem("s_a")
        self.OUT = p.sem("s_o")
        for dst, key in [(self.cp, "g_cp"), (self.onb, "g_onb"), (self.onf, "g_onf"),
                         (self.idf, "g_idf")]:
            p.op("sync", lambda e, d=dst, s=dram[key]: e.dma_start(out=d[:, :], in_=s),
                 inc=self.IN, k=16)
        self.init_in = p.n(self.IN)


def common_host_inputs():
    import ml_dtypes
    return {
        "g_cp": np.broadcast_to(CP_VALS, (128, 12)).copy(),
        "g_onb": np.ones((128, 1), dtype=ml_dtypes.bfloat16),
        "g_onf": np.ones((1, 128), dtype=np.float32),
        "g_idf": np.eye(128, dtype=np.float32),
    }


def emit_ln(p, cm, i, src, dst, tmp, xb, sq, gcol=None, bcol=None, guards=(), tc=TC):
    TT_, VV, AA = cm.TT, cm.VV, cm.AA
    pa, pb, pbm, pbr = cm.pj[0], cm.pj[1], cm.pj[2], cm.pj[3]
    sf = src if callable(src) else (lambda c: src[:, cs(c, tc)])
    df = dst if callable(dst) else (lambda c: dst[:, cs(c, tc)])
    v_src = p.n(VV)
    for c in range(NF):
        p.op("scalar", lambda e, c=c: e.copy(out=xb[:, cs(c, tc)], in_=sf(c)),
             waits=[(VV, v_src)] + list(guards), inc=None)
    for c in range(NF):
        p.op("scalar", lambda e, c=c: e.activation(out=sq[:, cs(c, tc)], in_=sf(c), func=AF.Square),
             inc=(AA if c == NF - 1 else None))
    a_cast = p.n(AA)
    for k in range(8):
        p.op("tensor", lambda e, k=k: e.matmul(pa[0:1, 0:tc], cm.onb[:, :], xb[:, cs(k, tc)],
                                               start=(k == 0), stop=(k == 7)),
             waits=[(AA, a_cast)] + list(guards) if k == 0 else (), inc=None)
    for k in range(8):
        p.op("tensor", lambda e, k=k: e.matmul(pb[0:1, 0:tc], cm.onb[:, :], sq[:, cs(k, tc)],
                                               start=(k == 0), stop=(k == 7)),
             inc=(TT_ if k == 7 else None))
    t_stat = p.n(TT_)
    R_mean, R_B, R_C = (cm.rows[0:1, j * TC: j * TC + tc] for j in range(3))
    R_ey2, R_msq, R_var, R_std, R_rstd = R_B, R_C, R_B, R_C, R_B
    v1 = p.op("vector", lambda e: e.tensor_scalar(out=R_mean, in0=pa[0:1, 0:tc], scalar1=cm.cp[0:1, CP_INVD:CP_INVD + 1],
                                                  scalar2=None, op0=ALU.mult),
              waits=[(TT_, t_stat)], inc=VV)
    p.op("vector", lambda e: e.tensor_scalar(out=R_ey2, in0=pb[0:1, 0:tc], scalar1=cm.cp[0:1, CP_INVD:CP_INVD + 1],
                                             scalar2=None, op0=ALU.mult), inc=VV)
    v2 = p.n(VV)
    a1 = p.op("scalar", lambda e: e.activation(out=R_msq, in_=R_mean, func=AF.Square),
              waits=[(VV, v2)], inc=AA)
    v3 = p.op("vector", lambda e: e.tensor_tensor(out=R_var, in0=R_ey2, in1=R_msq, op=ALU.subtract),
              waits=[(AA, a1)], inc=VV)
    a2 = p.op("scalar", lambda e: e.activation(out=R_std, in_=R_var, func=AF.Sqrt,
                                               bias=cm.cp[0:1, CP_EPS:CP_EPS + 1]),
              waits=[(VV, v3)], inc=AA)
    v4 = p.op("vector", lambda e: e.reciprocal(out=R_rstd, in_=R_std), waits=[(AA, a2)], inc=VV)
    p.op("tensor", lambda e: e.matmul(pbm[:, 0:tc], cm.onf[:, :], R_mean, start=True, stop=True),
         waits=[(VV, v1)], inc=None)
    p.op("tensor", lambda e: e.matmul(pbr[:, 0:tc], cm.onf[:, :], R_rstd, start=True, stop=True),
         waits=[(VV, v4)], inc=TT_)
    t_bc = p.n(TT_)
    for c in range(NF):
        p.op("vector", lambda e, c=c: e.tensor_tensor(out=tmp[:, cs(c, tc)], in0=sf(c),
                                                      in1=pbm[:, 0:tc], op=ALU.subtract),
             waits=[(TT_, t_bc)], inc=None)
        if gcol is None:
            p.op("vector", lambda e, c=c: e.tensor_tensor(out=df(c), in0=tmp[:, cs(c, tc)],
                                                          in1=pbr[:, 0:tc], op=ALU.mult),
                 inc=(VV if c == NF - 1 else None))
        else:
            p.op("vector", lambda e, c=c: e.tensor_tensor(out=tmp[:, cs(c, tc)], in0=tmp[:, cs(c, tc)],
                                                          in1=pbr[:, 0:tc], op=ALU.mult), inc=None)
            p.op("vector", lambda e, c=c: e.tensor_scalar(out=df(c), in0=tmp[:, cs(c, tc)],
                                                          scalar1=gcol[c], scalar2=bcol[c],
                                                          op0=ALU.mult, op1=ALU.add),
                 inc=(VV if c == NF - 1 else None))
    return p.n(VV)



def emit_p0(p, cm, A):
    p.phase_reset()
    s_wq = p.sbuf("p0_wq", [128, 8 * D], BF16)
    s_wv = p.sbuf("p0_wv", [128, 8 * D], BF16)
    s_wg = p.sbuf("p0_wg", [128, 8 * D], BF16)
    s_wo = p.sbuf("p0_wo", [128, 8 * D], BF16)
    xf = p.sbuf("p0_xf", [128, T], F32)
    xb = p.sbuf("p0_xb", [128, T], BF16)
    sq = p.sbuf("p0_sq", [128, T], BF16)
    qf = p.sbuf("p0_qf", [128, T], F32)
    vf = p.sbuf("p0_vf", [128, T], F32)
    stt = p.sbuf("p0_st", [128, T], F32)
    gf = p.sbuf("p0_gf", [128, T], F32)
    ub = p.sbuf("p0_ub", [128, T], BF16)
    lamt = p.sbuf("p0_lamt", [128, T], F32)
    s_lam = p.sbuf("p0_lam8", [128, 8], F32)
    s_sig = p.sbuf("p0_sig8", [128, 8], F32)
    s_g8 = p.sbuf("p0_g8", [128, 8], F32)
    s_b8 = p.sbuf("p0_b8", [128, 8], F32)
    carry = p.sbuf("p0_carry", [128, 8], F32)

    pj = cm.pj
    IN, GIN, TT_, VV, AA, OUT = cm.IN, cm.GIN, cm.TT, cm.VV, cm.AA, cm.OUT
    x0, x1 = A["x0"], A["x1"]

    for dst, src in [(s_lam, A["lamsrc"]), (s_g8, A["g8a"]), (s_b8, A["b8a"])]:
        p.op("sync", lambda e, d=dst, s=src: e.dma_start(out=d[:, :], in_=s), inc=IN, k=16)
    in_init = p.n(IN)
    for w_sb, w_hbm in [(s_wq, A["wq"]), (s_wv, A["wv"]), (s_wg, A["wg"]), (s_wo, A["wo"])]:
        for k in range(8):
            p.op("gpsimd", lambda e, d=w_sb, s=w_hbm, k=k:
                 e.dma_start(out=d[:, cs(k, D)], in_=s[k * 128:(k + 1) * 128, :]), inc=GIN, k=16)

    a_sig = p.op("scalar", lambda e: e.activation(out=s_sig[:, :], in_=s_lam[:, :], func=AF.Sigmoid),
                 waits=[(IN, in_init)], inc=AA)
    v0 = p.op("vector", lambda e: e.memset(gf[:, 0:TC], 1.0), inc=VV)
    for c in range(NF):
        p.op("vector", lambda e, c=c: e.tensor_scalar(
            out=lamt[:, cs(c)], in0=gf[:, 0:TC], scalar1=s_sig[:, c:c + 1], scalar2=None, op0=ALU.mult),
            waits=[(VV, v0), (AA, a_sig)], inc=(VV if c == NF - 1 else None))
    p.op("vector", lambda e: e.memset(carry[:, :], 0.0), inc=VV)
    v_init = p.n(VV)

    t_end = a_end = out_end = 0
    v_end = v_init
    gcols = [s_g8[:, c:c + 1] for c in range(NF)]
    bcols = [s_b8[:, c:c + 1] for c in range(NF)]

    for i in range(NT):
        tsl = slice(i * TC, (i + 1) * TC)
        bar = [(TT_, t_end), (VV, v_end), (AA, a_end), (OUT, out_end)]
        for c in range(NF):
            p.op("sync", lambda e, c=c, tsl=tsl: e.dma_start(out=xf[:, cs(c)], in_=x0[c * 128:(c + 1) * 128, tsl]),
                 waits=bar, inc=IN, k=16)
            p.op("gpsimd", lambda e, c=c, tsl=tsl: e.dma_start(out=xb[:, cs(c)], in_=x0[c * 128:(c + 1) * 128, tsl]),
                 waits=bar, inc=GIN, k=16)
        in_x = p.n(IN)
        gin_x = p.n(GIN)

        a_evac0 = p.n(AA)
        t_grp = []
        for g24 in range(24):
            m = g24 % 8
            wsb = (s_wq, s_wv, s_wg)[g24 // 8]
            pjt = pj[4 + g24 % 4]
            w = [(GIN, gin_x)]
            if g24 >= 4:
                w.append((AA, a_evac0 + g24 - 3))
            for k in range(8):
                p.op("tensor", lambda e, pjt=pjt, wsb=wsb, k=k, m=m: e.matmul(
                    pjt[:, :], wsb[:, k * D + m * 128: k * D + (m + 1) * 128], xb[:, cs(k)],
                    start=(k == 0), stop=(k == 7)),
                    waits=w if k == 0 else (), inc=(TT_ if k == 7 else None))
            t_grp.append(p.n(TT_))
        for g24 in range(24):
            m = g24 % 8
            pjt = pj[4 + g24 % 4]
            w = [(TT_, t_grp[g24])]
            if g24 < 8:
                fn = lambda e, pjt=pjt, m=m: e.copy(out=qf[:, cs(m)], in_=pjt[:, :])
            elif g24 < 16:
                fn = lambda e, pjt=pjt, m=m: e.copy(out=vf[:, cs(m)], in_=pjt[:, :])
            else:
                fn = lambda e, pjt=pjt, m=m: e.activation(out=gf[:, cs(m)], in_=pjt[:, :], func=AF.Silu)
                w.append((OUT, out_end))
            p.op("scalar", fn, waits=w, inc=AA)
        a_evacs = p.n(AA)

        v_pre = p.n(VV)
        for c in range(NF):
            p.op("vector", lambda e, c=c: e.tensor_tensor_scan(
                out=stt[:, cs(c)], data0=lamt[:, cs(c)], data1=vf[:, cs(c)],
                initial=carry[:, c:c + 1], op0=ALU.mult, op1=ALU.add),
                waits=[(AA, a_evac0 + 16), (VV, v_init if i == 0 else v_pre)],
                inc=(VV if c == NF - 1 else None))
        v_scan = p.n(VV)
        for c in range(NF):
            p.op("vector", lambda e, c=c: e.tensor_copy(out=carry[:, c:c + 1],
                                                        in_=stt[:, c * TC + TC - 1: c * TC + TC]),
                 waits=[(VV, v_scan)], inc=(VV if c == NF - 1 else None))
        v_carry = p.n(VV)
        for c in range(NF):
            p.op("vector", lambda e, c=c: e.tensor_tensor(
                out=vf[:, cs(c)], in0=qf[:, cs(c)], in1=stt[:, cs(c)], op=ALU.mult),
                waits=[(VV, v_carry), (AA, a_evac0 + 8)], inc=(VV if c == NF - 1 else None))

        v_ln1 = emit_ln(p, cm, i, vf, stt, stt, xb, sq, gcol=gcols, bcol=bcols,
                        guards=[(AA, a_evacs)])

        for c in range(NF):
            p.op("vector", lambda e, c=c: e.tensor_tensor(out=ub[:, cs(c)], in0=stt[:, cs(c)],
                                                          in1=gf[:, cs(c)], op=ALU.mult),
                 waits=[(VV, v_ln1), (AA, a_evacs)], inc=(VV if c == NF - 1 else None))
        v_u = p.n(VV)

        t_wo = []
        for m in range(8):
            w = [(VV, v_u)]
            if m >= 4:
                w.append((VV, v_u + (m - 4) + 1))
            for k in range(8):
                p.op("tensor", lambda e, m=m, k=k: e.matmul(
                    pj[4 + m % 4][:, :], s_wo[:, k * D + m * 128: k * D + (m + 1) * 128], ub[:, cs(k)],
                    start=(k == 0), stop=(k == 7)),
                    waits=w if k == 0 else (), inc=(TT_ if k == 7 else None))
            t_wo.append(p.n(TT_))
        for m in range(8):
            p.op("vector", lambda e, m=m: e.scalar_tensor_tensor(
                out=qf[:, cs(m)], in0=pj[4 + m % 4][:, :], scalar=cm.cp[:, CP_ZERO:CP_ZERO + 1],
                in1=xf[:, cs(m)], op0=ALU.bypass, op1=ALU.add),
                waits=[(TT_, t_wo[m]), (IN, in_x)], inc=VV)

        v_x1 = emit_ln(p, cm, i, qf, gf, stt, xb, sq)

        for c in range(NF):
            p.op("sync", lambda e, c=c, tsl=tsl: e.dma_start(out=x1[c * 128:(c + 1) * 128, tsl], in_=gf[:, cs(c)]),
                 waits=[(VV, v_x1)], inc=OUT, k=16)

        t_end, v_end, a_end, out_end = p.n(TT_), p.n(VV), p.n(AA), p.n(OUT)



def emit_p1(p, cm, A):
    p.phase_reset()
    NE = 32
    s_w1 = p.sbuf("p1_w1", [128, 8 * 4096], BF16)
    s_rwt = p.sbuf("p1_rwt", [128, 8 * 16], F32)
    s_rb = p.sbuf("p1_rb", [16, 1], F32)
    s_b1 = p.sbuf("p1_b1", [128, NE], F32)
    s_b2 = p.sbuf("p1_b2", [128, 8], F32)
    s_sel = p.sbuf("p1_sel", [16, 16 * 128], BF16)
    xf = p.sbuf("p1_xf", [128, T], F32)
    xb = p.sbuf("p1_xb", [128, T], BF16)
    sq = p.sbuf("p1_sq", [128, T], BF16)
    hb = p.sbuf("p1_hb", [128, NE * TC], BF16)
    w2b = p.sbuf("p1_w2b", [128, 2 * D], BF16)
    zf = p.sbuf("p1_zf", [128, T], F32)
    tmp = p.sbuf("p1_tmp", [128, T], F32)
    rl = p.sbuf("p1_rl", [16, TC], F32)
    rlT = p.sbuf("p1_rlT", [128, 64], F32)
    work = p.sbuf("p1_work", [128, 64], F32)
    mtm = p.sbuf("p1_mtm", [128, 64], F32)
    mcol = p.sbuf("p1_mcol", [128, 1], F32)
    eqt = p.sbuf("p1_eqt", [128, 16], F32)
    mbf = p.sbuf("p1_mbf", [16, TC], BF16)

    pj = cm.pj
    IN, GIN, TT_, VV, AA, OUT = cm.IN, cm.GIN, cm.TT, cm.VV, cm.AA, cm.OUT
    GW = cm.GW
    x1, x2, w2t = A["x1"], A["x2"], A["w2t"]

    for dst, src in [(s_rwt, A["rwt"]), (s_rb, A["rb16"]), (s_b1, A["b1c"]), (s_b2, A["b2c"])]:
        p.op("sync", lambda e, d=dst, s=src: e.dma_start(out=d[:, :], in_=s), inc=IN, k=16)
    p.op("gpsimd", lambda e: e.dma_start(out=s_sel[:, :], in_=A["sel16s"]), inc=GIN, k=16)
    in_init = p.n(IN)
    for k in range(8):
        p.op("gpsimd", lambda e, k=k: e.dma_start(out=s_w1[:, k * 4096:(k + 1) * 4096],
                                                  in_=A["w1t"][k * 128:(k + 1) * 128, :]), inc=GIN, k=16)

    t_end, v_end, a_end, out_end = p.n(TT_), p.n(VV), p.n(AA), p.n(OUT)

    for i in range(NT):
        tsl = slice(i * TC, (i + 1) * TC)
        bar = [(TT_, t_end), (VV, v_end), (AA, a_end), (OUT, out_end)]
        for c in range(NF):
            p.op("sync", lambda e, c=c, tsl=tsl: e.dma_start(out=xf[:, cs(c)], in_=x1[c * 128:(c + 1) * 128, tsl]),
                 waits=bar, inc=IN, k=16)
            p.op("gpsimd", lambda e, c=c, tsl=tsl: e.dma_start(out=xb[:, cs(c)], in_=x1[c * 128:(c + 1) * 128, tsl]),
                 waits=bar, inc=GIN, k=16)
        in_x = p.n(IN)
        gin_x = p.n(GIN)

        for k in range(8):
            p.op("tensor", lambda e, k=k: e.matmul(pj[4][0:16, :], s_rwt[:, k * 16:(k + 1) * 16],
                                                   xf[:, cs(k)], start=(k == 0), stop=(k == 7)),
                 waits=[(IN, in_x)] + bar if k == 0 else (), inc=(TT_ if k == 7 else None))
        t_rl = p.n(TT_)
        v_rl = p.op("vector", lambda e: e.tensor_scalar(out=rl[:, :], in0=pj[4][0:16, :],
                                                        scalar1=s_rb[:, :], scalar2=None, op0=ALU.add),
                    waits=[(TT_, t_rl)] + bar, inc=VV)
        for j in range(4):
            p.op("tensor", lambda e, j=j: e.transpose(pj[5][:, j * 16:(j + 1) * 16],
                                                      rl[0:16, j * 128:(j + 1) * 128], cm.idf[0:16, 0:16]),
                 waits=[(VV, v_rl)] if j == 0 else (), inc=(TT_ if j == 3 else None))
        t_tp = p.n(TT_)
        v_t0 = p.op("vector", lambda e: e.tensor_copy(out=rlT[:, :], in_=pj[5][:, 0:64]),
                    waits=[(TT_, t_tp)], inc=VV)
        for j in range(4):
            blk = rlT[:, j * 16:(j + 1) * 16]
            wblk = work[:, j * 16:(j + 1) * 16]
            last = p.n(VV)
            for r in range(3):
                srcb = blk if r == 0 else wblk
                v_m = p.op("vector", lambda e, srcb=srcb: e.reduce_max(out=mcol[:, :], in_=srcb, axis=AX.X),
                           waits=[(VV, last)], inc=VV)
                v_e = p.op("vector", lambda e, srcb=srcb: e.tensor_scalar(
                    out=eqt[:, :], in0=srcb, scalar1=mcol[:, :], scalar2=None, op0=ALU.is_ge),
                    waits=[(VV, v_m)], inc=VV)
                last = p.op("vector", lambda e, srcb=srcb, wblk=wblk: e.scalar_tensor_tensor(
                    out=wblk, in0=eqt[:, :], scalar=cm.cp[:, CP_NEG:CP_NEG + 1], in1=srcb,
                    op0=ALU.mult, op1=ALU.add),
                    waits=[(VV, v_e)], inc=VV)
            v_m4 = p.op("vector", lambda e, wblk=wblk: e.reduce_max(out=mcol[:, :], in_=wblk, axis=AX.X),
                        waits=[(VV, last)], inc=VV)
            p.op("vector", lambda e, blk=blk, j=j: e.tensor_scalar(
                out=mtm[:, j * 16:(j + 1) * 16], in0=blk, scalar1=mcol[:, :], scalar2=None, op0=ALU.is_ge),
                waits=[(VV, v_m4)], inc=VV)
        v_mask = p.n(VV)
        for j in range(4):
            p.op("tensor", lambda e, j=j: e.transpose(pj[4][0:16, j * 128:(j + 1) * 128],
                                                      mtm[:, j * 16:(j + 1) * 16], cm.idf[:, :]),
                 waits=[(VV, v_mask)] if j == 0 else (), inc=(TT_ if j == 3 else None))
        t_mt = p.n(TT_)
        v_mbf = p.op("vector", lambda e: e.tensor_copy(out=mbf[:, :], in_=pj[4][0:16, 0:TC]),
                     waits=[(TT_, t_mt)], inc=VV)
        if i == 0 and "dbg_rl" in A:
            p.op("sync", lambda e: e.dma_start(out=A["dbg_rl"], in_=rl[:, :]),
                 waits=[(VV, v_mbf)], inc=OUT, k=16)
            p.op("sync", lambda e: e.dma_start(out=A["dbg_rlT"], in_=rlT[:, :]),
                 waits=[(VV, v_mbf)], inc=OUT, k=16)
            p.op("sync", lambda e: e.dma_start(out=A["dbg_mtm"], in_=mtm[:, :]),
                 waits=[(VV, v_mbf)], inc=OUT, k=16)
            p.op("sync", lambda e: e.dma_start(out=A["dbg_mbf"], in_=mbf[:, :]),
                 waits=[(VV, v_mbf)], inc=OUT, k=16)
            out_end = p.n(OUT)

        a_h0 = p.n(AA)
        v_h0 = p.n(VV)
        t_hgrp = []
        t_bc = []
        for c in range(NE):
            pjt = pj[c % 4]
            w = [(GIN, gin_x)]
            if c >= 4:
                w.append((AA, a_h0 + (c - 4) + 1))
            for k in range(8):
                p.op("tensor", lambda e, pjt=pjt, c=c, k=k: e.matmul(
                    pjt[:, :], s_w1[:, k * 4096 + c * 128: k * 4096 + (c + 1) * 128], xb[:, cs(k)],
                    start=(k == 0), stop=(k == 7)),
                    waits=w if k == 0 else (), inc=None)
            wb = [(VV, v_mbf)]
            if c >= 2:
                wb.append((VV, v_mbf + (c - 2) + 1))
            p.op("tensor", lambda e, c=c: e.matmul(
                pj[4 + c % 2][:, :], s_sel[:, (c // 2) * 128:(c // 2 + 1) * 128], mbf[:, :],
                start=True, stop=True),
                waits=wb, inc=TT_)
            t_bc.append(p.n(TT_))
        for c in range(NE):
            p.op("scalar", lambda e, c=c: e.activation(out=hb[:, cs(c)], in_=pj[c % 4][:, :],
                                                       func=AF.Relu, bias=s_b1[:, c:c + 1]),
                 waits=[(TT_, t_bc[c])], inc=AA)
            p.op("vector", lambda e, c=c: e.tensor_tensor(out=hb[:, cs(c)], in0=hb[:, cs(c)],
                                                          in1=pj[4 + c % 2][:, :], op=ALU.mult),
                 waits=[(AA, a_h0 + c + 1)], inc=VV)
        v_hdone = p.n(VV)
        a_hdone = p.n(AA)
        if i == 0 and "dbg_hb" in A:
            p.op("sync", lambda e: e.dma_start(out=A["dbg_hb"], in_=hb[:, 0:4 * TC]),
                 waits=[(VV, v_hdone)], inc=OUT, k=16)
            out_end = p.n(OUT)

        t_y0 = p.n(TT_)
        for c in range(NE):
            par = c % 2
            gw_r = p.op("gpsimd", lambda e, c=c, par=par: e.dma_start(
                out=w2b[:, par * D:(par + 1) * D], in_=w2t[c * 128:(c + 1) * 128, :]),
                waits=[(TT_, t_y0 if c < 2 else t_hgrp[c - 2])] if (i > 0 or c >= 2) else [],
                inc=GW[par], k=16)
            w = [(GW[par], gw_r), (VV, v_hdone)]
            for m in range(8):
                p.op("tensor", lambda e, c=c, par=par, m=m: e.matmul(
                    pj[m][:, :], w2b[:, par * D + m * 128: par * D + (m + 1) * 128], hb[:, cs(c)],
                    start=(c == 0), stop=(c == NE - 1)),
                    waits=w if m == 0 else (), inc=(TT_ if m == 7 else None))
            t_hgrp.append(p.n(TT_))
        t_y = p.n(TT_)
        for m in range(8):
            p.op("vector", lambda e, m=m: e.scalar_tensor_tensor(
                out=zf[:, cs(m)], in0=pj[m][:, :], scalar=s_b2[:, m:m + 1], in1=xf[:, cs(m)],
                op0=ALU.add, op1=ALU.add),
                waits=[(TT_, t_y)], inc=(VV if m == 7 else None))

        v_x2 = emit_ln(p, cm, i, zf, tmp, tmp, xb, sq)
        for c in range(NF):
            p.op("sync", lambda e, c=c, tsl=tsl: e.dma_start(out=x2[c * 128:(c + 1) * 128, tsl], in_=tmp[:, cs(c)]),
                 waits=[(VV, v_x2)], inc=OUT, k=16)

        t_end, v_end, a_end, out_end = p.n(TT_), p.n(VV), p.n(AA), p.n(OUT)



def emit_p2(p, cm, A):
    p.phase_reset()
    tc = 256
    nt = T // tc
    s_wqkv = p.sbuf("p2_wqkv", [128, 8 * 3072], BF16)
    s_wout = p.sbuf("p2_wout", [128, 16 * D], BF16)
    s_e2h = p.sbuf("p2_e2h", [128, 8 * 16], BF16)
    s_h2e = p.sbuf("p2_h2e", [16, 8 * 128], BF16)
    s_ab = p.sbuf("p2_ab", [128, 16], F32)
    s_sig = p.sbuf("p2_sig", [128, 16], F32)
    lam_t = p.sbuf("p2_lamt", [128, 16 * tc], F32)
    xf = p.sbuf("p2_xf", [128, 8 * tc], F32)
    xb = p.sbuf("p2_xb", [128, 8 * tc], BF16)
    sq = p.sbuf("p2_sq", [128, 8 * tc], BF16)
    phiq = p.sbuf("p2_phiq", [128, 8 * tc], F32)
    phik = p.sbuf("p2_phik", [128, 8 * tc], F32)
    vf = p.sbuf("p2_vf", [128, 8 * tc], F32)
    kvf = p.sbuf("p2_kvf", [128, 8 * tc], F32)
    ta = p.sbuf("p2_ta", [128, 8 * tc], F32)
    tb = p.sbuf("p2_tb", [128, 8 * tc], F32)
    occ = p.sbuf("p2_occ", [128, 16 * tc], BF16)
    den = p.sbuf("p2_den", [16, 2 * tc], F32)
    denb = p.sbuf("p2_denb", [16, 2 * tc], BF16)
    carry = p.sbuf("p2_carry", [128, 32], F32)

    pj = cm.pj
    IN, GIN, TT_, VV, AA, OUT = cm.IN, cm.GIN, cm.TT, cm.VV, cm.AA, cm.OUT
    x2, x3 = A["x2"], A["x3"]

    for dst, src in [(s_ab, A["ab_rep"])]:
        p.op("sync", lambda e, d=dst, s=src: e.dma_start(out=d[:, :], in_=s), inc=IN, k=16)
    in_init = p.n(IN)
    p.op("gpsimd", lambda e: e.dma_start(out=s_e2h[:, :], in_=A["e2h"]), inc=GIN, k=16)
    p.op("gpsimd", lambda e: e.dma_start(out=s_h2e[:, :], in_=A["h2e"]), inc=GIN, k=16)
    for k in range(8):
        p.op("gpsimd", lambda e, k=k: e.dma_start(out=s_wqkv[:, k * 3072:(k + 1) * 3072],
                                                  in_=A["wqkv"][:, k * 3072:(k + 1) * 3072]), inc=GIN, k=16)
    for k in range(4):
        p.op("gpsimd", lambda e, k=k: e.dma_start(out=s_wout[:, k * 4 * D:(k + 1) * 4 * D],
                                                  in_=A["wout"][:, k * 4 * D:(k + 1) * 4 * D]), inc=GIN, k=16)

    a_sig = p.op("scalar", lambda e: e.activation(out=s_sig[:, :], in_=s_ab[:, :], func=AF.Sigmoid),
                 waits=[(IN, in_init)], inc=AA)
    v_cl = p.op("vector", lambda e: e.tensor_scalar(out=s_sig[:, :], in0=s_sig[:, :],
                                                    scalar1=cm.cp[:, CP_EPS8:CP_EPS8 + 1], scalar2=None,
                                                    op0=ALU.max),
                waits=[(AA, a_sig)], inc=VV)
    v0 = p.op("vector", lambda e: e.memset(xf[:, 0:tc], 1.0), waits=[(VV, v_cl)], inc=VV)
    for c in range(16):
        p.op("vector", lambda e, c=c: e.tensor_scalar(
            out=lam_t[:, cs(c, tc)], in0=xf[:, 0:tc], scalar1=s_sig[:, c:c + 1], scalar2=None, op0=ALU.mult),
            waits=[(VV, v0)], inc=(VV if c == 15 else None))
    p.op("vector", lambda e: e.memset(carry[:, :], 0.0), inc=VV)
    v_init = p.n(VV)

    t_end, a_end, out_end = p.n(TT_), p.n(AA), p.n(OUT)
    v_end = max(p.n(VV), v_init)

    for i in range(nt):
        tsl = slice(i * tc, (i + 1) * tc)
        bar = [(TT_, t_end), (VV, v_end), (AA, a_end), (OUT, out_end)]
        for c in range(NF):
            p.op("sync", lambda e, c=c, tsl=tsl: e.dma_start(out=xf[:, cs(c, tc)],
                                                             in_=x2[c * 128:(c + 1) * 128, tsl]),
                 waits=bar, inc=IN, k=16)
            p.op("gpsimd", lambda e, c=c, tsl=tsl: e.dma_start(out=xb[:, cs(c, tc)],
                                                               in_=x2[c * 128:(c + 1) * 128, tsl]),
                 waits=bar, inc=GIN, k=16)
        in_x = p.n(IN)
        gin_x = p.n(GIN)

        v_ev0 = p.n(VV)
        free_mark = []
        t_grp = []
        for g in range(24):
            pjt = pj[4 + g % 4]
            w = [(GIN, gin_x)] + (bar if g == 0 else [])
            if g >= 4:
                w.append(free_mark[g - 4])
            for k in range(8):
                p.op("tensor", lambda e, pjt=pjt, g=g, k=k: e.matmul(
                    pjt[:, 0:tc], s_wqkv[:, k * 3072 + g * 128: k * 3072 + (g + 1) * 128],
                    xb[:, cs(k, tc)], start=(k == 0), stop=(k == 7)),
                    waits=w if k == 0 else (), inc=(TT_ if k == 7 else None))
            t_grp.append(p.n(TT_))
            m = g % 8
            if g < 16:
                dst = phiq if g < 8 else phik
                v1 = p.op("vector", lambda e, pjt=pjt, dst=dst, m=m: e.tensor_scalar(
                    out=dst[:, cs(m, tc)], in0=pjt[:, 0:tc], scalar1=cm.cp[:, CP_ZERO:CP_ZERO + 1],
                    scalar2=None, op0=ALU.min),
                    waits=[(TT_, t_grp[-1])], inc=VV)
                a1 = p.op("scalar", lambda e, dst=dst, m=m: e.activation(
                    out=dst[:, cs(m, tc)], in_=dst[:, cs(m, tc)], func=AF.Exp),
                    waits=[(VV, v1)], inc=AA)
                v2 = p.op("vector", lambda e, pjt=pjt, dst=dst, m=m: e.scalar_tensor_tensor(
                    out=dst[:, cs(m, tc)], in0=pjt[:, 0:tc], scalar=cm.cp[:, CP_ZERO:CP_ZERO + 1],
                    in1=dst[:, cs(m, tc)], op0=ALU.max, op1=ALU.add),
                    waits=[(AA, a1)], inc=VV)
                free_mark.append((VV, v2))
            else:
                a1 = p.op("scalar", lambda e, pjt=pjt, m=m: e.copy(out=vf[:, cs(m, tc)], in_=pjt[:, 0:tc]),
                          waits=[(TT_, t_grp[-1]), (VV, v_end)], inc=AA)
                free_mark.append((AA, a1))
        a_v = p.n(AA)

        v_kv0 = p.n(VV)
        for c in range(NF):
            p.op("vector", lambda e, c=c: e.tensor_tensor(out=kvf[:, cs(c, tc)], in0=phik[:, cs(c, tc)],
                                                          in1=vf[:, cs(c, tc)], op=ALU.mult),
                 waits=[(AA, a_v), (VV, v_kv0)], inc=(VV if c == NF - 1 else None))
        v_kv = p.n(VV)

        scans = [
            (0, kvf, vf),
            (1, kvf, ta),
            (2, phik, tb),
            (3, phik, phik),
        ]
        v_before = p.n(VV)
        for c in range(NF):
            p.op("vector", lambda e, c=c: e.tensor_tensor_scan(
                out=vf[:, cs(c, tc)], data0=lam_t[:, cs(c, tc)], data1=kvf[:, cs(c, tc)],
                initial=carry[:, c:c + 1], op0=ALU.mult, op1=ALU.add),
                waits=[(VV, v_init if i == 0 else v_before), (VV, v_kv)],
                inc=(VV if c == NF - 1 else None))
        v_s1 = p.n(VV)
        for c in range(NF):
            p.op("vector", lambda e, c=c: e.tensor_copy(out=carry[:, c:c + 1],
                                                        in_=vf[:, c * tc + tc - 1: c * tc + tc]),
                 waits=[(VV, v_s1)], inc=(VV if c == NF - 1 else None))
        for c in range(NF):
            p.op("vector", lambda e, c=c: e.tensor_tensor_scan(
                out=ta[:, cs(c, tc)], data0=lam_t[:, cs(8 + c, tc)], data1=kvf[:, cs(c, tc)],
                initial=carry[:, 8 + c:9 + c], op0=ALU.mult, op1=ALU.add),
                inc=(VV if c == NF - 1 else None))
        v_s2 = p.n(VV)
        for c in range(NF):
            p.op("vector", lambda e, c=c: e.tensor_copy(out=carry[:, 8 + c:9 + c],
                                                        in_=ta[:, c * tc + tc - 1: c * tc + tc]),
                 waits=[(VV, v_s2)], inc=(VV if c == NF - 1 else None))
        for c in range(NF):
            p.op("vector", lambda e, c=c: e.tensor_tensor_scan(
                out=tb[:, cs(c, tc)], data0=lam_t[:, cs(c, tc)], data1=phik[:, cs(c, tc)],
                initial=carry[:, 16 + c:17 + c], op0=ALU.mult, op1=ALU.add),
                inc=(VV if c == NF - 1 else None))
        v_s3 = p.n(VV)
        for c in range(NF):
            p.op("vector", lambda e, c=c: e.tensor_copy(out=carry[:, 16 + c:17 + c],
                                                        in_=tb[:, c * tc + tc - 1: c * tc + tc]),
                 waits=[(VV, v_s3)], inc=(VV if c == NF - 1 else None))
        for c in range(NF):
            p.op("vector", lambda e, c=c: e.tensor_tensor_scan(
                out=kvf[:, cs(c, tc)], data0=lam_t[:, cs(8 + c, tc)], data1=phik[:, cs(c, tc)],
                initial=carry[:, 24 + c:25 + c], op0=ALU.mult, op1=ALU.add),
                inc=(VV if c == NF - 1 else None))
        v_s4 = p.n(VV)
        for c in range(NF):
            p.op("vector", lambda e, c=c: e.tensor_copy(out=carry[:, 24 + c:25 + c],
                                                        in_=kvf[:, c * tc + tc - 1: c * tc + tc]),
                 waits=[(VV, v_s4)], inc=(VV if c == NF - 1 else None))
        v_scans = p.n(VV)

        for ch, (rks, pd) in enumerate([(tb, pj[4]), (kvf, pj[5])]):
            for c in range(NF):
                p.op("vector", lambda e, c=c, rks=rks: e.tensor_tensor(
                    out=xb[:, cs(c, tc)], in0=phiq[:, cs(c, tc)], in1=rks[:, cs(c, tc)], op=ALU.mult),
                    waits=[(VV, v_scans), (TT_, t_grp[-1])], inc=(VV if c == NF - 1 else None))
            v_pr = p.n(VV)
            for c in range(NF):
                p.op("tensor", lambda e, c=c, pd=pd: e.matmul(
                    pd[0:16, 0:tc], s_e2h[:, c * 16:(c + 1) * 16], xb[:, cs(c, tc)],
                    start=(c == 0), stop=(c == 7)),
                    waits=[(VV, v_pr)] if c == 0 else (), inc=(TT_ if c == 7 else None))
        t_den = p.n(TT_)
        v_d1 = p.op("vector", lambda e: e.tensor_scalar(out=den[:, 0:tc], in0=pj[4][0:16, 0:tc],
                                                        scalar1=cm.cp[0:16, CP_EPS6:CP_EPS6 + 1],
                                                        scalar2=None, op0=ALU.max),
                    waits=[(TT_, t_den)], inc=VV)
        p.op("vector", lambda e: e.tensor_scalar(out=den[:, tc:2 * tc], in0=pj[5][0:16, 0:tc],
                                                 scalar1=cm.cp[0:16, CP_EPS6:CP_EPS6 + 1],
                                                 scalar2=None, op0=ALU.max), inc=VV)
        v_d2 = p.n(VV)
        v_d3 = p.op("vector", lambda e: e.reciprocal(out=den[:, 0:tc], in_=den[:, 0:tc]),
                    waits=[(VV, v_d2)], inc=VV)
        p.op("vector", lambda e: e.reciprocal(out=den[:, tc:2 * tc], in_=den[:, tc:2 * tc]), inc=VV)
        v_d4 = p.op("vector", lambda e: e.tensor_copy(out=denb[:, :], in_=den[:, :]), inc=VV)

        v_occ0 = p.n(VV)
        for ch in range(2):
            rkv = vf if ch == 0 else ta
            for c in range(NF):
                pb_ = pj[6 + (c % 2)]
                wb_ = [(VV, v_d4)]
                if ch * 8 + c >= 2:
                    wb_.append((VV, v_occ0 + 2 * (ch * 8 + c) - 2))
                p.op("tensor", lambda e, c=c, ch=ch, pb_=pb_: e.matmul(
                    pb_[:, 0:tc], s_h2e[:, c * 128:(c + 1) * 128],
                    denb[:, ch * tc:(ch + 1) * tc], start=True, stop=True),
                    waits=wb_, inc=TT_)
                t_b = p.n(TT_)
                p.op("vector", lambda e, c=c, ch=ch, rkv=rkv: e.tensor_tensor(
                    out=occ[:, cs(ch * 8 + c, tc)], in0=phiq[:, cs(c, tc)], in1=rkv[:, cs(c, tc)],
                    op=ALU.mult),
                    waits=[(VV, v_d4)], inc=VV)
                p.op("vector", lambda e, c=c, ch=ch, pb_=pb_: e.tensor_tensor(
                    out=occ[:, cs(ch * 8 + c, tc)], in0=occ[:, cs(ch * 8 + c, tc)], in1=pb_[:, 0:tc],
                    op=ALU.mult),
                    waits=[(TT_, t_b)], inc=VV)
        v_occ = p.n(VV)

        t_wo = []
        for m in range(8):
            w = [(VV, v_occ)]
            if m >= 4:
                w.append((VV, v_occ + (m - 4) + 1))
            for kk in range(16):
                p.op("tensor", lambda e, m=m, kk=kk: e.matmul(
                    pj[m % 4][:, 0:tc], s_wout[:, kk * D + m * 128: kk * D + (m + 1) * 128],
                    occ[:, cs(kk, tc)], start=(kk == 0), stop=(kk == 15)),
                    waits=w if kk == 0 else (), inc=(TT_ if kk == 15 else None))
            t_wo.append(p.n(TT_))
        for m in range(8):
            p.op("vector", lambda e, m=m: e.scalar_tensor_tensor(
                out=vf[:, cs(m, tc)], in0=pj[m % 4][:, 0:tc], scalar=cm.cp[:, CP_ZERO:CP_ZERO + 1],
                in1=xf[:, cs(m, tc)], op0=ALU.bypass, op1=ALU.add),
                waits=[(TT_, t_wo[m]), (IN, in_x)], inc=VV)

        v_x3 = emit_ln(p, cm, i, vf, kvf, kvf, xb, sq, tc=tc)
        for c in range(NF):
            p.op("sync", lambda e, c=c, tsl=tsl: e.dma_start(out=x3[c * 128:(c + 1) * 128, tsl],
                                                             in_=kvf[:, cs(c, tc)]),
                 waits=[(VV, v_x3)], inc=OUT, k=16)

        t_end, v_end, a_end, out_end = p.n(TT_), p.n(VV), p.n(AA), p.n(OUT)



W1_WC = [128] * 21 + [42]
W1_OFF = [0]
for _wc in W1_WC:
    W1_OFF.append(W1_OFF[-1] + 16 * _wc)
W1_COLS = W1_OFF[-1]
GB_OFF = [0]
for _wc in W1_WC:
    GB_OFF.append(GB_OFF[-1] + TC)
GB_COLS = GB_OFF[-1]


def emit_p3f(p, cm, A, li, xin, xout, final=False, x3s=None):
    p.phase_reset()
    NEC = 22
    s_w2r = p.sbuf(f"f{li}_w2r", [128, NEC * D], BF16)
    w1b = p.sbuf(f"f{li}_w1b", [128, 2 * 2048], BF16)
    xsf = p.sbuf(f"f{li}_xsf", [128, 8 * 1025], F32)
    xnb_off = p._off
    xnb = p.sbuf(f"f{li}_xnb", [128, 8 * 1024], BF16)
    xb = p.sbuf(f"f{li}_xb", [128, T], BF16)
    sq = p.sbuf(f"f{li}_sq", [128, T], BF16)
    gb = p.sbuf(f"f{li}_gb", [128, 2 * GB_COLS], BF16)
    asb = p.sbuf(f"f{li}_asb", [128, TC], F32)
    gsb = p.sbuf(f"f{li}_gsb", [128, TC], F32)
    tmpf = p.sbuf(f"f{li}_tmp", [128, T], F32)
    off_xnb = None
    xtr = p.nc.alloc_sbuf_tensor_at(f"f{li}_xtr", [128, T], F32, offset=xnb_off)
    lngc = p.sbuf(f"f{li}_lng", [128, 8], F32)
    lnbc = p.sbuf(f"f{li}_lnb", [128, 8], F32)
    b1a = p.sbuf(f"f{li}_b1a", [128, NEC], F32)
    b1g = p.sbuf(f"f{li}_b1g", [128, NEC], F32)
    b2c = p.sbuf(f"f{li}_b2", [128, 8], F32)

    pj = cm.pj
    IN, GIN, TT_, VV, AA, OUT = cm.IN, cm.GIN, cm.TT, cm.VV, cm.AA, cm.OUT
    GW = cm.GW

    for dst, key in [(lngc, "lng"), (lnbc, "lnb"), (b1a, "b1a"), (b1g, "b1g"), (b2c, "b2c3")]:
        p.op("sync", lambda e, d=dst, s=A[key], li=li: e.dma_start(out=d[:, :], in_=s[li]), inc=IN, k=16)
    in_init = p.n(IN)
    for k in range(11):
        p.op("gpsimd", lambda e, k=k, li=li: e.dma_start(
            out=s_w2r[:, k * 2048:(k + 1) * 2048], in_=A["w2s"][li, :, k * 2048:(k + 1) * 2048]),
            inc=GIN, k=16)

    t_end, v_end, a_end, out_end = p.n(TT_), p.n(VV), p.n(AA), p.n(OUT)
    gcols = [lngc[:, c:c + 1] for c in range(NF)]
    bcols = [lnbc[:, c:c + 1] for c in range(NF)]

    GGs = p.sem("s_g")
    g_end = p.n(GGs)
    for i2 in range(4):
        t0 = i2 * 1024
        bar = [(TT_, t_end), (VV, v_end), (AA, a_end), (OUT, out_end), (GGs, g_end)]
        if i2 == 0:
            vz = p.op("vector", lambda e: e.memset(xsf[:, 0:8 * 1025:1025], 0.0), inc=VV)
            for c in range(NF):
                p.op("sync", lambda e, c=c: e.dma_start(
                    out=xsf[:, c * 1025 + 1: (c + 1) * 1025],
                    in_=xin[c * 128:(c + 1) * 128, 0:1024]),
                    waits=bar + [(VV, vz)], inc=IN, k=16)
        else:
            for c in range(NF):
                p.op("sync", lambda e, c=c, t0=t0: e.dma_start(
                    out=xsf[:, c * 1025: (c + 1) * 1025],
                    in_=xin[c * 128:(c + 1) * 128, t0 - 1: t0 + 1024]),
                    waits=bar, inc=IN, k=16)
        in_x = p.n(IN)

        def sh_ap(c, s):
            off = c * 1025 + (1 if c < 4 else 0) + s * TC
            return xsf[:, off: off + TC]

        def un_ap(c, s):
            off = c * 1025 + 1 + s * TC
            return xsf[:, off: off + TC]

        v_lns = []
        for s in range(2):
            p.op("vector", None, waits=[(IN, in_x)], inc=VV)
            v_ln = emit_ln(p, cm, i2, (lambda c, s=s: sh_ap(c, s)),
                           (lambda c, s=s: xnb[:, c * 1024 + s * TC: c * 1024 + (s + 1) * TC]),
                           tmpf, xb, sq, gcol=gcols, bcol=bcols,
                           guards=[(IN, in_x), (IN, in_init)], tc=TC)
            v_lns.append(v_ln)
        v_ln_all = p.n(VV)

        t_wgrp = [0] * NEC
        g_gb0 = p.n(p.sem("s_g"))
        a_w0 = p.n(AA)
        for c in range(NEC):
            wc = W1_WC[c]
            par = c % 2
            gw_r = p.op("gpsimd", lambda e, c=c, par=par, li=li: e.dma_start(
                out=w1b[:, par * 2048: par * 2048 + 16 * W1_WC[c]],
                in_=A["w1s"][li, :, W1_OFF[c]:W1_OFF[c + 1]]),
                waits=[(TT_, t_wgrp[c - 2] if c >= 2 else t_end)], inc=GW[par], k=16)
            for s in range(2):
                idx = 2 * c + s
                pa = pj[(idx % 2) * 2]
                pg = pj[(idx % 2) * 2 + 1]
                w = [(GW[par], gw_r), (VV, v_ln_all)]
                if idx >= 2:
                    w.append(("s_g", g_gb0 + (idx - 2) + 1))
                for k in range(8):
                    p.op("tensor", lambda e, c=c, s=s, k=k, par=par, pa=pa, wc=wc: e.matmul(
                        pa[0:wc, 0:TC], w1b[:, par * 2048 + k * 2 * wc: par * 2048 + k * 2 * wc + wc],
                        xnb[:, k * 1024 + s * TC: k * 1024 + (s + 1) * TC],
                        start=(k == 0), stop=(k == 7)),
                        waits=w if k == 0 else (), inc=None)
                for k in range(8):
                    p.op("tensor", lambda e, c=c, s=s, k=k, par=par, pg=pg, wc=wc: e.matmul(
                        pg[0:wc, 0:TC], w1b[:, par * 2048 + k * 2 * wc + wc: par * 2048 + (k + 1) * 2 * wc],
                        xnb[:, k * 1024 + s * TC: k * 1024 + (s + 1) * TC],
                        start=(k == 0), stop=(k == 7)),
                        inc=(TT_ if k == 7 else None))
                t_mm = p.n(TT_)
                a1 = p.op("scalar", lambda e, pa=pa, wc=wc, c=c: e.activation(
                    out=asb[0:wc, :], in_=pa[0:wc, 0:TC], func=AF.Identity, bias=b1a[0:wc, c:c + 1]),
                    waits=[(TT_, t_mm), (IN, in_init)], inc=AA)
                a2 = p.op("scalar", lambda e, pg=pg, wc=wc, c=c: e.activation(
                    out=gsb[0:wc, :], in_=pg[0:wc, 0:TC], func=AF.Gelu, bias=b1g[0:wc, c:c + 1]),
                    inc=AA)
                p.op("gpsimd", lambda e, c=c, s=s, wc=wc: e.tensor_tensor(
                    out=gb[0:wc, s * GB_COLS + GB_OFF[c]: s * GB_COLS + GB_OFF[c] + TC],
                    in0=asb[0:wc, :], in1=gsb[0:wc, :], op=ALU.mult),
                    waits=[(AA, a2)], inc=p.sem("s_g"))
            t_wgrp[c] = p.n(TT_)
        v_gb = ("s_g", p.n(p.sem("s_g")))

        gin_w2 = 11 * 16
        for s in range(2):
            for c in range(NEC):
                wc = W1_WC[c]
                w = [v_gb, (GIN, p.n(GIN)), (VV, p.n(VV))]
                for m in range(8):
                    p.op("tensor", lambda e, c=c, s=s, m=m, wc=wc: e.matmul(
                        pj[m][:, 0:TC], s_w2r[0:wc, c * D + m * 128: c * D + (m + 1) * 128],
                        gb[0:wc, s * GB_COLS + GB_OFF[c]: s * GB_COLS + GB_OFF[c] + TC],
                        start=(c == 0), stop=(c == NEC - 1)),
                        waits=w if (m == 0 and c == 0) else (), inc=(TT_ if (m == 7 and c == NEC - 1) else None))
            t_y = p.n(TT_)
            for m in range(8):
                p.op("vector", lambda e, m=m, s=s: e.scalar_tensor_tensor(
                    out=tmpf[:, cs(m)], in0=pj[m][:, 0:TC], scalar=b2c[:, m:m + 1],
                    in1=un_ap(m, s), op0=ALU.add, op1=ALU.add),
                    waits=[(TT_, t_y)], inc=(VV if m == 7 else None))
            v_z = p.n(VV)
            if not final:
                for c in range(NF):
                    p.op("sync", lambda e, c=c, s=s, t0=t0: e.dma_start(
                        out=xout[c * 128:(c + 1) * 128, t0 + s * TC: t0 + (s + 1) * TC],
                        in_=tmpf[:, cs(c)]),
                        waits=[(VV, v_z)], inc=OUT, k=16)
            else:
                for c in range(NF):
                    p.op("sync", lambda e, c=c, s=s, t0=t0: e.dma_start(
                        out=xtr[:, cs(c)], in_=x3s[c * 128:(c + 1) * 128, t0 + s * TC: t0 + (s + 1) * TC]),
                        waits=[(VV, v_end), v_gb], inc=IN, k=16)
                in_x3 = p.n(IN)
                for c in range(NF):
                    p.op("vector", lambda e, c=c: e.tensor_tensor(
                        out=xtr[:, cs(c)], in0=tmpf[:, cs(c)], in1=xtr[:, cs(c)], op=ALU.add),
                        waits=[(VV, v_z), (IN, in_x3)], inc=(VV if c == NF - 1 else None))
                v_x4 = emit_ln(p, cm, i2, xtr, tmpf, tmpf, xb, sq, tc=TC)
                for c in range(NF):
                    p.op("sync", lambda e, c=c, s=s, t0=t0: e.dma_start(
                        out=xout[c * 128:(c + 1) * 128, t0 + s * TC: t0 + (s + 1) * TC],
                        in_=tmpf[:, cs(c)]),
                        waits=[(VV, v_x4)], inc=OUT, k=16)

        t_end, v_end, a_end, out_end = p.n(TT_), p.n(VV), p.n(AA), p.n(OUT)
        g_end = p.n(GGs)



def emit_p3a(p, cm, A, li, xin, xout):
    p.phase_reset()
    tc = TC
    s_wqkv = p.sbuf(f"a{li}_wqkv", [128, 8 * 3072], BF16)
    s_wo = p.sbuf(f"a{li}_wo", [128, 8 * D], BF16)
    s_h2e = p.sbuf(f"a{li}_h2e", [16, 8 * 128], BF16)
    s_e2h = p.sbuf(f"a{li}_e2h", [128, 8 * 16], BF16)
    s_prot = p.sbuf(f"a{li}_prot", [128, 128], BF16)
    s_trim = p.sbuf(f"a{li}_trim", [128, 256], BF16)
    qcol = p.sbuf(f"a{li}_qcol", [128, 4], F32)
    xsb = p.sbuf(f"a{li}_xsb", [128, 8 * 513], BF16)
    xf = p.sbuf(f"a{li}_xf", [128, T], F32)
    tabs = p.sbuf(f"a{li}_tabs", [128, 4 * tc], F32)
    qf_off = p._off
    qf = p.sbuf(f"a{li}_qf", [128, T], F32)
    kb_off = p._off
    kb = p.sbuf(f"a{li}_kb", [128, T], BF16)
    fb_off = p._off
    fb = p.sbuf(f"a{li}_fb", [128, T], BF16)
    rsc = p.sbuf(f"a{li}_rsc", [128, 2 * tc], F32)
    rsc2 = p.sbuf(f"a{li}_rsc2", [128, 2 * tc], F32)
    sqq = p.sbuf(f"a{li}_sqq", [128, 2 * tc], BF16)
    rnb = p.sbuf(f"a{li}_rnb", [16, 2 * tc], BF16)
    rnr = p.sbuf(f"a{li}_rnr", [16, 2 * tc], F32)
    q_aug = p.sbuf(f"a{li}_qaug", [65, 16 * tc], BF16)
    k_aug = p.sbuf(f"a{li}_kaug", [65, 16 * 768], BF16)
    vtm = p.sbuf(f"a{li}_vtm", [128, 6 * 1040], BF16)
    at = p.nc.alloc_sbuf_tensor_at
    ab = at(f"a{li}_ab", [128, T], BF16, offset=kb_off)
    zt = at(f"a{li}_zt", [128, T], F32, offset=fb_off)
    expb = at(f"a{li}_expb", [128, 2 * 1024], BF16, offset=qf_off)
    mstack = at(f"a{li}_mst", [128, 32], F32, offset=qf_off + 4096)
    mT = at(f"a{li}_mT", [32, 128], BF16, offset=qf_off + 4352)
    denw = at(f"a{li}_denw", [1, 512], F32, offset=qf_off + 4608)
    reccols = at(f"a{li}_recc", [128, 32], F32, offset=qf_off + 6656)
    recT = at(f"a{li}_recT", [32, 128], F32, offset=qf_off + 6784)
    recrow = at(f"a{li}_recrow", [1, 4096], F32, offset=fb_off)

    pj = cm.pj
    IN, GIN, TT_, VV, AA, OUT = cm.IN, cm.GIN, cm.TT, cm.VV, cm.AA, cm.OUT
    GG = p.sem("s_g")

    p.op("sync", lambda e, li=li: e.dma_start(out=qcol[:, :], in_=A["qcols"][:, 4 * li:4 * li + 4]),
         inc=IN, k=16)
    in_init = p.n(IN)
    for dst, key in [(s_h2e, "h2e"), (s_e2h, "e2h"), (s_prot, "prot"), (s_trim, "trim")]:
        p.op("gpsimd", lambda e, d=dst, s=A[key]: e.dma_start(out=d[:, :], in_=s), inc=GIN, k=16)
    for k in range(8):
        p.op("gpsimd", lambda e, k=k, li=li: e.dma_start(
            out=s_wqkv[:, k * 3072:(k + 1) * 3072], in_=A["wqkv3"][li, :, k * 3072:(k + 1) * 3072]),
            inc=GIN, k=16)
    for k in range(4):
        p.op("gpsimd", lambda e, k=k, li=li: e.dma_start(
            out=s_wo[:, k * 2048:(k + 1) * 2048], in_=A["wo3"][li, :, k * 2048:(k + 1) * 2048]),
            inc=GIN, k=16)
    p.op("vector", lambda e: e.memset(k_aug[64:65, :], 1.0), inc=VV)
    p.op("vector", lambda e: e.memset(vtm[:, 64:6 * 1040:65], 1.0), inc=VV)
    v_init = p.n(VV)

    t_end, a_end, out_end = p.n(TT_), p.n(AA), p.n(OUT)
    v_end = max(p.n(VV), v_init)
    t_pv_last = 0

    for i in range(NT):
        t0 = i * tc
        bar = [(TT_, t_end), (VV, v_end), (AA, a_end), (OUT, out_end)]
        if i == 0:
            vz = p.op("vector", lambda e: e.memset(xsb[:, 0:8 * 513:513], 0.0), waits=bar, inc=VV)
            for c in range(NF):
                p.op("gpsimd", lambda e, c=c: e.dma_start(
                    out=xsb[:, c * 513 + 1:(c + 1) * 513], in_=xin[c * 128:(c + 1) * 128, 0:tc]),
                    waits=bar + [(VV, vz)], inc=GIN, k=16)
        else:
            for c in range(NF):
                p.op("gpsimd", lambda e, c=c, t0=t0: e.dma_start(
                    out=xsb[:, c * 513:(c + 1) * 513], in_=xin[c * 128:(c + 1) * 128, t0 - 1:t0 + tc]),
                    waits=bar, inc=GIN, k=16)
        for c in range(NF):
            p.op("sync", lambda e, c=c, t0=t0: e.dma_start(
                out=xf[:, cs(c)], in_=xin[c * 128:(c + 1) * 128, t0:t0 + tc]),
                waits=bar, inc=IN, k=16)
        for j in range(4):
            p.op("sync", lambda e, j=j, t0=t0: e.dma_start(
                out=tabs[:, j * tc:(j + 1) * tc], in_=A["tabs"][:, j * T + t0: j * T + t0 + tc]),
                waits=bar, inc=IN, k=16)
        in_x = p.n(IN)
        gin_x = p.n(GIN)

        for j in range(4):
            p.op("vector", lambda e, j=j: e.tensor_scalar(
                out=tabs[:, j * tc:(j + 1) * tc], in0=tabs[:, j * tc:(j + 1) * tc],
                scalar1=qcol[:, j:j + 1], scalar2=None, op0=ALU.mult),
                waits=[(IN, in_x), (IN, in_init)] + bar, inc=(VV if j == 3 else None))
        v_eff = p.n(VV)

        def sh(c):
            return c * 513 + (1 if c < 4 else 0)

        free_mark = []
        a_sq = []
        t_n2m = []
        for g in range(16):
            m = g % 8
            pjt = pj[4 + g % 4]
            w = [(GIN, gin_x)] + (bar if g == 0 else [])
            if g >= 4:
                w = w + [free_mark[g - 4]]
            for k in range(8):
                p.op("tensor", lambda e, pjt=pjt, g=g, k=k: e.matmul(
                    pjt[:, :], s_wqkv[:, k * 3072 + g * 128: k * 3072 + (g + 1) * 128],
                    xsb[:, sh(k): sh(k) + tc], start=(k == 0), stop=(k == 7)),
                    waits=w if k == 0 else (), inc=(TT_ if k == 7 else None))
            t_g = p.n(TT_)
            dstf = qf if g < 8 else kb
            p.op("scalar", lambda e, pjt=pjt, m=m, dstf=dstf: e.copy(out=dstf[:, cs(m)], in_=pjt[:, :]),
                 waits=[(TT_, t_g), (VV, v_end)], inc=None)
            if g < 8:
                p.op("scalar", lambda e, pjt=pjt, m=m: e.copy(out=fb[:, cs(m)], in_=pjt[:, :]), inc=None)
            wsq = [(TT_, t_n2m[g - 2])] if g >= 2 else []
            a2 = p.op("scalar", lambda e, pjt=pjt, g=g: e.activation(
                out=sqq[:, (g % 2) * tc:(g % 2 + 1) * tc], in_=pjt[:, :], func=AF.Square),
                waits=wsq, inc=AA)
            a_sq.append(a2)
            pn = pj[0] if g < 8 else pj[1]
            p.op("tensor", lambda e, g=g, m=m, pn=pn: e.matmul(
                pn[0:16, :], s_e2h[:, m * 16:(m + 1) * 16], sqq[:, (g % 2) * tc:(g % 2 + 1) * tc],
                start=(m == 0), stop=(m == 7)),
                waits=[(AA, a2)] + ([(VV, v_end)] if g == 0 else []), inc=TT_)
            t_n2m.append(p.n(TT_))
            free_mark.append((AA, a2))
        t_n2 = p.n(TT_)
        a_evacs = p.n(AA)

        v_vt0 = p.n(VV)
        for g in range(8):
            tt, n = g // 2, g % 2
            pjt = pj[4 + g % 4]
            w = [free_mark[12 + g % 4]] if g < 4 else [(VV, v_vt0 + (g - 4) + 1)]
            for k in range(8):
                p.op("tensor", lambda e, pjt=pjt, tt=tt, n=n, k=k: e.matmul(
                    pjt[:, :], xsb[:, sh(k) + tt * 128: sh(k) + (tt + 1) * 128],
                    s_wqkv[:, k * 3072 + 2048 + n * 512: k * 3072 + 2048 + (n + 1) * 512],
                    start=(k == 0), stop=(k == 7)),
                    waits=w if k == 0 else (), inc=(TT_ if k == 7 else None))
            t_v = p.n(TT_)
            slot = (4 * i + tt) % 6
            p.op("vector", lambda e, pjt=pjt, slot=slot, n=n: e.tensor_copy(
                out=vtm[:, slot * 1040 + 520 * n: slot * 1040 + 520 * (n + 1)].rearrange(
                    "p (h x) -> p h x", h=8)[:, :, 0:64],
                in_=pjt[:, :].rearrange("p (h x) -> p h x", h=8)),
                waits=[(TT_, t_v)], inc=VV)
        v_vt = p.n(VV)

        a_sq0 = p.op("scalar", lambda e: e.activation(out=rnr[:, 0:tc], in_=pj[0][0:16, :], func=AF.Sqrt),
                     waits=[(TT_, t_n2)], inc=AA)
        a_sq1 = p.op("scalar", lambda e: e.activation(out=rnr[:, tc:2 * tc], in_=pj[1][0:16, :], func=AF.Sqrt),
                     inc=AA)
        v_r1 = p.op("vector", lambda e: e.tensor_scalar(
            out=rnr[:, :], in0=rnr[:, :], scalar1=cm.cp[0:16, CP_EPS12:CP_EPS12 + 1],
            scalar2=None, op0=ALU.max), waits=[(AA, a_sq1)], inc=VV)
        v_r2 = p.op("vector", lambda e: e.reciprocal(out=rnr[:, :], in_=rnr[:, :]),
                    waits=[(VV, v_r1)], inc=VV)
        v_rn = p.op("vector", lambda e: e.tensor_copy(out=rnb[:, :], in_=rnr[:, :]),
                    waits=[(VV, v_r2)], inc=VV)

        def rotary(src_f32, tab_off, rn_off, dest_write, mm_extra, v_guard, perm_src):
            marks = []
            t_perm_list = []
            for c in range(NF):
                pp = pj[4 + c % 2]
                pb = pj[6 + c % 2]
                w1 = [(AA, a_evacs), (VV, v_vt), (VV, v_guard)] + mm_extra
                if c >= 2:
                    w1.append((VV, marks[c - 2]))
                p.op("tensor", lambda e, c=c, pp=pp, perm_src=perm_src: e.matmul(
                    pp[:, :], s_prot[:, :], perm_src[:, cs(c)], start=True, stop=True),
                    waits=w1, inc=None)
                p.op("tensor", lambda e, c=c, pb=pb, rn_off=rn_off: e.matmul(
                    pb[:, :], s_h2e[:, c * 128:(c + 1) * 128], rnb[0:16, rn_off:rn_off + tc],
                    start=True, stop=True),
                    waits=[(VV, v_rn)], inc=TT_)
                t_c = p.n(TT_)
                t_perm_list.append(t_c)
                g1 = p.op("gpsimd", lambda e, c=c, src_f32=src_f32, tab_off=tab_off: e.tensor_tensor(
                    out=rsc[:, (c % 2) * tc:(c % 2 + 1) * tc], in0=src_f32[:, cs(c)],
                    in1=tabs[:, tab_off:tab_off + tc], op=ALU.mult),
                    waits=[(VV, v_eff), (VV, v_guard), (AA, a_evacs)] +
                          ([(VV, marks[c - 2])] if c >= 2 else []),
                    inc=GG)
                v1 = p.op("vector", lambda e, c=c, pp=pp, tab_off=tab_off: e.tensor_tensor(
                    out=rsc2[:, (c % 2) * tc:(c % 2 + 1) * tc], in0=pp[:, :],
                    in1=tabs[:, tab_off + tc:tab_off + 2 * tc], op=ALU.mult),
                    waits=[(TT_, t_c), (VV, v_eff)] + ([(VV, marks[c - 2])] if c >= 2 else []),
                    inc=VV)
                g2 = p.op("gpsimd", lambda e, c=c: e.tensor_tensor(
                    out=rsc[:, (c % 2) * tc:(c % 2 + 1) * tc], in0=rsc[:, (c % 2) * tc:(c % 2 + 1) * tc],
                    in1=rsc2[:, (c % 2) * tc:(c % 2 + 1) * tc], op=ALU.add),
                    waits=[(GG, g1), (VV, v1)], inc=GG)
                last = None
                for half in range(2):
                    for (dst_ap, src_sl) in dest_write(c, half):
                        last = p.op("vector",
                                    lambda e, dst_ap=dst_ap, src_sl=src_sl, c=c, half=half, pb=pb:
                                    e.tensor_tensor(
                                        out=dst_ap,
                                        in0=rsc[half * 64:(half + 1) * 64,
                                                (c % 2) * tc + src_sl.start:(c % 2) * tc + src_sl.stop],
                                        in1=pb[half * 64:(half + 1) * 64, src_sl],
                                        op=ALU.mult),
                                    waits=[(GG, g2), (TT_, t_c)], inc=VV)
                marks.append(last)
            return marks, t_perm_list

        _, t_qperm = rotary(qf, 0, 0,
                            lambda c, half: [(q_aug[0:64, (2 * c + half) * tc: (2 * c + half + 1) * tc],
                                              slice(0, tc))],
                            [], v_eff, fb)
        v_qrot = p.n(VV)

        def k_dest(c, half):
            h = 2 * c + half
            out = []
            for wl in range(2):
                ss = (2 * i + wl) % 3
                out.append((k_aug[0:64, h * 768 + ss * 256: h * 768 + (ss + 1) * 256],
                            slice(wl * 256, (wl + 1) * 256)))
            return out

        rotary(kb, 2 * tc, tc, k_dest, [], v_qrot, kb)
        v_krot = p.n(VV)

        t_tp_prev = 0
        v_attn_prev = p.n(VV)
        for wl in range(2):
            gw = 2 * i + wl
            kts = [kt for kt in range(2 * gw - 2, 2 * gw + 2) if kt >= 0]
            v_qm0 = p.n(VV)
            nh = 0
            v_red = []
            for h in range(16):
                for qt in range(2):
                    valid = [kt for kt in kts if 2 * gw - 2 + qt <= kt <= 2 * gw + qt]
                    pq = pj[4 + nh % 4]
                    w = [(VV, v_krot)]
                    if nh >= 4:
                        w.append((VV, v_red[nh - 4]))
                    else:
                        w.append((VV, v_qm0))
                    if nh == 0:
                        w.append((TT_, t_tp_prev))
                    first = True
                    for kt in valid:
                        kl = kt - (2 * gw - 2)
                        ss = (kt // 2) % 3
                        hf = kt % 2
                        p.op("tensor",
                             lambda e, h=h, qt=qt, kl=kl, ss=ss, hf=hf, pq=pq, wl=wl, first=first:
                             e.matmul(pq[:, kl * 128:(kl + 1) * 128],
                                      q_aug[0:64, h * tc + wl * 256 + qt * 128: h * tc + wl * 256 + (qt + 1) * 128],
                                      k_aug[0:64, h * 768 + ss * 256 + hf * 128: h * 768 + ss * 256 + (hf + 1) * 128],
                                      start=first, stop=(kt == valid[-1])),
                             waits=w if first else (), inc=(TT_ if kt == valid[-1] else None))
                        first = False
                    t_qm = p.n(TT_)
                    lo = (valid[0] - (2 * gw - 2)) * 128
                    hi = (valid[-1] - (2 * gw - 2) + 1) * 128
                    r = p.op("vector", lambda e, h=h, qt=qt, lo=lo, hi=hi, pq=pq: e.reduce_max(
                        out=mstack[:, h + 16 * qt: h + 16 * qt + 1],
                        in_=pq[:, lo:hi], axis=AX.X, negate=True),
                        waits=[(TT_, t_qm)], inc=VV)
                    v_red.append(r)
                    nh += 1
            g_red = v_red[-1]
            t_tp = p.op("tensor", lambda e: e.transpose(pj[2][0:32, 0:128], mstack[:, :], cm.idf[:, :]),
                        waits=[(VV, g_red), (VV, v_attn_prev)], inc=TT_)
            t_tp_prev = t_tp
            v_mT = p.op("vector", lambda e: e.tensor_copy(out=mT[:, :], in_=pj[2][0:32, 0:128]),
                        waits=[(TT_, t_tp)], inc=VV)
            reh = q_aug[64:65, :].rearrange("a (h x) -> a h x", x=512)
            for qt in range(2):
                dst_ap = reh[:, :, wl * 256 + qt * 128: wl * 256 + (qt + 1) * 128]
                p.op("sync", lambda e, dst_ap=dst_ap, qt=qt: e.dma_start(
                    out=dst_ap, in_=mT[16 * qt:16 * (qt + 1), :]),
                    waits=[(VV, v_mT)], inc=OUT, k=16)
            sc = p.n(OUT)
            t_pv = {}
            t_dtr = {}
            v_evac = {}
            a_exp = {}
            a_dw = {}
            hs = {}

            def emit_qk_exp(h):
                st = hs[h] = {}
                sb6 = 6 if h % 2 == 0 else 2
                eoff = (h % 2) * 1024
                st["sb6"], st["eoff"] = sb6, eoff
                wqk = [(OUT, sc)]
                if h >= 2:
                    wqk.append((AA, a_exp[h - 2]))
                elif h == 1:
                    wqk.append((VV, v_mT))
                else:
                    wqk.append((VV, v_attn_prev))
                t_sc = []
                firstq = True
                for kt in kts:
                    kl = kt - (2 * gw - 2)
                    ss = (kt // 2) % 3
                    hf = kt % 2
                    role = kl
                    if role == 0:
                        qsl = slice(0, 128)
                    elif role == 3:
                        qsl = slice(128, 256)
                    else:
                        qsl = slice(0, 256)
                    pb6 = pj[sb6 + kl // 2]
                    coff = (kl % 2) * 256
                    p.op("tensor",
                         lambda e, h=h, ss=ss, hf=hf, qsl=qsl, pb6=pb6, coff=coff, wl=wl:
                         e.matmul(pb6[:, coff + qsl.start: coff + qsl.stop],
                                  k_aug[0:65, h * 768 + ss * 256 + hf * 128: h * 768 + ss * 256 + (hf + 1) * 128],
                                  q_aug[0:65, h * tc + wl * 256 + qsl.start: h * tc + wl * 256 + qsl.stop],
                                  start=True, stop=True),
                         waits=wqk if firstq else (), inc=TT_)
                    firstq = False
                    t_sc.append((p.n(TT_), kl, role, qsl, pb6, coff))
                st["t_sc"] = t_sc
                t_all_qk = t_sc[-1][0]
                banks = sorted(set((kl // 2) for (_, kl, _, _, _, _) in t_sc))
                for bki, bk in enumerate(banks):
                    kls = [kl for (_, kl, _, _, _, _) in t_sc if kl // 2 == bk]
                    c0 = (min(kls) % 2) * 256
                    c1 = (max(kls) % 2) * 256 + 256
                    wexp = [(TT_, t_all_qk)]
                    if h >= 2:
                        wexp.append((TT_, t_pv[h - 2]))
                    elif i > 0 or wl > 0:
                        wexp.append((TT_, t_pv_last2[h]))
                    p.op("scalar", lambda e, bk=bk, c0=c0, c1=c1, sb6=sb6, eoff=eoff: e.activation(
                        out=expb[:, eoff + bk * 512 + c0: eoff + bk * 512 + c1],
                        in_=pj[sb6 + bk][:, c0:c1], func=AF.Exp),
                        waits=wexp, inc=(AA if bki == len(banks) - 1 else None))
                a_exp[h] = p.n(AA)
                for (tmark, kl, role, qsl, pb6, coff) in t_sc:
                    if role == 0:
                        msl, esl = slice(0, 128), slice(kl * 256, kl * 256 + 128)
                    elif role == 1:
                        msl, esl = slice(0, 128), slice(kl * 256 + 128, kl * 256 + 256)
                    elif role == 2:
                        msl, esl = slice(128, 256), slice(kl * 256, kl * 256 + 128)
                    else:
                        msl, esl = slice(128, 256), slice(kl * 256 + 128, kl * 256 + 256)
                    p.op("vector", lambda e, msl=msl, esl=esl, eoff=eoff: e.tensor_tensor(
                        out=expb[:, eoff + esl.start: eoff + esl.stop],
                        in0=expb[:, eoff + esl.start: eoff + esl.stop],
                        in1=s_trim[:, msl], op=ALU.mult),
                        waits=[(AA, a_exp[h])], inc=VV)
                st["v_msk"] = (VV, p.n(VV))

            def emit_pv_evac(h):
                st = hs[h]
                pv = pj[h % 2]
                eoff = st["eoff"]
                doff = (h % 2) * 256
                t_sc = st["t_sc"]
                firstp = True
                for j, (tmark, kl, role, qsl, pb6, coff) in enumerate(t_sc):
                    slot = (2 * gw - 2 + kl) % 6
                    wpv = []
                    if firstp:
                        wpv = [st["v_msk"]]
                        if h >= 2:
                            wpv.append((VV, v_evac[h - 2]))
                            wpv.append((AA, a_dw[h - 2]))
                    p.op("tensor",
                         lambda e, h=h, kl=kl, qsl=qsl, slot=slot, pv=pv, firstp=firstp, eoff=eoff,
                         lastp=(j == len(t_sc) - 1):
                         e.matmul(pv[0:65, qsl],
                                  vtm[:, slot * 1040 + h * 65: slot * 1040 + (h + 1) * 65],
                                  expb[:, eoff + kl * 256 + qsl.start: eoff + kl * 256 + qsl.stop],
                                  start=firstp, stop=lastp),
                         waits=wpv, inc=(TT_ if j == len(t_sc) - 1 else None))
                    firstp = False
                t_pv[h] = p.n(TT_)
                co = h // 2
                ro = (h % 2) * 64
                v_evac[h] = p.op("vector", lambda e, pv=pv, co=co, ro=ro, wl=wl: e.tensor_copy(
                    out=ab[ro:ro + 64, co * tc + wl * 256: co * tc + wl * 256 + 256],
                    in_=pv[0:64, 0:256]),
                    waits=[(TT_, t_pv[h])], inc=VV)
                a_dw[h] = p.op("scalar", lambda e, pv=pv, doff=doff: e.copy(
                    out=denw[0:1, doff:doff + 256], in_=pv[64:65, 0:256]),
                    waits=[(TT_, t_pv[h])] + ([(TT_, t_dtr[h - 2])] if h >= 2 else []), inc=AA)

            def emit_dtr(h):
                doff = (h % 2) * 256
                for qt in range(2):
                    p.op("tensor", lambda e, h=h, qt=qt, doff=doff: e.transpose(
                        pj[5][:, 2 * h + qt: 2 * h + qt + 1],
                        denw[0:1, doff + qt * 128: doff + (qt + 1) * 128],
                        cm.idf[0:1, 0:1]),
                        waits=[(AA, a_dw[h])] if qt == 0 else (), inc=(TT_ if qt == 1 else None))
                t_dtr[h] = p.n(TT_)

            for hh in range(18):
                if hh < 16:
                    emit_qk_exp(hh)
                if 1 <= hh <= 16:
                    emit_pv_evac(hh - 1)
                if 2 <= hh <= 17:
                    emit_dtr(hh - 2)
            v_rc = p.op("vector", lambda e: e.reciprocal(out=reccols[:, :], in_=pj[5][:, 0:32]),
                        waits=[(TT_, t_dtr[15])], inc=VV)
            t_rT = p.op("tensor", lambda e: e.transpose(pj[4][0:32, 0:128], reccols[:, :], cm.idf[:, :]),
                        waits=[(VV, v_rc)], inc=TT_)
            v_rT = p.op("vector", lambda e: e.tensor_copy(out=recT[:, :], in_=pj[4][0:32, 0:128]),
                        waits=[(TT_, t_rT)], inc=VV)
            scr = p.op("sync", lambda e: e.dma_start(out=recrow[:, :], in_=recT[:, :]),
                       waits=[(VV, v_rT)], inc=OUT, k=16)
            v_nrm0 = p.n(VV)
            for h in range(16):
                pcb = pj[2 + h % 2]
                wn = [(OUT, scr)]
                if h >= 2:
                    wn.append((VV, v_nrm0 + (h - 2) + 1))
                t_nb = p.op("tensor", lambda e, pcb=pcb, h=h: e.matmul(
                    pcb[0:64, 0:256], cm.onf[0:1, 0:64], recrow[0:1, 2 * h * 128: 2 * h * 128 + 256],
                    start=True, stop=True),
                    waits=wn, inc=TT_)
                co = h // 2
                ro = (h % 2) * 64
                p.op("vector", lambda e, pcb=pcb, co=co, ro=ro, wl=wl, h=h: e.tensor_tensor(
                    out=ab[ro:ro + 64, co * tc + wl * 256: co * tc + wl * 256 + 256],
                    in0=ab[ro:ro + 64, co * tc + wl * 256: co * tc + wl * 256 + 256],
                    in1=pcb[0:64, 0:256], op=ALU.mult),
                    waits=[(TT_, t_nb), (VV, v_evac[h])], inc=VV)
            t_pv_last2 = (t_pv[14], t_pv[15])
            v_attn_prev = p.n(VV)
        v_attn = p.n(VV)
        a_attn = p.n(AA)

        t_wo = []
        v_wo0 = p.n(VV)
        for m in range(8):
            w = [(VV, v_attn), (AA, a_attn)]
            if m >= 4:
                w.append((VV, v_wo0 + (m - 4) + 1))
            for k in range(8):
                p.op("tensor", lambda e, m=m, k=k: e.matmul(
                    pj[4 + m % 4][:, :], s_wo[:, k * D + m * 128: k * D + (m + 1) * 128], ab[:, cs(k)],
                    start=(k == 0), stop=(k == 7)),
                    waits=w if k == 0 else (), inc=(TT_ if k == 7 else None))
            t_wo.append(p.n(TT_))
        for m in range(8):
            p.op("vector", lambda e, m=m: e.scalar_tensor_tensor(
                out=zt[:, cs(m)], in0=pj[4 + m % 4][:, :], scalar=cm.cp[:, CP_ZERO:CP_ZERO + 1],
                in1=xf[:, cs(m)], op0=ALU.bypass, op1=ALU.add),
                waits=[(TT_, t_wo[m]), (IN, in_x)], inc=VV)
        v_z = p.n(VV)
        for c in range(NF):
            p.op("sync", lambda e, c=c, t0=t0: e.dma_start(
                out=xout[c * 128:(c + 1) * 128, t0:t0 + tc], in_=zt[:, cs(c)]),
                waits=[(VV, v_z)], inc=OUT, k=16)

        t_end, v_end, a_end, out_end = p.n(TT_), p.n(VV), p.n(AA), p.n(OUT)



def _t(a):
    return np.ascontiguousarray(np.asarray(a).T)


def _col8(vec, n=8):
    return np.ascontiguousarray(np.asarray(vec).reshape(n, 128).T)


def _chunked(a):
    a = np.asarray(a)
    k = a.shape[0] // 128
    return np.ascontiguousarray(a.reshape(k, 128, a.shape[1]).transpose(1, 0, 2).reshape(128, -1))


def p0_inputs(inp, b):
    beta_rep = np.repeat(np.asarray(inp["L0_attn_beta"]), 64)
    return {
        "x0": _t(inp["L0_attn_x"][b]),
        "wq": _t(inp["L0_attn_Wq"]),
        "wv": _t(inp["L0_attn_Wv"]),
        "wg": _t(inp["L0_attn_Wg"]),
        "wo": _t(inp["L0_attn_Wo"]),
        "lamsrc": _col8(beta_rep),
        "g8a": _col8(inp["L0_attn_ln_gamma"]),
        "b8a": _col8(inp["L0_attn_ln_beta"]),
    }


def p1_inputs(inp, b=None):
    import ml_dtypes
    sel = np.zeros((16, 16 * 128), dtype=ml_dtypes.bfloat16)
    for e in range(16):
        sel[e, e * 128:(e + 1) * 128] = 1
    return {
        "rwt": _chunked(_t(inp["L0_ffn_router_w"])),
        "rb16": np.asarray(inp["L0_ffn_router_b"], np.float32).reshape(16, 1),
        "w1t": _t(inp["L0_ffn_w1"]),
        "b1c": _col8(inp["L0_ffn_b1"], 32),
        "w2t": _t(inp["L0_ffn_w2"]),
        "b2c": _col8(inp["L0_ffn_b2"]),
        "sel16s": sel,
    }


def p2_inputs(inp, b=None):
    import ml_dtypes
    alpha_rep = np.repeat(np.asarray(inp["L1_attn_alpha"]), 64)
    beta_rep = np.repeat(np.asarray(inp["L1_attn_beta"]), 64)
    ab = np.concatenate([_col8(alpha_rep), _col8(beta_rep)], axis=1)
    e2h = np.zeros((128, 8 * 16), dtype=ml_dtypes.bfloat16)
    h2e = np.zeros((16, 8 * 128), dtype=ml_dtypes.bfloat16)
    for c in range(8):
        for pp in range(128):
            h = 2 * c + pp // 64
            e2h[pp, c * 16 + h] = 1
            h2e[h, c * 128 + pp] = 1
    return {
        "ab_rep": np.ascontiguousarray(ab, np.float32),
        "e2h": e2h,
        "h2e": h2e,
        "wqkv": _chunked(np.asarray(inp["L1_attn_w_qkv"])),
        "wout": _chunked(np.asarray(inp["L1_attn_w_out"])),
    }


def p3_ffn_inputs(inp):
    W1 = np.asarray(inp["L1_ffn_W1"])
    W2 = np.asarray(inp["L1_ffn_W2"])
    b1 = np.asarray(inp["L1_ffn_b1"])
    w1s = np.zeros((6, 128, W1_COLS), np.float32)
    for li in range(6):
        for c in range(22):
            wc = W1_WC[c]
            blocks = []
            for k in range(8):
                blocks.append(W1[li, k * 128:(k + 1) * 128, c * 128:c * 128 + wc])
                blocks.append(W1[li, k * 128:(k + 1) * 128, 2730 + c * 128:2730 + c * 128 + wc])
            w1s[li, :, W1_OFF[c]:W1_OFF[c + 1]] = np.concatenate(blocks, axis=1)
    w2s = np.zeros((6, 128, 22 * 1024), np.float32)
    for li in range(6):
        for c in range(22):
            wc = W1_WC[c]
            w2s[li, 0:wc, c * 1024:(c + 1) * 1024] = W2[li, c * 128:c * 128 + wc, :]
    b1a = np.zeros((6, 128, 22), np.float32)
    b1g = np.zeros((6, 128, 22), np.float32)
    for li in range(6):
        for c in range(22):
            wc = W1_WC[c]
            b1a[li, 0:wc, c] = b1[li, c * 128:c * 128 + wc]
            b1g[li, 0:wc, c] = b1[li, 2730 + c * 128:2730 + c * 128 + wc]
    return {
        "w1s": w1s,
        "w2s": w2s,
        "b1a": b1a,
        "b1g": b1g,
        "b2c3": np.stack([_col8(np.asarray(inp["L1_ffn_b2"])[li]) for li in range(6)]),
        "lng": np.stack([_col8(np.asarray(inp["L1_ffn_ln_g"])[li]) for li in range(6)]),
        "lnb": np.stack([_col8(np.asarray(inp["L1_ffn_ln_b"])[li]) for li in range(6)]),
    }


def p3_attn_inputs(inp):
    import ml_dtypes
    Wqkv = np.asarray(inp["L1_ffn_Wqkv"])
    Wo = np.asarray(inp["L1_ffn_Wo"])
    qs = np.asarray(inp["L1_ffn_q_scale"])
    ks = np.asarray(inp["L1_ffn_k_scale"])
    wqkv3 = np.stack([_chunked(Wqkv[li]) for li in range(6)])
    wo3 = np.stack([_chunked(Wo[li]) for li in range(6)])
    half = np.arange(0, 64, 2, dtype=np.float64)
    inv_freq = 1.0 / 10000.0 ** (half / 64)
    j = np.arange(64)
    freq_j = inv_freq[j % 32]
    t = np.arange(T, dtype=np.float64)
    freqs = t[:, None] * freq_j[None, :]
    base_scale = (half + 0.4 * 64) / (1.4 * 64)
    bs_j = base_scale[j % 32]
    power = (t - T // 2) / 128.0
    scale = bs_j[None, :] ** power[:, None]
    sign = np.where(j < 32, -1.0, 1.0)
    cosf = np.cos(freqs)
    sinf = np.sin(freqs)
    t1q = 8.0 * cosf * scale
    t2q = 8.0 * sign[None, :] * sinf * scale
    t1k = cosf / scale
    t2k = sign[None, :] * sinf / scale
    tabs = np.zeros((128, 4 * T), np.float32)
    for jj, tab in enumerate([t1q, t2q, t1k, t2k]):
        block = tab.T.astype(np.float32)
        tabs[0:64, jj * T:(jj + 1) * T] = block
        tabs[64:128, jj * T:(jj + 1) * T] = block
    qcols = np.zeros((128, 4 * 6), np.float32)
    perm64 = (j + 32) % 64
    for li in range(6):
        for (jj, vec) in enumerate([qs[li], qs[li][perm64], ks[li], ks[li][perm64]]):
            qcols[0:64, 4 * li + jj] = vec
            qcols[64:128, 4 * li + jj] = vec
    prot = np.zeros((128, 128), dtype=ml_dtypes.bfloat16)
    for m in range(128):
        k = (m // 64) * 64 + ((m % 64) + 32) % 64
        prot[k, m] = 1
    trim = np.zeros((128, 256), dtype=ml_dtypes.bfloat16)
    for jj in range(128):
        trim[jj, 0:128][np.arange(128) <= jj] = 1
        trim[jj, 128:256][np.arange(128) >= jj] = 1
    e2h = np.zeros((128, 8 * 16), dtype=ml_dtypes.bfloat16)
    h2e = np.zeros((16, 8 * 128), dtype=ml_dtypes.bfloat16)
    for c in range(8):
        for pp in range(128):
            h = 2 * c + pp // 64
            e2h[pp, c * 16 + h] = 1
            h2e[h, c * 128 + pp] = 1
    return {"wqkv3": wqkv3, "wo3": wo3, "tabs": tabs, "qcols": qcols, "prot": prot,
            "trim": trim, "e2h_a": e2h, "h2e_a": h2e}


def dram_io(nc, names_in, names_out, shapes):
    d = {}
    for n in names_in:
        d[n] = nc.dram_tensor(n, shapes[n], BF16 if n in ("g_onb", "sel16s", "e2h", "h2e", "prot", "trim") else F32,
                              kind="ExternalInput").ap()
    for n in names_out:
        d[n] = nc.dram_tensor(n, shapes[n], F32, kind="ExternalOutput").ap()
    return d


COMMON_SHAPES = {"g_cp": [128, 12], "g_onb": [128, 1], "g_onf": [1, 128], "g_idf": [128, 128]}


def build_p0_solo():
    p = Prog()
    shapes = dict(COMMON_SHAPES)
    shapes.update({"x0": [D, T], "wq": [D, D], "wv": [D, D], "wg": [D, D], "wo": [D, D],
                   "lamsrc": [128, 8], "g8a": [128, 8], "b8a": [128, 8], "x1": [D, T]})
    A = dram_io(p.nc, ["g_cp", "g_onb", "g_onf", "g_idf", "x0", "wq", "wv", "wg", "wo",
                       "lamsrc", "g8a", "b8a"], ["x1"], shapes)
    cm = Common(p, A)
    p.mark_global()
    emit_p0(p, cm, A)
    p.op("sync", None, waits=[(cm.OUT, p.n(cm.OUT))])
    return p.build()


def build_p1_solo():
    p = Prog()
    shapes = dict(COMMON_SHAPES)
    shapes.update({"x1": [D, T], "rwt": [128, 128], "rb16": [16, 1], "w1t": [D, 4096],
                   "b1c": [128, 32], "w2t": [4096, D], "b2c": [128, 8],
                   "sel16s": [16, 16 * 128], "x2": [D, T]})
    A = dram_io(p.nc, ["g_cp", "g_onb", "g_onf", "g_idf", "x1", "rwt", "rb16", "w1t",
                       "b1c", "w2t", "b2c", "sel16s"], ["x2"], shapes)
    cm = Common(p, A)
    p.mark_global()
    emit_p1(p, cm, A)
    p.op("sync", None, waits=[(cm.OUT, p.n(cm.OUT))])
    return p.build()


def run_p0(inp):
    nc = build_p0_solo()
    cmh = common_host_inputs()
    in_maps = [dict(cmh, **p0_inputs(inp, b)) for b in range(NCORE)]
    res = run_bass_kernel_spmd(nc, in_maps, core_ids=list(range(NCORE)))
    return np.stack([np.ascontiguousarray(r["x1"].T) for r in res.results])


def build_p1_solo_dbg():
    p = Prog()
    shapes = dict(COMMON_SHAPES)
    shapes.update({"x1": [D, T], "rwt": [128, 128], "rb16": [16, 1], "w1t": [D, 4096],
                   "b1c": [128, 32], "w2t": [4096, D], "b2c": [128, 8],
                   "sel16s": [16, 16 * 128], "x2": [D, T],
                   "dbg_rl": [16, TC], "dbg_rlT": [128, 64], "dbg_mtm": [128, 64],
                   "dbg_mbf": [16, TC], "dbg_hb": [128, 4 * TC]})
    A = dram_io(p.nc, ["g_cp", "g_onb", "g_onf", "g_idf", "x1", "rwt", "rb16", "w1t",
                       "b1c", "w2t", "b2c", "sel16s"],
                ["x2", "dbg_rl", "dbg_rlT", "dbg_mtm", "dbg_mbf", "dbg_hb"], shapes)
    A["dbg_mbf"] = p.nc.dram_tensor("dbg_mbf_b", [16, TC], BF16, kind="ExternalOutput").ap()
    A["dbg_hb"] = p.nc.dram_tensor("dbg_hb_b", [128, 4 * TC], BF16, kind="ExternalOutput").ap()
    cm = Common(p, A)
    p.mark_global()
    emit_p1(p, cm, A)
    p.op("sync", None, waits=[(cm.OUT, p.n(cm.OUT))])
    return p.build()


def run_p1_dbg(inp, x1_full):
    nc = build_p1_solo_dbg()
    cmh = common_host_inputs()
    w = p1_inputs(inp)
    in_maps = [dict(cmh, **w, x1=_t(x1_full[b])) for b in range(1)] * 1
    res = run_bass_kernel_spmd(nc, in_maps, core_ids=[0])
    return res.results[0]


def build_p2_solo():
    p = Prog()
    shapes = dict(COMMON_SHAPES)
    shapes.update({"x2": [D, T], "ab_rep": [128, 16], "e2h": [128, 8 * 16], "h2e": [16, 8 * 128],
                   "wqkv": [128, 8 * 3072], "wout": [128, 16 * D], "x3": [D, T]})
    A = dram_io(p.nc, ["g_cp", "g_onb", "g_onf", "g_idf", "x2", "ab_rep", "e2h", "h2e",
                       "wqkv", "wout"], ["x3"], shapes)
    cm = Common(p, A)
    p.mark_global()
    emit_p2(p, cm, A)
    p.op("sync", None, waits=[(cm.OUT, p.n(cm.OUT))])
    return p.build()


def run_p2(inp, x2_full):
    nc = build_p2_solo()
    cmh = common_host_inputs()
    w = p2_inputs(inp)
    in_maps = [dict(cmh, **w, x2=_t(x2_full[b])) for b in range(NCORE)]
    res = run_bass_kernel_spmd(nc, in_maps, core_ids=list(range(NCORE)))
    return np.stack([np.ascontiguousarray(r["x3"].T) for r in res.results])


def build_p3f_solo(li, final=False):
    p = Prog()
    shapes = dict(COMMON_SHAPES)
    shapes.update({"xin": [D, T], "w1s": [1, 128, W1_COLS], "w2s": [1, 128, 22 * D],
                   "b1a": [1, 128, 22], "b1g": [1, 128, 22], "b2c3": [1, 128, 8],
                   "lng": [1, 128, 8], "lnb": [1, 128, 8], "x3s": [D, T], "xo": [D, T]})
    names = ["g_cp", "g_onb", "g_onf", "g_idf", "xin", "w1s", "w2s", "b1a", "b1g",
             "b2c3", "lng", "lnb"] + (["x3s"] if final else [])
    A = dram_io(p.nc, names, ["xo"], shapes)
    cm = Common(p, A)
    p.mark_global()
    emit_p3f(p, cm, A, 0, A["xin"], A["xo"], final=final, x3s=A.get("x3s"))
    p.op("sync", None, waits=[(cm.OUT, p.n(cm.OUT))])
    return p.build()


def run_p3f(inp, x_full, li, final=False, x3_full=None):
    nc = build_p3f_solo(li, final)
    cmh = common_host_inputs()
    w = slice_p3f_w(p3_ffn_inputs(inp), li)
    in_maps = []
    for b in range(NCORE):
        m = dict(cmh, **w, xin=_t(x_full[b]))
        if final:
            m["x3s"] = _t(x3_full[b])
        in_maps.append(m)
    res = run_bass_kernel_spmd(nc, in_maps, core_ids=list(range(NCORE)))
    return np.stack([np.ascontiguousarray(r["xo"].T) for r in res.results])


def build_p3a_solo(li):
    p = Prog()
    shapes = dict(COMMON_SHAPES)
    shapes.update({"xin": [D, T], "wqkv3": [1, 128, 8 * 3072], "wo3": [1, 128, 8 * D],
                   "tabs": [128, 4 * T], "qcols": [128, 4], "prot": [128, 128],
                   "trim": [128, 256], "e2h": [128, 8 * 16], "h2e": [16, 8 * 128],
                   "xo": [D, T]})
    A = dram_io(p.nc, ["g_cp", "g_onb", "g_onf", "g_idf", "xin", "wqkv3", "wo3", "tabs",
                       "qcols", "prot", "trim", "e2h", "h2e"], ["xo"], shapes)
    cm = Common(p, A)
    p.mark_global()
    emit_p3a(p, cm, A, 0, A["xin"], A["xo"])
    p.op("sync", None, waits=[(cm.OUT, p.n(cm.OUT))])
    return p.build()


def slice_p3a_w(w, li):
    w = dict(w)
    w["e2h"] = w.pop("e2h_a", w.get("e2h"))
    w["h2e"] = w.pop("h2e_a", w.get("h2e"))
    w["wqkv3"] = w["wqkv3"][li:li + 1]
    w["wo3"] = w["wo3"][li:li + 1]
    w["qcols"] = np.ascontiguousarray(w["qcols"][:, 4 * li:4 * li + 4])
    return w


def slice_p3f_w(w, li):
    w = dict(w)
    for k in ("w1s", "w2s", "b1a", "b1g", "b2c3", "lng", "lnb"):
        w[k] = w[k][li:li + 1]
    return w


def run_p3a(inp, x_full, li):
    nc = build_p3a_solo(li)
    cmh = common_host_inputs()
    w = slice_p3a_w(p3_attn_inputs(inp), li)
    in_maps = [dict(cmh, **w, xin=_t(x_full[b])) for b in range(NCORE)]
    res = run_bass_kernel_spmd(nc, in_maps, core_ids=list(range(NCORE)))
    return np.stack([np.ascontiguousarray(r["xo"].T) for r in res.results])


def run_p1(inp, x1_full):
    nc = build_p1_solo()
    cmh = common_host_inputs()
    w = p1_inputs(inp)
    in_maps = [dict(cmh, **w, x1=_t(x1_full[b])) for b in range(NCORE)]
    res = run_bass_kernel_spmd(nc, in_maps, core_ids=list(range(NCORE)))
    return np.stack([np.ascontiguousarray(r["x2"].T) for r in res.results])


def _run(nc, in_maps, outkey):
    res = run_bass_kernel_spmd(nc, in_maps, core_ids=list(range(NCORE)))
    return [r[outkey] for r in res.results]





def build_fused():
    p = Prog()
    nc = p.nc
    shapes = dict(COMMON_SHAPES)
    shapes.update({
        "x0": [D, T],
        "wq": [D, D], "wv": [D, D], "wg": [D, D], "wo": [D, D],
        "lamsrc": [128, 8], "g8a": [128, 8], "b8a": [128, 8],
        "rwt": [128, 128], "rb16": [16, 1], "w1t": [D, 4096], "b1c": [128, 32],
        "w2t": [4096, D], "b2c": [128, 8], "sel16s": [16, 16 * 128],
        "ab_rep": [128, 16], "e2h": [128, 8 * 16], "h2e": [16, 8 * 128],
        "wqkv": [128, 8 * 3072], "wout": [128, 16 * D],
        "wqkv3": [6, 128, 8 * 3072], "wo3": [6, 128, 8 * D],
        "tabs": [128, 4 * T], "qcols": [128, 24], "prot": [128, 128], "trim": [128, 256],
        "w1s": [6, 128, W1_COLS], "w2s": [6, 128, 22 * D],
        "b1a": [6, 128, 22], "b1g": [6, 128, 22], "b2c3": [6, 128, 8],
        "lng": [6, 128, 8], "lnb": [6, 128, 8],
    })
    names_in = list(shapes.keys())
    A = dram_io(nc, names_in, ["x4"], dict(shapes, x4=[D, T]))
    sA = nc.dram_tensor("sA", [D, T], F32).ap()
    sB = nc.dram_tensor("sB", [D, T], F32).ap()
    sC = nc.dram_tensor("sC", [D, T], F32).ap()
    cm = Common(p, A)
    p.mark_global()

    A0 = dict(A)
    A0["x0"], A0["x1"] = A["x0"], sA
    emit_p0(p, cm, A0)
    A1 = dict(A)
    A1["x1"], A1["x2"] = sA, sB
    emit_p1(p, cm, A1)
    A2 = dict(A)
    A2["x2"], A2["x3"] = sB, sC
    emit_p2(p, cm, A2)
    bufs = [sC, sA, sB]
    cur = 0
    for li in range(6):
        nxt = 1 if bufs[cur] is not sA else 2
        cand = [b for b in (sA, sB) if b is not bufs[cur]]
        xo = cand[0]
        emit_p3a(p, cm, A, li, bufs[cur], xo)
        cur = bufs.index(xo)
        cand = [b for b in (sA, sB) if b is not bufs[cur]]
        xo2 = A["x4"] if li == 5 else cand[0]
        emit_p3f(p, cm, A, li, bufs[cur], xo2, final=(li == 5), x3s=sC)
        if li < 5:
            cur = bufs.index(xo2)
    p.op("sync", None, waits=[(cm.OUT, p.n(cm.OUT))])
    return p.build()


def fused_inputs(inputs):
    cmh = common_host_inputs()
    w0 = p0_inputs(inputs, 0)
    del w0["x0"]
    w1 = p1_inputs(inputs)
    w2 = p2_inputs(inputs)
    wa = p3_attn_inputs(inputs)
    wa = dict(wa)
    wa.pop("e2h_a")
    wa.pop("h2e_a")
    wf = p3_ffn_inputs(inputs)
    base = {}
    for d in (cmh, w0, w1, w2, wa, wf):
        base.update(d)
    return base


_FUSED_NC = None


def kernel(**inputs):
    global _FUSED_NC
    if _FUSED_NC is None:
        _FUSED_NC = build_fused()
    base = fused_inputs(inputs)
    x = np.asarray(inputs["L0_attn_x"])
    in_maps = [dict(base, x0=_t(x[b])) for b in range(NCORE)]
    res = run_bass_kernel_spmd(_FUSED_NC, in_maps, core_ids=list(range(NCORE)))
    return np.stack([np.ascontiguousarray(r["x4"].T) for r in res.results]).astype(np.float32)


def kernel_unfused(**inputs):
    cmh = common_host_inputs()
    xT = [_t(np.asarray(inputs["L0_attn_x"])[b]) for b in range(NCORE)]

    w0 = p0_inputs(inputs, 0)
    del w0["x0"]
    x1 = _run(build_p0_solo(), [dict(cmh, **w0, x0=xT[b]) for b in range(NCORE)], "x1")

    w1 = p1_inputs(inputs)
    x2 = _run(build_p1_solo(), [dict(cmh, **w1, x1=x1[b]) for b in range(NCORE)], "x2")

    w2 = p2_inputs(inputs)
    x3 = _run(build_p2_solo(), [dict(cmh, **w2, x2=x2[b]) for b in range(NCORE)], "x3")

    wa = p3_attn_inputs(inputs)
    wf = p3_ffn_inputs(inputs)
    x = x3
    for li in range(6):
        wal = slice_p3a_w(wa, li)
        x = _run(build_p3a_solo(li), [dict(cmh, **wal, xin=x[b]) for b in range(NCORE)], "xo")
        wfl = slice_p3f_w(wf, li)
        maps = []
        for b in range(NCORE):
            m = dict(cmh, **wfl, xin=x[b])
            if li == 5:
                m["x3s"] = x3[b]
            maps.append(m)
        x = _run(build_p3f_solo(li, final=(li == 5)), maps, "xo")

    return np.stack([np.ascontiguousarray(x[b].T) for b in range(NCORE)]).astype(np.float32)


# version 06
# speedup vs baseline: 1.2399x; 1.0539x over previous
import numpy as np
import concourse.bass as bass
import concourse.mybir as mybir
from concourse.bass_utils import run_bass_kernel_spmd

F32 = mybir.dt.float32
BF16 = mybir.dt.bfloat16
AF = mybir.ActivationFunctionType
ALU = mybir.AluOpType
AX = mybir.AxisListType

T = 4096
D = 1024
TC = 512
NT = T // TC
NF = D // 128
NCORE = 8

SBUF_BASE = 16384
SBUF_BUDGET = 229376 - 384

ENGINES = ("sync", "gpsimd", "vector", "scalar", "tensor")

CP_ZERO, CP_EPS, CP_INVD, CP_ONE, CP_NEG, CP_EPS6, CP_EPS12, CP_EPS8, CP_M1, CP_MH, CP_EPS24, CP_SPARE = range(12)
CP_VALS = np.array([0.0, 1e-5, 1.0 / D, 1.0, -1e30, 1e-6, 1e-12, 1e-8, -1.0, -0.5, 1e-24, 0.0], np.float32)


class Prog:

    def __init__(self):
        self.nc = bass.Bass("TRN2", target_bir_lowering=False)
        self.q = {e: [] for e in ENGINES}
        self._sems = {}
        self._off = SBUF_BASE
        self._phase_base = SBUF_BASE
        self._entry_marks = []
        self._entered = set(ENGINES)

    def sbuf(self, name, shape, dtype):
        nbytes = int(np.prod(shape[1:])) * mybir.dt.size(dtype)
        t = self.nc.alloc_sbuf_tensor_at(name, shape, dtype, offset=self._off)
        self._off = (self._off + nbytes + 31) & ~31
        assert self._off <= SBUF_BUDGET, f"SBUF overflow at {name}: {self._off}"
        return t

    def mark_global(self):
        self._phase_base = self._off

    def phase_reset(self):
        self._off = self._phase_base
        self._entry_marks = [(s, rec[1]) for s, rec in self._sems.items()
                             if s in ("s_t", "s_v", "s_a", "s_o")]
        self._entered = set()

    def sem(self, name):
        if name not in self._sems:
            self._sems[name] = [self.nc.alloc_semaphore(name), 0]
        return name

    def n(self, s):
        return self._sems[s][1]

    def op(self, eng, fn, waits=(), inc=None, k=1):
        waits = list(waits)
        if eng not in self._entered:
            self._entered.add(eng)
            waits = self._entry_marks + waits
        w = [(self._sems[s][0], int(v)) for (s, v) in waits if int(v) > 0]
        entry_inc = None
        r = None
        if inc is not None:
            rec = self._sems[inc]
            rec[1] += k
            entry_inc = (rec[0], k)
            r = rec[1]
        self.q[eng].append((w, fn, entry_inc))
        return r

    def barrier(self):
        return {s: rec[1] for s, rec in self._sems.items()}

    def build(self):
        with self.nc.Block() as block:
            for e in ENGINES:
                entries = self.q[e]
                if not entries:
                    continue

                def mk(entries=entries):
                    def body(eng):
                        hi = {}
                        for w, fn, inc in entries:
                            for (h, v) in w:
                                if hi.get(h.num, -1) < v:
                                    eng.wait_ge(h, v)
                                    hi[h.num] = v
                            inst = fn(eng) if fn is not None else None
                            if inc is not None:
                                if inst is None:
                                    inst = eng.nop()
                                inst.then_inc(inc[0], inc[1])
                    return body

                getattr(block, e)(mk())
        return self.nc


def cs(c, w=TC):
    return slice(c * w, (c + 1) * w)


class Common:

    def __init__(self, p, dram):
        nc = p.nc
        self.pj = [nc.alloc_psum_tensor(f"pj{i}", [128, TC], F32) for i in range(8)]
        self.cp = p.sbuf("g_cp", [128, 12], F32)
        self.onb = p.sbuf("g_onb", [128, 1], BF16)
        self.onf = p.sbuf("g_onf", [1, 128], F32)
        self.idf = p.sbuf("g_idf", [128, 128], F32)
        self.onbr = p.sbuf("g_onbr", [1, 128], BF16)
        self.rows = p.sbuf("g_rows", [1, 3 * TC], F32)
        self.IN = p.sem("s_in")
        self.GIN = p.sem("s_gin")
        self.GW = [p.sem("s_gw0"), p.sem("s_gw1"), p.sem("s_gw2"), p.sem("s_gw3")]
        self.TT = p.sem("s_t")
        self.VV = p.sem("s_v")
        self.AA = p.sem("s_a")
        self.OUT = p.sem("s_o")
        for dst, key in [(self.cp, "g_cp"), (self.onb, "g_onb"), (self.onf, "g_onf"),
                         (self.idf, "g_idf"), (self.onbr, "g_onbr")]:
            p.op("sync", lambda e, d=dst, s=dram[key]: e.dma_start(out=d[:, :], in_=s),
                 inc=self.IN, k=16)
        self.init_in = p.n(self.IN)


def common_host_inputs():
    import ml_dtypes
    return {
        "g_cp": np.broadcast_to(CP_VALS, (128, 12)).copy(),
        "g_onb": np.ones((128, 1), dtype=ml_dtypes.bfloat16),
        "g_onbr": np.ones((1, 128), dtype=ml_dtypes.bfloat16),
        "g_onf": np.ones((1, 128), dtype=np.float32),
        "g_idf": np.eye(128, dtype=np.float32),
    }


def emit_ln(p, cm, i, src, dst, tmp, xb, sq, gcol=None, bcol=None, guards=(), tc=TC):
    TT_, VV, AA = cm.TT, cm.VV, cm.AA
    pa, pb, pbm, pbr = cm.pj[0], cm.pj[1], cm.pj[2], cm.pj[3]
    sf = src if callable(src) else (lambda c: src[:, cs(c, tc)])
    df = dst if callable(dst) else (lambda c: dst[:, cs(c, tc)])
    v_src = p.n(VV)
    for c in range(NF):
        p.op("scalar", lambda e, c=c: e.copy(out=xb[:, cs(c, tc)], in_=sf(c)),
             waits=[(VV, v_src)] + list(guards), inc=None)
    for c in range(NF):
        p.op("scalar", lambda e, c=c: e.activation(out=sq[:, cs(c, tc)], in_=sf(c), func=AF.Square),
             inc=(AA if c == NF - 1 else None))
    a_cast = p.n(AA)
    for k in range(8):
        p.op("tensor", lambda e, k=k: e.matmul(pa[0:1, 0:tc], cm.onb[:, :], xb[:, cs(k, tc)],
                                               start=(k == 0), stop=(k == 7)),
             waits=[(AA, a_cast)] + list(guards) if k == 0 else (), inc=None)
    for k in range(8):
        p.op("tensor", lambda e, k=k: e.matmul(pb[0:1, 0:tc], cm.onb[:, :], sq[:, cs(k, tc)],
                                               start=(k == 0), stop=(k == 7)),
             inc=(TT_ if k == 7 else None))
    t_stat = p.n(TT_)
    R_mean, R_B, R_C = (cm.rows[0:1, j * TC: j * TC + tc] for j in range(3))
    R_ey2, R_msq, R_var, R_std, R_rstd = R_B, R_C, R_B, R_C, R_B
    v1 = p.op("vector", lambda e: e.tensor_scalar(out=R_mean, in0=pa[0:1, 0:tc], scalar1=cm.cp[0:1, CP_INVD:CP_INVD + 1],
                                                  scalar2=None, op0=ALU.mult),
              waits=[(TT_, t_stat)], inc=VV)
    p.op("vector", lambda e: e.tensor_scalar(out=R_ey2, in0=pb[0:1, 0:tc], scalar1=cm.cp[0:1, CP_INVD:CP_INVD + 1],
                                             scalar2=None, op0=ALU.mult), inc=VV)
    v2 = p.n(VV)
    a1 = p.op("scalar", lambda e: e.activation(out=R_msq, in_=R_mean, func=AF.Square),
              waits=[(VV, v2)], inc=AA)
    v3 = p.op("vector", lambda e: e.tensor_tensor(out=R_var, in0=R_ey2, in1=R_msq, op=ALU.subtract),
              waits=[(AA, a1)], inc=VV)
    a2 = p.op("scalar", lambda e: e.activation(out=R_std, in_=R_var, func=AF.Sqrt,
                                               bias=cm.cp[0:1, CP_EPS:CP_EPS + 1]),
              waits=[(VV, v3)], inc=AA)
    v4 = p.op("vector", lambda e: e.reciprocal(out=R_rstd, in_=R_std), waits=[(AA, a2)], inc=VV)
    p.op("tensor", lambda e: e.matmul(pbm[:, 0:tc], cm.onf[:, :], R_mean, start=True, stop=True),
         waits=[(VV, v1)], inc=None)
    p.op("tensor", lambda e: e.matmul(pbr[:, 0:tc], cm.onf[:, :], R_rstd, start=True, stop=True),
         waits=[(VV, v4)], inc=TT_)
    t_bc = p.n(TT_)
    for c in range(NF):
        p.op("vector", lambda e, c=c: e.tensor_tensor(out=tmp[:, cs(c, tc)], in0=sf(c),
                                                      in1=pbm[:, 0:tc], op=ALU.subtract),
             waits=[(TT_, t_bc)], inc=None)
        if gcol is None:
            p.op("vector", lambda e, c=c: e.tensor_tensor(out=df(c), in0=tmp[:, cs(c, tc)],
                                                          in1=pbr[:, 0:tc], op=ALU.mult),
                 inc=(VV if c == NF - 1 else None))
        else:
            p.op("vector", lambda e, c=c: e.tensor_tensor(out=tmp[:, cs(c, tc)], in0=tmp[:, cs(c, tc)],
                                                          in1=pbr[:, 0:tc], op=ALU.mult), inc=None)
            p.op("vector", lambda e, c=c: e.tensor_scalar(out=df(c), in0=tmp[:, cs(c, tc)],
                                                          scalar1=gcol[c], scalar2=bcol[c],
                                                          op0=ALU.mult, op1=ALU.add),
                 inc=(VV if c == NF - 1 else None))
    return p.n(VV)


def emit_wcast(p, cm, pieces, stage, slot_w, nslots, eng_cycle=("vector", "gpsimd"),
               extra_load_waits=(), extra_cast_waits=()):
    marks = []
    sems = {"vector": cm.VV, "gpsimd": p.sem("s_g"), "scalar": cm.AA}
    for idx, (dst, srcap, ncols) in enumerate(pieces):
        sl = idx % nslots
        w = list(extra_load_waits)
        if idx >= nslots:
            w.append(marks[idx - nslots])
        r = p.op("sync", lambda e, srcap=srcap, sl=sl, ncols=ncols: e.dma_start(
            out=stage[:, sl * slot_w: sl * slot_w + ncols], in_=srcap),
            waits=w, inc=cm.GW[sl], k=16)
        eng = eng_cycle[idx % len(eng_cycle)]
        mk = p.op(eng, lambda e, dst=dst, sl=sl, ncols=ncols: e.tensor_copy(
            out=dst, in_=stage[:, sl * slot_w: sl * slot_w + ncols]),
            waits=[(cm.GW[sl], r)] + list(extra_cast_waits), inc=sems[eng])
        marks.append((sems[eng], mk))
    return marks



def emit_p0(p, cm, A):
    p.phase_reset()
    s_wq = p.sbuf("p0_wq", [128, 8 * D], BF16)
    s_wv = p.sbuf("p0_wv", [128, 8 * D], BF16)
    s_wg = p.sbuf("p0_wg", [128, 8 * D], BF16)
    s_wo = p.sbuf("p0_wo", [128, 8 * D], BF16)
    xf = p.sbuf("p0_xf", [128, T], F32)
    xb = p.sbuf("p0_xb", [128, T], BF16)
    sq = p.sbuf("p0_sq", [128, T], BF16)
    qf = p.sbuf("p0_qf", [128, T], F32)
    vf = p.sbuf("p0_vf", [128, T], F32)
    stt = p.sbuf("p0_st", [128, T], F32)
    gf = p.sbuf("p0_gf", [128, T], F32)
    ub = p.sbuf("p0_ub", [128, T], BF16)
    lamt = p.sbuf("p0_lamt", [128, T], F32)
    s_lam = p.sbuf("p0_lam8", [128, 8], F32)
    s_sig = p.sbuf("p0_sig8", [128, 8], F32)
    s_g8 = p.sbuf("p0_g8", [128, 8], F32)
    s_b8 = p.sbuf("p0_b8", [128, 8], F32)
    carry = p.sbuf("p0_carry", [128, 8], F32)

    pj = cm.pj
    IN, GIN, TT_, VV, AA, OUT = cm.IN, cm.GIN, cm.TT, cm.VV, cm.AA, cm.OUT
    x0, x1 = A["x0"], A["x1"]

    for dst, src in [(s_lam, A["lamsrc"]), (s_g8, A["g8a"]), (s_b8, A["b8a"])]:
        p.op("sync", lambda e, d=dst, s=src: e.dma_start(out=d[:, :], in_=s), inc=IN, k=16)
    in_init = p.n(IN)
    wstage = p.sbuf("p0_wstage", [128, 4 * 1024], F32)
    pieces = []
    for w_sb, w_hbm in [(s_wq, A["wq"]), (s_wv, A["wv"]), (s_wg, A["wg"]), (s_wo, A["wo"])]:
        for k in range(8):
            pieces.append((w_sb[:, cs(k, D)], w_hbm[k * 128:(k + 1) * 128, :], 1024))
    wmarks = emit_wcast(p, cm, pieces, wstage, 1024, 4)
    wm = list({s: v for (s, v) in wmarks}.items())

    a_sig = p.op("scalar", lambda e: e.activation(out=s_sig[:, :], in_=s_lam[:, :], func=AF.Sigmoid),
                 waits=[(IN, in_init)], inc=AA)
    v0 = p.op("vector", lambda e: e.memset(gf[:, 0:TC], 1.0), inc=VV)
    for c in range(NF):
        p.op("vector", lambda e, c=c: e.tensor_scalar(
            out=lamt[:, cs(c)], in0=gf[:, 0:TC], scalar1=s_sig[:, c:c + 1], scalar2=None, op0=ALU.mult),
            waits=[(VV, v0), (AA, a_sig)], inc=(VV if c == NF - 1 else None))
    p.op("vector", lambda e: e.memset(carry[:, :], 0.0), inc=VV)
    v_init = p.n(VV)

    t_end = a_end = out_end = 0
    v_end = v_init
    gcols = [s_g8[:, c:c + 1] for c in range(NF)]
    bcols = [s_b8[:, c:c + 1] for c in range(NF)]

    for i in range(NT):
        tsl = slice(i * TC, (i + 1) * TC)
        bar = [(TT_, t_end), (VV, v_end), (AA, a_end), (OUT, out_end)]
        for c in range(NF):
            p.op("sync", lambda e, c=c, tsl=tsl: e.dma_start(out=xf[:, cs(c)], in_=x0[c * 128:(c + 1) * 128, tsl]),
                 waits=bar, inc=IN, k=16)
            p.op("gpsimd", lambda e, c=c, tsl=tsl: e.dma_start(out=xb[:, cs(c)], in_=x0[c * 128:(c + 1) * 128, tsl]),
                 waits=bar, inc=GIN, k=16)
        in_x = p.n(IN)
        gin_x = p.n(GIN)

        a_evac0 = p.n(AA)
        t_grp = []
        for g24 in range(24):
            m = g24 % 8
            wsb = (s_wq, s_wv, s_wg)[g24 // 8]
            pjt = pj[4 + g24 % 4]
            w = [(GIN, gin_x)] + wm
            if g24 >= 4:
                w.append((AA, a_evac0 + g24 - 3))
            for k in range(8):
                p.op("tensor", lambda e, pjt=pjt, wsb=wsb, k=k, m=m: e.matmul(
                    pjt[:, :], wsb[:, k * D + m * 128: k * D + (m + 1) * 128], xb[:, cs(k)],
                    start=(k == 0), stop=(k == 7)),
                    waits=w if k == 0 else (), inc=(TT_ if k == 7 else None))
            t_grp.append(p.n(TT_))
        for g24 in range(24):
            m = g24 % 8
            pjt = pj[4 + g24 % 4]
            w = [(TT_, t_grp[g24])]
            if g24 < 8:
                fn = lambda e, pjt=pjt, m=m: e.copy(out=qf[:, cs(m)], in_=pjt[:, :])
            elif g24 < 16:
                fn = lambda e, pjt=pjt, m=m: e.copy(out=vf[:, cs(m)], in_=pjt[:, :])
            else:
                fn = lambda e, pjt=pjt, m=m: e.activation(out=gf[:, cs(m)], in_=pjt[:, :], func=AF.Silu)
                w.append((OUT, out_end))
            p.op("scalar", fn, waits=w, inc=AA)
        a_evacs = p.n(AA)

        v_pre = p.n(VV)
        for c in range(NF):
            p.op("vector", lambda e, c=c: e.tensor_tensor_scan(
                out=stt[:, cs(c)], data0=lamt[:, cs(c)], data1=vf[:, cs(c)],
                initial=carry[:, c:c + 1], op0=ALU.mult, op1=ALU.add),
                waits=[(AA, a_evac0 + 16), (VV, v_init if i == 0 else v_pre)],
                inc=(VV if c == NF - 1 else None))
        v_scan = p.n(VV)
        for c in range(NF):
            p.op("vector", lambda e, c=c: e.tensor_copy(out=carry[:, c:c + 1],
                                                        in_=stt[:, c * TC + TC - 1: c * TC + TC]),
                 waits=[(VV, v_scan)], inc=(VV if c == NF - 1 else None))
        v_carry = p.n(VV)
        for c in range(NF):
            p.op("vector", lambda e, c=c: e.tensor_tensor(
                out=vf[:, cs(c)], in0=qf[:, cs(c)], in1=stt[:, cs(c)], op=ALU.mult),
                waits=[(VV, v_carry), (AA, a_evac0 + 8)], inc=(VV if c == NF - 1 else None))

        v_ln1 = emit_ln(p, cm, i, vf, stt, stt, xb, sq, gcol=gcols, bcol=bcols,
                        guards=[(AA, a_evacs)])

        for c in range(NF):
            p.op("vector", lambda e, c=c: e.tensor_tensor(out=ub[:, cs(c)], in0=stt[:, cs(c)],
                                                          in1=gf[:, cs(c)], op=ALU.mult),
                 waits=[(VV, v_ln1), (AA, a_evacs)], inc=(VV if c == NF - 1 else None))
        v_u = p.n(VV)

        t_wo = []
        for m in range(8):
            w = [(VV, v_u)] + wm
            if m >= 4:
                w.append((VV, v_u + (m - 4) + 1))
            for k in range(8):
                p.op("tensor", lambda e, m=m, k=k: e.matmul(
                    pj[4 + m % 4][:, :], s_wo[:, k * D + m * 128: k * D + (m + 1) * 128], ub[:, cs(k)],
                    start=(k == 0), stop=(k == 7)),
                    waits=w if k == 0 else (), inc=(TT_ if k == 7 else None))
            t_wo.append(p.n(TT_))
        for m in range(8):
            p.op("vector", lambda e, m=m: e.scalar_tensor_tensor(
                out=qf[:, cs(m)], in0=pj[4 + m % 4][:, :], scalar=cm.cp[:, CP_ZERO:CP_ZERO + 1],
                in1=xf[:, cs(m)], op0=ALU.bypass, op1=ALU.add),
                waits=[(TT_, t_wo[m]), (IN, in_x)], inc=VV)

        v_x1 = emit_ln(p, cm, i, qf, gf, stt, xb, sq)

        for c in range(NF):
            p.op("sync", lambda e, c=c, tsl=tsl: e.dma_start(out=x1[c * 128:(c + 1) * 128, tsl], in_=gf[:, cs(c)]),
                 waits=[(VV, v_x1)], inc=OUT, k=16)

        t_end, v_end, a_end, out_end = p.n(TT_), p.n(VV), p.n(AA), p.n(OUT)



def emit_p1(p, cm, A):
    p.phase_reset()
    NE = 32
    s_w1 = p.sbuf("p1_w1", [128, 8 * 4096], BF16)
    s_rwt = p.sbuf("p1_rwt", [128, 8 * 16], F32)
    s_rb = p.sbuf("p1_rb", [16, 1], F32)
    s_b1 = p.sbuf("p1_b1", [128, NE], F32)
    s_b2 = p.sbuf("p1_b2", [128, 8], F32)
    s_sel = p.sbuf("p1_sel", [16, 16 * 128], BF16)
    xf = p.sbuf("p1_xf", [128, T], F32)
    xb = p.sbuf("p1_xb", [128, T], BF16)
    sq = p.sbuf("p1_sq", [128, T], BF16)
    hb = p.sbuf("p1_hb", [128, NE * TC], BF16)
    w2b = p.sbuf("p1_w2b", [128, 2 * D], BF16)
    pw2s = p.sbuf("p1_pw2s", [128, 2 * D], F32)
    zf = p.sbuf("p1_zf", [128, T], F32)
    tmp = p.sbuf("p1_tmp", [128, T], F32)
    rl = p.sbuf("p1_rl", [16, TC], F32)
    rlT = p.sbuf("p1_rlT", [128, 64], F32)
    work = p.sbuf("p1_work", [128, 64], F32)
    mtm = p.sbuf("p1_mtm", [128, 64], F32)
    mcol = p.sbuf("p1_mcol", [128, 1], F32)
    eqt = p.sbuf("p1_eqt", [128, 16], F32)
    mbf = p.sbuf("p1_mbf", [16, TC], BF16)

    pj = cm.pj
    IN, GIN, TT_, VV, AA, OUT = cm.IN, cm.GIN, cm.TT, cm.VV, cm.AA, cm.OUT
    GW = cm.GW
    x1, x2, w2t = A["x1"], A["x2"], A["w2t"]

    for dst, src in [(s_rwt, A["rwt"]), (s_rb, A["rb16"]), (s_b1, A["b1c"]), (s_b2, A["b2c"])]:
        p.op("sync", lambda e, d=dst, s=src: e.dma_start(out=d[:, :], in_=s), inc=IN, k=16)
    p.op("gpsimd", lambda e: e.dma_start(out=s_sel[:, :], in_=A["sel16s"]), inc=GIN, k=16)
    in_init = p.n(IN)
    wstage = p.sbuf("p1_wstage", [128, 2 * 2048], F32)
    pieces = []
    for k in range(8):
        for hf in range(2):
            pieces.append((s_w1[:, k * 4096 + hf * 2048: k * 4096 + (hf + 1) * 2048],
                           A["w1t"][k * 128:(k + 1) * 128, hf * 2048:(hf + 1) * 2048], 2048))
    wmarks = emit_wcast(p, cm, pieces, wstage, 2048, 2)
    wm = list({s: v for (s, v) in wmarks}.items())

    t_end, v_end, a_end, out_end = p.n(TT_), p.n(VV), p.n(AA), p.n(OUT)

    for i in range(NT):
        tsl = slice(i * TC, (i + 1) * TC)
        bar = [(TT_, t_end), (VV, v_end), (AA, a_end), (OUT, out_end)]
        for c in range(NF):
            p.op("sync", lambda e, c=c, tsl=tsl: e.dma_start(out=xf[:, cs(c)], in_=x1[c * 128:(c + 1) * 128, tsl]),
                 waits=bar, inc=IN, k=16)
            p.op("gpsimd", lambda e, c=c, tsl=tsl: e.dma_start(out=xb[:, cs(c)], in_=x1[c * 128:(c + 1) * 128, tsl]),
                 waits=bar, inc=GIN, k=16)
        in_x = p.n(IN)
        gin_x = p.n(GIN)

        for k in range(8):
            p.op("tensor", lambda e, k=k: e.matmul(pj[4][0:16, :], s_rwt[:, k * 16:(k + 1) * 16],
                                                   xf[:, cs(k)], start=(k == 0), stop=(k == 7)),
                 waits=[(IN, in_x)] + bar if k == 0 else (), inc=(TT_ if k == 7 else None))
        t_rl = p.n(TT_)
        v_rl = p.op("vector", lambda e: e.tensor_scalar(out=rl[:, :], in0=pj[4][0:16, :],
                                                        scalar1=s_rb[:, :], scalar2=None, op0=ALU.add),
                    waits=[(TT_, t_rl)] + bar, inc=VV)
        for j in range(4):
            p.op("tensor", lambda e, j=j: e.transpose(pj[5][:, j * 16:(j + 1) * 16],
                                                      rl[0:16, j * 128:(j + 1) * 128], cm.idf[0:16, 0:16]),
                 waits=[(VV, v_rl)] if j == 0 else (), inc=(TT_ if j == 3 else None))
        t_tp = p.n(TT_)
        v_t0 = p.op("vector", lambda e: e.tensor_copy(out=rlT[:, :], in_=pj[5][:, 0:64]),
                    waits=[(TT_, t_tp)], inc=VV)
        for j in range(4):
            blk = rlT[:, j * 16:(j + 1) * 16]
            wblk = work[:, j * 16:(j + 1) * 16]
            last = p.n(VV)
            for r in range(3):
                srcb = blk if r == 0 else wblk
                v_m = p.op("vector", lambda e, srcb=srcb: e.reduce_max(out=mcol[:, :], in_=srcb, axis=AX.X),
                           waits=[(VV, last)], inc=VV)
                v_e = p.op("vector", lambda e, srcb=srcb: e.tensor_scalar(
                    out=eqt[:, :], in0=srcb, scalar1=mcol[:, :], scalar2=None, op0=ALU.is_ge),
                    waits=[(VV, v_m)], inc=VV)
                last = p.op("vector", lambda e, srcb=srcb, wblk=wblk: e.scalar_tensor_tensor(
                    out=wblk, in0=eqt[:, :], scalar=cm.cp[:, CP_NEG:CP_NEG + 1], in1=srcb,
                    op0=ALU.mult, op1=ALU.add),
                    waits=[(VV, v_e)], inc=VV)
            v_m4 = p.op("vector", lambda e, wblk=wblk: e.reduce_max(out=mcol[:, :], in_=wblk, axis=AX.X),
                        waits=[(VV, last)], inc=VV)
            p.op("vector", lambda e, blk=blk, j=j: e.tensor_scalar(
                out=mtm[:, j * 16:(j + 1) * 16], in0=blk, scalar1=mcol[:, :], scalar2=None, op0=ALU.is_ge),
                waits=[(VV, v_m4)], inc=VV)
        v_mask = p.n(VV)
        for j in range(4):
            p.op("tensor", lambda e, j=j: e.transpose(pj[4][0:16, j * 128:(j + 1) * 128],
                                                      mtm[:, j * 16:(j + 1) * 16], cm.idf[:, :]),
                 waits=[(VV, v_mask)] if j == 0 else (), inc=(TT_ if j == 3 else None))
        t_mt = p.n(TT_)
        v_mbf = p.op("vector", lambda e: e.tensor_copy(out=mbf[:, :], in_=pj[4][0:16, 0:TC]),
                     waits=[(TT_, t_mt)], inc=VV)
        if i == 0 and "dbg_rl" in A:
            p.op("sync", lambda e: e.dma_start(out=A["dbg_rl"], in_=rl[:, :]),
                 waits=[(VV, v_mbf)], inc=OUT, k=16)
            p.op("sync", lambda e: e.dma_start(out=A["dbg_rlT"], in_=rlT[:, :]),
                 waits=[(VV, v_mbf)], inc=OUT, k=16)
            p.op("sync", lambda e: e.dma_start(out=A["dbg_mtm"], in_=mtm[:, :]),
                 waits=[(VV, v_mbf)], inc=OUT, k=16)
            p.op("sync", lambda e: e.dma_start(out=A["dbg_mbf"], in_=mbf[:, :]),
                 waits=[(VV, v_mbf)], inc=OUT, k=16)
            out_end = p.n(OUT)

        a_h0 = p.n(AA)
        v_h0 = p.n(VV)
        t_hgrp = []
        t_bc = []
        for c in range(NE):
            pjt = pj[c % 4]
            w = [(GIN, gin_x)] + wm
            if c >= 4:
                w.append((AA, a_h0 + (c - 4) + 1))
            for k in range(8):
                p.op("tensor", lambda e, pjt=pjt, c=c, k=k: e.matmul(
                    pjt[:, :], s_w1[:, k * 4096 + c * 128: k * 4096 + (c + 1) * 128], xb[:, cs(k)],
                    start=(k == 0), stop=(k == 7)),
                    waits=w if k == 0 else (), inc=None)
            wb = [(VV, v_mbf)]
            if c >= 2:
                wb.append((VV, v_mbf + (c - 2) + 1))
            p.op("tensor", lambda e, c=c: e.matmul(
                pj[4 + c % 2][:, :], s_sel[:, (c // 2) * 128:(c // 2 + 1) * 128], mbf[:, :],
                start=True, stop=True),
                waits=wb, inc=TT_)
            t_bc.append(p.n(TT_))
        for c in range(NE):
            p.op("scalar", lambda e, c=c: e.activation(out=hb[:, cs(c)], in_=pj[c % 4][:, :],
                                                       func=AF.Relu, bias=s_b1[:, c:c + 1]),
                 waits=[(TT_, t_bc[c])], inc=AA)
            p.op("vector", lambda e, c=c: e.tensor_tensor(out=hb[:, cs(c)], in0=hb[:, cs(c)],
                                                          in1=pj[4 + c % 2][:, :], op=ALU.mult),
                 waits=[(AA, a_h0 + c + 1)], inc=VV)
        v_hdone = p.n(VV)
        a_hdone = p.n(AA)
        if i == 0 and "dbg_hb" in A:
            p.op("sync", lambda e: e.dma_start(out=A["dbg_hb"], in_=hb[:, 0:4 * TC]),
                 waits=[(VV, v_hdone)], inc=OUT, k=16)
            out_end = p.n(OUT)

        t_y0 = p.n(TT_)
        wcm2 = []
        for c in range(NE):
            par = c % 2
            gw_r = p.op("sync", lambda e, c=c, par=par: e.dma_start(
                out=pw2s[:, par * D:(par + 1) * D], in_=w2t[c * 128:(c + 1) * 128, :]),
                waits=[(wcm2[c - 2])] if (i > 0 or c >= 2) and c >= 2 else
                      ([(TT_, t_y0)] if i > 0 else []),
                inc=GW[par], k=16)
            ceng = "vector" if c % 2 == 0 else "gpsimd"
            cmk = p.op(ceng, lambda e, c=c, par=par: e.tensor_copy(
                out=w2b[:, par * D:(par + 1) * D], in_=pw2s[:, par * D:(par + 1) * D]),
                waits=[(GW[par], gw_r)] + ([(TT_, t_y0 if c < 2 else t_hgrp[c - 2])] if (i > 0 or c >= 2) else []),
                inc=(VV if ceng == "vector" else p.sem("s_g")))
            wcm2.append((VV if ceng == "vector" else "s_g", cmk))
            w = [wcm2[c], (VV, v_hdone)]
            for m in range(8):
                p.op("tensor", lambda e, c=c, par=par, m=m: e.matmul(
                    pj[m][:, :], w2b[:, par * D + m * 128: par * D + (m + 1) * 128], hb[:, cs(c)],
                    start=(c == 0), stop=(c == NE - 1)),
                    waits=w if m == 0 else (), inc=(TT_ if m == 7 else None))
            t_hgrp.append(p.n(TT_))
        t_y = p.n(TT_)
        for m in range(8):
            p.op("vector", lambda e, m=m: e.scalar_tensor_tensor(
                out=zf[:, cs(m)], in0=pj[m][:, :], scalar=s_b2[:, m:m + 1], in1=xf[:, cs(m)],
                op0=ALU.add, op1=ALU.add),
                waits=[(TT_, t_y)], inc=(VV if m == 7 else None))

        v_x2 = emit_ln(p, cm, i, zf, tmp, tmp, xb, sq)
        for c in range(NF):
            p.op("sync", lambda e, c=c, tsl=tsl: e.dma_start(out=x2[c * 128:(c + 1) * 128, tsl], in_=tmp[:, cs(c)]),
                 waits=[(VV, v_x2)], inc=OUT, k=16)

        t_end, v_end, a_end, out_end = p.n(TT_), p.n(VV), p.n(AA), p.n(OUT)



def emit_p2(p, cm, A):
    p.phase_reset()
    tc = 256
    nt = T // tc
    s_wqkv = p.sbuf("p2_wqkv", [128, 8 * 3072], BF16)
    s_wout = p.sbuf("p2_wout", [128, 16 * D], BF16)
    s_e2h = p.sbuf("p2_e2h", [128, 8 * 16], BF16)
    s_h2e = p.sbuf("p2_h2e", [16, 8 * 128], BF16)
    s_ab = p.sbuf("p2_ab", [128, 16], F32)
    s_sig = p.sbuf("p2_sig", [128, 16], F32)
    lam_t = p.sbuf("p2_lamt", [128, 16 * tc], F32)
    xf = p.sbuf("p2_xf", [128, 8 * tc], F32)
    xb = p.sbuf("p2_xb", [128, 8 * tc], BF16)
    sq = p.sbuf("p2_sq", [128, 8 * tc], BF16)
    phiq = p.sbuf("p2_phiq", [128, 8 * tc], F32)
    phik = p.sbuf("p2_phik", [128, 8 * tc], F32)
    vf = p.sbuf("p2_vf", [128, 8 * tc], F32)
    kvf = p.sbuf("p2_kvf", [128, 8 * tc], F32)
    ta = p.sbuf("p2_ta", [128, 8 * tc], F32)
    tb = p.sbuf("p2_tb", [128, 8 * tc], F32)
    occ = p.sbuf("p2_occ", [128, 16 * tc], BF16)
    den = p.sbuf("p2_den", [16, 2 * tc], F32)
    denb = p.sbuf("p2_denb", [16, 2 * tc], BF16)
    carry = p.sbuf("p2_carry", [128, 32], F32)

    pj = cm.pj
    IN, GIN, TT_, VV, AA, OUT = cm.IN, cm.GIN, cm.TT, cm.VV, cm.AA, cm.OUT
    x2, x3 = A["x2"], A["x3"]

    for dst, src in [(s_ab, A["ab_rep"])]:
        p.op("sync", lambda e, d=dst, s=src: e.dma_start(out=d[:, :], in_=s), inc=IN, k=16)
    in_init = p.n(IN)
    p.op("gpsimd", lambda e: e.dma_start(out=s_e2h[:, :], in_=A["e2h"]), inc=GIN, k=16)
    p.op("gpsimd", lambda e: e.dma_start(out=s_h2e[:, :], in_=A["h2e"]), inc=GIN, k=16)
    wstage = p.sbuf("p2_wstage", [128, 2 * 2048], F32)
    pieces = []
    for k in range(16):
        pieces.append((s_wqkv[:, k * 1536:(k + 1) * 1536], A["wqkv"][:, k * 1536:(k + 1) * 1536], 1536))
    for k in range(8):
        pieces.append((s_wout[:, k * 2048:(k + 1) * 2048], A["wout"][:, k * 2048:(k + 1) * 2048], 2048))
    wmarks = emit_wcast(p, cm, pieces, wstage, 2048, 2)
    wm = list({s: v for (s, v) in wmarks}.items())

    a_sig = p.op("scalar", lambda e: e.activation(out=s_sig[:, :], in_=s_ab[:, :], func=AF.Sigmoid),
                 waits=[(IN, in_init)], inc=AA)
    v_cl = p.op("vector", lambda e: e.tensor_scalar(out=s_sig[:, :], in0=s_sig[:, :],
                                                    scalar1=cm.cp[:, CP_EPS8:CP_EPS8 + 1], scalar2=None,
                                                    op0=ALU.max),
                waits=[(AA, a_sig)], inc=VV)
    v0 = p.op("vector", lambda e: e.memset(xf[:, 0:tc], 1.0), waits=[(VV, v_cl)], inc=VV)
    for c in range(16):
        p.op("vector", lambda e, c=c: e.tensor_scalar(
            out=lam_t[:, cs(c, tc)], in0=xf[:, 0:tc], scalar1=s_sig[:, c:c + 1], scalar2=None, op0=ALU.mult),
            waits=[(VV, v0)], inc=(VV if c == 15 else None))
    p.op("vector", lambda e: e.memset(carry[:, :], 0.0), inc=VV)
    v_init = p.n(VV)

    t_end, a_end, out_end = p.n(TT_), p.n(AA), p.n(OUT)
    v_end = max(p.n(VV), v_init)

    for i in range(nt):
        tsl = slice(i * tc, (i + 1) * tc)
        bar = [(TT_, t_end), (VV, v_end), (AA, a_end), (OUT, out_end)]
        for c in range(NF):
            p.op("sync", lambda e, c=c, tsl=tsl: e.dma_start(out=xf[:, cs(c, tc)],
                                                             in_=x2[c * 128:(c + 1) * 128, tsl]),
                 waits=bar, inc=IN, k=16)
            p.op("gpsimd", lambda e, c=c, tsl=tsl: e.dma_start(out=xb[:, cs(c, tc)],
                                                               in_=x2[c * 128:(c + 1) * 128, tsl]),
                 waits=bar, inc=GIN, k=16)
        in_x = p.n(IN)
        gin_x = p.n(GIN)

        v_ev0 = p.n(VV)
        free_mark = []
        t_grp = []
        for g in range(24):
            pjt = pj[4 + g % 4]
            w = [(GIN, gin_x)] + wm + (bar if g == 0 else [])
            if g >= 4:
                w.append(free_mark[g - 4])
            for k in range(8):
                p.op("tensor", lambda e, pjt=pjt, g=g, k=k: e.matmul(
                    pjt[:, 0:tc], s_wqkv[:, k * 3072 + g * 128: k * 3072 + (g + 1) * 128],
                    xb[:, cs(k, tc)], start=(k == 0), stop=(k == 7)),
                    waits=w if k == 0 else (), inc=(TT_ if k == 7 else None))
            t_grp.append(p.n(TT_))
            m = g % 8
            if g < 16:
                dst = phiq if g < 8 else phik
                v1 = p.op("vector", lambda e, pjt=pjt, dst=dst, m=m: e.tensor_scalar(
                    out=dst[:, cs(m, tc)], in0=pjt[:, 0:tc], scalar1=cm.cp[:, CP_ZERO:CP_ZERO + 1],
                    scalar2=None, op0=ALU.min),
                    waits=[(TT_, t_grp[-1])], inc=VV)
                a1 = p.op("scalar", lambda e, dst=dst, m=m: e.activation(
                    out=dst[:, cs(m, tc)], in_=dst[:, cs(m, tc)], func=AF.Exp),
                    waits=[(VV, v1)], inc=AA)
                v2 = p.op("vector", lambda e, pjt=pjt, dst=dst, m=m: e.scalar_tensor_tensor(
                    out=dst[:, cs(m, tc)], in0=pjt[:, 0:tc], scalar=cm.cp[:, CP_ZERO:CP_ZERO + 1],
                    in1=dst[:, cs(m, tc)], op0=ALU.max, op1=ALU.add),
                    waits=[(AA, a1)], inc=VV)
                free_mark.append((VV, v2))
            else:
                a1 = p.op("scalar", lambda e, pjt=pjt, m=m: e.copy(out=vf[:, cs(m, tc)], in_=pjt[:, 0:tc]),
                          waits=[(TT_, t_grp[-1]), (VV, v_end)], inc=AA)
                free_mark.append((AA, a1))
        a_v = p.n(AA)

        v_kv0 = p.n(VV)
        for c in range(NF):
            p.op("vector", lambda e, c=c: e.tensor_tensor(out=kvf[:, cs(c, tc)], in0=phik[:, cs(c, tc)],
                                                          in1=vf[:, cs(c, tc)], op=ALU.mult),
                 waits=[(AA, a_v), (VV, v_kv0)], inc=(VV if c == NF - 1 else None))
        v_kv = p.n(VV)

        scans = [
            (0, kvf, vf),
            (1, kvf, ta),
            (2, phik, tb),
            (3, phik, phik),
        ]
        v_before = p.n(VV)
        for c in range(NF):
            p.op("vector", lambda e, c=c: e.tensor_tensor_scan(
                out=vf[:, cs(c, tc)], data0=lam_t[:, cs(c, tc)], data1=kvf[:, cs(c, tc)],
                initial=carry[:, c:c + 1], op0=ALU.mult, op1=ALU.add),
                waits=[(VV, v_init if i == 0 else v_before), (VV, v_kv)],
                inc=(VV if c == NF - 1 else None))
        v_s1 = p.n(VV)
        for c in range(NF):
            p.op("vector", lambda e, c=c: e.tensor_copy(out=carry[:, c:c + 1],
                                                        in_=vf[:, c * tc + tc - 1: c * tc + tc]),
                 waits=[(VV, v_s1)], inc=(VV if c == NF - 1 else None))
        for c in range(NF):
            p.op("vector", lambda e, c=c: e.tensor_tensor_scan(
                out=ta[:, cs(c, tc)], data0=lam_t[:, cs(8 + c, tc)], data1=kvf[:, cs(c, tc)],
                initial=carry[:, 8 + c:9 + c], op0=ALU.mult, op1=ALU.add),
                inc=(VV if c == NF - 1 else None))
        v_s2 = p.n(VV)
        for c in range(NF):
            p.op("vector", lambda e, c=c: e.tensor_copy(out=carry[:, 8 + c:9 + c],
                                                        in_=ta[:, c * tc + tc - 1: c * tc + tc]),
                 waits=[(VV, v_s2)], inc=(VV if c == NF - 1 else None))
        for c in range(NF):
            p.op("vector", lambda e, c=c: e.tensor_tensor_scan(
                out=tb[:, cs(c, tc)], data0=lam_t[:, cs(c, tc)], data1=phik[:, cs(c, tc)],
                initial=carry[:, 16 + c:17 + c], op0=ALU.mult, op1=ALU.add),
                inc=(VV if c == NF - 1 else None))
        v_s3 = p.n(VV)
        for c in range(NF):
            p.op("vector", lambda e, c=c: e.tensor_copy(out=carry[:, 16 + c:17 + c],
                                                        in_=tb[:, c * tc + tc - 1: c * tc + tc]),
                 waits=[(VV, v_s3)], inc=(VV if c == NF - 1 else None))
        for c in range(NF):
            p.op("vector", lambda e, c=c: e.tensor_tensor_scan(
                out=kvf[:, cs(c, tc)], data0=lam_t[:, cs(8 + c, tc)], data1=phik[:, cs(c, tc)],
                initial=carry[:, 24 + c:25 + c], op0=ALU.mult, op1=ALU.add),
                inc=(VV if c == NF - 1 else None))
        v_s4 = p.n(VV)
        for c in range(NF):
            p.op("vector", lambda e, c=c: e.tensor_copy(out=carry[:, 24 + c:25 + c],
                                                        in_=kvf[:, c * tc + tc - 1: c * tc + tc]),
                 waits=[(VV, v_s4)], inc=(VV if c == NF - 1 else None))
        v_scans = p.n(VV)

        for ch, (rks, pd) in enumerate([(tb, pj[4]), (kvf, pj[5])]):
            for c in range(NF):
                p.op("vector", lambda e, c=c, rks=rks: e.tensor_tensor(
                    out=xb[:, cs(c, tc)], in0=phiq[:, cs(c, tc)], in1=rks[:, cs(c, tc)], op=ALU.mult),
                    waits=[(VV, v_scans), (TT_, t_grp[-1])], inc=(VV if c == NF - 1 else None))
            v_pr = p.n(VV)
            for c in range(NF):
                p.op("tensor", lambda e, c=c, pd=pd: e.matmul(
                    pd[0:16, 0:tc], s_e2h[:, c * 16:(c + 1) * 16], xb[:, cs(c, tc)],
                    start=(c == 0), stop=(c == 7)),
                    waits=[(VV, v_pr)] if c == 0 else (), inc=(TT_ if c == 7 else None))
        t_den = p.n(TT_)
        v_d1 = p.op("vector", lambda e: e.tensor_scalar(out=den[:, 0:tc], in0=pj[4][0:16, 0:tc],
                                                        scalar1=cm.cp[0:16, CP_EPS6:CP_EPS6 + 1],
                                                        scalar2=None, op0=ALU.max),
                    waits=[(TT_, t_den)], inc=VV)
        p.op("vector", lambda e: e.tensor_scalar(out=den[:, tc:2 * tc], in0=pj[5][0:16, 0:tc],
                                                 scalar1=cm.cp[0:16, CP_EPS6:CP_EPS6 + 1],
                                                 scalar2=None, op0=ALU.max), inc=VV)
        v_d2 = p.n(VV)
        v_d3 = p.op("vector", lambda e: e.reciprocal(out=den[:, 0:tc], in_=den[:, 0:tc]),
                    waits=[(VV, v_d2)], inc=VV)
        p.op("vector", lambda e: e.reciprocal(out=den[:, tc:2 * tc], in_=den[:, tc:2 * tc]), inc=VV)
        v_d4 = p.op("vector", lambda e: e.tensor_copy(out=denb[:, :], in_=den[:, :]), inc=VV)

        v_occ0 = p.n(VV)
        for ch in range(2):
            rkv = vf if ch == 0 else ta
            for c in range(NF):
                pb_ = pj[6 + (c % 2)]
                wb_ = [(VV, v_d4)]
                if ch * 8 + c >= 2:
                    wb_.append((VV, v_occ0 + 2 * (ch * 8 + c) - 2))
                p.op("tensor", lambda e, c=c, ch=ch, pb_=pb_: e.matmul(
                    pb_[:, 0:tc], s_h2e[:, c * 128:(c + 1) * 128],
                    denb[:, ch * tc:(ch + 1) * tc], start=True, stop=True),
                    waits=wb_, inc=TT_)
                t_b = p.n(TT_)
                p.op("vector", lambda e, c=c, ch=ch, rkv=rkv: e.tensor_tensor(
                    out=occ[:, cs(ch * 8 + c, tc)], in0=phiq[:, cs(c, tc)], in1=rkv[:, cs(c, tc)],
                    op=ALU.mult),
                    waits=[(VV, v_d4)], inc=VV)
                p.op("vector", lambda e, c=c, ch=ch, pb_=pb_: e.tensor_tensor(
                    out=occ[:, cs(ch * 8 + c, tc)], in0=occ[:, cs(ch * 8 + c, tc)], in1=pb_[:, 0:tc],
                    op=ALU.mult),
                    waits=[(TT_, t_b)], inc=VV)
        v_occ = p.n(VV)

        t_wo = []
        for m in range(8):
            w = [(VV, v_occ)]
            if m >= 4:
                w.append((VV, v_occ + (m - 4) + 1))
            for kk in range(16):
                p.op("tensor", lambda e, m=m, kk=kk: e.matmul(
                    pj[m % 4][:, 0:tc], s_wout[:, kk * D + m * 128: kk * D + (m + 1) * 128],
                    occ[:, cs(kk, tc)], start=(kk == 0), stop=(kk == 15)),
                    waits=w if kk == 0 else (), inc=(TT_ if kk == 15 else None))
            t_wo.append(p.n(TT_))
        for m in range(8):
            p.op("vector", lambda e, m=m: e.scalar_tensor_tensor(
                out=vf[:, cs(m, tc)], in0=pj[m % 4][:, 0:tc], scalar=cm.cp[:, CP_ZERO:CP_ZERO + 1],
                in1=xf[:, cs(m, tc)], op0=ALU.bypass, op1=ALU.add),
                waits=[(TT_, t_wo[m]), (IN, in_x)], inc=VV)

        v_x3 = emit_ln(p, cm, i, vf, kvf, kvf, xb, sq, tc=tc)
        for c in range(NF):
            p.op("sync", lambda e, c=c, tsl=tsl: e.dma_start(out=x3[c * 128:(c + 1) * 128, tsl],
                                                             in_=kvf[:, cs(c, tc)]),
                 waits=[(VV, v_x3)], inc=OUT, k=16)

        t_end, v_end, a_end, out_end = p.n(TT_), p.n(VV), p.n(AA), p.n(OUT)



W1_WC = [128] * 21 + [42]
W1_OFF = [0]
for _wc in W1_WC:
    W1_OFF.append(W1_OFF[-1] + 16 * _wc)
W1_COLS = W1_OFF[-1]
GB_OFF = [0]
for _wc in W1_WC:
    GB_OFF.append(GB_OFF[-1] + TC)
GB_COLS = GB_OFF[-1]


def emit_p3f(p, cm, A, li, xin, xout, final=False, x3s=None):
    p.phase_reset()
    NEC = 22
    s_w2r = p.sbuf(f"f{li}_w2r", [128, NEC * D], BF16)
    w1b = p.sbuf(f"f{li}_w1b", [128, 2 * 2048], BF16)
    w1f = p.sbuf(f"f{li}_w1f", [128, 2 * 2048], F32)
    xsf = p.sbuf(f"f{li}_xsf", [128, 8 * 1025], F32)
    xnb_off = p._off
    xnb = p.sbuf(f"f{li}_xnb", [128, 8 * 1024], BF16)
    xb = p.sbuf(f"f{li}_xb", [128, T], BF16)
    sq = p.sbuf(f"f{li}_sq", [128, T], BF16)
    gb = p.sbuf(f"f{li}_gb", [128, 2 * GB_COLS], BF16)
    asb = p.sbuf(f"f{li}_asb", [128, TC], F32)
    gsb = p.sbuf(f"f{li}_gsb", [128, TC], F32)
    tmpf = p.sbuf(f"f{li}_tmp", [128, T], F32)
    off_xnb = None
    xtr = p.nc.alloc_sbuf_tensor_at(f"f{li}_xtr", [128, T], F32, offset=xnb_off)
    lngc = p.sbuf(f"f{li}_lng", [128, 8], F32)
    lnbc = p.sbuf(f"f{li}_lnb", [128, 8], F32)
    b1a = p.sbuf(f"f{li}_b1a", [128, NEC], F32)
    b1g = p.sbuf(f"f{li}_b1g", [128, NEC], F32)
    b2c = p.sbuf(f"f{li}_b2", [128, 8], F32)

    pj = cm.pj
    IN, GIN, TT_, VV, AA, OUT = cm.IN, cm.GIN, cm.TT, cm.VV, cm.AA, cm.OUT
    GW = cm.GW

    for dst, key in [(lngc, "lng"), (lnbc, "lnb"), (b1a, "b1a"), (b1g, "b1g"), (b2c, "b2c3")]:
        p.op("sync", lambda e, d=dst, s=A[key], li=li: e.dma_start(out=d[:, :], in_=s[li]), inc=IN, k=16)
    in_init = p.n(IN)
    for k in range(11):
        p.op("gpsimd", lambda e, k=k, li=li: e.dma_start(
            out=s_w2r[:, k * 2048:(k + 1) * 2048], in_=A["w2s"][li, :, k * 2048:(k + 1) * 2048]),
            inc=GIN, k=16)

    t_end, v_end, a_end, out_end = p.n(TT_), p.n(VV), p.n(AA), p.n(OUT)
    gcols = [lngc[:, c:c + 1] for c in range(NF)]
    bcols = [lnbc[:, c:c + 1] for c in range(NF)]

    GGs = p.sem("s_g")
    g_end = p.n(GGs)
    for i2 in range(4):
        t0 = i2 * 1024
        bar = [(TT_, t_end), (VV, v_end), (AA, a_end), (OUT, out_end), (GGs, g_end)]
        if i2 == 0:
            vz = p.op("vector", lambda e: e.memset(xsf[:, 0:8 * 1025:1025], 0.0), inc=VV)
            for c in range(NF):
                p.op("sync", lambda e, c=c: e.dma_start(
                    out=xsf[:, c * 1025 + 1: (c + 1) * 1025],
                    in_=xin[c * 128:(c + 1) * 128, 0:1024]),
                    waits=bar + [(VV, vz)], inc=IN, k=16)
        else:
            for c in range(NF):
                p.op("sync", lambda e, c=c, t0=t0: e.dma_start(
                    out=xsf[:, c * 1025: (c + 1) * 1025],
                    in_=xin[c * 128:(c + 1) * 128, t0 - 1: t0 + 1024]),
                    waits=bar, inc=IN, k=16)
        in_x = p.n(IN)

        def sh_ap(c, s):
            off = c * 1025 + (1 if c < 4 else 0) + s * TC
            return xsf[:, off: off + TC]

        def un_ap(c, s):
            off = c * 1025 + 1 + s * TC
            return xsf[:, off: off + TC]

        v_lns = []
        for s in range(2):
            p.op("vector", None, waits=[(IN, in_x)], inc=VV)
            v_ln = emit_ln(p, cm, i2, (lambda c, s=s: sh_ap(c, s)),
                           (lambda c, s=s: xnb[:, c * 1024 + s * TC: c * 1024 + (s + 1) * TC]),
                           tmpf, xb, sq, gcol=gcols, bcol=bcols,
                           guards=[(IN, in_x), (IN, in_init)], tc=TC)
            v_lns.append(v_ln)
        v_ln_all = p.n(VV)

        t_wgrp = [0] * NEC
        g_gb0 = p.n(p.sem("s_g"))
        a_w0 = p.n(AA)
        w1cm = []
        for c in range(NEC):
            wc = W1_WC[c]
            par = c % 2
            gw_r = p.op("sync", lambda e, c=c, par=par, li=li: e.dma_start(
                out=w1f[:, par * 2048: par * 2048 + 16 * W1_WC[c]],
                in_=A["w1s"][li, :, W1_OFF[c]:W1_OFF[c + 1]]),
                waits=[w1cm[c - 2]] if c >= 2 else [], inc=GW[par], k=16)
            ceng = "vector" if c % 2 == 0 else "gpsimd"
            cm_r = p.op(ceng, lambda e, c=c, par=par: e.tensor_copy(
                out=w1b[:, par * 2048: par * 2048 + 16 * W1_WC[c]],
                in_=w1f[:, par * 2048: par * 2048 + 16 * W1_WC[c]]),
                waits=[(GW[par], gw_r), (TT_, t_wgrp[c - 2] if c >= 2 else t_end)],
                inc=(VV if ceng == "vector" else p.sem("s_g")))
            w1cm.append((VV if ceng == "vector" else "s_g", cm_r))
            for s in range(2):
                idx = 2 * c + s
                pa = pj[(idx % 2) * 2]
                pg = pj[(idx % 2) * 2 + 1]
                w = [w1cm[c], (VV, v_ln_all)]
                if idx >= 2:
                    w.append(("s_g", g_gb0 + (idx - 2) + 1))
                for k in range(8):
                    p.op("tensor", lambda e, c=c, s=s, k=k, par=par, pa=pa, wc=wc: e.matmul(
                        pa[0:wc, 0:TC], w1b[:, par * 2048 + k * 2 * wc: par * 2048 + k * 2 * wc + wc],
                        xnb[:, k * 1024 + s * TC: k * 1024 + (s + 1) * TC],
                        start=(k == 0), stop=(k == 7)),
                        waits=w if k == 0 else (), inc=None)
                for k in range(8):
                    p.op("tensor", lambda e, c=c, s=s, k=k, par=par, pg=pg, wc=wc: e.matmul(
                        pg[0:wc, 0:TC], w1b[:, par * 2048 + k * 2 * wc + wc: par * 2048 + (k + 1) * 2 * wc],
                        xnb[:, k * 1024 + s * TC: k * 1024 + (s + 1) * TC],
                        start=(k == 0), stop=(k == 7)),
                        inc=(TT_ if k == 7 else None))
                t_mm = p.n(TT_)
                a1 = p.op("scalar", lambda e, pa=pa, wc=wc, c=c: e.activation(
                    out=asb[0:wc, :], in_=pa[0:wc, 0:TC], func=AF.Identity, bias=b1a[0:wc, c:c + 1]),
                    waits=[(TT_, t_mm), (IN, in_init)], inc=AA)
                a2 = p.op("scalar", lambda e, pg=pg, wc=wc, c=c: e.activation(
                    out=gsb[0:wc, :], in_=pg[0:wc, 0:TC], func=AF.Gelu, bias=b1g[0:wc, c:c + 1]),
                    inc=AA)
                p.op("gpsimd", lambda e, c=c, s=s, wc=wc: e.tensor_tensor(
                    out=gb[0:wc, s * GB_COLS + GB_OFF[c]: s * GB_COLS + GB_OFF[c] + TC],
                    in0=asb[0:wc, :], in1=gsb[0:wc, :], op=ALU.mult),
                    waits=[(AA, a2)], inc=p.sem("s_g"))
            t_wgrp[c] = p.n(TT_)
        v_gb = ("s_g", p.n(p.sem("s_g")))

        gin_w2 = 11 * 16
        for s in range(2):
            for c in range(NEC):
                wc = W1_WC[c]
                w = [v_gb, (GIN, p.n(GIN)), (VV, p.n(VV))]
                for m in range(8):
                    p.op("tensor", lambda e, c=c, s=s, m=m, wc=wc: e.matmul(
                        pj[m][:, 0:TC], s_w2r[0:wc, c * D + m * 128: c * D + (m + 1) * 128],
                        gb[0:wc, s * GB_COLS + GB_OFF[c]: s * GB_COLS + GB_OFF[c] + TC],
                        start=(c == 0), stop=(c == NEC - 1)),
                        waits=w if (m == 0 and c == 0) else (), inc=(TT_ if (m == 7 and c == NEC - 1) else None))
            t_y = p.n(TT_)
            for m in range(8):
                p.op("vector", lambda e, m=m, s=s: e.scalar_tensor_tensor(
                    out=tmpf[:, cs(m)], in0=pj[m][:, 0:TC], scalar=b2c[:, m:m + 1],
                    in1=un_ap(m, s), op0=ALU.add, op1=ALU.add),
                    waits=[(TT_, t_y)], inc=(VV if m == 7 else None))
            v_z = p.n(VV)
            if not final:
                for c in range(NF):
                    p.op("sync", lambda e, c=c, s=s, t0=t0: e.dma_start(
                        out=xout[c * 128:(c + 1) * 128, t0 + s * TC: t0 + (s + 1) * TC],
                        in_=tmpf[:, cs(c)]),
                        waits=[(VV, v_z)], inc=OUT, k=16)
            else:
                for c in range(NF):
                    p.op("sync", lambda e, c=c, s=s, t0=t0: e.dma_start(
                        out=xtr[:, cs(c)], in_=x3s[c * 128:(c + 1) * 128, t0 + s * TC: t0 + (s + 1) * TC]),
                        waits=[(VV, v_end), v_gb], inc=IN, k=16)
                in_x3 = p.n(IN)
                for c in range(NF):
                    p.op("vector", lambda e, c=c: e.tensor_tensor(
                        out=xtr[:, cs(c)], in0=tmpf[:, cs(c)], in1=xtr[:, cs(c)], op=ALU.add),
                        waits=[(VV, v_z), (IN, in_x3)], inc=(VV if c == NF - 1 else None))
                v_x4 = emit_ln(p, cm, i2, xtr, tmpf, tmpf, xb, sq, tc=TC)
                for c in range(NF):
                    p.op("sync", lambda e, c=c, s=s, t0=t0: e.dma_start(
                        out=xout[c * 128:(c + 1) * 128, t0 + s * TC: t0 + (s + 1) * TC],
                        in_=tmpf[:, cs(c)]),
                        waits=[(VV, v_x4)], inc=OUT, k=16)

        t_end, v_end, a_end, out_end = p.n(TT_), p.n(VV), p.n(AA), p.n(OUT)
        g_end = p.n(GGs)



def emit_p3a(p, cm, A, li, xin, xout):
    p.phase_reset()
    tc = TC
    s_wqkv = p.sbuf(f"a{li}_wqkv", [128, 8 * 3072], BF16)
    s_wo = p.sbuf(f"a{li}_wo", [128, 8 * D], BF16)
    s_h2e = p.sbuf(f"a{li}_h2e", [16, 8 * 128], BF16)
    s_e2h = p.sbuf(f"a{li}_e2h", [128, 8 * 16], BF16)
    s_prot = p.sbuf(f"a{li}_prot", [128, 128], BF16)
    s_trim = p.sbuf(f"a{li}_trim", [128, 256], BF16)
    qcol = p.sbuf(f"a{li}_qcol", [128, 4], F32)
    xsb = p.sbuf(f"a{li}_xsb", [128, 8 * 513], BF16)
    xf = p.sbuf(f"a{li}_xf", [128, T], F32)
    tabs = p.sbuf(f"a{li}_tabs", [128, 4 * tc], F32)
    qf_off = p._off
    qf = p.sbuf(f"a{li}_qf", [128, T], F32)
    kb_off = p._off
    kb = p.sbuf(f"a{li}_kb", [128, T], BF16)
    fb_off = p._off
    fb = p.sbuf(f"a{li}_fb", [128, T], BF16)
    rsc = p.sbuf(f"a{li}_rsc", [128, 2 * tc], F32)
    rsc2 = p.sbuf(f"a{li}_rsc2", [128, 2 * tc], F32)
    sqq = p.sbuf(f"a{li}_sqq", [128, 2 * tc], BF16)
    rnb = p.sbuf(f"a{li}_rnb", [16, 2 * tc], BF16)
    rnr = p.sbuf(f"a{li}_rnr", [16, 2 * tc], F32)
    q_aug = p.sbuf(f"a{li}_qaug", [65, 16 * tc], BF16)
    k_aug = p.sbuf(f"a{li}_kaug", [65, 16 * 768], BF16)
    vtm = p.sbuf(f"a{li}_vtm", [128, 6 * 1040], BF16)
    at = p.nc.alloc_sbuf_tensor_at
    ab = at(f"a{li}_ab", [128, T], BF16, offset=kb_off)
    zt = at(f"a{li}_zt", [128, T], F32, offset=fb_off)
    expb = at(f"a{li}_expb", [128, 2 * 1024], BF16, offset=qf_off)
    mstack = at(f"a{li}_mst", [128, 32], F32, offset=qf_off + 4096)
    mT = at(f"a{li}_mT", [32, 128], BF16, offset=qf_off + 4352)
    denw = at(f"a{li}_denw", [1, 512], F32, offset=qf_off + 4608)
    reccols = at(f"a{li}_recc", [128, 32], F32, offset=qf_off + 6656)
    recT = at(f"a{li}_recT", [32, 128], BF16, offset=qf_off + 6784)
    recrow = at(f"a{li}_recrow", [1, 4096], BF16, offset=fb_off)

    pj = cm.pj
    IN, GIN, TT_, VV, AA, OUT = cm.IN, cm.GIN, cm.TT, cm.VV, cm.AA, cm.OUT
    GG = p.sem("s_g")

    p.op("sync", lambda e, li=li: e.dma_start(out=qcol[:, :], in_=A["qcols"][:, 4 * li:4 * li + 4]),
         inc=IN, k=16)
    in_init = p.n(IN)
    for dst, key in [(s_h2e, "h2e"), (s_e2h, "e2h"), (s_prot, "prot"), (s_trim, "trim")]:
        p.op("gpsimd", lambda e, d=dst, s=A[key]: e.dma_start(out=d[:, :], in_=s), inc=GIN, k=16)
    wstage = p.nc.alloc_sbuf_tensor_at(f"a{li}_wstage", [128, 2 * 3072], F32, offset=qf_off)
    pieces = []
    for k in range(8):
        pieces.append((s_wqkv[:, k * 3072:(k + 1) * 3072], A["wqkv3"][li, :, k * 3072:(k + 1) * 3072], 3072))
    for k in range(8):
        pieces.append((s_wo[:, k * 1024:(k + 1) * 1024], A["wo3"][li, :, k * 1024:(k + 1) * 1024], 1024))
    wmarks = emit_wcast(p, cm, pieces, wstage, 3072, 2)
    wm = list({s: v for (s, v) in wmarks}.items())
    p.op("vector", lambda e: e.memset(k_aug[64:65, :], 1.0), inc=VV)
    p.op("vector", lambda e: e.memset(vtm[:, 64:6 * 1040:65], 1.0), inc=VV)
    v_init = p.n(VV)

    t_end, a_end, out_end = p.n(TT_), p.n(AA), p.n(OUT)
    v_end = max(p.n(VV), v_init)
    t_pv_last = 0

    for i in range(NT):
        t0 = i * tc
        bar = [(TT_, t_end), (VV, v_end), (AA, a_end), (OUT, out_end)]
        if i == 0:
            vz = p.op("vector", lambda e: e.memset(xsb[:, 0:8 * 513:513], 0.0), waits=bar, inc=VV)
            for c in range(NF):
                p.op("gpsimd", lambda e, c=c: e.dma_start(
                    out=xsb[:, c * 513 + 1:(c + 1) * 513], in_=xin[c * 128:(c + 1) * 128, 0:tc]),
                    waits=bar + [(VV, vz)], inc=GIN, k=16)
        else:
            for c in range(NF):
                p.op("gpsimd", lambda e, c=c, t0=t0: e.dma_start(
                    out=xsb[:, c * 513:(c + 1) * 513], in_=xin[c * 128:(c + 1) * 128, t0 - 1:t0 + tc]),
                    waits=bar, inc=GIN, k=16)
        for c in range(NF):
            p.op("sync", lambda e, c=c, t0=t0: e.dma_start(
                out=xf[:, cs(c)], in_=xin[c * 128:(c + 1) * 128, t0:t0 + tc]),
                waits=bar, inc=IN, k=16)
        for j in range(4):
            p.op("sync", lambda e, j=j, t0=t0: e.dma_start(
                out=tabs[:, j * tc:(j + 1) * tc], in_=A["tabs"][:, j * T + t0: j * T + t0 + tc]),
                waits=bar, inc=IN, k=16)
        in_x = p.n(IN)
        gin_x = p.n(GIN)

        for j in range(4):
            p.op("vector", lambda e, j=j: e.tensor_scalar(
                out=tabs[:, j * tc:(j + 1) * tc], in0=tabs[:, j * tc:(j + 1) * tc],
                scalar1=qcol[:, j:j + 1], scalar2=None, op0=ALU.mult),
                waits=[(IN, in_x), (IN, in_init)] + bar, inc=(VV if j == 3 else None))
        v_eff = p.n(VV)

        def sh(c):
            return c * 513 + (1 if c < 4 else 0)

        free_mark = []
        a_sq = []
        t_n2m = []
        for g in range(16):
            m = g % 8
            pjt = pj[4 + g % 4]
            w = [(GIN, gin_x)] + (bar if g == 0 else [])
            if g >= 4:
                w = w + [free_mark[g - 4]]
            for k in range(8):
                p.op("tensor", lambda e, pjt=pjt, g=g, k=k: e.matmul(
                    pjt[:, :], s_wqkv[:, k * 3072 + g * 128: k * 3072 + (g + 1) * 128],
                    xsb[:, sh(k): sh(k) + tc], start=(k == 0), stop=(k == 7)),
                    waits=w if k == 0 else (), inc=(TT_ if k == 7 else None))
            t_g = p.n(TT_)
            dstf = qf if g < 8 else kb
            p.op("scalar", lambda e, pjt=pjt, m=m, dstf=dstf: e.copy(out=dstf[:, cs(m)], in_=pjt[:, :]),
                 waits=[(TT_, t_g), (VV, v_end)], inc=None)
            if g < 8:
                p.op("scalar", lambda e, pjt=pjt, m=m: e.copy(out=fb[:, cs(m)], in_=pjt[:, :]), inc=None)
            wsq = [(TT_, t_n2m[g - 2])] if g >= 2 else []
            a2 = p.op("scalar", lambda e, pjt=pjt, g=g: e.activation(
                out=sqq[:, (g % 2) * tc:(g % 2 + 1) * tc], in_=pjt[:, :], func=AF.Square),
                waits=wsq, inc=AA)
            a_sq.append(a2)
            pn = pj[0] if g < 8 else pj[1]
            p.op("tensor", lambda e, g=g, m=m, pn=pn: e.matmul(
                pn[0:16, :], s_e2h[:, m * 16:(m + 1) * 16], sqq[:, (g % 2) * tc:(g % 2 + 1) * tc],
                start=(m == 0), stop=(m == 7)),
                waits=[(AA, a2)] + ([(VV, v_end)] if g == 0 else []), inc=TT_)
            t_n2m.append(p.n(TT_))
            free_mark.append((AA, a2))
        t_n2 = p.n(TT_)
        a_evacs = p.n(AA)

        v_vt0 = p.n(VV)
        for g in range(8):
            tt, n = g // 2, g % 2
            pjt = pj[4 + g % 4]
            w = [free_mark[12 + g % 4]] if g < 4 else [(VV, v_vt0 + (g - 4) + 1)]
            for k in range(8):
                p.op("tensor", lambda e, pjt=pjt, tt=tt, n=n, k=k: e.matmul(
                    pjt[:, :], xsb[:, sh(k) + tt * 128: sh(k) + (tt + 1) * 128],
                    s_wqkv[:, k * 3072 + 2048 + n * 512: k * 3072 + 2048 + (n + 1) * 512],
                    start=(k == 0), stop=(k == 7)),
                    waits=w if k == 0 else (), inc=(TT_ if k == 7 else None))
            t_v = p.n(TT_)
            slot = (4 * i + tt) % 6
            p.op("vector", lambda e, pjt=pjt, slot=slot, n=n: e.tensor_copy(
                out=vtm[:, slot * 1040 + 520 * n: slot * 1040 + 520 * (n + 1)].rearrange(
                    "p (h x) -> p h x", h=8)[:, :, 0:64],
                in_=pjt[:, :].rearrange("p (h x) -> p h x", h=8)),
                waits=[(TT_, t_v)], inc=VV)
        v_vt = p.n(VV)

        a_sq0 = p.op("scalar", lambda e: e.activation(out=rnr[:, 0:tc], in_=pj[0][0:16, :], func=AF.Sqrt),
                     waits=[(TT_, t_n2)], inc=AA)
        a_sq1 = p.op("scalar", lambda e: e.activation(out=rnr[:, tc:2 * tc], in_=pj[1][0:16, :], func=AF.Sqrt),
                     inc=AA)
        v_r1 = p.op("vector", lambda e: e.tensor_scalar(
            out=rnr[:, :], in0=rnr[:, :], scalar1=cm.cp[0:16, CP_EPS12:CP_EPS12 + 1],
            scalar2=None, op0=ALU.max), waits=[(AA, a_sq1)], inc=VV)
        v_r2 = p.op("vector", lambda e: e.reciprocal(out=rnr[:, :], in_=rnr[:, :]),
                    waits=[(VV, v_r1)], inc=VV)
        v_rn = p.op("vector", lambda e: e.tensor_copy(out=rnb[:, :], in_=rnr[:, :]),
                    waits=[(VV, v_r2)], inc=VV)

        def rotary(src_f32, tab_off, rn_off, dest_write, mm_extra, v_guard, perm_src):
            marks = []
            t_perm_list = []
            for c in range(NF):
                pp = pj[4 + c % 2]
                pb = pj[6 + c % 2]
                w1 = [(AA, a_evacs), (VV, v_vt), (VV, v_guard)] + mm_extra
                if c >= 2:
                    w1.append((VV, marks[c - 2]))
                p.op("tensor", lambda e, c=c, pp=pp, perm_src=perm_src: e.matmul(
                    pp[:, :], s_prot[:, :], perm_src[:, cs(c)], start=True, stop=True),
                    waits=w1, inc=None)
                p.op("tensor", lambda e, c=c, pb=pb, rn_off=rn_off: e.matmul(
                    pb[:, :], s_h2e[:, c * 128:(c + 1) * 128], rnb[0:16, rn_off:rn_off + tc],
                    start=True, stop=True),
                    waits=[(VV, v_rn)], inc=TT_)
                t_c = p.n(TT_)
                t_perm_list.append(t_c)
                g1 = p.op("gpsimd", lambda e, c=c, src_f32=src_f32, tab_off=tab_off: e.tensor_tensor(
                    out=rsc[:, (c % 2) * tc:(c % 2 + 1) * tc], in0=src_f32[:, cs(c)],
                    in1=tabs[:, tab_off:tab_off + tc], op=ALU.mult),
                    waits=[(VV, v_eff), (VV, v_guard), (AA, a_evacs)] +
                          ([(VV, marks[c - 2])] if c >= 2 else []),
                    inc=GG)
                v1 = p.op("vector", lambda e, c=c, pp=pp, tab_off=tab_off: e.tensor_tensor(
                    out=rsc2[:, (c % 2) * tc:(c % 2 + 1) * tc], in0=pp[:, :],
                    in1=tabs[:, tab_off + tc:tab_off + 2 * tc], op=ALU.mult),
                    waits=[(TT_, t_c), (VV, v_eff)] + ([(VV, marks[c - 2])] if c >= 2 else []),
                    inc=VV)
                g2 = p.op("gpsimd", lambda e, c=c: e.tensor_tensor(
                    out=rsc[:, (c % 2) * tc:(c % 2 + 1) * tc], in0=rsc[:, (c % 2) * tc:(c % 2 + 1) * tc],
                    in1=rsc2[:, (c % 2) * tc:(c % 2 + 1) * tc], op=ALU.add),
                    waits=[(GG, g1), (VV, v1)], inc=GG)
                last = None
                for half in range(2):
                    for (dst_ap, src_sl) in dest_write(c, half):
                        last = p.op("vector",
                                    lambda e, dst_ap=dst_ap, src_sl=src_sl, c=c, half=half, pb=pb:
                                    e.tensor_tensor(
                                        out=dst_ap,
                                        in0=rsc[half * 64:(half + 1) * 64,
                                                (c % 2) * tc + src_sl.start:(c % 2) * tc + src_sl.stop],
                                        in1=pb[half * 64:(half + 1) * 64, src_sl],
                                        op=ALU.mult),
                                    waits=[(GG, g2), (TT_, t_c)], inc=VV)
                marks.append(last)
            return marks, t_perm_list

        _, t_qperm = rotary(qf, 0, 0,
                            lambda c, half: [(q_aug[0:64, (2 * c + half) * tc: (2 * c + half + 1) * tc],
                                              slice(0, tc))],
                            [], v_eff, fb)
        v_qrot = p.n(VV)

        def k_dest(c, half):
            h = 2 * c + half
            out = []
            for wl in range(2):
                ss = (2 * i + wl) % 3
                out.append((k_aug[0:64, h * 768 + ss * 256: h * 768 + (ss + 1) * 256],
                            slice(wl * 256, (wl + 1) * 256)))
            return out

        rotary(kb, 2 * tc, tc, k_dest, [], v_qrot, kb)
        v_krot = p.n(VV)

        t_tp_prev = 0
        v_attn_prev = p.n(VV)
        for wl in range(2):
            gw = 2 * i + wl
            kts = [kt for kt in range(2 * gw - 2, 2 * gw + 2) if kt >= 0]
            v_qm0 = p.n(VV)
            nh = 0
            v_red = []
            for h in range(16):
                for qt in range(2):
                    valid = [kt for kt in kts if 2 * gw - 2 + qt <= kt <= 2 * gw + qt]
                    pq = pj[4 + nh % 4]
                    w = [(VV, v_krot)]
                    if nh >= 4:
                        w.append((VV, v_red[nh - 4]))
                    else:
                        w.append((VV, v_qm0))
                    if nh == 0:
                        w.append((TT_, t_tp_prev))
                    first = True
                    for kt in valid:
                        kl = kt - (2 * gw - 2)
                        ss = (kt // 2) % 3
                        hf = kt % 2
                        p.op("tensor",
                             lambda e, h=h, qt=qt, kl=kl, ss=ss, hf=hf, pq=pq, wl=wl, first=first:
                             e.matmul(pq[:, kl * 128:(kl + 1) * 128],
                                      q_aug[0:64, h * tc + wl * 256 + qt * 128: h * tc + wl * 256 + (qt + 1) * 128],
                                      k_aug[0:64, h * 768 + ss * 256 + hf * 128: h * 768 + ss * 256 + (hf + 1) * 128],
                                      start=first, stop=(kt == valid[-1])),
                             waits=w if first else (), inc=(TT_ if kt == valid[-1] else None))
                        first = False
                    t_qm = p.n(TT_)
                    lo = (valid[0] - (2 * gw - 2)) * 128
                    hi = (valid[-1] - (2 * gw - 2) + 1) * 128
                    r = p.op("vector", lambda e, h=h, qt=qt, lo=lo, hi=hi, pq=pq: e.reduce_max(
                        out=mstack[:, h + 16 * qt: h + 16 * qt + 1],
                        in_=pq[:, lo:hi], axis=AX.X, negate=True),
                        waits=[(TT_, t_qm)], inc=VV)
                    v_red.append(r)
                    nh += 1
            g_red = v_red[-1]
            t_tp = p.op("tensor", lambda e: e.transpose(pj[2][0:32, 0:128], mstack[:, :], cm.idf[:, :]),
                        waits=[(VV, g_red), (VV, v_attn_prev)], inc=TT_)
            t_tp_prev = t_tp
            v_mT = p.op("vector", lambda e: e.tensor_copy(out=mT[:, :], in_=pj[2][0:32, 0:128]),
                        waits=[(TT_, t_tp)], inc=VV)
            reh = q_aug[64:65, :].rearrange("a (h x) -> a h x", x=512)
            for qt in range(2):
                dst_ap = reh[:, :, wl * 256 + qt * 128: wl * 256 + (qt + 1) * 128]
                p.op("sync", lambda e, dst_ap=dst_ap, qt=qt: e.dma_start(
                    out=dst_ap, in_=mT[16 * qt:16 * (qt + 1), :]),
                    waits=[(VV, v_mT)], inc=OUT, k=16)
            sc = p.n(OUT)
            t_pv = {}
            t_dtr = {}
            v_evac = {}
            a_exp = {}
            a_dw = {}
            hs = {}

            def emit_qk_exp(h):
                st = hs[h] = {}
                sb6 = 6 if h % 2 == 0 else 2
                eoff = (h % 2) * 1024
                st["sb6"], st["eoff"] = sb6, eoff
                wqk = [(OUT, sc)]
                if h >= 2:
                    wqk.append((AA, a_exp[h - 2]))
                elif h == 1:
                    wqk.append((VV, v_mT))
                else:
                    wqk.append((VV, v_attn_prev))
                t_sc = []
                firstq = True
                for kt in kts:
                    kl = kt - (2 * gw - 2)
                    ss = (kt // 2) % 3
                    hf = kt % 2
                    role = kl
                    if role == 0:
                        qsl = slice(0, 128)
                    elif role == 3:
                        qsl = slice(128, 256)
                    else:
                        qsl = slice(0, 256)
                    pb6 = pj[sb6 + kl // 2]
                    coff = (kl % 2) * 256
                    p.op("tensor",
                         lambda e, h=h, ss=ss, hf=hf, qsl=qsl, pb6=pb6, coff=coff, wl=wl:
                         e.matmul(pb6[:, coff + qsl.start: coff + qsl.stop],
                                  k_aug[0:65, h * 768 + ss * 256 + hf * 128: h * 768 + ss * 256 + (hf + 1) * 128],
                                  q_aug[0:65, h * tc + wl * 256 + qsl.start: h * tc + wl * 256 + qsl.stop],
                                  start=True, stop=True),
                         waits=wqk if firstq else (), inc=TT_)
                    firstq = False
                    t_sc.append((p.n(TT_), kl, role, qsl, pb6, coff))
                st["t_sc"] = t_sc
                t_all_qk = t_sc[-1][0]
                banks = sorted(set((kl // 2) for (_, kl, _, _, _, _) in t_sc))
                for bki, bk in enumerate(banks):
                    kls = [kl for (_, kl, _, _, _, _) in t_sc if kl // 2 == bk]
                    c0 = (min(kls) % 2) * 256
                    c1 = (max(kls) % 2) * 256 + 256
                    wexp = [(TT_, t_all_qk)]
                    if h >= 2:
                        wexp.append((TT_, t_pv[h - 2]))
                    elif i > 0 or wl > 0:
                        wexp.append((TT_, t_pv_last2[h]))
                    p.op("scalar", lambda e, bk=bk, c0=c0, c1=c1, sb6=sb6, eoff=eoff: e.activation(
                        out=expb[:, eoff + bk * 512 + c0: eoff + bk * 512 + c1],
                        in_=pj[sb6 + bk][:, c0:c1], func=AF.Exp),
                        waits=wexp, inc=(AA if bki == len(banks) - 1 else None))
                a_exp[h] = p.n(AA)
                for (tmark, kl, role, qsl, pb6, coff) in t_sc:
                    if role == 0:
                        msl, esl = slice(0, 128), slice(kl * 256, kl * 256 + 128)
                    elif role == 1:
                        msl, esl = slice(0, 128), slice(kl * 256 + 128, kl * 256 + 256)
                    elif role == 2:
                        msl, esl = slice(128, 256), slice(kl * 256, kl * 256 + 128)
                    else:
                        msl, esl = slice(128, 256), slice(kl * 256 + 128, kl * 256 + 256)
                    p.op("vector", lambda e, msl=msl, esl=esl, eoff=eoff: e.tensor_tensor(
                        out=expb[:, eoff + esl.start: eoff + esl.stop],
                        in0=expb[:, eoff + esl.start: eoff + esl.stop],
                        in1=s_trim[:, msl], op=ALU.mult),
                        waits=[(AA, a_exp[h])], inc=VV)
                st["v_msk"] = (VV, p.n(VV))

            def emit_pv_evac(h):
                st = hs[h]
                pv = pj[h % 2]
                eoff = st["eoff"]
                doff = (h % 2) * 256
                t_sc = st["t_sc"]
                firstp = True
                for j, (tmark, kl, role, qsl, pb6, coff) in enumerate(t_sc):
                    slot = (2 * gw - 2 + kl) % 6
                    wpv = []
                    if firstp:
                        wpv = [st["v_msk"]]
                        if h >= 2:
                            wpv.append((VV, v_evac[h - 2]))
                            wpv.append((AA, a_dw[h - 2]))
                    p.op("tensor",
                         lambda e, h=h, kl=kl, qsl=qsl, slot=slot, pv=pv, firstp=firstp, eoff=eoff,
                         lastp=(j == len(t_sc) - 1):
                         e.matmul(pv[0:65, qsl],
                                  vtm[:, slot * 1040 + h * 65: slot * 1040 + (h + 1) * 65],
                                  expb[:, eoff + kl * 256 + qsl.start: eoff + kl * 256 + qsl.stop],
                                  start=firstp, stop=lastp),
                         waits=wpv, inc=(TT_ if j == len(t_sc) - 1 else None))
                    firstp = False
                t_pv[h] = p.n(TT_)
                co = h // 2
                ro = (h % 2) * 64
                v_evac[h] = p.op("vector", lambda e, pv=pv, co=co, ro=ro, wl=wl: e.tensor_copy(
                    out=ab[ro:ro + 64, co * tc + wl * 256: co * tc + wl * 256 + 256],
                    in_=pv[0:64, 0:256]),
                    waits=[(TT_, t_pv[h])], inc=VV)
                a_dw[h] = p.op("scalar", lambda e, pv=pv, doff=doff: e.copy(
                    out=denw[0:1, doff:doff + 256], in_=pv[64:65, 0:256]),
                    waits=[(TT_, t_pv[h])] + ([(TT_, t_dtr[h - 2])] if h >= 2 else []), inc=AA)

            def emit_dtr(h):
                doff = (h % 2) * 256
                for qt in range(2):
                    p.op("tensor", lambda e, h=h, qt=qt, doff=doff: e.transpose(
                        pj[5][:, 2 * h + qt: 2 * h + qt + 1],
                        denw[0:1, doff + qt * 128: doff + (qt + 1) * 128],
                        cm.idf[0:1, 0:1]),
                        waits=[(AA, a_dw[h])] if qt == 0 else (), inc=(TT_ if qt == 1 else None))
                t_dtr[h] = p.n(TT_)

            for hh in range(18):
                if hh < 16:
                    emit_qk_exp(hh)
                if 1 <= hh <= 16:
                    emit_pv_evac(hh - 1)
                if 2 <= hh <= 17:
                    emit_dtr(hh - 2)
            v_rc = p.op("vector", lambda e: e.reciprocal(out=reccols[:, :], in_=pj[5][:, 0:32]),
                        waits=[(TT_, t_dtr[15])], inc=VV)
            t_rT = p.op("tensor", lambda e: e.transpose(pj[4][0:32, 0:128], reccols[:, :], cm.idf[:, :]),
                        waits=[(VV, v_rc)], inc=TT_)
            v_rT = p.op("vector", lambda e: e.tensor_copy(out=recT[:, :], in_=pj[4][0:32, 0:128]),
                        waits=[(TT_, t_rT)], inc=VV)
            scr = p.op("sync", lambda e: e.dma_start(out=recrow[:, :], in_=recT[:, :]),
                       waits=[(VV, v_rT)], inc=OUT, k=16)
            v_nrm0 = p.n(VV)
            for hp in range(8):
                pcb = pj[2 + hp % 2]
                wn = [(OUT, scr)]
                if hp >= 2:
                    wn.append((VV, v_nrm0 + 2 * (hp - 2) + 2))
                t_nb = p.op("tensor", lambda e, pcb=pcb, hp=hp: e.matmul(
                    pcb[0:64, 0:512], cm.onbr[0:1, 0:64],
                    recrow[0:1, 512 * hp: 512 * hp + 512],
                    start=True, stop=True),
                    waits=wn, inc=TT_)
                for sub in range(2):
                    h = 2 * hp + sub
                    co = h // 2
                    ro = (h % 2) * 64
                    p.op("vector", lambda e, pcb=pcb, co=co, ro=ro, wl=wl, sub=sub: e.tensor_tensor(
                        out=ab[ro:ro + 64, co * tc + wl * 256: co * tc + wl * 256 + 256],
                        in0=ab[ro:ro + 64, co * tc + wl * 256: co * tc + wl * 256 + 256],
                        in1=pcb[0:64, sub * 256:(sub + 1) * 256], op=ALU.mult),
                        waits=[(TT_, t_nb), (VV, v_evac[h])], inc=VV)
            t_pv_last2 = (t_pv[14], t_pv[15])
            v_attn_prev = p.n(VV)
        v_attn = p.n(VV)
        a_attn = p.n(AA)

        t_wo = []
        v_wo0 = p.n(VV)
        for m in range(8):
            w = [(VV, v_attn), (AA, a_attn)] + wm
            if m >= 4:
                w.append((VV, v_wo0 + (m - 4) + 1))
            for k in range(8):
                p.op("tensor", lambda e, m=m, k=k: e.matmul(
                    pj[4 + m % 4][:, :], s_wo[:, k * D + m * 128: k * D + (m + 1) * 128], ab[:, cs(k)],
                    start=(k == 0), stop=(k == 7)),
                    waits=w if k == 0 else (), inc=(TT_ if k == 7 else None))
            t_wo.append(p.n(TT_))
        for m in range(8):
            p.op("vector", lambda e, m=m: e.scalar_tensor_tensor(
                out=zt[:, cs(m)], in0=pj[4 + m % 4][:, :], scalar=cm.cp[:, CP_ZERO:CP_ZERO + 1],
                in1=xf[:, cs(m)], op0=ALU.bypass, op1=ALU.add),
                waits=[(TT_, t_wo[m]), (IN, in_x)], inc=VV)
        v_z = p.n(VV)
        for c in range(NF):
            p.op("sync", lambda e, c=c, t0=t0: e.dma_start(
                out=xout[c * 128:(c + 1) * 128, t0:t0 + tc], in_=zt[:, cs(c)]),
                waits=[(VV, v_z)], inc=OUT, k=16)

        t_end, v_end, a_end, out_end = p.n(TT_), p.n(VV), p.n(AA), p.n(OUT)



def _t(a):
    return np.ascontiguousarray(np.asarray(a).T)


def _col8(vec, n=8):
    return np.ascontiguousarray(np.asarray(vec).reshape(n, 128).T)


def _chunked(a):
    a = np.asarray(a)
    k = a.shape[0] // 128
    return np.ascontiguousarray(a.reshape(k, 128, a.shape[1]).transpose(1, 0, 2).reshape(128, -1))


def p0_inputs(inp, b):
    beta_rep = np.repeat(np.asarray(inp["L0_attn_beta"]), 64)
    return {
        "x0": _t(inp["L0_attn_x"][b]),
        "wq": _t(inp["L0_attn_Wq"]),
        "wv": _t(inp["L0_attn_Wv"]),
        "wg": _t(inp["L0_attn_Wg"]),
        "wo": _t(inp["L0_attn_Wo"]),
        "lamsrc": _col8(beta_rep),
        "g8a": _col8(inp["L0_attn_ln_gamma"]),
        "b8a": _col8(inp["L0_attn_ln_beta"]),
    }


def p1_inputs(inp, b=None):
    import ml_dtypes
    sel = np.zeros((16, 16 * 128), dtype=ml_dtypes.bfloat16)
    for e in range(16):
        sel[e, e * 128:(e + 1) * 128] = 1
    return {
        "rwt": _chunked(_t(inp["L0_ffn_router_w"])),
        "rb16": np.asarray(inp["L0_ffn_router_b"], np.float32).reshape(16, 1),
        "w1t": _t(inp["L0_ffn_w1"]),
        "b1c": _col8(inp["L0_ffn_b1"], 32),
        "w2t": _t(inp["L0_ffn_w2"]),
        "b2c": _col8(inp["L0_ffn_b2"]),
        "sel16s": sel,
    }


def p2_inputs(inp, b=None):
    import ml_dtypes
    alpha_rep = np.repeat(np.asarray(inp["L1_attn_alpha"]), 64)
    beta_rep = np.repeat(np.asarray(inp["L1_attn_beta"]), 64)
    ab = np.concatenate([_col8(alpha_rep), _col8(beta_rep)], axis=1)
    e2h = np.zeros((128, 8 * 16), dtype=ml_dtypes.bfloat16)
    h2e = np.zeros((16, 8 * 128), dtype=ml_dtypes.bfloat16)
    for c in range(8):
        for pp in range(128):
            h = 2 * c + pp // 64
            e2h[pp, c * 16 + h] = 1
            h2e[h, c * 128 + pp] = 1
    return {
        "ab_rep": np.ascontiguousarray(ab, np.float32),
        "e2h": e2h,
        "h2e": h2e,
        "wqkv": _chunked(np.asarray(inp["L1_attn_w_qkv"])),
        "wout": _chunked(np.asarray(inp["L1_attn_w_out"])),
    }


def p3_ffn_inputs(inp):
    W1 = np.asarray(inp["L1_ffn_W1"])
    W2 = np.asarray(inp["L1_ffn_W2"])
    b1 = np.asarray(inp["L1_ffn_b1"])
    w1s = np.zeros((6, 128, W1_COLS), np.float32)
    for li in range(6):
        for c in range(22):
            wc = W1_WC[c]
            blocks = []
            for k in range(8):
                blocks.append(W1[li, k * 128:(k + 1) * 128, c * 128:c * 128 + wc])
                blocks.append(W1[li, k * 128:(k + 1) * 128, 2730 + c * 128:2730 + c * 128 + wc])
            w1s[li, :, W1_OFF[c]:W1_OFF[c + 1]] = np.concatenate(blocks, axis=1)
    w2s = np.zeros((6, 128, 22 * 1024), np.float32)
    for li in range(6):
        for c in range(22):
            wc = W1_WC[c]
            w2s[li, 0:wc, c * 1024:(c + 1) * 1024] = W2[li, c * 128:c * 128 + wc, :]
    b1a = np.zeros((6, 128, 22), np.float32)
    b1g = np.zeros((6, 128, 22), np.float32)
    for li in range(6):
        for c in range(22):
            wc = W1_WC[c]
            b1a[li, 0:wc, c] = b1[li, c * 128:c * 128 + wc]
            b1g[li, 0:wc, c] = b1[li, 2730 + c * 128:2730 + c * 128 + wc]
    return {
        "w1s": w1s,
        "w2s": w2s,
        "b1a": b1a,
        "b1g": b1g,
        "b2c3": np.stack([_col8(np.asarray(inp["L1_ffn_b2"])[li]) for li in range(6)]),
        "lng": np.stack([_col8(np.asarray(inp["L1_ffn_ln_g"])[li]) for li in range(6)]),
        "lnb": np.stack([_col8(np.asarray(inp["L1_ffn_ln_b"])[li]) for li in range(6)]),
    }


def p3_attn_inputs(inp):
    import ml_dtypes
    Wqkv = np.asarray(inp["L1_ffn_Wqkv"])
    Wo = np.asarray(inp["L1_ffn_Wo"])
    qs = np.asarray(inp["L1_ffn_q_scale"])
    ks = np.asarray(inp["L1_ffn_k_scale"])
    wqkv3 = np.stack([_chunked(Wqkv[li]) for li in range(6)])
    wo3 = np.stack([_chunked(Wo[li]) for li in range(6)])
    half = np.arange(0, 64, 2, dtype=np.float64)
    inv_freq = 1.0 / 10000.0 ** (half / 64)
    j = np.arange(64)
    freq_j = inv_freq[j % 32]
    t = np.arange(T, dtype=np.float64)
    freqs = t[:, None] * freq_j[None, :]
    base_scale = (half + 0.4 * 64) / (1.4 * 64)
    bs_j = base_scale[j % 32]
    power = (t - T // 2) / 128.0
    scale = bs_j[None, :] ** power[:, None]
    sign = np.where(j < 32, -1.0, 1.0)
    cosf = np.cos(freqs)
    sinf = np.sin(freqs)
    t1q = 8.0 * cosf * scale
    t2q = 8.0 * sign[None, :] * sinf * scale
    t1k = cosf / scale
    t2k = sign[None, :] * sinf / scale
    tabs = np.zeros((128, 4 * T), np.float32)
    for jj, tab in enumerate([t1q, t2q, t1k, t2k]):
        block = tab.T.astype(np.float32)
        tabs[0:64, jj * T:(jj + 1) * T] = block
        tabs[64:128, jj * T:(jj + 1) * T] = block
    qcols = np.zeros((128, 4 * 6), np.float32)
    perm64 = (j + 32) % 64
    for li in range(6):
        for (jj, vec) in enumerate([qs[li], qs[li][perm64], ks[li], ks[li][perm64]]):
            qcols[0:64, 4 * li + jj] = vec
            qcols[64:128, 4 * li + jj] = vec
    prot = np.zeros((128, 128), dtype=ml_dtypes.bfloat16)
    for m in range(128):
        k = (m // 64) * 64 + ((m % 64) + 32) % 64
        prot[k, m] = 1
    trim = np.zeros((128, 256), dtype=ml_dtypes.bfloat16)
    for jj in range(128):
        trim[jj, 0:128][np.arange(128) <= jj] = 1
        trim[jj, 128:256][np.arange(128) >= jj] = 1
    e2h = np.zeros((128, 8 * 16), dtype=ml_dtypes.bfloat16)
    h2e = np.zeros((16, 8 * 128), dtype=ml_dtypes.bfloat16)
    for c in range(8):
        for pp in range(128):
            h = 2 * c + pp // 64
            e2h[pp, c * 16 + h] = 1
            h2e[h, c * 128 + pp] = 1
    return {"wqkv3": wqkv3, "wo3": wo3, "tabs": tabs, "qcols": qcols, "prot": prot,
            "trim": trim, "e2h_a": e2h, "h2e_a": h2e}


def dram_io(nc, names_in, names_out, shapes):
    d = {}
    for n in names_in:
        d[n] = nc.dram_tensor(n, shapes[n], BF16 if n in ("g_onb", "g_onbr", "sel16s", "e2h", "h2e", "prot", "trim") else F32,
                              kind="ExternalInput").ap()
    for n in names_out:
        d[n] = nc.dram_tensor(n, shapes[n], F32, kind="ExternalOutput").ap()
    return d


COMMON_SHAPES = {"g_cp": [128, 12], "g_onbr": [1, 128], "g_onb": [128, 1], "g_onf": [1, 128], "g_idf": [128, 128]}


def build_p0_solo():
    p = Prog()
    shapes = dict(COMMON_SHAPES)
    shapes.update({"x0": [D, T], "wq": [D, D], "wv": [D, D], "wg": [D, D], "wo": [D, D],
                   "lamsrc": [128, 8], "g8a": [128, 8], "b8a": [128, 8], "x1": [D, T]})
    A = dram_io(p.nc, ["g_cp", "g_onb", "g_onf", "g_idf", "g_onbr", "x0", "wq", "wv", "wg", "wo",
                       "lamsrc", "g8a", "b8a"], ["x1"], shapes)
    cm = Common(p, A)
    p.mark_global()
    emit_p0(p, cm, A)
    p.op("sync", None, waits=[(cm.OUT, p.n(cm.OUT))])
    return p.build()


def build_p1_solo():
    p = Prog()
    shapes = dict(COMMON_SHAPES)
    shapes.update({"x1": [D, T], "rwt": [128, 128], "rb16": [16, 1], "w1t": [D, 4096],
                   "b1c": [128, 32], "w2t": [4096, D], "b2c": [128, 8],
                   "sel16s": [16, 16 * 128], "x2": [D, T]})
    A = dram_io(p.nc, ["g_cp", "g_onb", "g_onf", "g_idf", "g_onbr", "x1", "rwt", "rb16", "w1t",
                       "b1c", "w2t", "b2c", "sel16s"], ["x2"], shapes)
    cm = Common(p, A)
    p.mark_global()
    emit_p1(p, cm, A)
    p.op("sync", None, waits=[(cm.OUT, p.n(cm.OUT))])
    return p.build()


def run_p0(inp):
    nc = build_p0_solo()
    cmh = common_host_inputs()
    in_maps = [dict(cmh, **p0_inputs(inp, b)) for b in range(NCORE)]
    res = run_bass_kernel_spmd(nc, in_maps, core_ids=list(range(NCORE)))
    return np.stack([np.ascontiguousarray(r["x1"].T) for r in res.results])


def build_p1_solo_dbg():
    p = Prog()
    shapes = dict(COMMON_SHAPES)
    shapes.update({"x1": [D, T], "rwt": [128, 128], "rb16": [16, 1], "w1t": [D, 4096],
                   "b1c": [128, 32], "w2t": [4096, D], "b2c": [128, 8],
                   "sel16s": [16, 16 * 128], "x2": [D, T],
                   "dbg_rl": [16, TC], "dbg_rlT": [128, 64], "dbg_mtm": [128, 64],
                   "dbg_mbf": [16, TC], "dbg_hb": [128, 4 * TC]})
    A = dram_io(p.nc, ["g_cp", "g_onb", "g_onf", "g_idf", "g_onbr", "x1", "rwt", "rb16", "w1t",
                       "b1c", "w2t", "b2c", "sel16s"],
                ["x2", "dbg_rl", "dbg_rlT", "dbg_mtm", "dbg_mbf", "dbg_hb"], shapes)
    A["dbg_mbf"] = p.nc.dram_tensor("dbg_mbf_b", [16, TC], BF16, kind="ExternalOutput").ap()
    A["dbg_hb"] = p.nc.dram_tensor("dbg_hb_b", [128, 4 * TC], BF16, kind="ExternalOutput").ap()
    cm = Common(p, A)
    p.mark_global()
    emit_p1(p, cm, A)
    p.op("sync", None, waits=[(cm.OUT, p.n(cm.OUT))])
    return p.build()


def run_p1_dbg(inp, x1_full):
    nc = build_p1_solo_dbg()
    cmh = common_host_inputs()
    w = p1_inputs(inp)
    in_maps = [dict(cmh, **w, x1=_t(x1_full[b])) for b in range(1)] * 1
    res = run_bass_kernel_spmd(nc, in_maps, core_ids=[0])
    return res.results[0]


def build_p2_solo():
    p = Prog()
    shapes = dict(COMMON_SHAPES)
    shapes.update({"x2": [D, T], "ab_rep": [128, 16], "e2h": [128, 8 * 16], "h2e": [16, 8 * 128],
                   "wqkv": [128, 8 * 3072], "wout": [128, 16 * D], "x3": [D, T]})
    A = dram_io(p.nc, ["g_cp", "g_onb", "g_onf", "g_idf", "g_onbr", "x2", "ab_rep", "e2h", "h2e",
                       "wqkv", "wout"], ["x3"], shapes)
    cm = Common(p, A)
    p.mark_global()
    emit_p2(p, cm, A)
    p.op("sync", None, waits=[(cm.OUT, p.n(cm.OUT))])
    return p.build()


def run_p2(inp, x2_full):
    nc = build_p2_solo()
    cmh = common_host_inputs()
    w = p2_inputs(inp)
    in_maps = [dict(cmh, **w, x2=_t(x2_full[b])) for b in range(NCORE)]
    res = run_bass_kernel_spmd(nc, in_maps, core_ids=list(range(NCORE)))
    return np.stack([np.ascontiguousarray(r["x3"].T) for r in res.results])


def build_p3f_solo(li, final=False):
    p = Prog()
    shapes = dict(COMMON_SHAPES)
    shapes.update({"xin": [D, T], "w1s": [1, 128, W1_COLS], "w2s": [1, 128, 22 * D],
                   "b1a": [1, 128, 22], "b1g": [1, 128, 22], "b2c3": [1, 128, 8],
                   "lng": [1, 128, 8], "lnb": [1, 128, 8], "x3s": [D, T], "xo": [D, T]})
    names = ["g_cp", "g_onb", "g_onf", "g_idf", "g_onbr", "xin", "w1s", "w2s", "b1a", "b1g",
             "b2c3", "lng", "lnb"] + (["x3s"] if final else [])
    A = dram_io(p.nc, names, ["xo"], shapes)
    cm = Common(p, A)
    p.mark_global()
    emit_p3f(p, cm, A, 0, A["xin"], A["xo"], final=final, x3s=A.get("x3s"))
    p.op("sync", None, waits=[(cm.OUT, p.n(cm.OUT))])
    return p.build()


def run_p3f(inp, x_full, li, final=False, x3_full=None):
    nc = build_p3f_solo(li, final)
    cmh = common_host_inputs()
    w = slice_p3f_w(p3_ffn_inputs(inp), li)
    in_maps = []
    for b in range(NCORE):
        m = dict(cmh, **w, xin=_t(x_full[b]))
        if final:
            m["x3s"] = _t(x3_full[b])
        in_maps.append(m)
    res = run_bass_kernel_spmd(nc, in_maps, core_ids=list(range(NCORE)))
    return np.stack([np.ascontiguousarray(r["xo"].T) for r in res.results])


def build_p3a_solo(li):
    p = Prog()
    shapes = dict(COMMON_SHAPES)
    shapes.update({"xin": [D, T], "wqkv3": [1, 128, 8 * 3072], "wo3": [1, 128, 8 * D],
                   "tabs": [128, 4 * T], "qcols": [128, 4], "prot": [128, 128],
                   "trim": [128, 256], "e2h": [128, 8 * 16], "h2e": [16, 8 * 128],
                   "xo": [D, T]})
    A = dram_io(p.nc, ["g_cp", "g_onb", "g_onf", "g_idf", "g_onbr", "xin", "wqkv3", "wo3", "tabs",
                       "qcols", "prot", "trim", "e2h", "h2e"], ["xo"], shapes)
    cm = Common(p, A)
    p.mark_global()
    emit_p3a(p, cm, A, 0, A["xin"], A["xo"])
    p.op("sync", None, waits=[(cm.OUT, p.n(cm.OUT))])
    return p.build()


def slice_p3a_w(w, li):
    w = dict(w)
    w["e2h"] = w.pop("e2h_a", w.get("e2h"))
    w["h2e"] = w.pop("h2e_a", w.get("h2e"))
    w["wqkv3"] = w["wqkv3"][li:li + 1]
    w["wo3"] = w["wo3"][li:li + 1]
    w["qcols"] = np.ascontiguousarray(w["qcols"][:, 4 * li:4 * li + 4])
    return w


def slice_p3f_w(w, li):
    w = dict(w)
    for k in ("w1s", "w2s", "b1a", "b1g", "b2c3", "lng", "lnb"):
        w[k] = w[k][li:li + 1]
    return w


def run_p3a(inp, x_full, li):
    nc = build_p3a_solo(li)
    cmh = common_host_inputs()
    w = slice_p3a_w(p3_attn_inputs(inp), li)
    in_maps = [dict(cmh, **w, xin=_t(x_full[b])) for b in range(NCORE)]
    res = run_bass_kernel_spmd(nc, in_maps, core_ids=list(range(NCORE)))
    return np.stack([np.ascontiguousarray(r["xo"].T) for r in res.results])


def run_p1(inp, x1_full):
    nc = build_p1_solo()
    cmh = common_host_inputs()
    w = p1_inputs(inp)
    in_maps = [dict(cmh, **w, x1=_t(x1_full[b])) for b in range(NCORE)]
    res = run_bass_kernel_spmd(nc, in_maps, core_ids=list(range(NCORE)))
    return np.stack([np.ascontiguousarray(r["x2"].T) for r in res.results])


def _run(nc, in_maps, outkey):
    res = run_bass_kernel_spmd(nc, in_maps, core_ids=list(range(NCORE)))
    return [r[outkey] for r in res.results]





def build_fused():
    p = Prog()
    nc = p.nc
    shapes = dict(COMMON_SHAPES)
    shapes.update({
        "x0": [D, T],
        "wq": [D, D], "wv": [D, D], "wg": [D, D], "wo": [D, D],
        "lamsrc": [128, 8], "g8a": [128, 8], "b8a": [128, 8],
        "rwt": [128, 128], "rb16": [16, 1], "w1t": [D, 4096], "b1c": [128, 32],
        "w2t": [4096, D], "b2c": [128, 8], "sel16s": [16, 16 * 128],
        "ab_rep": [128, 16], "e2h": [128, 8 * 16], "h2e": [16, 8 * 128],
        "wqkv": [128, 8 * 3072], "wout": [128, 16 * D],
        "wqkv3": [6, 128, 8 * 3072], "wo3": [6, 128, 8 * D],
        "tabs": [128, 4 * T], "qcols": [128, 24], "prot": [128, 128], "trim": [128, 256],
        "w1s": [6, 128, W1_COLS], "w2s": [6, 128, 22 * D],
        "b1a": [6, 128, 22], "b1g": [6, 128, 22], "b2c3": [6, 128, 8],
        "lng": [6, 128, 8], "lnb": [6, 128, 8],
    })
    names_in = list(shapes.keys())
    A = dram_io(nc, names_in, ["x4"], dict(shapes, x4=[D, T]))
    sA = nc.dram_tensor("sA", [D, T], F32).ap()
    sB = nc.dram_tensor("sB", [D, T], F32).ap()
    sC = nc.dram_tensor("sC", [D, T], F32).ap()
    cm = Common(p, A)
    p.mark_global()

    A0 = dict(A)
    A0["x0"], A0["x1"] = A["x0"], sA
    emit_p0(p, cm, A0)
    A1 = dict(A)
    A1["x1"], A1["x2"] = sA, sB
    emit_p1(p, cm, A1)
    A2 = dict(A)
    A2["x2"], A2["x3"] = sB, sC
    emit_p2(p, cm, A2)
    bufs = [sC, sA, sB]
    cur = 0
    for li in range(6):
        nxt = 1 if bufs[cur] is not sA else 2
        cand = [b for b in (sA, sB) if b is not bufs[cur]]
        xo = cand[0]
        emit_p3a(p, cm, A, li, bufs[cur], xo)
        cur = bufs.index(xo)
        cand = [b for b in (sA, sB) if b is not bufs[cur]]
        xo2 = A["x4"] if li == 5 else cand[0]
        emit_p3f(p, cm, A, li, bufs[cur], xo2, final=(li == 5), x3s=sC)
        if li < 5:
            cur = bufs.index(xo2)
    p.op("sync", None, waits=[(cm.OUT, p.n(cm.OUT))])
    return p.build()


def fused_inputs(inputs):
    cmh = common_host_inputs()
    w0 = p0_inputs(inputs, 0)
    del w0["x0"]
    w1 = p1_inputs(inputs)
    w2 = p2_inputs(inputs)
    wa = p3_attn_inputs(inputs)
    wa = dict(wa)
    wa.pop("e2h_a")
    wa.pop("h2e_a")
    wf = p3_ffn_inputs(inputs)
    base = {}
    for d in (cmh, w0, w1, w2, wa, wf):
        base.update(d)
    return base


_FUSED_NC = None


def kernel(**inputs):
    global _FUSED_NC
    if _FUSED_NC is None:
        _FUSED_NC = build_fused()
    base = fused_inputs(inputs)
    x = np.asarray(inputs["L0_attn_x"])
    in_maps = [dict(base, x0=_t(x[b])) for b in range(NCORE)]
    res = run_bass_kernel_spmd(_FUSED_NC, in_maps, core_ids=list(range(NCORE)))
    return np.stack([np.ascontiguousarray(r["x4"].T) for r in res.results]).astype(np.float32)


def kernel_unfused(**inputs):
    cmh = common_host_inputs()
    xT = [_t(np.asarray(inputs["L0_attn_x"])[b]) for b in range(NCORE)]

    w0 = p0_inputs(inputs, 0)
    del w0["x0"]
    x1 = _run(build_p0_solo(), [dict(cmh, **w0, x0=xT[b]) for b in range(NCORE)], "x1")

    w1 = p1_inputs(inputs)
    x2 = _run(build_p1_solo(), [dict(cmh, **w1, x1=x1[b]) for b in range(NCORE)], "x2")

    w2 = p2_inputs(inputs)
    x3 = _run(build_p2_solo(), [dict(cmh, **w2, x2=x2[b]) for b in range(NCORE)], "x3")

    wa = p3_attn_inputs(inputs)
    wf = p3_ffn_inputs(inputs)
    x = x3
    for li in range(6):
        wal = slice_p3a_w(wa, li)
        x = _run(build_p3a_solo(li), [dict(cmh, **wal, xin=x[b]) for b in range(NCORE)], "xo")
        wfl = slice_p3f_w(wf, li)
        maps = []
        for b in range(NCORE):
            m = dict(cmh, **wfl, xin=x[b])
            if li == 5:
                m["x3s"] = x3[b]
            maps.append(m)
        x = _run(build_p3f_solo(li, final=(li == 5)), maps, "xo")

    return np.stack([np.ascontiguousarray(x[b].T) for b in range(NCORE)]).astype(np.float32)


# version 07
# speedup vs baseline: 1.2563x; 1.0132x over previous
import numpy as np
import concourse.bass as bass
import concourse.mybir as mybir
from concourse.bass_utils import run_bass_kernel_spmd

F32 = mybir.dt.float32
BF16 = mybir.dt.bfloat16
AF = mybir.ActivationFunctionType
ALU = mybir.AluOpType
AX = mybir.AxisListType

T = 4096
D = 1024
TC = 512
NT = T // TC
NF = D // 128
NCORE = 8

SBUF_BASE = 16384
SBUF_BUDGET = 229376 - 384

ENGINES = ("sync", "gpsimd", "vector", "scalar", "tensor")

CP_ZERO, CP_EPS, CP_INVD, CP_ONE, CP_NEG, CP_EPS6, CP_EPS12, CP_EPS8, CP_M1, CP_MH, CP_EPS24, CP_SPARE = range(12)
CP_VALS = np.array([0.0, 1e-5, 1.0 / D, 1.0, -1e30, 1e-6, 1e-12, 1e-8, -1.0, -0.5, 1e-24, 0.0], np.float32)


class Prog:

    def __init__(self):
        self.nc = bass.Bass("TRN2", target_bir_lowering=False)
        self.q = {e: [] for e in ENGINES}
        self._sems = {}
        self._off = SBUF_BASE
        self._phase_base = SBUF_BASE
        self._entry_marks = []
        self._entered = set(ENGINES)

    def sbuf(self, name, shape, dtype):
        nbytes = int(np.prod(shape[1:])) * mybir.dt.size(dtype)
        t = self.nc.alloc_sbuf_tensor_at(name, shape, dtype, offset=self._off)
        self._off = (self._off + nbytes + 31) & ~31
        assert self._off <= SBUF_BUDGET, f"SBUF overflow at {name}: {self._off}"
        return t

    def mark_global(self):
        self._phase_base = self._off

    def phase_reset(self):
        self._off = self._phase_base
        self._entry_marks = [(s, rec[1]) for s, rec in self._sems.items()
                             if s in ("s_t", "s_v", "s_a", "s_o")]
        self._entered = set()

    def sem(self, name):
        if name not in self._sems:
            self._sems[name] = [self.nc.alloc_semaphore(name), 0]
        return name

    def n(self, s):
        return self._sems[s][1]

    def op(self, eng, fn, waits=(), inc=None, k=1):
        waits = list(waits)
        if eng not in self._entered:
            self._entered.add(eng)
            waits = self._entry_marks + waits
        w = [(self._sems[s][0], int(v)) for (s, v) in waits if int(v) > 0]
        entry_inc = None
        r = None
        if inc is not None:
            rec = self._sems[inc]
            rec[1] += k
            entry_inc = (rec[0], k)
            r = rec[1]
        self.q[eng].append((w, fn, entry_inc))
        return r

    def barrier(self):
        return {s: rec[1] for s, rec in self._sems.items()}

    def build(self):
        with self.nc.Block() as block:
            for e in ENGINES:
                entries = self.q[e]
                if not entries:
                    continue

                def mk(entries=entries):
                    def body(eng):
                        hi = {}
                        for w, fn, inc in entries:
                            for (h, v) in w:
                                if hi.get(h.num, -1) < v:
                                    eng.wait_ge(h, v)
                                    hi[h.num] = v
                            inst = fn(eng) if fn is not None else None
                            if inc is not None:
                                if inst is None:
                                    inst = eng.nop()
                                inst.then_inc(inc[0], inc[1])
                    return body

                getattr(block, e)(mk())
        return self.nc


def cs(c, w=TC):
    return slice(c * w, (c + 1) * w)


class Common:

    def __init__(self, p, dram):
        nc = p.nc
        self.pj = [nc.alloc_psum_tensor(f"pj{i}", [128, TC], F32) for i in range(8)]
        self.cp = p.sbuf("g_cp", [128, 12], F32)
        self.onb = p.sbuf("g_onb", [128, 1], BF16)
        self.onf = p.sbuf("g_onf", [1, 128], F32)
        self.idf = p.sbuf("g_idf", [128, 128], F32)
        self.onbr = p.sbuf("g_onbr", [1, 128], BF16)
        self.rows = p.sbuf("g_rows", [1, 3 * TC], F32)
        self.IN = p.sem("s_in")
        self.GIN = p.sem("s_gin")
        self.GW = [p.sem("s_gw0"), p.sem("s_gw1"), p.sem("s_gw2"), p.sem("s_gw3")]
        self.TT = p.sem("s_t")
        self.VV = p.sem("s_v")
        self.AA = p.sem("s_a")
        self.OUT = p.sem("s_o")
        self.SL = p.sem("s_sl")
        for dst, key in [(self.cp, "g_cp"), (self.onb, "g_onb"), (self.onf, "g_onf"),
                         (self.idf, "g_idf"), (self.onbr, "g_onbr")]:
            p.op("sync", lambda e, d=dst, s=dram[key]: e.dma_start(out=d[:, :], in_=s),
                 inc=self.IN, k=16)
        self.init_in = p.n(self.IN)


def common_host_inputs():
    import ml_dtypes
    return {
        "g_cp": np.broadcast_to(CP_VALS, (128, 12)).copy(),
        "g_onb": np.ones((128, 1), dtype=ml_dtypes.bfloat16),
        "g_onbr": np.ones((1, 128), dtype=ml_dtypes.bfloat16),
        "g_onf": np.ones((1, 128), dtype=np.float32),
        "g_idf": np.eye(128, dtype=np.float32),
    }


def emit_ln(p, cm, i, src, dst, tmp, xb, sq, gcol=None, bcol=None, guards=(), tc=TC):
    TT_, VV, AA = cm.TT, cm.VV, cm.AA
    pa, pb, pbm, pbr = cm.pj[0], cm.pj[1], cm.pj[2], cm.pj[3]
    sf = src if callable(src) else (lambda c: src[:, cs(c, tc)])
    df = dst if callable(dst) else (lambda c: dst[:, cs(c, tc)])
    v_src = p.n(VV)
    for c in range(NF):
        p.op("scalar", lambda e, c=c: e.copy(out=xb[:, cs(c, tc)], in_=sf(c)),
             waits=[(VV, v_src)] + list(guards), inc=None)
    for c in range(NF):
        p.op("scalar", lambda e, c=c: e.activation(out=sq[:, cs(c, tc)], in_=sf(c), func=AF.Square),
             inc=(AA if c == NF - 1 else None))
    a_cast = p.n(AA)
    for k in range(8):
        p.op("tensor", lambda e, k=k: e.matmul(pa[0:1, 0:tc], cm.onb[:, :], xb[:, cs(k, tc)],
                                               start=(k == 0), stop=(k == 7)),
             waits=[(AA, a_cast)] + list(guards) if k == 0 else (), inc=None)
    for k in range(8):
        p.op("tensor", lambda e, k=k: e.matmul(pb[0:1, 0:tc], cm.onb[:, :], sq[:, cs(k, tc)],
                                               start=(k == 0), stop=(k == 7)),
             inc=(TT_ if k == 7 else None))
    t_stat = p.n(TT_)
    R_mean, R_B, R_C = (cm.rows[0:1, j * TC: j * TC + tc] for j in range(3))
    R_ey2, R_msq, R_var, R_std, R_rstd = R_B, R_C, R_B, R_C, R_B
    v1 = p.op("vector", lambda e: e.tensor_scalar(out=R_mean, in0=pa[0:1, 0:tc], scalar1=cm.cp[0:1, CP_INVD:CP_INVD + 1],
                                                  scalar2=None, op0=ALU.mult),
              waits=[(TT_, t_stat)], inc=VV)
    p.op("vector", lambda e: e.tensor_scalar(out=R_ey2, in0=pb[0:1, 0:tc], scalar1=cm.cp[0:1, CP_INVD:CP_INVD + 1],
                                             scalar2=None, op0=ALU.mult), inc=VV)
    v2 = p.n(VV)
    a1 = p.op("scalar", lambda e: e.activation(out=R_msq, in_=R_mean, func=AF.Square),
              waits=[(VV, v2)], inc=AA)
    v3 = p.op("vector", lambda e: e.tensor_tensor(out=R_var, in0=R_ey2, in1=R_msq, op=ALU.subtract),
              waits=[(AA, a1)], inc=VV)
    a2 = p.op("scalar", lambda e: e.activation(out=R_std, in_=R_var, func=AF.Sqrt,
                                               bias=cm.cp[0:1, CP_EPS:CP_EPS + 1]),
              waits=[(VV, v3)], inc=AA)
    v4 = p.op("vector", lambda e: e.reciprocal(out=R_rstd, in_=R_std), waits=[(AA, a2)], inc=VV)
    p.op("tensor", lambda e: e.matmul(pbm[:, 0:tc], cm.onf[:, :], R_mean, start=True, stop=True),
         waits=[(VV, v1)], inc=None)
    p.op("tensor", lambda e: e.matmul(pbr[:, 0:tc], cm.onf[:, :], R_rstd, start=True, stop=True),
         waits=[(VV, v4)], inc=TT_)
    t_bc = p.n(TT_)
    for c in range(NF):
        p.op("vector", lambda e, c=c: e.tensor_tensor(out=tmp[:, cs(c, tc)], in0=sf(c),
                                                      in1=pbm[:, 0:tc], op=ALU.subtract),
             waits=[(TT_, t_bc)], inc=None)
        if gcol is None:
            p.op("vector", lambda e, c=c: e.tensor_tensor(out=df(c), in0=tmp[:, cs(c, tc)],
                                                          in1=pbr[:, 0:tc], op=ALU.mult),
                 inc=(VV if c == NF - 1 else None))
        else:
            p.op("vector", lambda e, c=c: e.tensor_tensor(out=tmp[:, cs(c, tc)], in0=tmp[:, cs(c, tc)],
                                                          in1=pbr[:, 0:tc], op=ALU.mult), inc=None)
            p.op("vector", lambda e, c=c: e.tensor_scalar(out=df(c), in0=tmp[:, cs(c, tc)],
                                                          scalar1=gcol[c], scalar2=bcol[c],
                                                          op0=ALU.mult, op1=ALU.add),
                 inc=(VV if c == NF - 1 else None))
    return p.n(VV)


def emit_wcast(p, cm, pieces, stage, slot_w, nslots, eng_cycle=("vector", "gpsimd"),
               extra_load_waits=(), extra_cast_waits=()):
    marks = []
    sems = {"vector": cm.VV, "gpsimd": p.sem("s_g"), "scalar": cm.AA}
    for idx, (dst, srcap, ncols) in enumerate(pieces):
        sl = idx % nslots
        w = list(extra_load_waits)
        if idx >= nslots:
            w.append(marks[idx - nslots])
        r = p.op("sync", lambda e, srcap=srcap, sl=sl, ncols=ncols: e.dma_start(
            out=stage[:, sl * slot_w: sl * slot_w + ncols], in_=srcap),
            waits=w, inc=cm.GW[sl], k=16)
        eng = eng_cycle[idx % len(eng_cycle)]
        mk = p.op(eng, lambda e, dst=dst, sl=sl, ncols=ncols: e.tensor_copy(
            out=dst, in_=stage[:, sl * slot_w: sl * slot_w + ncols]),
            waits=[(cm.GW[sl], r)] + list(extra_cast_waits), inc=sems[eng])
        marks.append((sems[eng], mk))
    return marks



def emit_p0(p, cm, A):
    p.phase_reset()
    s_wq = p.sbuf("p0_wq", [128, 8 * D], BF16)
    s_wv = p.sbuf("p0_wv", [128, 8 * D], BF16)
    s_wg = p.sbuf("p0_wg", [128, 8 * D], BF16)
    s_wo = p.sbuf("p0_wo", [128, 8 * D], BF16)
    xf = p.sbuf("p0_xf", [128, T], F32)
    xb = p.sbuf("p0_xb", [128, T], BF16)
    sq = p.sbuf("p0_sq", [128, T], BF16)
    qf = p.sbuf("p0_qf", [128, T], F32)
    vf = p.sbuf("p0_vf", [128, T], F32)
    stt = p.sbuf("p0_st", [128, T], F32)
    gf = p.sbuf("p0_gf", [128, T], F32)
    ub = p.sbuf("p0_ub", [128, T], BF16)
    lamt = p.sbuf("p0_lamt", [128, T], F32)
    s_lam = p.sbuf("p0_lam8", [128, 8], F32)
    s_sig = p.sbuf("p0_sig8", [128, 8], F32)
    s_g8 = p.sbuf("p0_g8", [128, 8], F32)
    s_b8 = p.sbuf("p0_b8", [128, 8], F32)
    carry = p.sbuf("p0_carry", [128, 8], F32)

    pj = cm.pj
    IN, GIN, TT_, VV, AA, OUT = cm.IN, cm.GIN, cm.TT, cm.VV, cm.AA, cm.OUT
    x0, x1 = A["x0"], A["x1"]

    for dst, src in [(s_lam, A["lamsrc"]), (s_g8, A["g8a"]), (s_b8, A["b8a"])]:
        p.op("sync", lambda e, d=dst, s=src: e.dma_start(out=d[:, :], in_=s), inc=IN, k=16)
    in_init = p.n(IN)
    wstage = p.sbuf("p0_wstage", [128, 4 * 1024], F32)
    pieces = []
    for w_sb, w_hbm in [(s_wq, A["wq"]), (s_wv, A["wv"]), (s_wg, A["wg"]), (s_wo, A["wo"])]:
        for k in range(8):
            pieces.append((w_sb[:, cs(k, D)], w_hbm[k * 128:(k + 1) * 128, :], 1024))
    wmarks = emit_wcast(p, cm, pieces, wstage, 1024, 4)
    wm = list({s: v for (s, v) in wmarks}.items())

    a_sig = p.op("scalar", lambda e: e.activation(out=s_sig[:, :], in_=s_lam[:, :], func=AF.Sigmoid),
                 waits=[(IN, in_init)], inc=AA)
    v0 = p.op("vector", lambda e: e.memset(gf[:, 0:TC], 1.0), inc=VV)
    for c in range(NF):
        p.op("vector", lambda e, c=c: e.tensor_scalar(
            out=lamt[:, cs(c)], in0=gf[:, 0:TC], scalar1=s_sig[:, c:c + 1], scalar2=None, op0=ALU.mult),
            waits=[(VV, v0), (AA, a_sig)], inc=(VV if c == NF - 1 else None))
    p.op("vector", lambda e: e.memset(carry[:, :], 0.0), inc=VV)
    v_init = p.n(VV)

    t_end = a_end = out_end = 0
    v_end = v_init
    gcols = [s_g8[:, c:c + 1] for c in range(NF)]
    bcols = [s_b8[:, c:c + 1] for c in range(NF)]

    for i in range(NT):
        tsl = slice(i * TC, (i + 1) * TC)
        bar = [(TT_, t_end), (VV, v_end), (AA, a_end), (OUT, out_end)]
        for c in range(NF):
            p.op("sync", lambda e, c=c, tsl=tsl: e.dma_start(out=xf[:, cs(c)], in_=x0[c * 128:(c + 1) * 128, tsl]),
                 waits=bar, inc=IN, k=16)
            p.op("gpsimd", lambda e, c=c, tsl=tsl: e.dma_start(out=xb[:, cs(c)], in_=x0[c * 128:(c + 1) * 128, tsl]),
                 waits=bar, inc=GIN, k=16)
        in_x = p.n(IN)
        gin_x = p.n(GIN)

        a_evac0 = p.n(AA)
        t_grp = []
        for g24 in range(24):
            m = g24 % 8
            wsb = (s_wq, s_wv, s_wg)[g24 // 8]
            pjt = pj[4 + g24 % 4]
            w = [(GIN, gin_x)] + wm
            if g24 >= 4:
                w.append((AA, a_evac0 + g24 - 3))
            for k in range(8):
                p.op("tensor", lambda e, pjt=pjt, wsb=wsb, k=k, m=m: e.matmul(
                    pjt[:, :], wsb[:, k * D + m * 128: k * D + (m + 1) * 128], xb[:, cs(k)],
                    start=(k == 0), stop=(k == 7)),
                    waits=w if k == 0 else (), inc=(TT_ if k == 7 else None))
            t_grp.append(p.n(TT_))
        for g24 in range(24):
            m = g24 % 8
            pjt = pj[4 + g24 % 4]
            w = [(TT_, t_grp[g24])]
            if g24 < 8:
                fn = lambda e, pjt=pjt, m=m: e.copy(out=qf[:, cs(m)], in_=pjt[:, :])
            elif g24 < 16:
                fn = lambda e, pjt=pjt, m=m: e.copy(out=vf[:, cs(m)], in_=pjt[:, :])
            else:
                fn = lambda e, pjt=pjt, m=m: e.activation(out=gf[:, cs(m)], in_=pjt[:, :], func=AF.Silu)
                w.append((OUT, out_end))
            p.op("scalar", fn, waits=w, inc=AA)
        a_evacs = p.n(AA)

        v_pre = p.n(VV)
        for c in range(NF):
            p.op("vector", lambda e, c=c: e.tensor_tensor_scan(
                out=stt[:, cs(c)], data0=lamt[:, cs(c)], data1=vf[:, cs(c)],
                initial=carry[:, c:c + 1], op0=ALU.mult, op1=ALU.add),
                waits=[(AA, a_evac0 + 16), (VV, v_init if i == 0 else v_pre)],
                inc=(VV if c == NF - 1 else None))
        v_scan = p.n(VV)
        for c in range(NF):
            p.op("vector", lambda e, c=c: e.tensor_copy(out=carry[:, c:c + 1],
                                                        in_=stt[:, c * TC + TC - 1: c * TC + TC]),
                 waits=[(VV, v_scan)], inc=(VV if c == NF - 1 else None))
        v_carry = p.n(VV)
        for c in range(NF):
            p.op("vector", lambda e, c=c: e.tensor_tensor(
                out=vf[:, cs(c)], in0=qf[:, cs(c)], in1=stt[:, cs(c)], op=ALU.mult),
                waits=[(VV, v_carry), (AA, a_evac0 + 8)], inc=(VV if c == NF - 1 else None))

        v_ln1 = emit_ln(p, cm, i, vf, stt, stt, xb, sq, gcol=gcols, bcol=bcols,
                        guards=[(AA, a_evacs)])

        for c in range(NF):
            p.op("vector", lambda e, c=c: e.tensor_tensor(out=ub[:, cs(c)], in0=stt[:, cs(c)],
                                                          in1=gf[:, cs(c)], op=ALU.mult),
                 waits=[(VV, v_ln1), (AA, a_evacs)], inc=(VV if c == NF - 1 else None))
        v_u = p.n(VV)

        t_wo = []
        for m in range(8):
            w = [(VV, v_u)] + wm
            if m >= 4:
                w.append((VV, v_u + (m - 4) + 1))
            for k in range(8):
                p.op("tensor", lambda e, m=m, k=k: e.matmul(
                    pj[4 + m % 4][:, :], s_wo[:, k * D + m * 128: k * D + (m + 1) * 128], ub[:, cs(k)],
                    start=(k == 0), stop=(k == 7)),
                    waits=w if k == 0 else (), inc=(TT_ if k == 7 else None))
            t_wo.append(p.n(TT_))
        for m in range(8):
            p.op("vector", lambda e, m=m: e.scalar_tensor_tensor(
                out=qf[:, cs(m)], in0=pj[4 + m % 4][:, :], scalar=cm.cp[:, CP_ZERO:CP_ZERO + 1],
                in1=xf[:, cs(m)], op0=ALU.bypass, op1=ALU.add),
                waits=[(TT_, t_wo[m]), (IN, in_x)], inc=VV)

        v_x1 = emit_ln(p, cm, i, qf, gf, stt, xb, sq)

        for c in range(NF):
            p.op("sync", lambda e, c=c, tsl=tsl: e.dma_start(out=x1[c * 128:(c + 1) * 128, tsl], in_=gf[:, cs(c)]),
                 waits=[(VV, v_x1)], inc=OUT, k=16)

        t_end, v_end, a_end, out_end = p.n(TT_), p.n(VV), p.n(AA), p.n(OUT)



def emit_p1(p, cm, A):
    p.phase_reset()
    NE = 32
    s_w1 = p.sbuf("p1_w1", [128, 8 * 4096], BF16)
    s_rwt = p.sbuf("p1_rwt", [128, 8 * 16], F32)
    s_rb = p.sbuf("p1_rb", [16, 1], F32)
    s_b1 = p.sbuf("p1_b1", [128, NE], F32)
    s_b2 = p.sbuf("p1_b2", [128, 8], F32)
    s_sel = p.sbuf("p1_sel", [16, 16 * 128], BF16)
    xf = p.sbuf("p1_xf", [128, T], F32)
    xb = p.sbuf("p1_xb", [128, T], BF16)
    sq = p.sbuf("p1_sq", [128, T], BF16)
    hb = p.sbuf("p1_hb", [128, NE * TC], BF16)
    w2b = p.sbuf("p1_w2b", [128, 2 * D], BF16)
    pw2s = p.sbuf("p1_pw2s", [128, 2 * D], F32)
    zf = p.sbuf("p1_zf", [128, T], F32)
    tmp = p.sbuf("p1_tmp", [128, T], F32)
    rl = p.sbuf("p1_rl", [16, TC], F32)
    rlT = p.sbuf("p1_rlT", [128, 64], F32)
    work = p.sbuf("p1_work", [128, 64], F32)
    mtm = p.sbuf("p1_mtm", [128, 64], F32)
    mcol = p.sbuf("p1_mcol", [128, 1], F32)
    eqt = p.sbuf("p1_eqt", [128, 16], F32)
    mbf = p.sbuf("p1_mbf", [16, TC], BF16)

    pj = cm.pj
    IN, GIN, TT_, VV, AA, OUT = cm.IN, cm.GIN, cm.TT, cm.VV, cm.AA, cm.OUT
    GW = cm.GW
    x1, x2, w2t = A["x1"], A["x2"], A["w2t"]

    for dst, src in [(s_rwt, A["rwt"]), (s_rb, A["rb16"]), (s_b1, A["b1c"]), (s_b2, A["b2c"])]:
        p.op("sync", lambda e, d=dst, s=src: e.dma_start(out=d[:, :], in_=s), inc=IN, k=16)
    p.op("gpsimd", lambda e: e.dma_start(out=s_sel[:, :], in_=A["sel16s"]), inc=GIN, k=16)
    in_init = p.n(IN)
    wstage = p.sbuf("p1_wstage", [128, 2 * 2048], F32)
    pieces = []
    for k in range(8):
        for hf in range(2):
            pieces.append((s_w1[:, k * 4096 + hf * 2048: k * 4096 + (hf + 1) * 2048],
                           A["w1t"][k * 128:(k + 1) * 128, hf * 2048:(hf + 1) * 2048], 2048))
    wmarks = emit_wcast(p, cm, pieces, wstage, 2048, 2)
    wm = list({s: v for (s, v) in wmarks}.items())

    t_end, v_end, a_end, out_end = p.n(TT_), p.n(VV), p.n(AA), p.n(OUT)

    for i in range(NT):
        tsl = slice(i * TC, (i + 1) * TC)
        bar = [(TT_, t_end), (VV, v_end), (AA, a_end), (OUT, out_end)]
        for c in range(NF):
            p.op("sync", lambda e, c=c, tsl=tsl: e.dma_start(out=xf[:, cs(c)], in_=x1[c * 128:(c + 1) * 128, tsl]),
                 waits=bar, inc=IN, k=16)
            p.op("gpsimd", lambda e, c=c, tsl=tsl: e.dma_start(out=xb[:, cs(c)], in_=x1[c * 128:(c + 1) * 128, tsl]),
                 waits=bar, inc=GIN, k=16)
        in_x = p.n(IN)
        gin_x = p.n(GIN)

        for k in range(8):
            p.op("tensor", lambda e, k=k: e.matmul(pj[4][0:16, :], s_rwt[:, k * 16:(k + 1) * 16],
                                                   xf[:, cs(k)], start=(k == 0), stop=(k == 7)),
                 waits=[(IN, in_x)] + bar if k == 0 else (), inc=(TT_ if k == 7 else None))
        t_rl = p.n(TT_)
        v_rl = p.op("vector", lambda e: e.tensor_scalar(out=rl[:, :], in0=pj[4][0:16, :],
                                                        scalar1=s_rb[:, :], scalar2=None, op0=ALU.add),
                    waits=[(TT_, t_rl)] + bar, inc=VV)
        for j in range(4):
            p.op("tensor", lambda e, j=j: e.transpose(pj[5][:, j * 16:(j + 1) * 16],
                                                      rl[0:16, j * 128:(j + 1) * 128], cm.idf[0:16, 0:16]),
                 waits=[(VV, v_rl)] if j == 0 else (), inc=(TT_ if j == 3 else None))
        t_tp = p.n(TT_)
        v_t0 = p.op("vector", lambda e: e.tensor_copy(out=rlT[:, :], in_=pj[5][:, 0:64]),
                    waits=[(TT_, t_tp)], inc=VV)
        for j in range(4):
            blk = rlT[:, j * 16:(j + 1) * 16]
            wblk = work[:, j * 16:(j + 1) * 16]
            last = p.n(VV)
            for r in range(3):
                srcb = blk if r == 0 else wblk
                v_m = p.op("vector", lambda e, srcb=srcb: e.reduce_max(out=mcol[:, :], in_=srcb, axis=AX.X),
                           waits=[(VV, last)], inc=VV)
                v_e = p.op("vector", lambda e, srcb=srcb: e.tensor_scalar(
                    out=eqt[:, :], in0=srcb, scalar1=mcol[:, :], scalar2=None, op0=ALU.is_ge),
                    waits=[(VV, v_m)], inc=VV)
                last = p.op("vector", lambda e, srcb=srcb, wblk=wblk: e.scalar_tensor_tensor(
                    out=wblk, in0=eqt[:, :], scalar=cm.cp[:, CP_NEG:CP_NEG + 1], in1=srcb,
                    op0=ALU.mult, op1=ALU.add),
                    waits=[(VV, v_e)], inc=VV)
            v_m4 = p.op("vector", lambda e, wblk=wblk: e.reduce_max(out=mcol[:, :], in_=wblk, axis=AX.X),
                        waits=[(VV, last)], inc=VV)
            p.op("vector", lambda e, blk=blk, j=j: e.tensor_scalar(
                out=mtm[:, j * 16:(j + 1) * 16], in0=blk, scalar1=mcol[:, :], scalar2=None, op0=ALU.is_ge),
                waits=[(VV, v_m4)], inc=VV)
        v_mask = p.n(VV)
        for j in range(4):
            p.op("tensor", lambda e, j=j: e.transpose(pj[4][0:16, j * 128:(j + 1) * 128],
                                                      mtm[:, j * 16:(j + 1) * 16], cm.idf[:, :]),
                 waits=[(VV, v_mask)] if j == 0 else (), inc=(TT_ if j == 3 else None))
        t_mt = p.n(TT_)
        v_mbf = p.op("vector", lambda e: e.tensor_copy(out=mbf[:, :], in_=pj[4][0:16, 0:TC]),
                     waits=[(TT_, t_mt)], inc=VV)
        if i == 0 and "dbg_rl" in A:
            p.op("sync", lambda e: e.dma_start(out=A["dbg_rl"], in_=rl[:, :]),
                 waits=[(VV, v_mbf)], inc=OUT, k=16)
            p.op("sync", lambda e: e.dma_start(out=A["dbg_rlT"], in_=rlT[:, :]),
                 waits=[(VV, v_mbf)], inc=OUT, k=16)
            p.op("sync", lambda e: e.dma_start(out=A["dbg_mtm"], in_=mtm[:, :]),
                 waits=[(VV, v_mbf)], inc=OUT, k=16)
            p.op("sync", lambda e: e.dma_start(out=A["dbg_mbf"], in_=mbf[:, :]),
                 waits=[(VV, v_mbf)], inc=OUT, k=16)
            out_end = p.n(OUT)

        a_h0 = p.n(AA)
        v_h0 = p.n(VV)
        t_hgrp = []
        t_bc = []
        for c in range(NE):
            pjt = pj[c % 4]
            w = [(GIN, gin_x)] + wm
            if c >= 4:
                w.append((AA, a_h0 + (c - 4) + 1))
            for k in range(8):
                p.op("tensor", lambda e, pjt=pjt, c=c, k=k: e.matmul(
                    pjt[:, :], s_w1[:, k * 4096 + c * 128: k * 4096 + (c + 1) * 128], xb[:, cs(k)],
                    start=(k == 0), stop=(k == 7)),
                    waits=w if k == 0 else (), inc=None)
            wb = [(VV, v_mbf)]
            if c >= 2:
                wb.append((VV, v_mbf + (c - 2) + 1))
            p.op("tensor", lambda e, c=c: e.matmul(
                pj[4 + c % 2][:, :], s_sel[:, (c // 2) * 128:(c // 2 + 1) * 128], mbf[:, :],
                start=True, stop=True),
                waits=wb, inc=TT_)
            t_bc.append(p.n(TT_))
        for c in range(NE):
            p.op("scalar", lambda e, c=c: e.activation(out=hb[:, cs(c)], in_=pj[c % 4][:, :],
                                                       func=AF.Relu, bias=s_b1[:, c:c + 1]),
                 waits=[(TT_, t_bc[c])], inc=AA)
            p.op("vector", lambda e, c=c: e.tensor_tensor(out=hb[:, cs(c)], in0=hb[:, cs(c)],
                                                          in1=pj[4 + c % 2][:, :], op=ALU.mult),
                 waits=[(AA, a_h0 + c + 1)], inc=VV)
        v_hdone = p.n(VV)
        a_hdone = p.n(AA)
        if i == 0 and "dbg_hb" in A:
            p.op("sync", lambda e: e.dma_start(out=A["dbg_hb"], in_=hb[:, 0:4 * TC]),
                 waits=[(VV, v_hdone)], inc=OUT, k=16)
            out_end = p.n(OUT)

        t_y0 = p.n(TT_)
        wcm2 = []
        for c in range(NE):
            par = c % 2
            gw_r = p.op("sync", lambda e, c=c, par=par: e.dma_start(
                out=pw2s[:, par * D:(par + 1) * D], in_=w2t[c * 128:(c + 1) * 128, :]),
                waits=[(wcm2[c - 2])] if (i > 0 or c >= 2) and c >= 2 else
                      ([(TT_, t_y0)] if i > 0 else []),
                inc=GW[par], k=16)
            ceng = "vector" if c % 2 == 0 else "gpsimd"
            cmk = p.op(ceng, lambda e, c=c, par=par: e.tensor_copy(
                out=w2b[:, par * D:(par + 1) * D], in_=pw2s[:, par * D:(par + 1) * D]),
                waits=[(GW[par], gw_r)] + ([(TT_, t_y0 if c < 2 else t_hgrp[c - 2])] if (i > 0 or c >= 2) else []),
                inc=(VV if ceng == "vector" else p.sem("s_g")))
            wcm2.append((VV if ceng == "vector" else "s_g", cmk))
            w = [wcm2[c], (VV, v_hdone)]
            for m in range(8):
                p.op("tensor", lambda e, c=c, par=par, m=m: e.matmul(
                    pj[m][:, :], w2b[:, par * D + m * 128: par * D + (m + 1) * 128], hb[:, cs(c)],
                    start=(c == 0), stop=(c == NE - 1)),
                    waits=w if m == 0 else (), inc=(TT_ if m == 7 else None))
            t_hgrp.append(p.n(TT_))
        t_y = p.n(TT_)
        for m in range(8):
            p.op("vector", lambda e, m=m: e.scalar_tensor_tensor(
                out=zf[:, cs(m)], in0=pj[m][:, :], scalar=s_b2[:, m:m + 1], in1=xf[:, cs(m)],
                op0=ALU.add, op1=ALU.add),
                waits=[(TT_, t_y)], inc=(VV if m == 7 else None))

        v_x2 = emit_ln(p, cm, i, zf, tmp, tmp, xb, sq)
        for c in range(NF):
            p.op("sync", lambda e, c=c, tsl=tsl: e.dma_start(out=x2[c * 128:(c + 1) * 128, tsl], in_=tmp[:, cs(c)]),
                 waits=[(VV, v_x2)], inc=OUT, k=16)

        t_end, v_end, a_end, out_end = p.n(TT_), p.n(VV), p.n(AA), p.n(OUT)



def emit_p2(p, cm, A):
    p.phase_reset()
    tc = 256
    nt = T // tc
    s_wqkv = p.sbuf("p2_wqkv", [128, 8 * 3072], BF16)
    s_wout = p.sbuf("p2_wout", [128, 16 * D], BF16)
    s_e2h = p.sbuf("p2_e2h", [128, 8 * 16], BF16)
    s_h2e = p.sbuf("p2_h2e", [16, 8 * 128], BF16)
    s_ab = p.sbuf("p2_ab", [128, 16], F32)
    s_sig = p.sbuf("p2_sig", [128, 16], F32)
    lam_t = p.sbuf("p2_lamt", [128, 16 * tc], F32)
    xf = p.sbuf("p2_xf", [128, 8 * tc], F32)
    xb = p.sbuf("p2_xb", [128, 8 * tc], BF16)
    sq = p.sbuf("p2_sq", [128, 8 * tc], BF16)
    phiq = p.sbuf("p2_phiq", [128, 8 * tc], F32)
    phik = p.sbuf("p2_phik", [128, 8 * tc], F32)
    vf = p.sbuf("p2_vf", [128, 8 * tc], F32)
    kvf = p.sbuf("p2_kvf", [128, 8 * tc], F32)
    ta = p.sbuf("p2_ta", [128, 8 * tc], F32)
    tb = p.sbuf("p2_tb", [128, 8 * tc], F32)
    occ = p.sbuf("p2_occ", [128, 16 * tc], BF16)
    den = p.sbuf("p2_den", [16, 2 * tc], F32)
    denb = p.sbuf("p2_denb", [16, 2 * tc], BF16)
    carry = p.sbuf("p2_carry", [128, 32], F32)

    pj = cm.pj
    IN, GIN, TT_, VV, AA, OUT = cm.IN, cm.GIN, cm.TT, cm.VV, cm.AA, cm.OUT
    x2, x3 = A["x2"], A["x3"]

    for dst, src in [(s_ab, A["ab_rep"])]:
        p.op("sync", lambda e, d=dst, s=src: e.dma_start(out=d[:, :], in_=s), inc=IN, k=16)
    in_init = p.n(IN)
    p.op("gpsimd", lambda e: e.dma_start(out=s_e2h[:, :], in_=A["e2h"]), inc=GIN, k=16)
    p.op("gpsimd", lambda e: e.dma_start(out=s_h2e[:, :], in_=A["h2e"]), inc=GIN, k=16)
    wstage = p.sbuf("p2_wstage", [128, 2 * 2048], F32)
    pieces = []
    for k in range(16):
        pieces.append((s_wqkv[:, k * 1536:(k + 1) * 1536], A["wqkv"][:, k * 1536:(k + 1) * 1536], 1536))
    for k in range(8):
        pieces.append((s_wout[:, k * 2048:(k + 1) * 2048], A["wout"][:, k * 2048:(k + 1) * 2048], 2048))
    wmarks = emit_wcast(p, cm, pieces, wstage, 2048, 2)
    wm = list({s: v for (s, v) in wmarks}.items())

    a_sig = p.op("scalar", lambda e: e.activation(out=s_sig[:, :], in_=s_ab[:, :], func=AF.Sigmoid),
                 waits=[(IN, in_init)], inc=AA)
    v_cl = p.op("vector", lambda e: e.tensor_scalar(out=s_sig[:, :], in0=s_sig[:, :],
                                                    scalar1=cm.cp[:, CP_EPS8:CP_EPS8 + 1], scalar2=None,
                                                    op0=ALU.max),
                waits=[(AA, a_sig)], inc=VV)
    v0 = p.op("vector", lambda e: e.memset(xf[:, 0:tc], 1.0), waits=[(VV, v_cl)], inc=VV)
    for c in range(16):
        p.op("vector", lambda e, c=c: e.tensor_scalar(
            out=lam_t[:, cs(c, tc)], in0=xf[:, 0:tc], scalar1=s_sig[:, c:c + 1], scalar2=None, op0=ALU.mult),
            waits=[(VV, v0)], inc=(VV if c == 15 else None))
    p.op("vector", lambda e: e.memset(carry[:, :], 0.0), inc=VV)
    v_init = p.n(VV)

    t_end, a_end, out_end = p.n(TT_), p.n(AA), p.n(OUT)
    v_end = max(p.n(VV), v_init)

    for i in range(nt):
        tsl = slice(i * tc, (i + 1) * tc)
        bar = [(TT_, t_end), (VV, v_end), (AA, a_end), (OUT, out_end)]
        for c in range(NF):
            p.op("sync", lambda e, c=c, tsl=tsl: e.dma_start(out=xf[:, cs(c, tc)],
                                                             in_=x2[c * 128:(c + 1) * 128, tsl]),
                 waits=bar, inc=IN, k=16)
            p.op("gpsimd", lambda e, c=c, tsl=tsl: e.dma_start(out=xb[:, cs(c, tc)],
                                                               in_=x2[c * 128:(c + 1) * 128, tsl]),
                 waits=bar, inc=GIN, k=16)
        in_x = p.n(IN)
        gin_x = p.n(GIN)

        v_ev0 = p.n(VV)
        free_mark = []
        t_grp = []
        for g in range(24):
            pjt = pj[4 + g % 4]
            w = [(GIN, gin_x)] + wm + (bar if g == 0 else [])
            if g >= 4:
                w.append(free_mark[g - 4])
            for k in range(8):
                p.op("tensor", lambda e, pjt=pjt, g=g, k=k: e.matmul(
                    pjt[:, 0:tc], s_wqkv[:, k * 3072 + g * 128: k * 3072 + (g + 1) * 128],
                    xb[:, cs(k, tc)], start=(k == 0), stop=(k == 7)),
                    waits=w if k == 0 else (), inc=(TT_ if k == 7 else None))
            t_grp.append(p.n(TT_))
            m = g % 8
            if g < 16:
                dst = phiq if g < 8 else phik
                v1 = p.op("vector", lambda e, pjt=pjt, dst=dst, m=m: e.tensor_scalar(
                    out=dst[:, cs(m, tc)], in0=pjt[:, 0:tc], scalar1=cm.cp[:, CP_ZERO:CP_ZERO + 1],
                    scalar2=None, op0=ALU.min),
                    waits=[(TT_, t_grp[-1])], inc=VV)
                a1 = p.op("scalar", lambda e, dst=dst, m=m: e.activation(
                    out=dst[:, cs(m, tc)], in_=dst[:, cs(m, tc)], func=AF.Exp),
                    waits=[(VV, v1)], inc=AA)
                v2 = p.op("vector", lambda e, pjt=pjt, dst=dst, m=m: e.scalar_tensor_tensor(
                    out=dst[:, cs(m, tc)], in0=pjt[:, 0:tc], scalar=cm.cp[:, CP_ZERO:CP_ZERO + 1],
                    in1=dst[:, cs(m, tc)], op0=ALU.max, op1=ALU.add),
                    waits=[(AA, a1)], inc=VV)
                free_mark.append((VV, v2))
            else:
                a1 = p.op("scalar", lambda e, pjt=pjt, m=m: e.copy(out=vf[:, cs(m, tc)], in_=pjt[:, 0:tc]),
                          waits=[(TT_, t_grp[-1]), (VV, v_end)], inc=AA)
                free_mark.append((AA, a1))
        a_v = p.n(AA)

        v_kv0 = p.n(VV)
        for c in range(NF):
            p.op("vector", lambda e, c=c: e.tensor_tensor(out=kvf[:, cs(c, tc)], in0=phik[:, cs(c, tc)],
                                                          in1=vf[:, cs(c, tc)], op=ALU.mult),
                 waits=[(AA, a_v), (VV, v_kv0)], inc=(VV if c == NF - 1 else None))
        v_kv = p.n(VV)

        scans = [
            (0, kvf, vf),
            (1, kvf, ta),
            (2, phik, tb),
            (3, phik, phik),
        ]
        v_before = p.n(VV)
        for c in range(NF):
            p.op("vector", lambda e, c=c: e.tensor_tensor_scan(
                out=vf[:, cs(c, tc)], data0=lam_t[:, cs(c, tc)], data1=kvf[:, cs(c, tc)],
                initial=carry[:, c:c + 1], op0=ALU.mult, op1=ALU.add),
                waits=[(VV, v_init if i == 0 else v_before), (VV, v_kv)],
                inc=(VV if c == NF - 1 else None))
        v_s1 = p.n(VV)
        for c in range(NF):
            p.op("vector", lambda e, c=c: e.tensor_copy(out=carry[:, c:c + 1],
                                                        in_=vf[:, c * tc + tc - 1: c * tc + tc]),
                 waits=[(VV, v_s1)], inc=(VV if c == NF - 1 else None))
        for c in range(NF):
            p.op("vector", lambda e, c=c: e.tensor_tensor_scan(
                out=ta[:, cs(c, tc)], data0=lam_t[:, cs(8 + c, tc)], data1=kvf[:, cs(c, tc)],
                initial=carry[:, 8 + c:9 + c], op0=ALU.mult, op1=ALU.add),
                inc=(VV if c == NF - 1 else None))
        v_s2 = p.n(VV)
        for c in range(NF):
            p.op("vector", lambda e, c=c: e.tensor_copy(out=carry[:, 8 + c:9 + c],
                                                        in_=ta[:, c * tc + tc - 1: c * tc + tc]),
                 waits=[(VV, v_s2)], inc=(VV if c == NF - 1 else None))
        for c in range(NF):
            p.op("vector", lambda e, c=c: e.tensor_tensor_scan(
                out=tb[:, cs(c, tc)], data0=lam_t[:, cs(c, tc)], data1=phik[:, cs(c, tc)],
                initial=carry[:, 16 + c:17 + c], op0=ALU.mult, op1=ALU.add),
                inc=(VV if c == NF - 1 else None))
        v_s3 = p.n(VV)
        for c in range(NF):
            p.op("vector", lambda e, c=c: e.tensor_copy(out=carry[:, 16 + c:17 + c],
                                                        in_=tb[:, c * tc + tc - 1: c * tc + tc]),
                 waits=[(VV, v_s3)], inc=(VV if c == NF - 1 else None))
        for c in range(NF):
            p.op("vector", lambda e, c=c: e.tensor_tensor_scan(
                out=kvf[:, cs(c, tc)], data0=lam_t[:, cs(8 + c, tc)], data1=phik[:, cs(c, tc)],
                initial=carry[:, 24 + c:25 + c], op0=ALU.mult, op1=ALU.add),
                inc=(VV if c == NF - 1 else None))
        v_s4 = p.n(VV)
        for c in range(NF):
            p.op("vector", lambda e, c=c: e.tensor_copy(out=carry[:, 24 + c:25 + c],
                                                        in_=kvf[:, c * tc + tc - 1: c * tc + tc]),
                 waits=[(VV, v_s4)], inc=(VV if c == NF - 1 else None))
        v_scans = p.n(VV)

        for ch, (rks, pd) in enumerate([(tb, pj[4]), (kvf, pj[5])]):
            for c in range(NF):
                p.op("vector", lambda e, c=c, rks=rks: e.tensor_tensor(
                    out=xb[:, cs(c, tc)], in0=phiq[:, cs(c, tc)], in1=rks[:, cs(c, tc)], op=ALU.mult),
                    waits=[(VV, v_scans), (TT_, t_grp[-1])], inc=(VV if c == NF - 1 else None))
            v_pr = p.n(VV)
            for c in range(NF):
                p.op("tensor", lambda e, c=c, pd=pd: e.matmul(
                    pd[0:16, 0:tc], s_e2h[:, c * 16:(c + 1) * 16], xb[:, cs(c, tc)],
                    start=(c == 0), stop=(c == 7)),
                    waits=[(VV, v_pr)] if c == 0 else (), inc=(TT_ if c == 7 else None))
        t_den = p.n(TT_)
        v_d1 = p.op("vector", lambda e: e.tensor_scalar(out=den[:, 0:tc], in0=pj[4][0:16, 0:tc],
                                                        scalar1=cm.cp[0:16, CP_EPS6:CP_EPS6 + 1],
                                                        scalar2=None, op0=ALU.max),
                    waits=[(TT_, t_den)], inc=VV)
        p.op("vector", lambda e: e.tensor_scalar(out=den[:, tc:2 * tc], in0=pj[5][0:16, 0:tc],
                                                 scalar1=cm.cp[0:16, CP_EPS6:CP_EPS6 + 1],
                                                 scalar2=None, op0=ALU.max), inc=VV)
        v_d2 = p.n(VV)
        v_d3 = p.op("vector", lambda e: e.reciprocal(out=den[:, 0:tc], in_=den[:, 0:tc]),
                    waits=[(VV, v_d2)], inc=VV)
        p.op("vector", lambda e: e.reciprocal(out=den[:, tc:2 * tc], in_=den[:, tc:2 * tc]), inc=VV)
        v_d4 = p.op("vector", lambda e: e.tensor_copy(out=denb[:, :], in_=den[:, :]), inc=VV)

        v_occ0 = p.n(VV)
        for ch in range(2):
            rkv = vf if ch == 0 else ta
            for c in range(NF):
                pb_ = pj[6 + (c % 2)]
                wb_ = [(VV, v_d4)]
                if ch * 8 + c >= 2:
                    wb_.append((VV, v_occ0 + 2 * (ch * 8 + c) - 2))
                p.op("tensor", lambda e, c=c, ch=ch, pb_=pb_: e.matmul(
                    pb_[:, 0:tc], s_h2e[:, c * 128:(c + 1) * 128],
                    denb[:, ch * tc:(ch + 1) * tc], start=True, stop=True),
                    waits=wb_, inc=TT_)
                t_b = p.n(TT_)
                p.op("vector", lambda e, c=c, ch=ch, rkv=rkv: e.tensor_tensor(
                    out=occ[:, cs(ch * 8 + c, tc)], in0=phiq[:, cs(c, tc)], in1=rkv[:, cs(c, tc)],
                    op=ALU.mult),
                    waits=[(VV, v_d4)], inc=VV)
                p.op("vector", lambda e, c=c, ch=ch, pb_=pb_: e.tensor_tensor(
                    out=occ[:, cs(ch * 8 + c, tc)], in0=occ[:, cs(ch * 8 + c, tc)], in1=pb_[:, 0:tc],
                    op=ALU.mult),
                    waits=[(TT_, t_b)], inc=VV)
        v_occ = p.n(VV)

        t_wo = []
        for m in range(8):
            w = [(VV, v_occ)]
            if m >= 4:
                w.append((VV, v_occ + (m - 4) + 1))
            for kk in range(16):
                p.op("tensor", lambda e, m=m, kk=kk: e.matmul(
                    pj[m % 4][:, 0:tc], s_wout[:, kk * D + m * 128: kk * D + (m + 1) * 128],
                    occ[:, cs(kk, tc)], start=(kk == 0), stop=(kk == 15)),
                    waits=w if kk == 0 else (), inc=(TT_ if kk == 15 else None))
            t_wo.append(p.n(TT_))
        for m in range(8):
            p.op("vector", lambda e, m=m: e.scalar_tensor_tensor(
                out=vf[:, cs(m, tc)], in0=pj[m % 4][:, 0:tc], scalar=cm.cp[:, CP_ZERO:CP_ZERO + 1],
                in1=xf[:, cs(m, tc)], op0=ALU.bypass, op1=ALU.add),
                waits=[(TT_, t_wo[m]), (IN, in_x)], inc=VV)

        v_x3 = emit_ln(p, cm, i, vf, kvf, kvf, xb, sq, tc=tc)
        for c in range(NF):
            p.op("sync", lambda e, c=c, tsl=tsl: e.dma_start(out=x3[c * 128:(c + 1) * 128, tsl],
                                                             in_=kvf[:, cs(c, tc)]),
                 waits=[(VV, v_x3)], inc=OUT, k=16)

        t_end, v_end, a_end, out_end = p.n(TT_), p.n(VV), p.n(AA), p.n(OUT)



W1_WC = [128] * 21 + [42]
W1_OFF = [0]
for _wc in W1_WC:
    W1_OFF.append(W1_OFF[-1] + 16 * _wc)
W1_COLS = W1_OFF[-1]
GB_OFF = [0]
for _wc in W1_WC:
    GB_OFF.append(GB_OFF[-1] + TC)
GB_COLS = GB_OFF[-1]


def emit_p3f(p, cm, A, li, xin, xout, final=False, x3s=None):
    p.phase_reset()
    NEC = 22
    s_w2r = p.sbuf(f"f{li}_w2r", [128, NEC * D], BF16)
    w1b = p.sbuf(f"f{li}_w1b", [128, 2 * 2048], BF16)
    w1f = p.sbuf(f"f{li}_w1f", [128, 2 * 2048], F32)
    xsf = p.sbuf(f"f{li}_xsf", [128, 8 * 1025], F32)
    xnb_off = p._off
    xnb = p.sbuf(f"f{li}_xnb", [128, 8 * 1024], BF16)
    xb = p.sbuf(f"f{li}_xb", [128, T], BF16)
    sq = p.sbuf(f"f{li}_sq", [128, T], BF16)
    gb = p.sbuf(f"f{li}_gb", [128, 2 * GB_COLS], BF16)
    asb = p.sbuf(f"f{li}_asb", [128, TC], F32)
    gsb = p.sbuf(f"f{li}_gsb", [128, TC], F32)
    tmpf = p.sbuf(f"f{li}_tmp", [128, T], F32)
    off_xnb = None
    xtr = p.nc.alloc_sbuf_tensor_at(f"f{li}_xtr", [128, T], F32, offset=xnb_off)
    lngc = p.sbuf(f"f{li}_lng", [128, 8], F32)
    lnbc = p.sbuf(f"f{li}_lnb", [128, 8], F32)
    b1a = p.sbuf(f"f{li}_b1a", [128, NEC], F32)
    b1g = p.sbuf(f"f{li}_b1g", [128, NEC], F32)
    b2c = p.sbuf(f"f{li}_b2", [128, 8], F32)

    pj = cm.pj
    IN, GIN, TT_, VV, AA, OUT = cm.IN, cm.GIN, cm.TT, cm.VV, cm.AA, cm.OUT
    GW = cm.GW

    for dst, key in [(lngc, "lng"), (lnbc, "lnb"), (b1a, "b1a"), (b1g, "b1g"), (b2c, "b2c3")]:
        p.op("sync", lambda e, d=dst, s=A[key], li=li: e.dma_start(out=d[:, :], in_=s[li]), inc=IN, k=16)
    in_init = p.n(IN)
    for k in range(11):
        p.op("gpsimd", lambda e, k=k, li=li: e.dma_start(
            out=s_w2r[:, k * 2048:(k + 1) * 2048], in_=A["w2s"][li, :, k * 2048:(k + 1) * 2048]),
            inc=GIN, k=16)

    t_end, v_end, a_end, out_end = p.n(TT_), p.n(VV), p.n(AA), p.n(OUT)
    gcols = [lngc[:, c:c + 1] for c in range(NF)]
    bcols = [lnbc[:, c:c + 1] for c in range(NF)]

    GGs = p.sem("s_g")
    g_end = p.n(GGs)
    SL = cm.SL
    v_zs1_prev = 0
    t_w1_prev = t_end
    for i2 in range(4):
        t0 = i2 * 1024
        bar = [(TT_, t_end), (VV, v_end), (AA, a_end), (OUT, out_end), (GGs, g_end)]
        ldw = [(VV, v_zs1_prev)] if i2 > 0 else bar
        if i2 == 0:
            vz = p.op("vector", lambda e: e.memset(xsf[:, 0:8 * 1025:1025], 0.0), inc=VV)
            for c in range(NF):
                p.op("scalar", lambda e, c=c: e.dma_start(
                    out=xsf[:, c * 1025 + 1: (c + 1) * 1025],
                    in_=xin[c * 128:(c + 1) * 128, 0:1024]),
                    waits=ldw + [(VV, vz)], inc=SL, k=16)
        else:
            for c in range(NF):
                p.op("scalar", lambda e, c=c, t0=t0: e.dma_start(
                    out=xsf[:, c * 1025: (c + 1) * 1025],
                    in_=xin[c * 128:(c + 1) * 128, t0 - 1: t0 + 1024]),
                    waits=ldw, inc=SL, k=16)
        in_x = p.n(SL)

        def sh_ap(c, s):
            off = c * 1025 + (1 if c < 4 else 0) + s * TC
            return xsf[:, off: off + TC]

        def un_ap(c, s):
            off = c * 1025 + 1 + s * TC
            return xsf[:, off: off + TC]

        v_lns = []
        for s in range(2):
            p.op("vector", None, waits=[(SL, in_x)], inc=VV)
            v_ln = emit_ln(p, cm, i2, (lambda c, s=s: sh_ap(c, s)),
                           (lambda c, s=s: xnb[:, c * 1024 + s * TC: c * 1024 + (s + 1) * TC]),
                           tmpf, xb, sq, gcol=gcols, bcol=bcols,
                           guards=[(SL, in_x), (IN, in_init)], tc=TC)
            v_lns.append(v_ln)
        v_ln_all = p.n(VV)

        t_wgrp = [0] * NEC
        g_gb0 = p.n(p.sem("s_g"))
        a_w0 = p.n(AA)
        w1cm = []
        for c in range(NEC):
            wc = W1_WC[c]
            par = c % 2
            gw_r = p.op("sync", lambda e, c=c, par=par, li=li: e.dma_start(
                out=w1f[:, par * 2048: par * 2048 + 16 * W1_WC[c]],
                in_=A["w1s"][li, :, W1_OFF[c]:W1_OFF[c + 1]]),
                waits=[w1cm[c - 2]] if c >= 2 else [], inc=GW[par], k=16)
            ceng = "vector" if c % 2 == 0 else "gpsimd"
            cm_r = p.op(ceng, lambda e, c=c, par=par: e.tensor_copy(
                out=w1b[:, par * 2048: par * 2048 + 16 * W1_WC[c]],
                in_=w1f[:, par * 2048: par * 2048 + 16 * W1_WC[c]]),
                waits=[(GW[par], gw_r), (TT_, t_wgrp[c - 2] if c >= 2 else t_w1_prev)],
                inc=(VV if ceng == "vector" else p.sem("s_g")))
            w1cm.append((VV if ceng == "vector" else "s_g", cm_r))
            for s in range(2):
                idx = 2 * c + s
                pa = pj[(idx % 2) * 2]
                pg = pj[(idx % 2) * 2 + 1]
                w = [w1cm[c], (VV, v_ln_all)]
                if idx >= 2:
                    w.append(("s_g", g_gb0 + (idx - 2) + 1))
                for k in range(8):
                    p.op("tensor", lambda e, c=c, s=s, k=k, par=par, pa=pa, wc=wc: e.matmul(
                        pa[0:wc, 0:TC], w1b[:, par * 2048 + k * 2 * wc: par * 2048 + k * 2 * wc + wc],
                        xnb[:, k * 1024 + s * TC: k * 1024 + (s + 1) * TC],
                        start=(k == 0), stop=(k == 7)),
                        waits=w if k == 0 else (), inc=None)
                for k in range(8):
                    p.op("tensor", lambda e, c=c, s=s, k=k, par=par, pg=pg, wc=wc: e.matmul(
                        pg[0:wc, 0:TC], w1b[:, par * 2048 + k * 2 * wc + wc: par * 2048 + (k + 1) * 2 * wc],
                        xnb[:, k * 1024 + s * TC: k * 1024 + (s + 1) * TC],
                        start=(k == 0), stop=(k == 7)),
                        inc=(TT_ if k == 7 else None))
                t_mm = p.n(TT_)
                a1 = p.op("scalar", lambda e, pa=pa, wc=wc, c=c: e.activation(
                    out=asb[0:wc, :], in_=pa[0:wc, 0:TC], func=AF.Identity, bias=b1a[0:wc, c:c + 1]),
                    waits=[(TT_, t_mm), (IN, in_init)], inc=AA)
                a2 = p.op("scalar", lambda e, pg=pg, wc=wc, c=c: e.activation(
                    out=gsb[0:wc, :], in_=pg[0:wc, 0:TC], func=AF.Gelu, bias=b1g[0:wc, c:c + 1]),
                    inc=AA)
                p.op("gpsimd", lambda e, c=c, s=s, wc=wc: e.tensor_tensor(
                    out=gb[0:wc, s * GB_COLS + GB_OFF[c]: s * GB_COLS + GB_OFF[c] + TC],
                    in0=asb[0:wc, :], in1=gsb[0:wc, :], op=ALU.mult),
                    waits=[(AA, a2)], inc=p.sem("s_g"))
            t_wgrp[c] = p.n(TT_)
        v_gb = ("s_g", p.n(p.sem("s_g")))

        gin_w2 = 11 * 16
        for s in range(2):
            for c in range(NEC):
                wc = W1_WC[c]
                w = [v_gb, (GIN, p.n(GIN)), (VV, p.n(VV))]
                for m in range(8):
                    p.op("tensor", lambda e, c=c, s=s, m=m, wc=wc: e.matmul(
                        pj[m][:, 0:TC], s_w2r[0:wc, c * D + m * 128: c * D + (m + 1) * 128],
                        gb[0:wc, s * GB_COLS + GB_OFF[c]: s * GB_COLS + GB_OFF[c] + TC],
                        start=(c == 0), stop=(c == NEC - 1)),
                        waits=w if (m == 0 and c == 0) else (), inc=(TT_ if (m == 7 and c == NEC - 1) else None))
            t_y = p.n(TT_)
            for m in range(8):
                p.op("vector", lambda e, m=m, s=s: e.scalar_tensor_tensor(
                    out=tmpf[:, cs(m)], in0=pj[m][:, 0:TC], scalar=b2c[:, m:m + 1],
                    in1=un_ap(m, s), op0=ALU.add, op1=ALU.add),
                    waits=[(TT_, t_y), (SL, in_x)], inc=(VV if m == 7 else None))
            v_z = p.n(VV)
            if s == 1:
                v_zs1_prev = v_z
            if not final:
                for c in range(NF):
                    p.op("sync", lambda e, c=c, s=s, t0=t0: e.dma_start(
                        out=xout[c * 128:(c + 1) * 128, t0 + s * TC: t0 + (s + 1) * TC],
                        in_=tmpf[:, cs(c)]),
                        waits=[(VV, v_z)], inc=OUT, k=16)
            else:
                for c in range(NF):
                    p.op("sync", lambda e, c=c, s=s, t0=t0: e.dma_start(
                        out=xtr[:, cs(c)], in_=x3s[c * 128:(c + 1) * 128, t0 + s * TC: t0 + (s + 1) * TC]),
                        waits=[(VV, v_end), v_gb], inc=IN, k=16)
                in_x3 = p.n(IN)
                for c in range(NF):
                    p.op("vector", lambda e, c=c: e.tensor_tensor(
                        out=xtr[:, cs(c)], in0=tmpf[:, cs(c)], in1=xtr[:, cs(c)], op=ALU.add),
                        waits=[(VV, v_z), (IN, in_x3)], inc=(VV if c == NF - 1 else None))
                v_x4 = emit_ln(p, cm, i2, xtr, tmpf, tmpf, xb, sq, tc=TC)
                for c in range(NF):
                    p.op("sync", lambda e, c=c, s=s, t0=t0: e.dma_start(
                        out=xout[c * 128:(c + 1) * 128, t0 + s * TC: t0 + (s + 1) * TC],
                        in_=tmpf[:, cs(c)]),
                        waits=[(VV, v_x4)], inc=OUT, k=16)

        t_end, v_end, a_end, out_end = p.n(TT_), p.n(VV), p.n(AA), p.n(OUT)
        g_end = p.n(GGs)
        t_w1_prev = t_wgrp[NEC - 1]



def emit_p3a(p, cm, A, li, xin, xout):
    p.phase_reset()
    tc = TC
    s_wqkv = p.sbuf(f"a{li}_wqkv", [128, 8 * 3072], BF16)
    s_wo = p.sbuf(f"a{li}_wo", [128, 8 * D], BF16)
    s_h2e = p.sbuf(f"a{li}_h2e", [16, 8 * 128], BF16)
    s_e2h = p.sbuf(f"a{li}_e2h", [128, 8 * 16], BF16)
    s_prot = p.sbuf(f"a{li}_prot", [128, 128], BF16)
    s_trim = p.sbuf(f"a{li}_trim", [128, 256], BF16)
    qcol = p.sbuf(f"a{li}_qcol", [128, 4], F32)
    xsb = p.sbuf(f"a{li}_xsb", [128, 8 * 513], BF16)
    xf = p.sbuf(f"a{li}_xf", [128, T], F32)
    tabs = p.sbuf(f"a{li}_tabs", [128, 4 * tc], F32)
    qf_off = p._off
    qf = p.sbuf(f"a{li}_qf", [128, T], F32)
    kb_off = p._off
    kb = p.sbuf(f"a{li}_kb", [128, T], BF16)
    fb_off = p._off
    fb = p.sbuf(f"a{li}_fb", [128, T], BF16)
    rsc = p.sbuf(f"a{li}_rsc", [128, 2 * tc], F32)
    rsc2 = p.sbuf(f"a{li}_rsc2", [128, 2 * tc], F32)
    sqq = p.sbuf(f"a{li}_sqq", [128, 2 * tc], BF16)
    rnb = p.sbuf(f"a{li}_rnb", [16, 2 * tc], BF16)
    rnr = p.sbuf(f"a{li}_rnr", [16, 2 * tc], F32)
    q_aug = p.sbuf(f"a{li}_qaug", [65, 16 * tc], BF16)
    k_aug = p.sbuf(f"a{li}_kaug", [65, 16 * 768], BF16)
    vtm = p.sbuf(f"a{li}_vtm", [128, 6 * 1040], BF16)
    at = p.nc.alloc_sbuf_tensor_at
    ab = at(f"a{li}_ab", [128, T], BF16, offset=kb_off)
    zt = at(f"a{li}_zt", [128, T], F32, offset=fb_off)
    expb = at(f"a{li}_expb", [128, 2 * 1024], BF16, offset=qf_off)
    mstack = at(f"a{li}_mst", [128, 32], F32, offset=qf_off + 4096)
    mT = at(f"a{li}_mT", [32, 128], BF16, offset=qf_off + 4352)
    denw = at(f"a{li}_denw", [1, 512], F32, offset=qf_off + 4608)
    reccols = at(f"a{li}_recc", [128, 32], F32, offset=qf_off + 6656)
    recT = at(f"a{li}_recT", [32, 128], BF16, offset=qf_off + 6784)
    recrow = at(f"a{li}_recrow", [1, 4096], BF16, offset=fb_off)

    pj = cm.pj
    IN, GIN, TT_, VV, AA, OUT = cm.IN, cm.GIN, cm.TT, cm.VV, cm.AA, cm.OUT
    GG = p.sem("s_g")

    p.op("sync", lambda e, li=li: e.dma_start(out=qcol[:, :], in_=A["qcols"][:, 4 * li:4 * li + 4]),
         inc=IN, k=16)
    in_init = p.n(IN)
    for dst, key in [(s_h2e, "h2e"), (s_e2h, "e2h"), (s_prot, "prot"), (s_trim, "trim")]:
        p.op("gpsimd", lambda e, d=dst, s=A[key]: e.dma_start(out=d[:, :], in_=s), inc=GIN, k=16)
    wstage = p.nc.alloc_sbuf_tensor_at(f"a{li}_wstage", [128, 2 * 3072], F32, offset=qf_off)
    pieces = []
    for k in range(8):
        pieces.append((s_wqkv[:, k * 3072:(k + 1) * 3072], A["wqkv3"][li, :, k * 3072:(k + 1) * 3072], 3072))
    for k in range(8):
        pieces.append((s_wo[:, k * 1024:(k + 1) * 1024], A["wo3"][li, :, k * 1024:(k + 1) * 1024], 1024))
    wmarks = emit_wcast(p, cm, pieces, wstage, 3072, 2)
    wm = list({s: v for (s, v) in wmarks}.items())
    p.op("vector", lambda e: e.memset(k_aug[64:65, :], 1.0), inc=VV)
    p.op("vector", lambda e: e.memset(vtm[:, 64:6 * 1040:65], 1.0), inc=VV)
    v_init = p.n(VV)

    t_end, a_end, out_end = p.n(TT_), p.n(AA), p.n(OUT)
    v_end = max(p.n(VV), v_init)
    t_pv_last2 = (0, 0)
    SL = cm.SL
    v_z_prev = 0
    t_vend_prev = 0
    v_krot_prev = 0

    for i in range(NT):
        t0 = i * tc
        bar = [(TT_, t_end), (VV, v_end), (AA, a_end), (OUT, out_end)]
        ldg = [(TT_, t_vend_prev)] if i > 0 else bar
        ldx = [(VV, v_z_prev)] if i > 0 else bar
        ldt = [(VV, v_krot_prev)] if i > 0 else bar
        if i == 0:
            vz = p.op("vector", lambda e: e.memset(xsb[:, 0:8 * 513:513], 0.0), waits=bar, inc=VV)
            for c in range(NF):
                p.op("gpsimd", lambda e, c=c: e.dma_start(
                    out=xsb[:, c * 513 + 1:(c + 1) * 513], in_=xin[c * 128:(c + 1) * 128, 0:tc]),
                    waits=ldg + [(VV, vz)], inc=GIN, k=16)
        else:
            for c in range(NF):
                p.op("gpsimd", lambda e, c=c, t0=t0: e.dma_start(
                    out=xsb[:, c * 513:(c + 1) * 513], in_=xin[c * 128:(c + 1) * 128, t0 - 1:t0 + tc]),
                    waits=ldg, inc=GIN, k=16)
        for c in range(NF):
            p.op("scalar", lambda e, c=c, t0=t0: e.dma_start(
                out=xf[:, cs(c)], in_=xin[c * 128:(c + 1) * 128, t0:t0 + tc]),
                waits=ldx, inc=SL, k=16)
        for j in range(4):
            p.op("sync", lambda e, j=j, t0=t0: e.dma_start(
                out=tabs[:, j * tc:(j + 1) * tc], in_=A["tabs"][:, j * T + t0: j * T + t0 + tc]),
                waits=ldt, inc=IN, k=16)
        in_x = p.n(SL)
        in_t = p.n(IN)
        gin_x = p.n(GIN)

        for j in range(4):
            p.op("vector", lambda e, j=j: e.tensor_scalar(
                out=tabs[:, j * tc:(j + 1) * tc], in0=tabs[:, j * tc:(j + 1) * tc],
                scalar1=qcol[:, j:j + 1], scalar2=None, op0=ALU.mult),
                waits=[(IN, in_t), (IN, in_init)], inc=(VV if j == 3 else None))
        v_eff = p.n(VV)

        def sh(c):
            return c * 513 + (1 if c < 4 else 0)

        free_mark = []
        a_sq = []
        t_n2m = []
        for g in range(16):
            m = g % 8
            pjt = pj[4 + g % 4]
            w = [(GIN, gin_x)] + (bar if g == 0 else [])
            if g >= 4:
                w = w + [free_mark[g - 4]]
            for k in range(8):
                p.op("tensor", lambda e, pjt=pjt, g=g, k=k: e.matmul(
                    pjt[:, :], s_wqkv[:, k * 3072 + g * 128: k * 3072 + (g + 1) * 128],
                    xsb[:, sh(k): sh(k) + tc], start=(k == 0), stop=(k == 7)),
                    waits=w if k == 0 else (), inc=(TT_ if k == 7 else None))
            t_g = p.n(TT_)
            dstf = qf if g < 8 else kb
            p.op("scalar", lambda e, pjt=pjt, m=m, dstf=dstf: e.copy(out=dstf[:, cs(m)], in_=pjt[:, :]),
                 waits=[(TT_, t_g), (VV, v_end)], inc=None)
            if g < 8:
                p.op("scalar", lambda e, pjt=pjt, m=m: e.copy(out=fb[:, cs(m)], in_=pjt[:, :]), inc=None)
            wsq = [(TT_, t_n2m[g - 2])] if g >= 2 else []
            a2 = p.op("scalar", lambda e, pjt=pjt, g=g: e.activation(
                out=sqq[:, (g % 2) * tc:(g % 2 + 1) * tc], in_=pjt[:, :], func=AF.Square),
                waits=wsq, inc=AA)
            a_sq.append(a2)
            pn = pj[0] if g < 8 else pj[1]
            p.op("tensor", lambda e, g=g, m=m, pn=pn: e.matmul(
                pn[0:16, :], s_e2h[:, m * 16:(m + 1) * 16], sqq[:, (g % 2) * tc:(g % 2 + 1) * tc],
                start=(m == 0), stop=(m == 7)),
                waits=[(AA, a2)] + ([(VV, v_end)] if g == 0 else []), inc=TT_)
            t_n2m.append(p.n(TT_))
            free_mark.append((AA, a2))
        t_n2 = p.n(TT_)
        a_evacs = p.n(AA)

        v_vt0 = p.n(VV)
        for g in range(8):
            tt, n = g // 2, g % 2
            pjt = pj[4 + g % 4]
            w = [free_mark[12 + g % 4]] if g < 4 else [(VV, v_vt0 + (g - 4) + 1)]
            for k in range(8):
                p.op("tensor", lambda e, pjt=pjt, tt=tt, n=n, k=k: e.matmul(
                    pjt[:, :], xsb[:, sh(k) + tt * 128: sh(k) + (tt + 1) * 128],
                    s_wqkv[:, k * 3072 + 2048 + n * 512: k * 3072 + 2048 + (n + 1) * 512],
                    start=(k == 0), stop=(k == 7)),
                    waits=w if k == 0 else (), inc=(TT_ if k == 7 else None))
            t_v = p.n(TT_)
            slot = (4 * i + tt) % 6
            p.op("vector", lambda e, pjt=pjt, slot=slot, n=n: e.tensor_copy(
                out=vtm[:, slot * 1040 + 520 * n: slot * 1040 + 520 * (n + 1)].rearrange(
                    "p (h x) -> p h x", h=8)[:, :, 0:64],
                in_=pjt[:, :].rearrange("p (h x) -> p h x", h=8)),
                waits=[(TT_, t_v)], inc=VV)
        v_vt = p.n(VV)
        t_vend_prev = p.n(TT_)

        a_sq0 = p.op("scalar", lambda e: e.activation(out=rnr[:, 0:tc], in_=pj[0][0:16, :], func=AF.Sqrt),
                     waits=[(TT_, t_n2)], inc=AA)
        a_sq1 = p.op("scalar", lambda e: e.activation(out=rnr[:, tc:2 * tc], in_=pj[1][0:16, :], func=AF.Sqrt),
                     inc=AA)
        v_r1 = p.op("vector", lambda e: e.tensor_scalar(
            out=rnr[:, :], in0=rnr[:, :], scalar1=cm.cp[0:16, CP_EPS12:CP_EPS12 + 1],
            scalar2=None, op0=ALU.max), waits=[(AA, a_sq1)], inc=VV)
        v_r2 = p.op("vector", lambda e: e.reciprocal(out=rnr[:, :], in_=rnr[:, :]),
                    waits=[(VV, v_r1)], inc=VV)
        v_rn = p.op("vector", lambda e: e.tensor_copy(out=rnb[:, :], in_=rnr[:, :]),
                    waits=[(VV, v_r2)], inc=VV)

        def rotary(src_f32, tab_off, rn_off, dest_write, mm_extra, v_guard, perm_src):
            marks = []
            t_perm_list = []
            for c in range(NF):
                pp = pj[4 + c % 2]
                pb = pj[6 + c % 2]
                w1 = [(AA, a_evacs), (VV, v_vt), (VV, v_guard)] + mm_extra
                if c >= 2:
                    w1.append((VV, marks[c - 2]))
                p.op("tensor", lambda e, c=c, pp=pp, perm_src=perm_src: e.matmul(
                    pp[:, :], s_prot[:, :], perm_src[:, cs(c)], start=True, stop=True),
                    waits=w1, inc=None)
                p.op("tensor", lambda e, c=c, pb=pb, rn_off=rn_off: e.matmul(
                    pb[:, :], s_h2e[:, c * 128:(c + 1) * 128], rnb[0:16, rn_off:rn_off + tc],
                    start=True, stop=True),
                    waits=[(VV, v_rn)], inc=TT_)
                t_c = p.n(TT_)
                t_perm_list.append(t_c)
                g1 = p.op("vector", lambda e, c=c, src_f32=src_f32, tab_off=tab_off: e.tensor_tensor(
                    out=rsc[:, (c % 2) * tc:(c % 2 + 1) * tc], in0=src_f32[:, cs(c)],
                    in1=tabs[:, tab_off:tab_off + tc], op=ALU.mult),
                    waits=[(VV, v_eff), (VV, v_guard), (AA, a_evacs)] +
                          ([(VV, marks[c - 2])] if c >= 2 else []),
                    inc=VV)
                v1 = p.op("vector", lambda e, c=c, pp=pp, tab_off=tab_off: e.tensor_tensor(
                    out=rsc2[:, (c % 2) * tc:(c % 2 + 1) * tc], in0=pp[:, :],
                    in1=tabs[:, tab_off + tc:tab_off + 2 * tc], op=ALU.mult),
                    waits=[(TT_, t_c), (VV, v_eff)] + ([(VV, marks[c - 2])] if c >= 2 else []),
                    inc=VV)
                g2 = p.op("vector", lambda e, c=c: e.tensor_tensor(
                    out=rsc[:, (c % 2) * tc:(c % 2 + 1) * tc], in0=rsc[:, (c % 2) * tc:(c % 2 + 1) * tc],
                    in1=rsc2[:, (c % 2) * tc:(c % 2 + 1) * tc], op=ALU.add),
                    waits=[(VV, g1), (VV, v1)], inc=VV)
                last = None
                for half in range(2):
                    for (dst_ap, src_sl) in dest_write(c, half):
                        last = p.op("vector",
                                    lambda e, dst_ap=dst_ap, src_sl=src_sl, c=c, half=half, pb=pb:
                                    e.tensor_tensor(
                                        out=dst_ap,
                                        in0=rsc[half * 64:(half + 1) * 64,
                                                (c % 2) * tc + src_sl.start:(c % 2) * tc + src_sl.stop],
                                        in1=pb[half * 64:(half + 1) * 64, src_sl],
                                        op=ALU.mult),
                                    waits=[(VV, g2), (TT_, t_c)], inc=VV)
                marks.append(last)
            return marks, t_perm_list

        _, t_qperm = rotary(qf, 0, 0,
                            lambda c, half: [(q_aug[0:64, (2 * c + half) * tc: (2 * c + half + 1) * tc],
                                              slice(0, tc))],
                            [], v_eff, fb)
        v_qrot = p.n(VV)

        def k_dest(c, half):
            h = 2 * c + half
            out = []
            for wl in range(2):
                ss = (2 * i + wl) % 3
                out.append((k_aug[0:64, h * 768 + ss * 256: h * 768 + (ss + 1) * 256],
                            slice(wl * 256, (wl + 1) * 256)))
            return out

        rotary(kb, 2 * tc, tc, k_dest, [], v_qrot, kb)
        v_krot = p.n(VV)

        t_tp_prev = 0
        v_attn_prev = p.n(VV)
        for wl in range(2):
            gw = 2 * i + wl
            kts = [kt for kt in range(2 * gw - 2, 2 * gw + 2) if kt >= 0]
            v_qm0 = p.n(VV)
            nh = 0
            v_red = []
            for h in range(16):
                for qt in range(2):
                    valid = [kt for kt in kts if 2 * gw - 2 + qt <= kt <= 2 * gw + qt]
                    pq = pj[4 + nh % 4]
                    w = [(VV, v_krot)]
                    if nh >= 4:
                        w.append((VV, v_red[nh - 4]))
                    else:
                        w.append((VV, v_qm0))
                    if nh == 0:
                        w.append((TT_, t_tp_prev))
                    first = True
                    for kt in valid:
                        kl = kt - (2 * gw - 2)
                        ss = (kt // 2) % 3
                        hf = kt % 2
                        p.op("tensor",
                             lambda e, h=h, qt=qt, kl=kl, ss=ss, hf=hf, pq=pq, wl=wl, first=first:
                             e.matmul(pq[:, kl * 128:(kl + 1) * 128],
                                      q_aug[0:64, h * tc + wl * 256 + qt * 128: h * tc + wl * 256 + (qt + 1) * 128],
                                      k_aug[0:64, h * 768 + ss * 256 + hf * 128: h * 768 + ss * 256 + (hf + 1) * 128],
                                      start=first, stop=(kt == valid[-1])),
                             waits=w if first else (), inc=(TT_ if kt == valid[-1] else None))
                        first = False
                    t_qm = p.n(TT_)
                    lo = (valid[0] - (2 * gw - 2)) * 128
                    hi = (valid[-1] - (2 * gw - 2) + 1) * 128
                    r = p.op("vector", lambda e, h=h, qt=qt, lo=lo, hi=hi, pq=pq: e.reduce_max(
                        out=mstack[:, h + 16 * qt: h + 16 * qt + 1],
                        in_=pq[:, lo:hi], axis=AX.X, negate=True),
                        waits=[(TT_, t_qm)], inc=VV)
                    v_red.append(r)
                    nh += 1
            g_red = v_red[-1]
            t_tp = p.op("tensor", lambda e: e.transpose(pj[2][0:32, 0:128], mstack[:, :], cm.idf[:, :]),
                        waits=[(VV, g_red), (VV, v_attn_prev)], inc=TT_)
            t_tp_prev = t_tp
            v_mT = p.op("vector", lambda e: e.tensor_copy(out=mT[:, :], in_=pj[2][0:32, 0:128]),
                        waits=[(TT_, t_tp)], inc=VV)
            reh = q_aug[64:65, :].rearrange("a (h x) -> a h x", x=512)
            for qt in range(2):
                dst_ap = reh[:, :, wl * 256 + qt * 128: wl * 256 + (qt + 1) * 128]
                p.op("sync", lambda e, dst_ap=dst_ap, qt=qt: e.dma_start(
                    out=dst_ap, in_=mT[16 * qt:16 * (qt + 1), :]),
                    waits=[(VV, v_mT)], inc=OUT, k=16)
            sc = p.n(OUT)
            t_pv = {}
            t_dtr = {}
            v_evac = {}
            a_exp = {}
            a_dw = {}
            hs = {}

            def emit_qk_exp(h):
                st = hs[h] = {}
                sb6 = 6 if h % 2 == 0 else 2
                eoff = (h % 2) * 1024
                st["sb6"], st["eoff"] = sb6, eoff
                wqk = [(OUT, sc)]
                if h >= 2:
                    wqk.append((AA, a_exp[h - 2]))
                elif h == 1:
                    wqk.append((VV, v_mT))
                else:
                    wqk.append((VV, v_attn_prev))
                t_sc = []
                firstq = True
                for kt in kts:
                    kl = kt - (2 * gw - 2)
                    ss = (kt // 2) % 3
                    hf = kt % 2
                    role = kl
                    if role == 0:
                        qsl = slice(0, 128)
                    elif role == 3:
                        qsl = slice(128, 256)
                    else:
                        qsl = slice(0, 256)
                    pb6 = pj[sb6 + kl // 2]
                    coff = (kl % 2) * 256
                    p.op("tensor",
                         lambda e, h=h, ss=ss, hf=hf, qsl=qsl, pb6=pb6, coff=coff, wl=wl:
                         e.matmul(pb6[:, coff + qsl.start: coff + qsl.stop],
                                  k_aug[0:65, h * 768 + ss * 256 + hf * 128: h * 768 + ss * 256 + (hf + 1) * 128],
                                  q_aug[0:65, h * tc + wl * 256 + qsl.start: h * tc + wl * 256 + qsl.stop],
                                  start=True, stop=True),
                         waits=wqk if firstq else (), inc=TT_)
                    firstq = False
                    t_sc.append((p.n(TT_), kl, role, qsl, pb6, coff))
                st["t_sc"] = t_sc
                t_all_qk = t_sc[-1][0]
                banks = sorted(set((kl // 2) for (_, kl, _, _, _, _) in t_sc))
                for bki, bk in enumerate(banks):
                    kls = [kl for (_, kl, _, _, _, _) in t_sc if kl // 2 == bk]
                    c0 = (min(kls) % 2) * 256
                    c1 = (max(kls) % 2) * 256 + 256
                    wexp = [(TT_, t_all_qk)]
                    if h >= 2:
                        wexp.append((TT_, t_pv[h - 2]))
                    elif i > 0 or wl > 0:
                        wexp.append((TT_, t_pv_last2[h]))
                    p.op("scalar", lambda e, bk=bk, c0=c0, c1=c1, sb6=sb6, eoff=eoff: e.activation(
                        out=expb[:, eoff + bk * 512 + c0: eoff + bk * 512 + c1],
                        in_=pj[sb6 + bk][:, c0:c1], func=AF.Exp),
                        waits=wexp, inc=(AA if bki == len(banks) - 1 else None))
                a_exp[h] = p.n(AA)
                for (tmark, kl, role, qsl, pb6, coff) in t_sc:
                    if role == 0:
                        msl, esl = slice(0, 128), slice(kl * 256, kl * 256 + 128)
                    elif role == 1:
                        msl, esl = slice(0, 128), slice(kl * 256 + 128, kl * 256 + 256)
                    elif role == 2:
                        msl, esl = slice(128, 256), slice(kl * 256, kl * 256 + 128)
                    else:
                        msl, esl = slice(128, 256), slice(kl * 256 + 128, kl * 256 + 256)
                    p.op("vector", lambda e, msl=msl, esl=esl, eoff=eoff: e.tensor_tensor(
                        out=expb[:, eoff + esl.start: eoff + esl.stop],
                        in0=expb[:, eoff + esl.start: eoff + esl.stop],
                        in1=s_trim[:, msl], op=ALU.mult),
                        waits=[(AA, a_exp[h])], inc=VV)
                st["v_msk"] = (VV, p.n(VV))

            def emit_pv_evac(h):
                st = hs[h]
                pv = pj[h % 2]
                eoff = st["eoff"]
                doff = (h % 2) * 256
                t_sc = st["t_sc"]
                firstp = True
                for j, (tmark, kl, role, qsl, pb6, coff) in enumerate(t_sc):
                    slot = (2 * gw - 2 + kl) % 6
                    wpv = []
                    if firstp:
                        wpv = [st["v_msk"]]
                        if h >= 2:
                            wpv.append((VV, v_evac[h - 2]))
                            wpv.append((AA, a_dw[h - 2]))
                    p.op("tensor",
                         lambda e, h=h, kl=kl, qsl=qsl, slot=slot, pv=pv, firstp=firstp, eoff=eoff,
                         lastp=(j == len(t_sc) - 1):
                         e.matmul(pv[0:65, qsl],
                                  vtm[:, slot * 1040 + h * 65: slot * 1040 + (h + 1) * 65],
                                  expb[:, eoff + kl * 256 + qsl.start: eoff + kl * 256 + qsl.stop],
                                  start=firstp, stop=lastp),
                         waits=wpv, inc=(TT_ if j == len(t_sc) - 1 else None))
                    firstp = False
                t_pv[h] = p.n(TT_)
                co = h // 2
                ro = (h % 2) * 64
                v_evac[h] = p.op("vector", lambda e, pv=pv, co=co, ro=ro, wl=wl: e.tensor_copy(
                    out=ab[ro:ro + 64, co * tc + wl * 256: co * tc + wl * 256 + 256],
                    in_=pv[0:64, 0:256]),
                    waits=[(TT_, t_pv[h])], inc=VV)
                a_dw[h] = p.op("scalar", lambda e, pv=pv, doff=doff: e.copy(
                    out=denw[0:1, doff:doff + 256], in_=pv[64:65, 0:256]),
                    waits=[(TT_, t_pv[h])] + ([(TT_, t_dtr[h - 2])] if h >= 2 else []), inc=AA)

            def emit_dtr(h):
                doff = (h % 2) * 256
                for qt in range(2):
                    p.op("tensor", lambda e, h=h, qt=qt, doff=doff: e.transpose(
                        pj[5][:, 2 * h + qt: 2 * h + qt + 1],
                        denw[0:1, doff + qt * 128: doff + (qt + 1) * 128],
                        cm.idf[0:1, 0:1]),
                        waits=[(AA, a_dw[h])] if qt == 0 else (), inc=(TT_ if qt == 1 else None))
                t_dtr[h] = p.n(TT_)

            for hh in range(18):
                if hh < 16:
                    emit_qk_exp(hh)
                if 1 <= hh <= 16:
                    emit_pv_evac(hh - 1)
                if 2 <= hh <= 17:
                    emit_dtr(hh - 2)
            v_rc = p.op("vector", lambda e: e.reciprocal(out=reccols[:, :], in_=pj[5][:, 0:32]),
                        waits=[(TT_, t_dtr[15])], inc=VV)
            t_rT = p.op("tensor", lambda e: e.transpose(pj[4][0:32, 0:128], reccols[:, :], cm.idf[:, :]),
                        waits=[(VV, v_rc)], inc=TT_)
            v_rT = p.op("vector", lambda e: e.tensor_copy(out=recT[:, :], in_=pj[4][0:32, 0:128]),
                        waits=[(TT_, t_rT)], inc=VV)
            scr = p.op("sync", lambda e: e.dma_start(out=recrow[:, :], in_=recT[:, :]),
                       waits=[(VV, v_rT)], inc=OUT, k=16)
            v_nrm0 = p.n(VV)
            for hp in range(8):
                pcb = pj[2 + hp % 2]
                wn = [(OUT, scr)]
                if hp >= 2:
                    wn.append((VV, v_nrm0 + 2 * (hp - 2) + 2))
                t_nb = p.op("tensor", lambda e, pcb=pcb, hp=hp: e.matmul(
                    pcb[0:64, 0:512], cm.onbr[0:1, 0:64],
                    recrow[0:1, 512 * hp: 512 * hp + 512],
                    start=True, stop=True),
                    waits=wn, inc=TT_)
                for sub in range(2):
                    h = 2 * hp + sub
                    co = h // 2
                    ro = (h % 2) * 64
                    p.op("vector", lambda e, pcb=pcb, co=co, ro=ro, wl=wl, sub=sub: e.tensor_tensor(
                        out=ab[ro:ro + 64, co * tc + wl * 256: co * tc + wl * 256 + 256],
                        in0=ab[ro:ro + 64, co * tc + wl * 256: co * tc + wl * 256 + 256],
                        in1=pcb[0:64, sub * 256:(sub + 1) * 256], op=ALU.mult),
                        waits=[(TT_, t_nb), (VV, v_evac[h])], inc=VV)
            t_pv_last2 = (t_pv[14], t_pv[15])
            v_attn_prev = p.n(VV)
        v_attn = p.n(VV)
        a_attn = p.n(AA)

        t_wo = []
        v_wo0 = p.n(VV)
        for m in range(8):
            w = [(VV, v_attn), (AA, a_attn)] + wm
            if m >= 4:
                w.append((VV, v_wo0 + (m - 4) + 1))
            for k in range(8):
                p.op("tensor", lambda e, m=m, k=k: e.matmul(
                    pj[4 + m % 4][:, :], s_wo[:, k * D + m * 128: k * D + (m + 1) * 128], ab[:, cs(k)],
                    start=(k == 0), stop=(k == 7)),
                    waits=w if k == 0 else (), inc=(TT_ if k == 7 else None))
            t_wo.append(p.n(TT_))
        for m in range(8):
            p.op("vector", lambda e, m=m: e.scalar_tensor_tensor(
                out=zt[:, cs(m)], in0=pj[4 + m % 4][:, :], scalar=cm.cp[:, CP_ZERO:CP_ZERO + 1],
                in1=xf[:, cs(m)], op0=ALU.bypass, op1=ALU.add),
                waits=[(TT_, t_wo[m]), (SL, in_x)], inc=VV)
        v_z = p.n(VV)
        v_z_prev = v_z
        v_krot_prev = v_krot
        for c in range(NF):
            p.op("sync", lambda e, c=c, t0=t0: e.dma_start(
                out=xout[c * 128:(c + 1) * 128, t0:t0 + tc], in_=zt[:, cs(c)]),
                waits=[(VV, v_z)], inc=OUT, k=16)

        t_end, v_end, a_end, out_end = p.n(TT_), p.n(VV), p.n(AA), p.n(OUT)



def _t(a):
    return np.ascontiguousarray(np.asarray(a).T)


def _col8(vec, n=8):
    return np.ascontiguousarray(np.asarray(vec).reshape(n, 128).T)


def _chunked(a):
    a = np.asarray(a)
    k = a.shape[0] // 128
    return np.ascontiguousarray(a.reshape(k, 128, a.shape[1]).transpose(1, 0, 2).reshape(128, -1))


def p0_inputs(inp, b):
    beta_rep = np.repeat(np.asarray(inp["L0_attn_beta"]), 64)
    return {
        "x0": _t(inp["L0_attn_x"][b]),
        "wq": _t(inp["L0_attn_Wq"]),
        "wv": _t(inp["L0_attn_Wv"]),
        "wg": _t(inp["L0_attn_Wg"]),
        "wo": _t(inp["L0_attn_Wo"]),
        "lamsrc": _col8(beta_rep),
        "g8a": _col8(inp["L0_attn_ln_gamma"]),
        "b8a": _col8(inp["L0_attn_ln_beta"]),
    }


def p1_inputs(inp, b=None):
    import ml_dtypes
    sel = np.zeros((16, 16 * 128), dtype=ml_dtypes.bfloat16)
    for e in range(16):
        sel[e, e * 128:(e + 1) * 128] = 1
    return {
        "rwt": _chunked(_t(inp["L0_ffn_router_w"])),
        "rb16": np.asarray(inp["L0_ffn_router_b"], np.float32).reshape(16, 1),
        "w1t": _t(inp["L0_ffn_w1"]),
        "b1c": _col8(inp["L0_ffn_b1"], 32),
        "w2t": _t(inp["L0_ffn_w2"]),
        "b2c": _col8(inp["L0_ffn_b2"]),
        "sel16s": sel,
    }


def p2_inputs(inp, b=None):
    import ml_dtypes
    alpha_rep = np.repeat(np.asarray(inp["L1_attn_alpha"]), 64)
    beta_rep = np.repeat(np.asarray(inp["L1_attn_beta"]), 64)
    ab = np.concatenate([_col8(alpha_rep), _col8(beta_rep)], axis=1)
    e2h = np.zeros((128, 8 * 16), dtype=ml_dtypes.bfloat16)
    h2e = np.zeros((16, 8 * 128), dtype=ml_dtypes.bfloat16)
    for c in range(8):
        for pp in range(128):
            h = 2 * c + pp // 64
            e2h[pp, c * 16 + h] = 1
            h2e[h, c * 128 + pp] = 1
    return {
        "ab_rep": np.ascontiguousarray(ab, np.float32),
        "e2h": e2h,
        "h2e": h2e,
        "wqkv": _chunked(np.asarray(inp["L1_attn_w_qkv"])),
        "wout": _chunked(np.asarray(inp["L1_attn_w_out"])),
    }


def p3_ffn_inputs(inp):
    W1 = np.asarray(inp["L1_ffn_W1"])
    W2 = np.asarray(inp["L1_ffn_W2"])
    b1 = np.asarray(inp["L1_ffn_b1"])
    w1s = np.zeros((6, 128, W1_COLS), np.float32)
    for li in range(6):
        for c in range(22):
            wc = W1_WC[c]
            blocks = []
            for k in range(8):
                blocks.append(W1[li, k * 128:(k + 1) * 128, c * 128:c * 128 + wc])
                blocks.append(W1[li, k * 128:(k + 1) * 128, 2730 + c * 128:2730 + c * 128 + wc])
            w1s[li, :, W1_OFF[c]:W1_OFF[c + 1]] = np.concatenate(blocks, axis=1)
    w2s = np.zeros((6, 128, 22 * 1024), np.float32)
    for li in range(6):
        for c in range(22):
            wc = W1_WC[c]
            w2s[li, 0:wc, c * 1024:(c + 1) * 1024] = W2[li, c * 128:c * 128 + wc, :]
    b1a = np.zeros((6, 128, 22), np.float32)
    b1g = np.zeros((6, 128, 22), np.float32)
    for li in range(6):
        for c in range(22):
            wc = W1_WC[c]
            b1a[li, 0:wc, c] = b1[li, c * 128:c * 128 + wc]
            b1g[li, 0:wc, c] = b1[li, 2730 + c * 128:2730 + c * 128 + wc]
    return {
        "w1s": w1s,
        "w2s": w2s,
        "b1a": b1a,
        "b1g": b1g,
        "b2c3": np.stack([_col8(np.asarray(inp["L1_ffn_b2"])[li]) for li in range(6)]),
        "lng": np.stack([_col8(np.asarray(inp["L1_ffn_ln_g"])[li]) for li in range(6)]),
        "lnb": np.stack([_col8(np.asarray(inp["L1_ffn_ln_b"])[li]) for li in range(6)]),
    }


def p3_attn_inputs(inp):
    import ml_dtypes
    Wqkv = np.asarray(inp["L1_ffn_Wqkv"])
    Wo = np.asarray(inp["L1_ffn_Wo"])
    qs = np.asarray(inp["L1_ffn_q_scale"])
    ks = np.asarray(inp["L1_ffn_k_scale"])
    wqkv3 = np.stack([_chunked(Wqkv[li]) for li in range(6)])
    wo3 = np.stack([_chunked(Wo[li]) for li in range(6)])
    half = np.arange(0, 64, 2, dtype=np.float64)
    inv_freq = 1.0 / 10000.0 ** (half / 64)
    j = np.arange(64)
    freq_j = inv_freq[j % 32]
    t = np.arange(T, dtype=np.float64)
    freqs = t[:, None] * freq_j[None, :]
    base_scale = (half + 0.4 * 64) / (1.4 * 64)
    bs_j = base_scale[j % 32]
    power = (t - T // 2) / 128.0
    scale = bs_j[None, :] ** power[:, None]
    sign = np.where(j < 32, -1.0, 1.0)
    cosf = np.cos(freqs)
    sinf = np.sin(freqs)
    t1q = 8.0 * cosf * scale
    t2q = 8.0 * sign[None, :] * sinf * scale
    t1k = cosf / scale
    t2k = sign[None, :] * sinf / scale
    tabs = np.zeros((128, 4 * T), np.float32)
    for jj, tab in enumerate([t1q, t2q, t1k, t2k]):
        block = tab.T.astype(np.float32)
        tabs[0:64, jj * T:(jj + 1) * T] = block
        tabs[64:128, jj * T:(jj + 1) * T] = block
    qcols = np.zeros((128, 4 * 6), np.float32)
    perm64 = (j + 32) % 64
    for li in range(6):
        for (jj, vec) in enumerate([qs[li], qs[li][perm64], ks[li], ks[li][perm64]]):
            qcols[0:64, 4 * li + jj] = vec
            qcols[64:128, 4 * li + jj] = vec
    prot = np.zeros((128, 128), dtype=ml_dtypes.bfloat16)
    for m in range(128):
        k = (m // 64) * 64 + ((m % 64) + 32) % 64
        prot[k, m] = 1
    trim = np.zeros((128, 256), dtype=ml_dtypes.bfloat16)
    for jj in range(128):
        trim[jj, 0:128][np.arange(128) <= jj] = 1
        trim[jj, 128:256][np.arange(128) >= jj] = 1
    e2h = np.zeros((128, 8 * 16), dtype=ml_dtypes.bfloat16)
    h2e = np.zeros((16, 8 * 128), dtype=ml_dtypes.bfloat16)
    for c in range(8):
        for pp in range(128):
            h = 2 * c + pp // 64
            e2h[pp, c * 16 + h] = 1
            h2e[h, c * 128 + pp] = 1
    return {"wqkv3": wqkv3, "wo3": wo3, "tabs": tabs, "qcols": qcols, "prot": prot,
            "trim": trim, "e2h_a": e2h, "h2e_a": h2e}


def dram_io(nc, names_in, names_out, shapes):
    d = {}
    for n in names_in:
        d[n] = nc.dram_tensor(n, shapes[n], BF16 if n in ("g_onb", "g_onbr", "sel16s", "e2h", "h2e", "prot", "trim") else F32,
                              kind="ExternalInput").ap()
    for n in names_out:
        d[n] = nc.dram_tensor(n, shapes[n], F32, kind="ExternalOutput").ap()
    return d


COMMON_SHAPES = {"g_cp": [128, 12], "g_onbr": [1, 128], "g_onb": [128, 1], "g_onf": [1, 128], "g_idf": [128, 128]}


def build_p0_solo():
    p = Prog()
    shapes = dict(COMMON_SHAPES)
    shapes.update({"x0": [D, T], "wq": [D, D], "wv": [D, D], "wg": [D, D], "wo": [D, D],
                   "lamsrc": [128, 8], "g8a": [128, 8], "b8a": [128, 8], "x1": [D, T]})
    A = dram_io(p.nc, ["g_cp", "g_onb", "g_onf", "g_idf", "g_onbr", "x0", "wq", "wv", "wg", "wo",
                       "lamsrc", "g8a", "b8a"], ["x1"], shapes)
    cm = Common(p, A)
    p.mark_global()
    emit_p0(p, cm, A)
    p.op("sync", None, waits=[(cm.OUT, p.n(cm.OUT))])
    return p.build()


def build_p1_solo():
    p = Prog()
    shapes = dict(COMMON_SHAPES)
    shapes.update({"x1": [D, T], "rwt": [128, 128], "rb16": [16, 1], "w1t": [D, 4096],
                   "b1c": [128, 32], "w2t": [4096, D], "b2c": [128, 8],
                   "sel16s": [16, 16 * 128], "x2": [D, T]})
    A = dram_io(p.nc, ["g_cp", "g_onb", "g_onf", "g_idf", "g_onbr", "x1", "rwt", "rb16", "w1t",
                       "b1c", "w2t", "b2c", "sel16s"], ["x2"], shapes)
    cm = Common(p, A)
    p.mark_global()
    emit_p1(p, cm, A)
    p.op("sync", None, waits=[(cm.OUT, p.n(cm.OUT))])
    return p.build()


def run_p0(inp):
    nc = build_p0_solo()
    cmh = common_host_inputs()
    in_maps = [dict(cmh, **p0_inputs(inp, b)) for b in range(NCORE)]
    res = run_bass_kernel_spmd(nc, in_maps, core_ids=list(range(NCORE)))
    return np.stack([np.ascontiguousarray(r["x1"].T) for r in res.results])


def build_p1_solo_dbg():
    p = Prog()
    shapes = dict(COMMON_SHAPES)
    shapes.update({"x1": [D, T], "rwt": [128, 128], "rb16": [16, 1], "w1t": [D, 4096],
                   "b1c": [128, 32], "w2t": [4096, D], "b2c": [128, 8],
                   "sel16s": [16, 16 * 128], "x2": [D, T],
                   "dbg_rl": [16, TC], "dbg_rlT": [128, 64], "dbg_mtm": [128, 64],
                   "dbg_mbf": [16, TC], "dbg_hb": [128, 4 * TC]})
    A = dram_io(p.nc, ["g_cp", "g_onb", "g_onf", "g_idf", "g_onbr", "x1", "rwt", "rb16", "w1t",
                       "b1c", "w2t", "b2c", "sel16s"],
                ["x2", "dbg_rl", "dbg_rlT", "dbg_mtm", "dbg_mbf", "dbg_hb"], shapes)
    A["dbg_mbf"] = p.nc.dram_tensor("dbg_mbf_b", [16, TC], BF16, kind="ExternalOutput").ap()
    A["dbg_hb"] = p.nc.dram_tensor("dbg_hb_b", [128, 4 * TC], BF16, kind="ExternalOutput").ap()
    cm = Common(p, A)
    p.mark_global()
    emit_p1(p, cm, A)
    p.op("sync", None, waits=[(cm.OUT, p.n(cm.OUT))])
    return p.build()


def run_p1_dbg(inp, x1_full):
    nc = build_p1_solo_dbg()
    cmh = common_host_inputs()
    w = p1_inputs(inp)
    in_maps = [dict(cmh, **w, x1=_t(x1_full[b])) for b in range(1)] * 1
    res = run_bass_kernel_spmd(nc, in_maps, core_ids=[0])
    return res.results[0]


def build_p2_solo():
    p = Prog()
    shapes = dict(COMMON_SHAPES)
    shapes.update({"x2": [D, T], "ab_rep": [128, 16], "e2h": [128, 8 * 16], "h2e": [16, 8 * 128],
                   "wqkv": [128, 8 * 3072], "wout": [128, 16 * D], "x3": [D, T]})
    A = dram_io(p.nc, ["g_cp", "g_onb", "g_onf", "g_idf", "g_onbr", "x2", "ab_rep", "e2h", "h2e",
                       "wqkv", "wout"], ["x3"], shapes)
    cm = Common(p, A)
    p.mark_global()
    emit_p2(p, cm, A)
    p.op("sync", None, waits=[(cm.OUT, p.n(cm.OUT))])
    return p.build()


def run_p2(inp, x2_full):
    nc = build_p2_solo()
    cmh = common_host_inputs()
    w = p2_inputs(inp)
    in_maps = [dict(cmh, **w, x2=_t(x2_full[b])) for b in range(NCORE)]
    res = run_bass_kernel_spmd(nc, in_maps, core_ids=list(range(NCORE)))
    return np.stack([np.ascontiguousarray(r["x3"].T) for r in res.results])


def build_p3f_solo(li, final=False):
    p = Prog()
    shapes = dict(COMMON_SHAPES)
    shapes.update({"xin": [D, T], "w1s": [1, 128, W1_COLS], "w2s": [1, 128, 22 * D],
                   "b1a": [1, 128, 22], "b1g": [1, 128, 22], "b2c3": [1, 128, 8],
                   "lng": [1, 128, 8], "lnb": [1, 128, 8], "x3s": [D, T], "xo": [D, T]})
    names = ["g_cp", "g_onb", "g_onf", "g_idf", "g_onbr", "xin", "w1s", "w2s", "b1a", "b1g",
             "b2c3", "lng", "lnb"] + (["x3s"] if final else [])
    A = dram_io(p.nc, names, ["xo"], shapes)
    cm = Common(p, A)
    p.mark_global()
    emit_p3f(p, cm, A, 0, A["xin"], A["xo"], final=final, x3s=A.get("x3s"))
    p.op("sync", None, waits=[(cm.OUT, p.n(cm.OUT))])
    return p.build()


def run_p3f(inp, x_full, li, final=False, x3_full=None):
    nc = build_p3f_solo(li, final)
    cmh = common_host_inputs()
    w = slice_p3f_w(p3_ffn_inputs(inp), li)
    in_maps = []
    for b in range(NCORE):
        m = dict(cmh, **w, xin=_t(x_full[b]))
        if final:
            m["x3s"] = _t(x3_full[b])
        in_maps.append(m)
    res = run_bass_kernel_spmd(nc, in_maps, core_ids=list(range(NCORE)))
    return np.stack([np.ascontiguousarray(r["xo"].T) for r in res.results])


def build_p3a_solo(li):
    p = Prog()
    shapes = dict(COMMON_SHAPES)
    shapes.update({"xin": [D, T], "wqkv3": [1, 128, 8 * 3072], "wo3": [1, 128, 8 * D],
                   "tabs": [128, 4 * T], "qcols": [128, 4], "prot": [128, 128],
                   "trim": [128, 256], "e2h": [128, 8 * 16], "h2e": [16, 8 * 128],
                   "xo": [D, T]})
    A = dram_io(p.nc, ["g_cp", "g_onb", "g_onf", "g_idf", "g_onbr", "xin", "wqkv3", "wo3", "tabs",
                       "qcols", "prot", "trim", "e2h", "h2e"], ["xo"], shapes)
    cm = Common(p, A)
    p.mark_global()
    emit_p3a(p, cm, A, 0, A["xin"], A["xo"])
    p.op("sync", None, waits=[(cm.OUT, p.n(cm.OUT))])
    return p.build()


def slice_p3a_w(w, li):
    w = dict(w)
    w["e2h"] = w.pop("e2h_a", w.get("e2h"))
    w["h2e"] = w.pop("h2e_a", w.get("h2e"))
    w["wqkv3"] = w["wqkv3"][li:li + 1]
    w["wo3"] = w["wo3"][li:li + 1]
    w["qcols"] = np.ascontiguousarray(w["qcols"][:, 4 * li:4 * li + 4])
    return w


def slice_p3f_w(w, li):
    w = dict(w)
    for k in ("w1s", "w2s", "b1a", "b1g", "b2c3", "lng", "lnb"):
        w[k] = w[k][li:li + 1]
    return w


def run_p3a(inp, x_full, li):
    nc = build_p3a_solo(li)
    cmh = common_host_inputs()
    w = slice_p3a_w(p3_attn_inputs(inp), li)
    in_maps = [dict(cmh, **w, xin=_t(x_full[b])) for b in range(NCORE)]
    res = run_bass_kernel_spmd(nc, in_maps, core_ids=list(range(NCORE)))
    return np.stack([np.ascontiguousarray(r["xo"].T) for r in res.results])


def run_p1(inp, x1_full):
    nc = build_p1_solo()
    cmh = common_host_inputs()
    w = p1_inputs(inp)
    in_maps = [dict(cmh, **w, x1=_t(x1_full[b])) for b in range(NCORE)]
    res = run_bass_kernel_spmd(nc, in_maps, core_ids=list(range(NCORE)))
    return np.stack([np.ascontiguousarray(r["x2"].T) for r in res.results])


def _run(nc, in_maps, outkey):
    res = run_bass_kernel_spmd(nc, in_maps, core_ids=list(range(NCORE)))
    return [r[outkey] for r in res.results]





def build_fused():
    p = Prog()
    nc = p.nc
    shapes = dict(COMMON_SHAPES)
    shapes.update({
        "x0": [D, T],
        "wq": [D, D], "wv": [D, D], "wg": [D, D], "wo": [D, D],
        "lamsrc": [128, 8], "g8a": [128, 8], "b8a": [128, 8],
        "rwt": [128, 128], "rb16": [16, 1], "w1t": [D, 4096], "b1c": [128, 32],
        "w2t": [4096, D], "b2c": [128, 8], "sel16s": [16, 16 * 128],
        "ab_rep": [128, 16], "e2h": [128, 8 * 16], "h2e": [16, 8 * 128],
        "wqkv": [128, 8 * 3072], "wout": [128, 16 * D],
        "wqkv3": [6, 128, 8 * 3072], "wo3": [6, 128, 8 * D],
        "tabs": [128, 4 * T], "qcols": [128, 24], "prot": [128, 128], "trim": [128, 256],
        "w1s": [6, 128, W1_COLS], "w2s": [6, 128, 22 * D],
        "b1a": [6, 128, 22], "b1g": [6, 128, 22], "b2c3": [6, 128, 8],
        "lng": [6, 128, 8], "lnb": [6, 128, 8],
    })
    names_in = list(shapes.keys())
    A = dram_io(nc, names_in, ["x4"], dict(shapes, x4=[D, T]))
    sA = nc.dram_tensor("sA", [D, T], F32).ap()
    sB = nc.dram_tensor("sB", [D, T], F32).ap()
    sC = nc.dram_tensor("sC", [D, T], F32).ap()
    cm = Common(p, A)
    p.mark_global()

    A0 = dict(A)
    A0["x0"], A0["x1"] = A["x0"], sA
    emit_p0(p, cm, A0)
    A1 = dict(A)
    A1["x1"], A1["x2"] = sA, sB
    emit_p1(p, cm, A1)
    A2 = dict(A)
    A2["x2"], A2["x3"] = sB, sC
    emit_p2(p, cm, A2)
    bufs = [sC, sA, sB]
    cur = 0
    for li in range(6):
        nxt = 1 if bufs[cur] is not sA else 2
        cand = [b for b in (sA, sB) if b is not bufs[cur]]
        xo = cand[0]
        emit_p3a(p, cm, A, li, bufs[cur], xo)
        cur = bufs.index(xo)
        cand = [b for b in (sA, sB) if b is not bufs[cur]]
        xo2 = A["x4"] if li == 5 else cand[0]
        emit_p3f(p, cm, A, li, bufs[cur], xo2, final=(li == 5), x3s=sC)
        if li < 5:
            cur = bufs.index(xo2)
    p.op("sync", None, waits=[(cm.OUT, p.n(cm.OUT))])
    return p.build()


def fused_inputs(inputs):
    cmh = common_host_inputs()
    w0 = p0_inputs(inputs, 0)
    del w0["x0"]
    w1 = p1_inputs(inputs)
    w2 = p2_inputs(inputs)
    wa = p3_attn_inputs(inputs)
    wa = dict(wa)
    wa.pop("e2h_a")
    wa.pop("h2e_a")
    wf = p3_ffn_inputs(inputs)
    base = {}
    for d in (cmh, w0, w1, w2, wa, wf):
        base.update(d)
    return base


_FUSED_NC = None


def kernel(**inputs):
    global _FUSED_NC
    if _FUSED_NC is None:
        _FUSED_NC = build_fused()
    base = fused_inputs(inputs)
    x = np.asarray(inputs["L0_attn_x"])
    in_maps = [dict(base, x0=_t(x[b])) for b in range(NCORE)]
    res = run_bass_kernel_spmd(_FUSED_NC, in_maps, core_ids=list(range(NCORE)))
    return np.stack([np.ascontiguousarray(r["x4"].T) for r in res.results]).astype(np.float32)


def kernel_unfused(**inputs):
    cmh = common_host_inputs()
    xT = [_t(np.asarray(inputs["L0_attn_x"])[b]) for b in range(NCORE)]

    w0 = p0_inputs(inputs, 0)
    del w0["x0"]
    x1 = _run(build_p0_solo(), [dict(cmh, **w0, x0=xT[b]) for b in range(NCORE)], "x1")

    w1 = p1_inputs(inputs)
    x2 = _run(build_p1_solo(), [dict(cmh, **w1, x1=x1[b]) for b in range(NCORE)], "x2")

    w2 = p2_inputs(inputs)
    x3 = _run(build_p2_solo(), [dict(cmh, **w2, x2=x2[b]) for b in range(NCORE)], "x3")

    wa = p3_attn_inputs(inputs)
    wf = p3_ffn_inputs(inputs)
    x = x3
    for li in range(6):
        wal = slice_p3a_w(wa, li)
        x = _run(build_p3a_solo(li), [dict(cmh, **wal, xin=x[b]) for b in range(NCORE)], "xo")
        wfl = slice_p3f_w(wf, li)
        maps = []
        for b in range(NCORE):
            m = dict(cmh, **wfl, xin=x[b])
            if li == 5:
                m["x3s"] = x3[b]
            maps.append(m)
        x = _run(build_p3f_solo(li, final=(li == 5)), maps, "xo")

    return np.stack([np.ascontiguousarray(x[b].T) for b in range(NCORE)]).astype(np.float32)
